# version 13
# baseline (speedup 1.0000x reference)
import os
import numpy as np
import ml_dtypes
from contextlib import ExitStack

import concourse.bass as bass
import concourse.mybir as mybir
from concourse.bass_utils import run_bass_kernel_spmd

F32 = mybir.dt.float32
BF16 = mybir.dt.bfloat16
I32 = mybir.dt.int32
ALU = mybir.AluOpType
AT = mybir.ActivationFunctionType
AX = mybir.AxisListType
AP = bass.AP

B, T, D = 16, 2048, 1024
NCORES = 8
NSEQ = 2
NCH = D // 128
NTC = T // 128
NQT = T // 512
H0, HD0 = 16, 64
H1, FD1, HD1 = 16, 16, 64
INNER = 4096
NIC = INNER // 128

DBG = int(os.environ.get("KDBG", "0"))


class Ctx:
    def __init__(self, nc, stack):
        self.nc = nc
        self.stack = stack
        self.sem = {}
        self.cnt = {}

    def S(self, name):
        if name not in self.sem:
            self.sem[name] = self.stack.enter_context(self.nc.semaphore(name))
            self.cnt[name] = 0
        return self.sem[name]

    def inc(self, instr, name, k=1):
        instr.then_inc(self.S(name), k)
        self.cnt[name] += k
        return self.cnt[name]

    def wait(self, eng, name, at=None):
        eng.wait_ge(self.S(name), self.cnt[name] if at is None else at)

    def base(self, *names):
        for n in names:
            self.S(n)
        return {n: self.cnt[n] for n in names}

    def dma_fence(self, *names):
        with self.nc.Block() as blk:
            def s_sync(sync):
                for n in names:
                    if n in self.cnt:
                        sync.wait_ge(self.S(n), self.cnt[n])
            blk.sync(s_sync)



def part_max_32(c, vector, tile, n, tmp=None):
    buf = tile if tmp is None else tmp
    off = 32 if tmp is None else 0
    for hh in (16, 8, 4, 2, 1):
        mask = [(i + hh) % 32 for i in range(32)]
        vector.stream_shuffle(buf[off:off + 32, 0:n], tile[0:32, 0:n], mask)
        vector.drain()
        vector.tensor_tensor(tile[0:32, 0:n], tile[0:32, 0:n], buf[off:off + 32, 0:n], ALU.max)
        vector.drain()


def part_tree(c, vector, tile, tmp, n):
    for h in (64, 32):
        vector.drain()
        vector.tensor_copy(tmp[0:h, 0:n], tile[h:2 * h, 0:n])
        vector.drain()
        vector.tensor_tensor(tile[0:h, 0:n], tile[0:h, 0:n], tmp[0:h, 0:n], ALU.max)
    vector.drain()
    part_max_32(c, vector, tile, n, tmp=tmp)


def bcast_row(eng_dma, c, dst_tile, row_ap, n, sem, parts=128, dst_off=0):
    assert int(row_ap.offset) == 0, "bcast source must be offset-0"
    src = AP(row_ap.tensor, 0, [[1, 1], [0, parts], [1, n]])
    return c.inc(eng_dma.dma_start(out=dst_tile[0:parts, dst_off:dst_off + n], in_=src), sem, 16)


def build_nc():
    nc = bass.Bass("TRN2", target_bir_lowering=False)
    st = ExitStack()
    c = Ctx(nc, st)

    x_in = nc.dram_tensor("x", [NSEQ, D, T], F32, kind="ExternalInput")
    out_t = nc.dram_tensor("out", [NSEQ, D, T], F32, kind="ExternalOutput")
    wT = {}
    for nm, sh in [("wqT", [D, D]), ("wkT", [D, D]), ("wvT", [D, D]),
                   ("woT", [D, D]), ("rhsw", [D, D]),
                   ("wq2T", [D, 256]), ("wk2T", [D, 256]),
                   ("wv2T", [D, D]), ("wo2T", [D, D]),
                   ("wgT", [D, INNER]), ("wuT", [D, INNER]), ("wdT", [INNER, D])]:
        wT[nm] = nc.dram_tensor(nm, sh, F32, kind="ExternalInput")
    bias_in = nc.dram_tensor("bqkvo", [128, 4 * NCH], F32, kind="ExternalInput")
    relk_in = nc.dram_tensor("relk", [41, 64], F32, kind="ExternalInput")
    relv_in = nc.dram_tensor("relv", [41, 64], F32, kind="ExternalInput")
    relvf_in = nc.dram_tensor("relvf", [41, 64], F32, kind="ExternalInput")
    gb_in = nc.dram_tensor("gb16", [2, 16], F32, kind="ExternalInput")
    ident_in = nc.dram_tensor("ident", [128, 128], BF16, kind="ExternalInput")
    triu_in = nc.dram_tensor("triu64", [64, 64], BF16, kind="ExternalInput")
    adm_in = nc.dram_tensor("adm", [128, 276], BF16, kind="ExternalInput")
    triu128_in = nc.dram_tensor("triu128p", [128, 128], BF16, kind="ExternalInput")
    ehd_in = nc.dram_tensor("ehd", [16, 8 * 128], F32, kind="ExternalInput")

    h_x1 = nc.dram_tensor("h_x1", [NSEQ, NCH, 128, T], F32, kind="Internal")
    h_xn1 = nc.dram_tensor("h_xn1", [NSEQ, NCH, 128, T], BF16, kind="Internal")
    h_s1 = nc.dram_tensor("h_s1", [NSEQ, NCH, 128, T], BF16, kind="Internal")
    h_rhsq = nc.dram_tensor("h_rhsq", [NCH, 128, D], BF16, kind="Internal")
    h_qt = nc.dram_tensor("h_qt", [H0, 64, T], BF16, kind="Internal")
    h_kt = nc.dram_tensor("h_kt", [H0, 64, T], BF16, kind="Internal")
    h_va = nc.dram_tensor("h_va", [NTC, 128, H0 * 65], BF16, kind="Internal")
    h_ps = nc.dram_tensor("h_ps", [2, 41, 2130], BF16, kind="Internal")
    h_gp = nc.dram_tensor("h_gp", [2, 1, 2208], BF16, kind="Internal")
    h_zz = nc.dram_tensor("h_zz", [1, 4096], BF16, kind="Internal")
    h_rowf = nc.dram_tensor("h_rowf", [1, 4096], F32, kind="Internal")
    h_c = nc.dram_tensor("h_c", [2, NTC, 128, 464], BF16, kind="Internal")
    h_strip = nc.dram_tensor("h_strip", [NTC, 128, 404], BF16, kind="Internal")
    h_h6 = nc.dram_tensor("h_h6", [276, 2624], BF16, kind="Internal")
    h_gu = nc.dram_tensor("h_gu", [NIC, 128, T], F32, kind="Internal")
    h_wgq = nc.dram_tensor("h_wgq", [NCH, 128, INNER], BF16, kind="Internal")
    h_wuq = nc.dram_tensor("h_wuq", [NCH, 128, INNER], BF16, kind="Internal")
    h_wdq = nc.dram_tensor("h_wdq", [NCH, NIC, 128, 128], BF16, kind="Internal")

    dbg = {}
    if DBG:
        for nm, sh, dt in [("ln0", [NSEQ, NCH, 128, T], BF16),
                           ("rows", [NSEQ, 16, T], F32),
                           ("qrhs", [NCH, 128, D], BF16),
                           ("qt0", [64, 2 * T], BF16),
                           ("aslab", [128, NTC * T], BF16), ("oslab", [128, NCH * T], BF16), ("zs", [16, T], F32),
                           ("s0", [NSEQ, NCH, 128, T], BF16),
                           ("g0", [NSEQ, NCH, 128, T], F32),
                           ("s1", [NSEQ, NCH, 128, T], BF16),
                           ("ufm", [NSEQ, 128, 16 * 512], BF16),
                           ("val1", [NSEQ, 128, 16 * 16 * 65], BF16),
                           ("zs1", [NSEQ, 16, T], F32),
                           ("g1", [NSEQ, NCH, 128, T], F32)]:
            dbg[nm] = nc.dram_tensor("dbg_" + nm, sh, dt, kind="ExternalOutput")

    bias_sb = st.enter_context(nc.sbuf_tensor("bias_sb", [128, 4 * NCH], F32))
    ident = st.enter_context(nc.sbuf_tensor("sb_ident", [128, 128], BF16))
    triu64 = st.enter_context(nc.sbuf_tensor("sb_triu64", [64, 64], BF16))
    adm = st.enter_context(nc.sbuf_tensor("sb_adm", [128, 276], BF16))
    triu128p = st.enter_context(nc.sbuf_tensor("sb_triu128p", [128, 128], BF16))
    ehd = st.enter_context(nc.sbuf_tensor("sb_ehd", [16, 8 * 128], F32))
    relkT = st.enter_context(nc.sbuf_tensor("relkT", [64, 41], BF16))
    relk_m40T = st.enter_context(nc.sbuf_tensor("relk_m40T", [64, 41], BF16))
    relcols = st.enter_context(nc.sbuf_tensor("relcols", [64, 2], F32))
    relv40r = st.enter_context(nc.sbuf_tensor("relv40r", [1, 64], BF16))
    dvrow = st.enter_context(nc.sbuf_tensor("dvrow", [1, 64], BF16))
    relvp = st.enter_context(nc.sbuf_tensor("relvp", [128, 128], BF16))
    gbrow = st.enter_context(nc.sbuf_tensor("gbrow", [2, 16], F32))
    gbhop = st.enter_context(nc.sbuf_tensor("gbhop", [1, 32], F32))
    swrow = st.enter_context(nc.sbuf_tensor("swrow", [1, 8], F32))
    swcol = st.enter_context(nc.sbuf_tensor("swcol", [128, 8], F32))
    gbig = st.enter_context(nc.sbuf_tensor("gbig", [128, 32], F32))
    onesrow = st.enter_context(nc.sbuf_tensor("onesrow", [1, 128], BF16))
    onescol_f = st.enter_context(nc.sbuf_tensor("onescol_f", [128, 1], F32))
    onescol_b = st.enter_context(nc.sbuf_tensor("onescol_b", [128, 1], BF16))
    rows = st.enter_context(nc.sbuf_tensor("rows", [16, T], F32))
    grows = st.enter_context(nc.sbuf_tensor("grows", [4, 1024], F32))
    cvals = st.enter_context(nc.sbuf_tensor("cvals", [128, 4], F32))
    R_A0, R_B0, R_LS0, R_A1, R_B1, R_LS1, R_T1, R_T2 = range(8)

    def row(s, r):
        return rows[8 * s + r:8 * s + r + 1, :]

    def load_weights_bf16(ps, tag, specs):
        f32x = ps.enter_context(nc.sbuf_tensor(f"wl_a_{tag}", [128, 1024], F32))
        f32y = ps.enter_context(nc.sbuf_tensor(f"wl_b_{tag}", [128, 1024], F32))
        with nc.Block() as blk:
            b0 = c.base('wld0', 'wld1', 'wcast')
            items = []
            for spec in specs:
                nm, dst, nch = spec[0], spec[1], spec[2]
                w = spec[3] if len(spec) > 3 else 1024
                stride = spec[4] if len(spec) > 4 else w
                off = spec[5] if len(spec) > 5 else 0
                for ch in range(nch):
                    items.append((nm, dst, ch, w, stride, off))

            def s_sync(sync):
                for n, (nm, dst, ch, w, stride, off) in enumerate(items):
                    if n >= 2:
                        sync.wait_ge(c.S('wcast'), b0['wcast'] + n - 1)
                    buf = f32x if n % 2 == 0 else f32y
                    c.inc(sync.dma_start(out=buf[:, 0:w], in_=wT[nm].ap()[ch * 128:(ch + 1) * 128, :]),
                          'wld%d' % (n % 2), 16)
            blk.sync(s_sync)

            def s_vector(vector):
                for n, (nm, dst, ch, w, stride, off) in enumerate(items):
                    vector.wait_ge(c.S('wld%d' % (n % 2)), b0['wld%d' % (n % 2)] + 16 * (n // 2 + 1))
                    buf = f32x if n % 2 == 0 else f32y
                    c.inc(vector.tensor_copy(dst[:, off + ch * stride:off + ch * stride + w], buf[:, 0:w]), 'wcast', 1)
            blk.vector(s_vector)

    with ExitStack() as ps:
        f32a = ps.enter_context(nc.sbuf_tensor("p0_f32a", [128, 1024], F32))
        f32b = ps.enter_context(nc.sbuf_tensor("p0_f32b", [128, 1024], F32))
        relf = ps.enter_context(nc.sbuf_tensor("p0_relf", [41, 192], F32))
        relb = ps.enter_context(nc.sbuf_tensor("p0_relb", [41, 192], BF16))
        zz = ps.enter_context(nc.sbuf_tensor("p0_zz", [128, 2600], BF16))
        oneb = ps.enter_context(nc.sbuf_tensor("p0_oneb", [128, 16], BF16))
        pr = ps.enter_context(nc.psum_tensor("p0_ps", [128, 128], F32))
        prb = ps.enter_context(nc.psum_tensor("p0_psb", [128, 128], BF16))

        with nc.Block() as blk:
            def s_sync(sync):
                for t_, src in [(ident, ident_in), (triu64, triu_in), (adm, adm_in), (triu128p, triu128_in), (ehd, ehd_in),
                                (bias_sb, bias_in), (gbrow, gb_in)]:
                    c.inc(sync.dma_start(out=t_[:], in_=src.ap()), 'p0c', 16)
                c.inc(sync.dma_start(out=relf[0:41, 0:64], in_=relk_in.ap()), 'p0c', 16)
                c.inc(sync.dma_start(out=relf[0:41, 64:128], in_=relv_in.ap()), 'p0c', 16)
                c.inc(sync.dma_start(out=relf[0:41, 128:192], in_=relvf_in.ap()), 'p0c', 16)
            blk.sync(s_sync)

            def s_vector(vector):
                c.wait(vector, 'p0c')
                vector.memset(onesrow[:], 1.0)
                vector.memset(onescol_f[:], 1.0)
                vector.memset(onescol_b[:], 1.0)
                vector.memset(zz[:], 0.0)
                vector.memset(oneb[:], 1.0)
                vector.memset(cvals[:, 0:1], 1e-5)
                vector.memset(cvals[:, 1:2], 1e-10)
                vector.memset(cvals[:, 2:3], 0.0)
                vector.memset(cvals[:, 3:4], 1.0)
                vector.memset(rows[:], 0.0)
                vector.memset(grows[:], 0.0)
                vector.tensor_copy(relb[:], relf[:])
                vector.tensor_scalar_mul(gbrow[:], gbrow[:], 0.5)
                c.inc(vector.drain(), 'p0v', 1)
            blk.vector(s_vector)

        with nc.Block() as blk:
            def s_sync(sync):
                c.wait(sync, 'p0v', 1)
                for par in range(2):
                    for ch in range(NTC):
                        c.inc(sync.dma_start(out=h_c.ap()[par, ch, :, 0:128], in_=zz[:, 0:128]), 'p0z', 16)
                        c.inc(sync.dma_start(out=h_c.ap()[par, ch, :, 296:464], in_=zz[:, 0:168]), 'p0z', 16)
                for ch in range(NTC):
                    c.inc(sync.dma_start(out=h_strip.ap()[ch, :, :], in_=zz[:, 0:404]), 'p0z', 16)
                for r0, n in ((0, 128), (128, 128), (256, 20)):
                    c.inc(sync.dma_start(out=h_h6.ap()[r0:r0 + n, 0:2600], in_=zz[0:n, :]), 'p0z', 16)
                    c.inc(sync.dma_start(out=h_h6.ap()[r0:r0 + n, 2600:2624], in_=zz[0:n, 0:24]), 'p0z', 16)
                for par in range(2):
                    c.inc(sync.dma_start(out=h_ps.ap()[par, :, :], in_=zz[0:41, 0:2130]), 'p0z', 16)
                c.inc(sync.dma_start(out=h_zz.ap()[0:1, 0:2600], in_=zz[0:1, 0:2600]), 'p0z', 16)
                c.inc(sync.dma_start(out=h_zz.ap()[0:1, 2600:4096], in_=zz[0:1, 0:1496]), 'p0z', 16)
                for tc in range(NTC):
                    dst = AP(h_va.ap().tensor, tc * 128 * 1040 + 64, [[1040, 128], [65, 16], [1, 1]])
                    c.inc(sync.dma_start(out=dst, in_=AP(oneb, 0, [[16, 128], [1, 16], [1, 1]]),
                                         allow_slow_non_contiguous=True), 'p0z', 16)
            blk.sync(s_sync)

            def s_tensor(tensor):
                c.wait(tensor, 'p0v', 1)
                c.inc(tensor.transpose(prb[0:64, 0:41], relb[0:41, 0:64], ident[0:41, 0:41]), 'p0t', 1)
            blk.tensor(s_tensor)

            def s_gp(g):
                c.wait(g, 'p0v', 1)
                c.inc(g.dma_start(out=relv40r[:], in_=relb[40:41, 64:128]), 'p0g', 16)
                c.inc(g.dma_start(out=gbhop[0:1, 0:16], in_=gbrow[0:1, :]), 'p0gb', 16)
                c.inc(g.dma_start(out=gbhop[0:1, 16:32], in_=gbrow[1:2, :]), 'p0gb', 16)
                c.wait(g, 'p0gb')
                bcast_row(g, c, gbig, gbhop[0:1, 0:32], 32, 'p0gb')
            blk.gpsimd(s_gp)

            def s_vector(vector):
                c.wait(vector, 'p0t', 1)
                c.wait(vector, 'p0g', 16)
                vector.tensor_copy(relkT[:], prb[0:64, 0:41])
                vector.drain()
                vector.tensor_copy(relcols[:, 0:1], relkT[:, 0:1])
                vector.tensor_copy(relcols[:, 1:2], relkT[:, 40:41])
                vector.drain()
                vector.tensor_scalar(relk_m40T[:], relkT[:], relcols[:, 0:1], None, ALU.subtract)
                vector.tensor_tensor(dvrow[:], relb[0:1, 64:128], relv40r[:], ALU.subtract)
                c.inc(vector.drain(), 'p0v', 1)
            blk.vector(s_vector)

        with nc.Block() as blk:
            def s_tensor(tensor):
                c.wait(tensor, 'p0v', 2)
                tensor.matmul(pr[:, 0:64], onesrow[:], relv40r[:], start=True, stop=True)
                c.inc(tensor.matmul(pr[:, 64:128], onesrow[:], dvrow[:], start=True, stop=True), 'p0t', 1)
            blk.tensor(s_tensor)

            def s_vector(vector):
                c.wait(vector, 'p0t', 2)
                vector.tensor_copy(relvp[:, 64:128], pr[:, 64:128])
                vector.tensor_copy(relvp[:, 0:64], pr[:, 64:128])
                vector.drain()
                vector.tensor_tensor(relvp[0:41, 0:64], relb[0:41, 128:192], pr[0:41, 0:64], ALU.subtract)
                c.inc(vector.drain(), 'p0v', 1)
            blk.vector(s_vector)


    c.dma_fence('p0z', 'p0g', 'p0gb')

    with ExitStack() as ps:
        rhsf = ps.enter_context(nc.sbuf_tensor("p0_rhsf", [128, NCH * 1024], F32))
        amx = ps.enter_context(nc.sbuf_tensor("p0_amx", [128, 1024], F32))
        rrow = ps.enter_context(nc.sbuf_tensor("p0_rrow", [1, 1024], F32))
        bcr = ps.enter_context(nc.sbuf_tensor("p0_bcr", [128, 1024], F32))
        q_ti = ps.enter_context(nc.sbuf_tensor("p0_ti", [128, 1024], I32))
        q_f = ps.enter_context(nc.sbuf_tensor("p0_f", [128, 1024], F32))
        q_d = ps.enter_context(nc.sbuf_tensor("p0_d", [128, 1024], F32))
        q_sg = ps.enter_context(nc.sbuf_tensor("p0_sg", [128, 1024], F32))
        q_o = ps.enter_context(nc.sbuf_tensor("p0_qo", [NCH, 128, 0] if False else [128, 1024], BF16))

        with nc.Block() as blk:
            def s_sync(sync):
                for ch in range(NCH):
                    c.inc(sync.dma_start(out=rhsf[:, ch * 1024:(ch + 1) * 1024],
                                         in_=wT["rhsw"].ap()[ch * 128:(ch + 1) * 128, :]), 'rld', 16)
            blk.sync(s_sync)

            def s_scalar(scalar):
                c.wait(scalar, 'rld')
                b0s = c.base('qs', 'qv')
                for ch in range(NCH):
                    if ch >= 2:
                        scalar.wait_ge(c.S('qv'), b0s['qv'] + ch - 1)
                    ab = q_f if ch % 2 == 0 else q_d
                    c.inc(scalar.activation(ab[:], rhsf[:, ch * 1024:(ch + 1) * 1024], AT.Abs), 'qs', 1)
            blk.scalar(s_scalar)

            def s_vector(vector):
                b0v = {'qs': c.cnt['qs'] - NCH, 'qv': c.cnt['qv']}
                for ch in range(NCH):
                    vector.wait_ge(c.S('qs'), b0v['qs'] + ch + 1)
                    ab = q_f if ch % 2 == 0 else q_d
                    vector.drain()
                    if ch == 0:
                        ins = vector.tensor_copy(amx[:], ab[:])
                    else:
                        ins = vector.tensor_tensor(amx[:], amx[:], ab[:], ALU.max)
                    c.inc(ins, 'qv', 1)
                part_tree(c, vector, amx, q_d, 1024)
                vector.tensor_scalar(grows[0:1, :], amx[0:1, :], 1e-12, 1.0 / 127.0, ALU.max, ALU.mult)
                vector.drain()
                c.inc(vector.reciprocal(rrow[:], grows[0:1, :]), 'rv', 1)
            blk.vector(s_vector)

            def s_gp(g):
                c.wait(g, 'rv', 1)
                bcast_row(g, c, bcr, rrow[0:1, 0:1024], 1024, 'rbc')
            blk.gpsimd(s_gp)

        with nc.Block() as blk:
            b0 = c.base('qv', 'qs', 'qst', 'qst2')

            def s_vector(vector):
                for ch in range(NCH):
                    src = rhsf[:, ch * 1024:(ch + 1) * 1024]
                    if ch == 0:
                        c.wait(vector, 'rbc')
                    if ch >= 1:
                        vector.wait_ge(c.S('qst'), b0['qst'] + 16 * ch)
                        if DBG:
                            vector.wait_ge(c.S('qst2'), b0['qst2'] + 16 * ch)
                    vector.tensor_tensor(q_d[:], src, bcr[:], ALU.mult)
                    vector.drain()
                    vector.tensor_copy(q_ti[:], q_d[:])
                    vector.drain()
                    vector.tensor_copy(q_f[:], q_ti[:])
                    vector.drain()
                    vector.tensor_tensor(q_d[:], q_d[:], q_f[:], ALU.subtract)
                    vector.drain()
                    vector.tensor_scalar(q_sg[:], q_d[:], 0.5, -0.5, ALU.min, ALU.max)
                    vector.drain()
                    c.inc(vector.tensor_tensor(q_d[:], q_d[:], q_sg[:], ALU.subtract), 'qv', 1)
                    vector.wait_ge(c.S('qs'), b0['qs'] + ch + 1)
                    vector.drain()
                    c.inc(vector.tensor_tensor(q_o[:], q_f[:], q_sg[:], ALU.add), 'qv', 1)
            blk.vector(s_vector)

            def s_scalar(scalar):
                for ch in range(NCH):
                    scalar.wait_ge(c.S('qv'), b0['qv'] + 2 * ch + 1)
                    c.inc(scalar.activation(q_sg[:], q_d[:], AT.Sign), 'qs', 1)
            blk.scalar(s_scalar)

            def s_gp(g):
                for ch in range(NCH):
                    g.wait_ge(c.S('qv'), b0['qv'] + 2 * ch + 2)
                    c.inc(g.dma_start(out=h_rhsq.ap()[ch], in_=q_o[:]), 'qst', 16)
                    if DBG:
                        c.inc(g.dma_start(out=dbg['qrhs'].ap()[ch], in_=q_o[:]), 'qst2', 16)
            blk.gpsimd(s_gp)

    c.dma_fence('qst', 'qst2', 'rbc')



    with ExitStack() as ps:
        wcf = [ps.enter_context(nc.sbuf_tensor(f"p3_wc{i}", [128, 4096], F32)) for i in range(2)]
        acc3 = ps.enter_context(nc.sbuf_tensor("p3_acc", [128, 48], F32))
        q_ti = ps.enter_context(nc.sbuf_tensor("p3_ti", [128, 4096], I32))
        q_f = ps.enter_context(nc.sbuf_tensor("p3_f", [128, 4096], F32))
        q_d = ps.enter_context(nc.sbuf_tensor("p3_d", [128, 4096], F32))
        q_sg = ps.enter_context(nc.sbuf_tensor("p3_sg", [128, 4096], F32))
        q_o = ps.enter_context(nc.sbuf_tensor("p3_qo", [128, 4096], BF16))
        prow3 = ps.enter_context(nc.psum_tensor("p3_pr", [1, 48], F32))

        items3 = ([("wgT", ch, 4096, h_wgq, ch, 0) for ch in range(NCH)]
                  + [("wuT", ch, 4096, h_wuq, 8 + ch, 1) for ch in range(NCH)]
                  + [("wdT", ch, 1024, h_wdq, 16 + ch, 2) for ch in range(NIC)])

        with nc.Block() as blk:
            b0 = c.base('lnl0', 'lnl1', 'qv')

            def s_sync(sync):
                for n, (nm, ch, w, hdst, j, mcol) in enumerate(items3):
                    if n >= 2:
                        sync.wait_ge(c.S('qv'), b0['qv'] + n - 1)
                    c.inc(sync.dma_start(out=wcf[n % 2][:, 0:w], in_=wT[nm].ap()[ch * 128:(ch + 1) * 128, :]),
                          'lnl%d' % (n % 2), 16)
            blk.sync(s_sync)

            def s_vector(vector):
                for n, (nm, ch, w, hdst, j, mcol) in enumerate(items3):
                    vector.wait_ge(c.S('lnl%d' % (n % 2)), b0['lnl%d' % (n % 2)] + 16 * (n // 2 + 1))
                    vector.drain()
                    c.inc(vector.tensor_reduce(acc3[:, j:j + 1], wcf[n % 2][:, 0:w], AX.X, ALU.add,
                                               apply_absolute_value=True), 'qv', 1)
            blk.vector(s_vector)

        with nc.Block() as blk:
            def s_tensor(tensor):
                c.inc(tensor.matmul(prow3[0:1, :], onescol_f[:], acc3[:, :], start=True, stop=True), 'pjt', 1)
            blk.tensor(s_tensor)

            def s_vector(vector):
                c.wait(vector, 'pjt')
                vector.memset(swrow[:], 0.0)
                vector.drain()
                vector.tensor_reduce(swrow[0:1, 4:5], prow3[0:1, 0:8], AX.X, ALU.add)
                vector.tensor_reduce(swrow[0:1, 5:6], prow3[0:1, 8:16], AX.X, ALU.add)
                vector.tensor_reduce(swrow[0:1, 6:7], prow3[0:1, 16:48], AX.X, ALU.add)
                vector.drain()
                vector.tensor_scalar(swrow[0:1, 4:7], swrow[0:1, 4:7], 1.0 / (1024.0 * 4096.0), 1e-5,
                                     ALU.mult, ALU.max)
                vector.drain()
                c.inc(vector.reciprocal(swrow[0:1, 0:3], swrow[0:1, 4:7]), 'rv', 1)
            blk.vector(s_vector)

            def s_gp(g):
                c.wait(g, 'rv')
                bcast_row(g, c, swcol, swrow[0:1, 0:8], 8, 'rbc')
            blk.gpsimd(s_gp)

        with nc.Block() as blk:
            b0 = c.base('lnm0', 'lnm1', 'qv', 'qs', 'qst')

            def s_sync(sync):
                c.wait(sync, 'rbc')
                for n, (nm, ch, w, hdst, j, mcol) in enumerate(items3):
                    if n >= 2:
                        sync.wait_ge(c.S('qv'), b0['qv'] + 2 * (n - 1))
                    c.inc(sync.dma_start(out=wcf[n % 2][:, 0:w], in_=wT[nm].ap()[ch * 128:(ch + 1) * 128, :]),
                          'lnm%d' % (n % 2), 16)
            blk.sync(s_sync)

            def s_vector(vector):
                c.wait(vector, 'rbc')
                for n, (nm, ch, w, hdst, j, mcol) in enumerate(items3):
                    vector.wait_ge(c.S('lnm%d' % (n % 2)), b0['lnm%d' % (n % 2)] + 16 * (n // 2 + 1))
                    vector.drain()
                    vector.tensor_scalar(q_d[:, 0:w], wcf[n % 2][:, 0:w], swcol[:, mcol:mcol + 1], None, ALU.mult)
                    vector.drain()
                    vector.tensor_scalar(q_f[:, 0:w], q_d[:, 0:w], 0.5, None, ALU.is_ge)
                    c.inc(vector.tensor_scalar(q_sg[:, 0:w], q_d[:, 0:w], -0.5, None, ALU.is_le), 'qv', 1)
                    if n >= 1:
                        nprev = 16 + 8 * max(0, n - 16) + (8 if n > 16 else 0) if False else (
                            n if n <= 16 else 16 + 8 * (n - 16))
                        vector.wait_ge(c.S('qst'), b0['qst'] + 16 * nprev)
                    vector.drain()
                    c.inc(vector.tensor_tensor(q_o[:, 0:w], q_f[:, 0:w], q_sg[:, 0:w], ALU.subtract), 'qv', 1)
            blk.vector(s_vector)

            def s_gp(g):
                nst = 0
                for n, (nm, ch, w, hdst, j, mcol) in enumerate(items3):
                    g.wait_ge(c.S('qv'), b0['qv'] + 2 * n + 2)
                    if nst >= 1:
                        g.wait_ge(c.S('qst'), b0['qst'] + 16 * nst)
                    if nm == "wdT":
                        for och in range(NCH):
                            nst += 1
                            c.inc(g.dma_start(out=h_wdq.ap()[och][ch],
                                              in_=q_o[:, och * 128:(och + 1) * 128]), 'qst', 16)
                    else:
                        nst += 1
                        c.inc(g.dma_start(out=hdst.ap()[ch], in_=q_o[:, 0:w]), 'qst', 16)
            blk.gpsimd(s_gp)
    c.dma_fence('qst', 'rbc')

    def p1a(s, xn0):
        with ExitStack() as ps:
            xc = [ps.enter_context(nc.sbuf_tensor(f"a_xc{i}_{s}", [128, T], F32)) for i in range(2)]
            sq = ps.enter_context(nc.sbuf_tensor(f"a_sq_{s}", [128, T], F32))
            amx = ps.enter_context(nc.sbuf_tensor(f"a_amx_{s}", [128, T], F32))
            abt = ps.enter_context(nc.sbuf_tensor(f"a_abt_{s}", [128, T], F32))
            bca = ps.enter_context(nc.sbuf_tensor(f"a_bca_{s}", [128, T], F32))
            bcb = ps.enter_context(nc.sbuf_tensor(f"a_bcb_{s}", [128, T], F32))
            rw = [bca, bcb, xc[0], xc[1]]
            pstat = ps.enter_context(nc.psum_tensor(f"a_pstat_{s}", [1, T], F32))
            pstatq = ps.enter_context(nc.psum_tensor(f"a_pstatq_{s}", [1, T], F32))
            rws = lambda k: rw[k][0:1, 0:T]

            with nc.Block() as blk:
                b0 = c.base('lnl0', 'lnl1', 'lnsq', 'lnu')

                def s_sync(sync):
                    for ch in range(NCH):
                        if ch >= 2:
                            sync.wait_ge(c.S('lnu'), b0['lnu'] + ch - 1)
                        c.inc(sync.dma_start(out=xc[ch % 2][:], in_=x_in.ap()[s, ch * 128:(ch + 1) * 128, :]),
                              'lnl%d' % (ch % 2), 16)
                blk.sync(s_sync)

                def s_scalar(scalar):
                    for ch in range(NCH):
                        scalar.wait_ge(c.S('lnl%d' % (ch % 2)), b0['lnl%d' % (ch % 2)] + 16 * (ch // 2 + 1))
                        if ch >= 1:
                            scalar.wait_ge(c.S('lnu'), b0['lnu'] + ch)
                        c.inc(scalar.activation(sq[:], xc[ch % 2][:], AT.Square), 'lnsq', 1)
                blk.scalar(s_scalar)

                def s_tensor(tensor):
                    for ch in range(NCH):
                        tensor.wait_ge(c.S('lnsq'), b0['lnsq'] + ch + 1)
                        for qt in range(NQT):
                            tensor.matmul(pstat[0:1, qt * 512:(qt + 1) * 512], onescol_f[:],
                                          xc[ch % 2][:, qt * 512:(qt + 1) * 512],
                                          start=(ch == 0), stop=(ch == NCH - 1))
                            ins = tensor.matmul(pstatq[0:1, qt * 512:(qt + 1) * 512], onescol_f[:],
                                                sq[:, qt * 512:(qt + 1) * 512],
                                                start=(ch == 0), stop=(ch == NCH - 1))
                        c.inc(ins, 'lnu', 1)
                blk.tensor(s_tensor)

            with nc.Block() as blk:
                def s_vector(vector):
                    vector.tensor_scalar_mul(rws(0), pstat[0:1, :], 1.0 / D)
                    vector.tensor_scalar_mul(rws(1), pstatq[0:1, :], 1.0 / D)
                    vector.drain()
                    vector.tensor_tensor(rws(2), rws(0), rws(0), ALU.mult)
                    vector.drain()
                    c.inc(vector.tensor_tensor(rws(1), rws(1), rws(2), ALU.subtract), 'lnr', 1)
                blk.vector(s_vector)

                def s_scalar(scalar):
                    c.wait(scalar, 'lnr')
                    c.inc(scalar.activation(rws(1), rws(1), AT.Sqrt, bias=cvals[0:1, 0:1]), 'lns', 1)
                blk.scalar(s_scalar)

            with nc.Block() as blk:
                def s_vector(vector):
                    c.wait(vector, 'lns')
                    vector.reciprocal(rws(2), rws(1))
                    vector.drain()
                    vector.tensor_tensor(rws(3), rws(0), rws(2), ALU.mult)
                    vector.drain()
                    c.inc(vector.tensor_scalar_mul(rws(3), rws(3), -1.0), 'lnr', 1)
                blk.vector(s_vector)

                def s_gp(g):
                    c.wait(g, 'lnr')
                    c.inc(g.dma_start(out=rows[8 * s + R_A0:8 * s + R_A0 + 1, :], in_=rws(2)), 'lnpk', 16)
                    c.inc(g.dma_start(out=rows[8 * s + R_B0:8 * s + R_B0 + 1, :], in_=rws(3)), 'lnpk', 16)
                    c.wait(g, 'lnpk')
                    bcast_row(g, c, bca, rws(2), T, 'lnbc')
                    bcast_row(g, c, bcb, rws(3), T, 'lnbc')
                blk.gpsimd(s_gp)

            with nc.Block() as blk:
                b0 = c.base('lnm0', 'lnm1', 'lnx', 'lnq')

                def s_sync(sync):
                    c.wait(sync, 'lnbc')
                    c.wait(sync, 'lnpk')
                    for ch in range(NCH):
                        if ch >= 2:
                            sync.wait_ge(c.S('lnx'), b0['lnx'] + 2 * (ch - 1))
                        c.inc(sync.dma_start(out=xc[ch % 2][:], in_=x_in.ap()[s, ch * 128:(ch + 1) * 128, :]),
                              'lnm%d' % (ch % 2), 16)
                blk.sync(s_sync)

                def s_vector(vector):
                    c.wait(vector, 'lnbc')
                    for ch in range(NCH):
                        vector.wait_ge(c.S('lnm%d' % (ch % 2)), b0['lnm%d' % (ch % 2)] + 16 * (ch // 2 + 1))
                        vector.drain()
                        vector.tensor_tensor(sq[:], xc[ch % 2][:], bca[:], ALU.mult)
                        vector.drain()
                        vector.tensor_tensor(sq[:], sq[:], bcb[:], ALU.add)
                        vector.drain()
                        c.inc(vector.tensor_copy(xn0[:, ch * T:(ch + 1) * T], sq[:]), 'lnx', 1)
                        vector.wait_ge(c.S('lnq'), b0['lnq'] + ch + 1)
                        vector.drain()
                        if ch == 0:
                            ins = vector.tensor_copy(amx[:], abt[:])
                        else:
                            ins = vector.tensor_tensor(amx[:], amx[:], abt[:], ALU.max)
                        c.inc(ins, 'lnx', 1)
                blk.vector(s_vector)

                def s_scalar(scalar):
                    for ch in range(NCH):
                        scalar.wait_ge(c.S('lnx'), b0['lnx'] + 2 * ch + 1)
                        c.inc(scalar.activation(abt[:], sq[:], AT.Abs), 'lnq', 1)
                blk.scalar(s_scalar)

            with nc.Block() as blk:
                def s_vector(vector):
                    part_tree(c, vector, amx, sq, T)
                    c.inc(vector.tensor_scalar(amx[0:1, 0:T], amx[0:1, :], 1e-12, 1.0 / 127.0,
                                               ALU.max, ALU.mult), 'lnd', 1)
                blk.vector(s_vector)

                def s_gp(g):
                    c.wait(g, 'lnd')
                    c.inc(g.dma_start(out=rows[8 * s + R_LS0:8 * s + R_LS0 + 1, :], in_=amx[0:1, 0:T]), 'lnbc', 16)
                    if DBG:
                        for ch in range(NCH):
                            c.inc(g.dma_start(out=dbg['ln0'].ap()[s, ch], in_=xn0[:, ch * T:(ch + 1) * T]), 'dbg', 16)
                        c.wait(g, 'lnbc')
                        c.inc(g.dma_start(out=dbg['rows'].ap()[s], in_=rows[:, :]), 'dbg', 16)
                blk.gpsimd(s_gp)

            c.dma_fence('lnbc', 'dbg')


    def l0_proj(s, xn0, w_qkv):
        with ExitStack() as ps:
            qtev = ps.enter_context(nc.sbuf_tensor(f"b_qtev_{s}", [128, T], BF16))
            ktev = ps.enter_context(nc.sbuf_tensor(f"b_ktev_{s}", [128, T], BF16))
            vtmp = ps.enter_context(nc.sbuf_tensor(f"b_vtmp_{s}", [128, T], BF16))
            vatm = ps.enter_context(nc.sbuf_tensor(f"b_vatm_{s}", [128, T], BF16))
            bstk = ExitStack()
            blk_all = bstk.enter_context(nc.Block())
            for g in range(8):
                gps = ExitStack()
                pQ = gps.enter_context(nc.psum_tensor(f"b_pQ_{s}_{g}", [128, T], F32))
                pK = gps.enter_context(nc.psum_tensor(f"b_pK_{s}_{g}", [128, T], F32))
                if True:
                    blk = blk_all
                    def s_tensor(tensor, g=g):
                        c.wait(tensor, 'pjv')
                        for i in range(NCH):
                            wq = w_qkv[0][:, i * D + g * 128:i * D + (g + 1) * 128]
                            wk = w_qkv[1][:, i * D + g * 128:i * D + (g + 1) * 128]
                            for qt in range(NQT):
                                xsl = xn0[:, i * T + qt * 512:i * T + (qt + 1) * 512]
                                tensor.matmul(pQ[:, qt * 512:(qt + 1) * 512], wq, xsl,
                                              start=(i == 0), stop=(i == NCH - 1))
                                ins = tensor.matmul(pK[:, qt * 512:(qt + 1) * 512], wk, xsl,
                                                    start=(i == 0), stop=(i == NCH - 1))
                        c.inc(ins, 'pjt', 1)
                    blk.tensor(s_tensor)

                    def s_scalar(scalar, g=g):
                        c.wait(scalar, 'pjt')
                        c.wait(scalar, 'pjk')
                        scalar.activation(qtev[:], pQ[:], AT.Identity, bias=bias_sb[:, 0 * NCH + g:0 * NCH + g + 1])
                        c.inc(scalar.activation(ktev[:], pK[:], AT.Identity, bias=bias_sb[:, 1 * NCH + g:1 * NCH + g + 1]), 'pjs', 1)
                    blk.scalar(s_scalar)

                    def s_gp(gp, g=g):
                        c.wait(gp, 'pjs')
                        c.inc(gp.dma_start(out=h_qt.ap()[2 * g], in_=qtev[0:64, :]), 'pjk', 16)
                        c.inc(gp.dma_start(out=h_qt.ap()[2 * g + 1], in_=qtev[64:128, :]), 'pjk', 16)
                        c.inc(gp.dma_start(out=h_kt.ap()[2 * g], in_=ktev[0:64, :]), 'pjk', 16)
                        c.inc(gp.dma_start(out=h_kt.ap()[2 * g + 1], in_=ktev[64:128, :]), 'pjk', 16)
                    blk.gpsimd(s_gp)

                gps.close()
                gps = ExitStack()
                pQ = gps.enter_context(nc.psum_tensor(f"b_pV_{s}_{g}", [128, T], F32))
                pVt = gps.enter_context(nc.psum_tensor(f"b_pVt_{s}_{g}", [128, T], BF16))
                if True:
                    def s_tensor(tensor, g=g):
                        c.wait(tensor, 'pjs')
                        for i in range(NCH):
                            wv = w_qkv[2][:, i * D + g * 128:i * D + (g + 1) * 128]
                            for qt in range(NQT):
                                xsl = xn0[:, i * T + qt * 512:i * T + (qt + 1) * 512]
                                ins = tensor.matmul(pQ[:, qt * 512:(qt + 1) * 512], wv, xsl,
                                                    start=(i == 0), stop=(i == NCH - 1))
                        c.inc(ins, 'pjt', 1)
                    blk.tensor(s_tensor)

                    def s_scalar(scalar, g=g):
                        c.wait(scalar, 'pjt')
                        c.inc(scalar.activation(vtmp[:], pQ[:], AT.Identity, bias=bias_sb[:, 2 * NCH + g:2 * NCH + g + 1]), 'pjs', 1)
                    blk.scalar(s_scalar)

                if True:
                    def s_tensor(tensor, g=g):
                        c.wait(tensor, 'pjs')
                        for tc in range(NTC):
                            ins = tensor.transpose(pVt[:, tc * 128:(tc + 1) * 128],
                                                   vtmp[:, tc * 128:(tc + 1) * 128], ident[:])
                        c.inc(ins, 'pjt', 1)
                    blk.tensor(s_tensor)

                    def s_vector(vector, g=g):
                        c.wait(vector, 'pjt')
                        c.wait(vector, 'pjk')
                        c.inc(vector.tensor_copy(vatm[:], pVt[:]), 'pjv', 1)
                    blk.vector(s_vector)

                    def s_gp(gp, g=g):
                        c.wait(gp, 'pjv')
                        for k in range(2):
                            src = AP(vatm, 64 * k, [[T, 128], [128, NTC], [1, 64]])
                            dst = AP(h_va.ap().tensor, (2 * g + k) * 65,
                                     [[1040, 128], [128 * 1040, NTC], [1, 64]])
                            c.inc(gp.dma_start(out=dst, in_=src), 'pjk', 16)
                    blk.gpsimd(s_gp)
                gps.close()
            bstk.close()
            c.dma_fence('pjk')

    def l0_r1(s, h, bufs, pst, blk_all):
        QTh, KTh, Vah, pslab, cstr, ptil = bufs[h % 2]
        psPh, ptp = pst['psPh'], pst['ptp']

        if True:
            def s_sync(sync):
                c.wait(sync, 'sct')
                c.wait(sync, 'kbv')
                c.wait(sync, 'avt')
                c.inc(sync.dma_start(out=QTh[:], in_=h_qt.ap()[h]), 'hld', 16)
                c.inc(sync.dma_start(out=KTh[:], in_=h_kt.ap()[h]), 'hld', 16)
                src = AP(h_va.ap().tensor, h * 65, [[1040, 128], [128 * 1040, NTC], [1, 65]])
                c.inc(sync.dma_start(out=Vah[:].rearrange("p (c u) -> p c u", u=65), in_=src), 'hld', 16)
            blk_all.sync(s_sync)

            def s_tensor(tensor):
                b0v = c.base('rlv')
                c.wait(tensor, 'hld')
                c.wait(tensor, 'rlv')
                for q4 in range(4):
                    if q4 >= 1:
                        tensor.wait_ge(c.S('rlv'), b0v['rlv'] + q4)
                    ins = tensor.matmul(psPh[:, 0:512], relk_m40T[:],
                                        QTh[:, q4 * 512:(q4 + 1) * 512], start=True, stop=True)
                    c.inc(ins, 'rlt', 1)
            blk_all.tensor(s_tensor)

            def s_vector(vector):
                b0r = c.base('rlt')
                c.wait(vector, 'w2t')
                c.wait(vector, 'rcw')
                vector.memset(pslab[0:41, 2048:2208], 0.0)
                vector.drain()
                for q4 in range(4):
                    vector.wait_ge(c.S('rlt'), b0r['rlt'] + q4 - 3)
                    vector.drain()
                    c.inc(vector.tensor_copy(pslab[0:41, q4 * 512:(q4 + 1) * 512], psPh[0:41, :]), 'rlv', 1)
            blk_all.vector(s_vector)

            def s_sync(sync):
                c.wait(sync, 'rlv')
                c.inc(sync.dma_start(out=h_ps.ap()[h % 2][0:41, 21:2069], in_=pslab[0:41, 0:2048]), 'rps', 16)
                c.inc(sync.dma_start(out=h_gp.ap()[h % 2][0:1, 0:2208], in_=pslab[40:41, :]), 'rps', 16)
                c.wait(sync, 'rps')
                c.inc(sync.dma_start(out=pslab[0:41, 0:2089],
                                     in_=AP(h_ps.ap().tensor, (h % 2) * 41 * 2130 + 1, [[2131, 41], [1, 2089]])),
                      'rps', 16)
                c.inc(sync.dma_start(
                    out=AP(h_c.ap().tensor, (h % 2) * NTC * 128 * 464 + 169, [[59392, NTC], [464, 128], [1, 127]]),
                    in_=AP(h_gp.ap().tensor, (h % 2) * 2208 + 21, [[128, NTC], [1, 128], [1, 127]])), 'rcw', 16)
            blk_all.sync(s_sync)

            def s_tensor(tensor):
                c.wait(tensor, 'rps')
                for ci in range(NTC):
                    ins = tensor.transpose(ptp[:, ci * 42:ci * 42 + 41],
                                           pslab[0:41, ci * 128:(ci + 1) * 128], ident[0:41, 0:41])
                c.inc(ins, 'rlt', 1)
            blk_all.tensor(s_tensor)

            def s_vector(vector):
                c.wait(vector, 'rlt')
                c.inc(vector.tensor_copy(
                    AP(ptil, 0, [[672, 128], [42, NTC], [1, 41]]),
                    AP(ptp, 0, [[672, 128], [42, NTC], [1, 41]])), 'rlv', 1)
            blk_all.vector(s_vector)

            def s_sync(sync):
                c.wait(sync, 'rlv')
                c.inc(sync.dma_start(
                    out=AP(h_c.ap().tensor, (h % 2) * NTC * 128 * 464 + 128, [[464, 128], [59392, NTC], [1, 41]]),
                    in_=AP(ptil, 0, [[672, 128], [42, NTC], [1, 41]])), 'rcw', 16)
                c.wait(sync, 'rcw')
                hb = (h % 2) * NTC * 128 * 464
                c.inc(sync.dma_start(out=cstr[:, 0:148],
                                     in_=AP(h_c.ap().tensor, hb + 148, [[463, 128], [1, 148]])), 'ccr0', 16)
                bulk_src = AP(h_c.ap().tensor, hb + 59392 + 128, [[463, 128], [59392, 14], [1, 168]])
                bulk_dst = AP(cstr, 168, [[16 * 168, 128], [168, 14], [1, 168]])
                c.inc(sync.dma_start(out=bulk_dst, in_=bulk_src), 'ccr0', 16)
                c.inc(sync.dma_start(out=cstr[:, 15 * 168:15 * 168 + 148],
                                     in_=AP(h_c.ap().tensor, hb + 15 * 59392 + 128, [[463, 128], [1, 148]])),
                      'ccr0', 16)
            blk_all.sync(s_sync)

    def l0_r234(s, h, bufs, pst, slabO, Zs, Aslab, kw, h4t, stg, zrow, zrowb, zsnb, blk_all):
        QTh, KTh, Vah, pslab, cstr, ptil = bufs[h % 2]
        pscA, pscB, psPh, ptp = pst['pscA'], pst['pscB'], pst['psPh'], pst['ptp']
        psc = [pscA, pscB]
        dgt2 = [stg[:, 3072:4608], stg[:, 4608:6144]]
        zsg = pslab[0:1, 0:1536]

        if True:
            b0 = c.base('sct', 'scs', 'kbv', 'ccr0')

            def s_vector(vector):
                for ci in range(NTC):
                    if ci >= 1:
                        vector.wait_ge(c.S('sct'), b0['sct'] + 2 * ci)
                    else:
                        vector.wait_ge(c.S('sct'), b0['sct'])
                    vector.tensor_scalar(kw[:, 0:128], KTh[:, ci * 128:(ci + 1) * 128],
                                         relcols[:, 0:1], None, ALU.add)
                    c.inc(vector.tensor_scalar(kw[:, 128:256], KTh[:, ci * 128:(ci + 1) * 128],
                                               relcols[:, 1:2], None, ALU.add), 'kbv', 1)
            blk_all.vector(s_vector)

            def s_tensor(tensor):
                tensor.wait_ge(c.S('ccr0'), b0['ccr0'])
                c.wait(tensor, 'avo')
                for ci in range(NTC):
                    k0 = ci * 128
                    tensor.wait_ge(c.S('kbv'), b0['kbv'] + ci + 1)
                    hi40 = min(k0 + 148, T)
                    lo = max(k0 - 20, 0)
                    for hf in range(2):
                        n = 2 * ci + hf
                        if n >= 2:
                            tensor.wait_ge(c.S('scs'), b0['scs'] + n - 1)
                        pc_ = psc[n % 2]
                        ins = None
                        for q in range(2):
                            w0, w1 = hf * 1024 + q * 512, hf * 1024 + (q + 1) * 512
                            pq0 = q * 512
                            a0, a1 = w0, max(min(lo, w1), w0)
                            if a1 > a0:
                                ins = tensor.matmul(pc_[:, pq0 + (a0 - w0):pq0 + (a1 - w0)], kw[:, 0:128],
                                                    QTh[:, a0:a1], start=True, stop=True)
                            c0_, c1_ = max(lo, w0), min(hi40, w1)
                            if c1_ > c0_:
                                tensor.matmul(pc_[:, pq0 + (c0_ - w0):pq0 + (c1_ - w0)], kw[:, 0:128],
                                              QTh[:, c0_:c1_], start=True, stop=False)
                                ins = tensor.matmul(pc_[:, pq0 + (c0_ - w0):pq0 + (c1_ - w0)], ident[:],
                                                    cstr[:, ci * 168 + (c0_ - lo):ci * 168 + (c1_ - lo)],
                                                    start=False, stop=True)
                            b0_, b1_ = max(hi40, w0), w1
                            if b1_ > b0_:
                                ins = tensor.matmul(pc_[:, pq0 + (b0_ - w0):pq0 + (b1_ - w0)], kw[:, 128:256],
                                                    QTh[:, b0_:b1_], start=True, stop=True)
                        c.inc(ins, 'sct', 1)
            blk_all.tensor(s_tensor)

            def s_scalar(scalar):
                for n in range(2 * NTC):
                    ci, hf = n // 2, n % 2
                    scalar.wait_ge(c.S('sct'), b0['sct'] + n + 1)
                    c.inc(scalar.activation(Aslab[:, ci * T + hf * 1024:ci * T + (hf + 1) * 1024],
                                            psc[n % 2][:], AT.Exp, scale=0.125), 'scs', 1)
            blk_all.scalar(s_scalar)

        if True:
            ptp2 = pst['ptp2']
            if True:
                b0 = c.base('dsw', 'h4r0', 'h4r1', 'h4v', 'h4t', 'h4e', 'h6w', 'h6g')

                def s_sync(sync):
                    c.wait(sync, 'dgr0')
                    c.wait(sync, 'dgr1')
                    c.inc(sync.dma_start(out=AP(h_h6.ap().tensor, 0, [[2625, 276], [1, 256]]),
                                         in_=AP(h_zz.ap().tensor, 0, [[1, 276], [1, 256]])), 'h6w', 16)
                    c.inc(sync.dma_start(out=AP(h_h6.ap().tensor, 2304, [[2625, 276], [1, 32]]),
                                         in_=AP(h_zz.ap().tensor, 256, [[1, 276], [1, 32]])), 'h6w', 16)
                    b0scs = c.cnt['scs'] - 2 * NTC
                    for ci in range(NTC):
                        k0 = ci * 128
                        lo = max(k0 - 20, 0)
                        hi = min(k0 + 383, T)
                        w0 = lo - (k0 - 20)
                        sync.wait_ge(c.S('scs'), b0scs + 2 * ci + 2)
                        c.inc(sync.dma_start(out=h_strip.ap()[ci, :, w0:w0 + hi - lo],
                                             in_=Aslab[:, ci * T + lo:ci * T + hi]), 'dsw', 16)
                    c.wait(sync, 'dsw')
                    for ci in range(NTC):
                        if ci >= 2:
                            sync.wait_ge(c.S('h4t'), b0['h4t'] + ci - 1)
                        src = AP(h_strip.ap().tensor, ci * 128 * 404, [[405, 128], [1, 276]])
                        c.inc(sync.dma_start(out=h4t[:, (ci % 2) * 276:(ci % 2 + 1) * 276], in_=src),
                              'h4r%d' % (ci % 2), 16)
                blk_all.sync(s_sync)

                def s_vector(vector):
                    c.wait(vector, 'h6g')
                    c.wait(vector, 'w2t')
                    for ci in range(NTC):
                        vector.wait_ge(c.S('h4r%d' % (ci % 2)), b0['h4r%d' % (ci % 2)] + 16 * (ci // 2 + 1))
                        if ci >= 2:
                            vector.wait_ge(c.S('h4t'), b0['h4t'] + ci - 1)
                        sl = h4t[:, (ci % 2) * 276:(ci % 2 + 1) * 276]
                        c.inc(vector.tensor_tensor(sl, sl, adm[:], ALU.mult), 'h4v', 1)
                        if ci >= 1:
                            vector.wait_ge(c.S('h4t'), b0['h4t'] + ci)
                            cj = (ci - 1) * 128
                            vector.tensor_copy(stg[:, cj:cj + 128], ptp2[:, 0:128])
                            vector.tensor_copy(stg[:, 2048 + cj:2048 + cj + 128], ptp2[:, 128:256])
                            c.inc(vector.tensor_copy(stg[0:20, 4096 + cj:4096 + cj + 128],
                                                     ptp2[0:20, 256:384]), 'h4e', 1)
                    vector.wait_ge(c.S('h4t'), b0['h4t'] + NTC)
                    cj = (NTC - 1) * 128
                    vector.tensor_copy(stg[:, cj:cj + 128], ptp2[:, 0:128])
                    vector.tensor_copy(stg[:, 2048 + cj:2048 + cj + 128], ptp2[:, 128:256])
                    c.inc(vector.tensor_copy(stg[0:20, 4096 + cj:4096 + cj + 128],
                                             ptp2[0:20, 256:384]), 'h4e', 1)
                blk_all.vector(s_vector)

                def s_tensor(tensor):
                    for ci in range(NTC):
                        tensor.wait_ge(c.S('h4v'), b0['h4v'] + ci + 1)
                        if ci >= 1:
                            tensor.wait_ge(c.S('h4e'), b0['h4e'] + ci)
                        sl = h4t[:, (ci % 2) * 276:(ci % 2 + 1) * 276]
                        tensor.transpose(ptp2[:, 0:128], sl[:, 0:128], ident[:])
                        tensor.transpose(ptp2[:, 128:256], sl[:, 128:256], ident[:])
                        ins = tensor.transpose(ptp2[0:20, 256:384], sl[:, 256:276], ident[:])
                        c.inc(ins, 'h4t', 1)
                blk_all.tensor(s_tensor)

                def s_gp(gp):
                    gp.wait_ge(c.S('h4e'), b0['h4e'] + NTC)
                    for pi, (u0, nu) in enumerate(((0, 128), (128, 128), (256, 20))):
                        if pi >= 1:
                            gp.wait_ge(c.S('h6g'), b0['h6g'] + 16 * pi)
                        dst = AP(h_h6.ap().tensor, u0 * 2625 + 256, [[2625, nu], [1, 2048]])
                        srcs = AP(stg, pi * 2048, [[3 * 2048, nu], [1, 2048]])
                        c.inc(gp.dma_start(out=dst, in_=srcs), 'h6g', 16)
                blk_all.gpsimd(s_gp)

        if True:
            pOa = pscA[0:65, 0:512]
            pOm = pscA[0:65, 512:1024]
            pOw = pscB[0:64, 0:512]
            sng = [pscB[0:1, 512:896], pst['pZ3'][0:1, 0:256], pscB[0:1, 896:1024]]
            b0 = c.base('avt', 'avv', 'avz', 'avo', 'snv', 'dgr0', 'dgr1', 'w2t', 'h6w', 'h6g', 'scs')

            def s_sync(sync):
                sync.wait_ge(c.S('h6w'), b0['h6w'])
                sync.wait_ge(c.S('h6g'), b0['h6g'])
                for qt in range(NQT):
                    if qt >= 2:
                        sync.wait_ge(c.S('w2t'), b0['w2t'] + qt - 1)
                    for pi, (u0, nu) in enumerate(((0, 128), (128, 128), (256, 20))):
                        src = AP(h_h6.ap().tensor, u0 * 2624 + 276 + qt * 512, [[2624, nu], [1, 512]])
                        c.inc(sync.dma_start(out=dgt2[qt % 2][0:nu, pi * 512:pi * 512 + 512], in_=src),
                              'dgr%d' % (qt % 2), 16)
            blk_all.sync(s_sync)

            def s_tensor(tensor):
                tensor.wait_ge(c.S('scs'), b0['scs'])
                for qt in range(NQT):
                    w0 = qt * 512
                    cb = 4 * qt - 2
                    if qt >= 1:
                        tensor.wait_ge(c.S('avo'), b0['avo'] + qt)
                    for ci in range(0, cb + 1):
                        tensor.matmul(pOa, Vah[:, ci * 65:(ci + 1) * 65],
                                      Aslab[:, ci * T + w0:ci * T + w0 + 512],
                                      start=(ci == 0), stop=(ci == cb))
                    for i in range(3):
                        cs = 4 * qt - 1 + i
                        if cs < 0:
                            continue
                        col0 = 128 * (1 + i)
                        tensor.matmul(sng[i][0:1, 0:512 - col0], onescol_b[:],
                                      Aslab[:, cs * T + w0 + col0:cs * T + w0 + 512],
                                      start=True, stop=True)
                    cm0 = max(4 * qt - 1, 0)
                    for ci in range(cm0, NTC):
                        ins = tensor.matmul(pOm, Vah[:, ci * 65:(ci + 1) * 65],
                                            Aslab[:, ci * T + w0:ci * T + w0 + 512],
                                            start=(ci == cm0), stop=(ci == NTC - 1))
                    c.inc(ins, 'avt', 1)
                    tensor.wait_ge(c.S('avv'), b0['avv'] + 2 * qt + 1)
                    tensor.wait_ge(c.S('snv'), b0['snv'] + qt + 1)
                    tensor.wait_ge(c.S('dgr%d' % (qt % 2)), b0['dgr%d' % (qt % 2)] + 48 * (qt // 2 + 1))
                    tensor.matmul(pOw, relv40r[:], zrowb[:], start=True, stop=False)
                    if cb >= 0:
                        tensor.matmul(pOw, dvrow[:], zsnb[0:1, 0:512], start=False, stop=False)
                    for i in range(3):
                        cs = 4 * qt - 1 + i
                        if cs < 0:
                            continue
                        col0 = 128 * (1 + i)
                        tensor.matmul(pOw[0:64, col0:512], dvrow[:],
                                      zsg[0:1, 512 * i:512 * i + (512 - col0)],
                                      start=False, stop=False)
                    dg_ = dgt2[qt % 2]
                    tensor.matmul(pOw, relvp[:, 0:64], dg_[0:128, 0:512], start=False, stop=False)
                    tensor.matmul(pOw, relvp[:, 64:128], dg_[0:128, 512:1024], start=False, stop=False)
                    ins = tensor.matmul(pOw[0:20, 0:512] if False else pOw, relvp[0:20, 64:128],
                                        dg_[0:20, 1024:1536], start=False, stop=True)
                    c.inc(ins, 'w2t', 1)
            blk_all.tensor(s_tensor)

            def s_vector(vector):
                for qt in range(NQT):
                    cb = 4 * qt - 2
                    vector.wait_ge(c.S('avt'), b0['avt'] + qt + 1)
                    if qt >= 1:
                        vector.wait_ge(c.S('avz'), b0['avz'] + 16 * qt)
                    else:
                        vector.wait_ge(c.S('avz'), b0['avz'])
                    vector.drain()
                    if cb >= 0:
                        vector.tensor_copy(zsnb[0:1, 0:512], pOa[64:65, :])
                    for i in range(3):
                        cs = 4 * qt - 1 + i
                        if cs < 0:
                            continue
                        col0 = 128 * (1 + i)
                        vector.tensor_copy(zsg[0:1, 512 * i:512 * i + (512 - col0)],
                                           sng[i][0:1, 0:512 - col0])
                    vector.drain()
                    if cb >= 0:
                        vector.tensor_copy(zrow[:], pOa[64:65, :])
                        vector.drain()
                        vector.tensor_tensor(zrow[:], zrow[:], pOm[64:65, :], ALU.add)
                    else:
                        vector.tensor_copy(zrow[:], pOm[64:65, :])
                    vector.drain()
                    c.inc(vector.tensor_copy(zrowb[:], zrow[:]), 'avv', 1)
                    c.inc(vector.drain(), 'snv', 1)
                    c.inc(vector.drain(), 'avv', 1)
                    vector.wait_ge(c.S('w2t'), b0['w2t'] + qt + 1)
                    vector.drain()
                    sl = slabO[(h % 2) * 64:(h % 2) * 64 + 64,
                               (h // 2) * T + qt * 512:(h // 2) * T + (qt + 1) * 512]
                    if cb >= 0:
                        vector.tensor_copy(sl, pOa[0:64, :])
                        vector.drain()
                        vector.tensor_tensor(sl, sl, pOm[0:64, :], ALU.add)
                    else:
                        vector.tensor_copy(sl, pOm[0:64, :])
                    vector.drain()
                    vector.tensor_tensor(sl, sl, pOw, ALU.add)
                    c.inc(vector.drain(), 'avo', 1)
            blk_all.vector(s_vector)

            def s_gp(gp):
                for qt in range(NQT):
                    gp.wait_ge(c.S('avv'), b0['avv'] + 2 * qt + 2)
                    c.inc(gp.dma_start(out=Zs[h:h + 1, qt * 512:(qt + 1) * 512], in_=zrow[:]), 'avz', 16)
            blk_all.gpsimd(s_gp)

    def l0_attn(s, s0slab):
        with ExitStack() as ps:
            slabO = ps.enter_context(nc.sbuf_tensor(f"slabO_{s}", [128, NCH * T], BF16))
            Zs = ps.enter_context(nc.sbuf_tensor(f"b_zs_{s}", [16, T], F32))
            htil = ExitStack()
            Aslab = htil.enter_context(nc.sbuf_tensor(f"b_aslab_{s}", [128, NTC * T], BF16))
            kw = htil.enter_context(nc.sbuf_tensor(f"b_kw_{s}", [64, 256], BF16))
            h4t = htil.enter_context(nc.sbuf_tensor(f"b_h4t_{s}", [128, 2 * 276], BF16))
            stg = htil.enter_context(nc.sbuf_tensor(f"b_stg_{s}", [128, 3 * 2048], BF16))
            zrow = htil.enter_context(nc.sbuf_tensor(f"b_zrow_{s}", [1, 512], F32))
            zrowb = htil.enter_context(nc.sbuf_tensor(f"b_zrowb_{s}", [1, 512], BF16))
            zsnb = htil.enter_context(nc.sbuf_tensor(f"b_zsnb_{s}", [1, 512], BF16))
            bufs = []
            for par in range(2):
                qkt = htil.enter_context(nc.sbuf_tensor(f"b_qkt{par}_{s}", [128, T], BF16))
                QTh, KTh = qkt[0:64, :], qkt[64:128, :]
                Vah = htil.enter_context(nc.sbuf_tensor(f"b_vah{par}_{s}", [128, NTC * 65], BF16))
                pslab = htil.enter_context(nc.sbuf_tensor(f"b_pslab{par}_{s}", [41, 2208], BF16))
                cstr = htil.enter_context(nc.sbuf_tensor(f"b_cstr{par}_{s}", [128, 16 * 168], BF16))
                ptil = htil.enter_context(nc.sbuf_tensor(f"b_ptil{par}_{s}", [128, 672], BF16))
                bufs.append((QTh, KTh, Vah, pslab, cstr, ptil))
            pstk = ExitStack()
            pst = {
                'pscA': pstk.enter_context(nc.psum_tensor(f"b_pscA_{s}", [128, 1024], F32)),
                'pscB': pstk.enter_context(nc.psum_tensor(f"b_pscB_{s}", [128, 1024], F32)),
                'psPh': pstk.enter_context(nc.psum_tensor(f"b_psPh_{s}", [41, 512], F32)),
                'ptp': pstk.enter_context(nc.psum_tensor(f"b_ptp_{s}", [128, 672], BF16)),
                'ptp2': pstk.enter_context(nc.psum_tensor(f"b_ptp2_{s}", [128, 384], BF16)),
                'pZ3': pstk.enter_context(nc.psum_tensor(f"b_pZ3_{s}", [1, 256], F32)),
            }
            hstk = ExitStack()
            blk_heads = hstk.enter_context(nc.Block())
            with nc.named_scope("l0head"):
                l0_r1(s, 0, bufs, pst, blk_heads)
                for h in range(H0):
                    if h + 1 < H0:
                        l0_r1(s, h + 1, bufs, pst, blk_heads)
                    l0_r234(s, h, bufs, pst, slabO, Zs, Aslab, kw, h4t, stg,
                            zrow, zrowb, zsnb, blk_heads)
            hstk.close()
            pstk.close()
            c.dma_fence('avz')
            htil.close()

            if DBG:
                with nc.Block() as blk:
                    def s_gp(gp):
                        for ch in range(NCH):
                            c.inc(gp.dma_start(out=dbg['oslab'].ap()[:, ch * T:(ch + 1) * T],
                                               in_=slabO[:, ch * T:(ch + 1) * T]), 'dbg', 16)
                        c.inc(gp.dma_start(out=dbg['zs'].ap()[:], in_=Zs[:]), 'dbg', 16)
                    blk.gpsimd(s_gp)

            with nc.Block() as blk:
                def s_vector(vector):
                    if DBG:
                        c.wait(vector, 'dbg')
                    c.inc(vector.reciprocal(Zs[:], Zs[:]), 'nzv', 1)
                blk.vector(s_vector)

            with ExitStack() as pnr:
                prz = [pnr.enter_context(nc.psum_tensor(f"b_prz{i}_{s}", [128, T], F32)) for i in range(2)]
                with nc.Block() as blk:
                    b0 = c.base('nzt', 'nzv')

                    def s_tensor(tensor):
                        c.wait(tensor, 'nzv')
                        for vch in range(NCH):
                            if vch >= 2:
                                tensor.wait_ge(c.S('nzv'), b0['nzv'] + vch - 1)
                            for qt in range(NQT):
                                ins = tensor.matmul(prz[vch % 2][:, qt * 512:(qt + 1) * 512],
                                                    ehd[0:16, vch * 128:(vch + 1) * 128],
                                                    Zs[0:16, qt * 512:(qt + 1) * 512],
                                                    start=True, stop=True)
                            c.inc(ins, 'nzt', 1)
                    blk.tensor(s_tensor)

                    def s_vector(vector):
                        for vch in range(NCH):
                            vector.wait_ge(c.S('nzt'), b0['nzt'] + vch + 1)
                            vector.drain()
                            sl = slabO[:, vch * T:(vch + 1) * T]
                            c.inc(vector.tensor_tensor(sl, sl, prz[vch % 2][:], ALU.mult), 'nzv', 1)
                    blk.vector(s_vector)

            wo_b = ps.enter_context(nc.sbuf_tensor(f"b_wo_{s}", [128, NCH * D], BF16))
            f32a = ps.enter_context(nc.sbuf_tensor(f"b_wof_{s}", [128, 1024], F32))
            f32b = ps.enter_context(nc.sbuf_tensor(f"b_wof2_{s}", [128, 1024], F32))
            pP = ps.enter_context(nc.psum_tensor(f"b_pP_{s}", [128, T], F32))
            with nc.Block() as blk:
                b0 = c.base('wol0', 'wol1', 'woc')

                def s_sync(sync):
                    for n in range(NCH):
                        if n >= 2:
                            sync.wait_ge(c.S('woc'), b0['woc'] + n - 1)
                        buf = f32a if n % 2 == 0 else f32b
                        c.inc(sync.dma_start(out=buf[:], in_=wT["woT"].ap()[n * 128:(n + 1) * 128, :]),
                              'wol%d' % (n % 2), 16)
                blk.sync(s_sync)

                def s_vector(vector):
                    for n in range(NCH):
                        vector.wait_ge(c.S('wol%d' % (n % 2)), b0['wol%d' % (n % 2)] + 16 * (n // 2 + 1))
                        buf = f32a if n % 2 == 0 else f32b
                        vector.drain()
                        c.inc(vector.tensor_copy(wo_b[:, n * D:(n + 1) * D], buf[:]), 'woc', 1)
                blk.vector(s_vector)

            with nc.Block() as blk:
                b0 = c.base('opt', 'ops')

                def s_tensor(tensor):
                    c.wait(tensor, 'woc')
                    c.wait(tensor, 'nzv')
                    for och in range(NCH):
                        if och >= 1:
                            tensor.wait_ge(c.S('ops'), b0['ops'] + och)
                        for vch in range(NCH):
                            wsl = wo_b[:, vch * D + och * 128:vch * D + (och + 1) * 128]
                            for qt in range(NQT):
                                ins = tensor.matmul(pP[:, qt * 512:(qt + 1) * 512], wsl,
                                                    slabO[:, vch * T + qt * 512:vch * T + (qt + 1) * 512],
                                                    start=(vch == 0), stop=(vch == NCH - 1))
                        c.inc(ins, 'opt', 1)
                blk.tensor(s_tensor)

                def s_scalar(scalar):
                    for och in range(NCH):
                        scalar.wait_ge(c.S('opt'), b0['opt'] + och + 1)
                        c.inc(scalar.activation(s0slab[:, och * T:(och + 1) * T], pP[:], AT.Identity,
                                                bias=bias_sb[:, 3 * NCH + och:3 * NCH + och + 1]), 'ops', 1)
                blk.scalar(s_scalar)

            if DBG:
                with nc.Block() as blk:
                    def s_gp(gp):
                        c.wait(gp, 'ops')
                        for ch in range(NCH):
                            c.inc(gp.dma_start(out=dbg['s0'].ap()[s, ch], in_=s0slab[:, ch * T:(ch + 1) * T]), 'dbg', 16)
                    blk.gpsimd(s_gp)
            c.dma_fence('dbg', 'nzb', 'nzc')


    def p1d(s, s0slab):
        with ExitStack() as ps:
            rhsq = ps.enter_context(nc.sbuf_tensor(f"d_rhsq_{s}", [128, NCH * 1024], BF16))
            xq0 = ps.enter_context(nc.sbuf_tensor(f"d_xq0_{s}", [128, NCH * T], BF16))
            xc = [ps.enter_context(nc.sbuf_tensor(f"d_xc{i}_{s}", [128, T], F32)) for i in range(2)]
            xnw = ps.enter_context(nc.sbuf_tensor(f"d_xnw_{s}", [128, T], F32))
            gw = ps.enter_context(nc.sbuf_tensor(f"d_gw_{s}", [128, T], F32))
            bca = ps.enter_context(nc.sbuf_tensor(f"d_bca_{s}", [128, T], F32))
            bcb = ps.enter_context(nc.sbuf_tensor(f"d_bcb_{s}", [128, T], F32))
            lsb = ps.enter_context(nc.sbuf_tensor(f"d_lsb_{s}", [128, T], F32))
            hop = ps.enter_context(nc.sbuf_tensor(f"d_hop_{s}", [1, T], F32))
            rscol = ps.enter_context(nc.sbuf_tensor(f"d_rscol_{s}", [128, NCH], F32))
            q_ti = ps.enter_context(nc.sbuf_tensor(f"d_ti_{s}", [128, T], I32))
            q_f = ps.enter_context(nc.sbuf_tensor(f"d_f_{s}", [128, T], F32))
            q_d = ps.enter_context(nc.sbuf_tensor(f"d_d_{s}", [128, T], F32))
            q_sg = ps.enter_context(nc.sbuf_tensor(f"d_sg_{s}", [128, T], F32))
            amx = ps.enter_context(nc.sbuf_tensor(f"d_amx_{s}", [128, T], F32))
            h_row = h_gp

            with nc.Block() as blk:
                def s_gp(gp):
                    c.wait(gp, 'qst')
                    for ch in range(NCH):
                        c.inc(gp.dma_start(out=rhsq[:, ch * 1024:(ch + 1) * 1024], in_=h_rhsq.ap()[ch]), 'dld', 16)
                    c.inc(gp.dma_start(out=h_rowf.ap()[0:1, 0:1024], in_=grows[0:1, :]), 'dld', 16)
                    c.wait(gp, 'dld')
                    c.inc(gp.dma_start(out=rscol[:, :],
                                       in_=AP(h_rowf.ap().tensor, 0, [[1, 128], [128, NCH], [1, 1]]),
                                       allow_slow_non_contiguous=True), 'dld', 16)
                    c.inc(gp.dma_start(out=hop[:], in_=rows[8 * s + R_A0:8 * s + R_A0 + 1, :]), 'dh', 16)
                    c.wait(gp, 'dh')
                    bcast_row(gp, c, bca, hop[0:1, :], T, 'dbc')
                    c.wait(gp, 'dbc')
                    c.inc(gp.dma_start(out=hop[:], in_=rows[8 * s + R_B0:8 * s + R_B0 + 1, :]), 'dh', 16)
                    c.wait(gp, 'dh')
                    bcast_row(gp, c, bcb, hop[0:1, :], T, 'dbc')
                    c.wait(gp, 'dbc')
                    c.inc(gp.dma_start(out=hop[:], in_=rows[8 * s + R_LS0:8 * s + R_LS0 + 1, :]), 'dh', 16)
                    c.wait(gp, 'dh')
                blk.gpsimd(s_gp)

                def s_vector(vector):
                    c.wait(vector, 'dh')
                    c.inc(vector.reciprocal(hop[:], hop[:]), 'dv', 1)
                blk.vector(s_vector)

            with nc.Block() as blk:
                def s_gp(gp):
                    c.wait(gp, 'dv')
                    bcast_row(gp, c, lsb, hop[0:1, :], T, 'dbc')
                blk.gpsimd(s_gp)

            with nc.Block() as blk:
                b0 = c.base('dxl0', 'dxl1', 'dxv', 'dxs')

                def s_sync(sync):
                    c.wait(sync, 'dbc')
                    for ch in range(NCH):
                        if ch >= 2:
                            sync.wait_ge(c.S('dxv'), b0['dxv'] + 2 * ch - 3)
                        c.inc(sync.dma_start(out=xc[ch % 2][:], in_=x_in.ap()[s, ch * 128:(ch + 1) * 128, :]),
                              'dxl%d' % (ch % 2), 16)
                blk.sync(s_sync)

                def s_vector(vector):
                    c.wait(vector, 'dbc')
                    for ch in range(NCH):
                        vector.wait_ge(c.S('dxl%d' % (ch % 2)), b0['dxl%d' % (ch % 2)] + 16 * (ch // 2 + 1))
                        vector.drain()
                        vector.tensor_tensor(xnw[:], xc[ch % 2][:], bca[:], ALU.mult)
                        vector.drain()
                        vector.tensor_tensor(xnw[:], xnw[:], bcb[:], ALU.add)
                        vector.drain()
                        vector.tensor_tensor(q_d[:], xnw[:], lsb[:], ALU.mult)
                        vector.drain()
                        vector.tensor_copy(q_ti[:], q_d[:])
                        vector.drain()
                        vector.tensor_copy(q_f[:], q_ti[:])
                        vector.drain()
                        vector.tensor_tensor(q_d[:], q_d[:], q_f[:], ALU.subtract)
                        vector.drain()
                        vector.tensor_scalar(q_sg[:], q_d[:], 0.5, -0.5, ALU.min, ALU.max)
                        vector.drain()
                        c.inc(vector.tensor_tensor(q_d[:], q_d[:], q_sg[:], ALU.subtract), 'dxv', 1)
                        vector.wait_ge(c.S('dxs'), b0['dxs'] + ch + 1)
                        vector.drain()
                        c.inc(vector.tensor_tensor(xq0[:, ch * T:(ch + 1) * T], q_f[:], q_sg[:], ALU.add), 'dxv', 1)
                blk.vector(s_vector)

                def s_scalar(scalar):
                    for ch in range(NCH):
                        scalar.wait_ge(c.S('dxv'), b0['dxv'] + 2 * ch + 1)
                        c.inc(scalar.activation(q_sg[:], q_d[:], AT.Sign), 'dxs', 1)
                blk.scalar(s_scalar)

            with nc.Block() as blk:
                def s_gp(gp):
                    c.wait(gp, 'dxv')
                    c.inc(gp.dma_start(out=hop[:], in_=rows[8 * s + R_LS0:8 * s + R_LS0 + 1, :]), 'dh', 16)
                    c.wait(gp, 'dh')
                    bcast_row(gp, c, bca, hop[0:1, :], T, 'dbc3')
                blk.gpsimd(s_gp)

            with ExitStack() as psr:
                pF = psr.enter_context(nc.psum_tensor(f"d_pF_{s}", [128, T], F32))
                with nc.Block() as blk:
                    b0 = c.base('dft', 'dfv', 'dfs', 'dfx', 'dx1', 'dxr0', 'dxr1', 'dbgg')

                    def s_tensor(tensor):
                        c.wait(tensor, 'dxv')
                        for och in range(NCH):
                            if och >= 1:
                                tensor.wait_ge(c.S('dfv'), b0['dfv'] + och)
                            for ich in range(NCH):
                                wsl = rhsq[:, ich * 1024 + och * 128:ich * 1024 + (och + 1) * 128]
                                for qt in range(NQT):
                                    ins = tensor.matmul(pF[:, qt * 512:(qt + 1) * 512], wsl,
                                                        xq0[:, ich * T + qt * 512:ich * T + (qt + 1) * 512],
                                                        start=(ich == 0), stop=(ich == NCH - 1))
                            c.inc(ins, 'dft', 1)
                    blk.tensor(s_tensor)

                    def s_sync(sync):
                        c.wait(sync, 'dxv')
                        for och in range(NCH):
                            if och >= 2:
                                sync.wait_ge(c.S('dfx'), b0['dfx'] + och - 1)
                                sync.wait_ge(c.S('dx1'), b0['dx1'] + 16 * (och - 1))
                            c.inc(sync.dma_start(out=xc[och % 2][:], in_=x_in.ap()[s, och * 128:(och + 1) * 128, :]),
                                  'dxr%d' % (och % 2), 16)
                    blk.sync(s_sync)

                    def s_vector(vector):
                        c.wait(vector, 'dbc3')
                        for och in range(NCH):
                            vector.wait_ge(c.S('dft'), b0['dft'] + och + 1)
                            if DBG and och >= 1:
                                vector.wait_ge(c.S('dbgg'), b0['dbgg'] + 16 * och)
                            vector.drain()
                            c.inc(vector.tensor_tensor(gw[:], pF[:], bca[:], ALU.mult), 'dfv', 1)
                            vector.wait_ge(c.S('dfs'), b0['dfs'] + och + 1)
                            vector.wait_ge(c.S('dxr%d' % (och % 2)), b0['dxr%d' % (och % 2)] + 16 * (och // 2 + 1))
                            vector.drain()
                            vector.tensor_tensor(xnw[:], s0slab[:, och * T:(och + 1) * T], gw[:], ALU.mult)
                            vector.drain()
                            c.inc(vector.tensor_tensor(xc[och % 2][:], xc[och % 2][:], xnw[:], ALU.add), 'dfx', 1)
                    blk.vector(s_vector)

                    def s_scalar(scalar):
                        c.wait(scalar, 'dld')
                        for och in range(NCH):
                            scalar.wait_ge(c.S('dfv'), b0['dfv'] + och + 1)
                            c.inc(scalar.activation(gw[:], gw[:], AT.Silu,
                                                    scale=rscol[:, och:och + 1]), 'dfs', 1)
                    blk.scalar(s_scalar)

                    def s_gp(gp):
                        for och in range(NCH):
                            gp.wait_ge(c.S('dfx'), b0['dfx'] + och + 1)
                            if och >= 1:
                                gp.wait_ge(c.S('dx1'), b0['dx1'] + 16 * och)
                                if DBG:
                                    gp.wait_ge(c.S('dbgg'), b0['dbgg'] + 16 * och)
                            c.inc(gp.dma_start(out=h_x1.ap()[s, och], in_=xc[och % 2][:]), 'dx1', 16)
                            if DBG:
                                c.inc(gp.dma_start(out=dbg['g0'].ap()[s, och], in_=gw[:]), 'dbgg', 16)
                    blk.gpsimd(s_gp)
            c.dma_fence('dx1', 'dbgg')

            with ExitStack() as psr:
                pstat = psr.enter_context(nc.psum_tensor(f"d_pstat_{s}", [1, T], F32))
                pstatq = psr.enter_context(nc.psum_tensor(f"d_pstatq_{s}", [1, T], F32))
                with nc.Block() as blk:
                    b0 = c.base('dll0', 'dll1', 'dlq', 'dlu')

                    def s_sync(sync):
                        for ch in range(NCH):
                            if ch >= 2:
                                sync.wait_ge(c.S('dlu'), b0['dlu'] + ch - 1)
                            c.inc(sync.dma_start(out=xc[ch % 2][:], in_=h_x1.ap()[s, ch]),
                                  'dll%d' % (ch % 2), 16)
                    blk.sync(s_sync)

                    def s_scalar(scalar):
                        for ch in range(NCH):
                            scalar.wait_ge(c.S('dll%d' % (ch % 2)), b0['dll%d' % (ch % 2)] + 16 * (ch // 2 + 1))
                            if ch >= 1:
                                scalar.wait_ge(c.S('dlu'), b0['dlu'] + ch)
                            c.inc(scalar.activation(q_d[:], xc[ch % 2][:], AT.Square), 'dlq', 1)
                    blk.scalar(s_scalar)

                    def s_tensor(tensor):
                        for ch in range(NCH):
                            tensor.wait_ge(c.S('dlq'), b0['dlq'] + ch + 1)
                            for qt in range(NQT):
                                tensor.matmul(pstat[0:1, qt * 512:(qt + 1) * 512], onescol_f[:],
                                              xc[ch % 2][:, qt * 512:(qt + 1) * 512],
                                              start=(ch == 0), stop=(ch == NCH - 1))
                                ins = tensor.matmul(pstatq[0:1, qt * 512:(qt + 1) * 512], onescol_f[:],
                                                    q_d[:, qt * 512:(qt + 1) * 512],
                                                    start=(ch == 0), stop=(ch == NCH - 1))
                            c.inc(ins, 'dlu', 1)
                    blk.tensor(s_tensor)

                rw0, rw1, rw2, rw3 = bca, bcb, gw, xnw
                with nc.Block() as blk:
                    def s_vector(vector):
                        vector.tensor_scalar_mul(rw0[0:1, 0:T], pstat[0:1, :], 1.0 / D)
                        vector.tensor_scalar_mul(rw1[0:1, 0:T], pstatq[0:1, :], 1.0 / D)
                        vector.drain()
                        vector.tensor_tensor(rw2[0:1, 0:T], rw0[0:1, 0:T], rw0[0:1, 0:T], ALU.mult)
                        vector.drain()
                        c.inc(vector.tensor_tensor(rw1[0:1, 0:T], rw1[0:1, 0:T], rw2[0:1, 0:T], ALU.subtract), 'dlr', 1)
                    blk.vector(s_vector)

                    def s_scalar(scalar):
                        c.wait(scalar, 'dlr')
                        c.inc(scalar.activation(rw1[0:1, 0:T], rw1[0:1, 0:T], AT.Sqrt, bias=cvals[0:1, 0:1]), 'dls', 1)
                    blk.scalar(s_scalar)

                with nc.Block() as blk:
                    def s_vector(vector):
                        c.wait(vector, 'dls')
                        vector.reciprocal(rw2[0:1, 0:T], rw1[0:1, 0:T])
                        vector.drain()
                        vector.tensor_tensor(rw3[0:1, 0:T], rw0[0:1, 0:T], rw2[0:1, 0:T], ALU.mult)
                        vector.drain()
                        c.inc(vector.tensor_scalar_mul(rw3[0:1, 0:T], rw3[0:1, 0:T], -1.0), 'dlr', 1)
                    blk.vector(s_vector)

                    def s_gp(gp):
                        c.wait(gp, 'dlr')
                        c.inc(gp.dma_start(out=rows[8 * s + R_A1:8 * s + R_A1 + 1, :], in_=rw2[0:1, 0:T]), 'dpk', 16)
                        c.inc(gp.dma_start(out=rows[8 * s + R_B1:8 * s + R_B1 + 1, :], in_=rw3[0:1, 0:T]), 'dpk', 16)
                        c.wait(gp, 'dpk')
                        bcast_row(gp, c, bca, rw2[0:1, 0:T], T, 'dbc2')
                        bcast_row(gp, c, bcb, rw3[0:1, 0:T], T, 'dbc2')
                    blk.gpsimd(s_gp)

            with nc.Block() as blk:
                b0 = c.base('dal0', 'dal1', 'dav', 'dast', 'dlq')

                def s_sync(sync):
                    c.wait(sync, 'dbc2')
                    for ch in range(NCH):
                        if ch >= 2:
                            sync.wait_ge(c.S('dav'), b0['dav'] + 2 * (ch - 1))
                        c.inc(sync.dma_start(out=xc[ch % 2][:], in_=h_x1.ap()[s, ch]),
                              'dal%d' % (ch % 2), 16)
                blk.sync(s_sync)

                def s_vector(vector):
                    c.wait(vector, 'dbc2')
                    for ch in range(NCH):
                        vector.wait_ge(c.S('dal%d' % (ch % 2)), b0['dal%d' % (ch % 2)] + 16 * (ch // 2 + 1))
                        if ch >= 2:
                            vector.wait_ge(c.S('dast'), b0['dast'] + 16 * (ch - 1))
                        vector.drain()
                        vector.tensor_tensor(q_d[:], xc[ch % 2][:], bca[:], ALU.mult)
                        vector.drain()
                        vector.tensor_tensor(q_d[:], q_d[:], bcb[:], ALU.add)
                        vector.drain()
                        c.inc(vector.tensor_copy(xq0[:, ch * T:(ch + 1) * T], q_d[:]), 'dav', 1)
                        vector.wait_ge(c.S('dlq'), b0['dlq'] + ch + 1)
                        vector.drain()
                        if ch == 0:
                            ins = vector.tensor_copy(amx[:], gw[:])
                        else:
                            ins = vector.tensor_tensor(amx[:], amx[:], gw[:], ALU.max)
                        c.inc(ins, 'dav', 1)
                blk.vector(s_vector)

                def s_scalar(scalar):
                    for ch in range(NCH):
                        scalar.wait_ge(c.S('dav'), b0['dav'] + 2 * ch + 1)
                        c.inc(scalar.activation(gw[:], q_d[:], AT.Abs), 'dlq', 1)
                blk.scalar(s_scalar)

                def s_gp(gp):
                    for ch in range(NCH):
                        gp.wait_ge(c.S('dav'), b0['dav'] + 2 * ch + 1)
                        if ch >= 1:
                            gp.wait_ge(c.S('dast'), b0['dast'] + 16 * ch)
                        c.inc(gp.dma_start(out=h_xn1.ap()[s, ch], in_=xq0[:, ch * T:(ch + 1) * T]), 'dast', 16)
                blk.gpsimd(s_gp)

            with nc.Block() as blk:
                def s_vector(vector):
                    c.wait(vector, 'dav')
                    part_tree(c, vector, amx, q_d, T)
                    vector.tensor_scalar(amx[0:1, 0:T], amx[0:1, :], 1e-5, None, ALU.max)
                    vector.drain()
                    vector.reciprocal(amx[0:1, 0:T], amx[0:1, 0:T])
                    vector.drain()
                    c.inc(vector.tensor_scalar_mul(amx[0:1, 0:T], amx[0:1, 0:T], 127.0), 'dlsv', 1)
                blk.vector(s_vector)

                def s_gp(gp):
                    c.wait(gp, 'dlsv')
                    c.inc(gp.dma_start(out=rows[8 * s + R_LS1:8 * s + R_LS1 + 1, :], in_=amx[0:1, 0:T]), 'dpk', 16)
                blk.gpsimd(s_gp)
            c.dma_fence('dast', 'dpk')


    def p2_l1attn(s):
        with ExitStack() as ps:
            uslab = ps.enter_context(nc.sbuf_tensor(f"e_u_{s}", [128, NTC * 512], BF16))
            VaL1 = ps.enter_context(nc.sbuf_tensor(f"e_va_{s}", [128, H1 * NTC * 65], BF16))
            oslab1 = ps.enter_context(nc.sbuf_tensor(f"e_os_{s}", [128, NCH * T], BF16))
            Zs2 = ps.enter_context(nc.sbuf_tensor(f"e_zs_{s}", [16, T], F32))
            zsc2 = ps.enter_context(nc.sbuf_tensor(f"e_zsc_{s}", [1, T], F32))

            pp_scope = nc.enter_named_scope("p2proj", False)
            with ExitStack() as pp:
                xn1b = pp.enter_context(nc.sbuf_tensor(f"e_xn_{s}", [128, NCH * T], BF16))
                w2qk = pp.enter_context(nc.sbuf_tensor(f"e_wqk_{s}", [128, NCH * 512], BF16))
                w2v = pp.enter_context(nc.sbuf_tensor(f"e_wv_{s}", [128, NCH * 1024], BF16))
                uw = pp.enter_context(nc.sbuf_tensor(f"e_uw_{s}", [128, 512], F32))
                sqw = pp.enter_context(nc.sbuf_tensor(f"e_sqw_{s}", [128, 512], F32))
                m32 = pp.enter_context(nc.sbuf_tensor(f"e_m32_{s}", [128, 32], F32))
                v32 = pp.enter_context(nc.sbuf_tensor(f"e_v32_{s}", [128, 32], F32))
                r32 = pp.enter_context(nc.sbuf_tensor(f"e_r32_{s}", [128, 32], F32))
                pQ = [pp.enter_context(nc.psum_tensor(f"e_pq{i}_{s}", [128, 1536], F32)) for i in range(2)]
                load_weights_bf16(pp, f"l1w_{s}", [("wq2T", w2qk, NCH, 256, 512, 0),
                                                   ("wk2T", w2qk, NCH, 256, 512, 256),
                                                   ("wv2T", w2v, NCH)])

                with nc.Block() as blk:
                    def s_sync(sync):
                        for ch in range(NCH):
                            c.inc(sync.dma_start(out=xn1b[:, ch * T:(ch + 1) * T], in_=h_xn1.ap()[s, ch]), 'rld', 16)
                    blk.sync(s_sync)

                    def s_vector(vector):
                        ones_ap = AP(VaL1, 64, [[H1 * NTC * 65, 128], [NTC * 65, H1], [65, NTC]])
                        c.inc(vector.memset(ones_ap, 1.0), 'qv', 1)
                    blk.vector(s_vector)

                with nc.Block() as blk:
                    b0 = c.base('pjt', 'pjv', 'lnx', 'lns')

                    def s_tensor(tensor):
                        c.wait(tensor, 'rld')
                        for tc in range(NTC):
                            p = pQ[tc % 2]
                            if tc >= 2:
                                tensor.wait_ge(c.S('pjv'), b0['pjv'] + tc - 1)
                            for i in range(NCH):
                                xsl = xn1b[:, i * T + tc * 128:i * T + (tc + 1) * 128]
                                tensor.matmul(p[:, 0:512], xsl, w2qk[:, i * 512:(i + 1) * 512],
                                              start=(i == 0), stop=(i == NCH - 1))
                                tensor.matmul(p[:, 512:1024], xsl, w2v[:, i * 1024:i * 1024 + 512],
                                              start=(i == 0), stop=(i == NCH - 1))
                                ins = tensor.matmul(p[:, 1024:1536], xsl, w2v[:, i * 1024 + 512:(i + 1) * 1024],
                                                    start=(i == 0), stop=(i == NCH - 1))
                            c.inc(ins, 'pjt', 1)
                    blk.tensor(s_tensor)

                    def s_vector(vector):
                        for tc in range(NTC):
                            p = pQ[tc % 2]
                            vector.wait_ge(c.S('pjt'), b0['pjt'] + tc + 1)
                            vector.drain()
                            vector.tensor_reduce(m32[:], p[:, 0:512].rearrange("p (a b) -> p a b", b=16),
                                                 AX.X, ALU.add)
                            vector.drain()
                            vector.tensor_scalar_mul(m32[:], m32[:], 1.0 / 16.0)
                            vector.drain()
                            c.inc(vector.tensor_tensor(uw[:].rearrange("p (a b) -> p a b", b=16),
                                                       p[:, 0:512].rearrange("p (a b) -> p a b", b=16),
                                                       AP(m32, 0, [[32, 128], [1, 32], [0, 16]]),
                                                       ALU.subtract), 'lnx', 1)
                            vector.wait_ge(c.S('lns'), b0['lns'] + 2 * tc + 1)
                            vector.drain()
                            vector.tensor_reduce(v32[:], sqw[:].rearrange("p (a b) -> p a b", b=16),
                                                 AX.X, ALU.add)
                            vector.drain()
                            c.inc(vector.tensor_scalar_mul(v32[:], v32[:], 1.0 / 16.0), 'lnx', 1)
                            vector.wait_ge(c.S('lns'), b0['lns'] + 2 * tc + 2)
                            vector.drain()
                            vector.reciprocal(r32[:], r32[:])
                            vector.drain()
                            vector.tensor_tensor(uw[:].rearrange("p (a b) -> p a b", b=16),
                                                 uw[:].rearrange("p (a b) -> p a b", b=16),
                                                 AP(r32, 0, [[32, 128], [1, 32], [0, 16]]), ALU.mult)
                            vector.drain()
                            vector.tensor_tensor(uw[:].rearrange("p (a b) -> p a b", b=16),
                                                 uw[:].rearrange("p (a b) -> p a b", b=16),
                                                 AP(gbig, 0, [[32, 128], [0, 32], [1, 16]]), ALU.mult)
                            vector.drain()
                            vector.tensor_tensor(uw[:].rearrange("p (a b) -> p a b", b=16),
                                                 uw[:].rearrange("p (a b) -> p a b", b=16),
                                                 AP(gbig, 16, [[32, 128], [0, 32], [1, 16]]), ALU.add)
                            vector.drain()
                            vector.tensor_copy(uslab[:, tc * 512:(tc + 1) * 512], uw[:])
                            vector.drain()
                            c.inc(vector.tensor_copy(
                                AP(VaL1, tc * 65, [[H1 * NTC * 65, 128], [NTC * 65, H1], [1, 64]]),
                                p[:, 512:1536].rearrange("p (a b) -> p a b", b=64)), 'pjv', 1)
                    blk.vector(s_vector)

                    def s_scalar(scalar):
                        for tc in range(NTC):
                            scalar.wait_ge(c.S('lnx'), b0['lnx'] + 2 * tc + 1)
                            c.inc(scalar.activation(sqw[:], uw[:], AT.Square), 'lns', 1)
                            scalar.wait_ge(c.S('lnx'), b0['lnx'] + 2 * tc + 2)
                            c.inc(scalar.activation(r32[:], v32[:], AT.Sqrt, bias=cvals[:, 0:1]), 'lns', 1)
                    blk.scalar(s_scalar)

            nc.leave_named_scope("p2proj", pp_scope[0], False)
            if DBG:
                with nc.Block() as blk:
                    def s_gp(gp):
                        c.inc(gp.dma_start(out=dbg['ufm'].ap()[s], in_=uslab[:]), 'dbg', 16)
                        for hh in range(H1):
                            c.inc(gp.dma_start(out=dbg['val1'].ap()[s][:, hh * 1040:(hh + 1) * 1040],
                                               in_=VaL1[:, hh * 1040:(hh + 1) * 1040]), 'dbg', 16)
                        c.wait(gp, 'dbg')
                    blk.gpsimd(s_gp)

            sg_scope = nc.enter_named_scope("p2scan", False)
            with ExitStack() as sg:
                qfg = sg.enter_context(nc.sbuf_tensor(f"e_qfg_{s}", [128, NTC * 512], BF16))
                kfg = sg.enter_context(nc.sbuf_tensor(f"e_kfg_{s}", [128, NTC * 512], BF16))
                uqTg = sg.enter_context(nc.sbuf_tensor(f"e_uqt_{s}", [16, 2 * T], BF16))
                ukTg = sg.enter_context(nc.sbuf_tensor(f"e_ukt_{s}", [16, 2 * T], BF16))
                qfT = sg.enter_context(nc.sbuf_tensor(f"e_qft_{s}", [128, 4 * T], BF16))
                attm2 = [sg.enter_context(nc.sbuf_tensor(f"e_attm{j}_{s}", [128, NTC * 128], BF16)) for j in range(2)]
                attsq = sg.enter_context(nc.sbuf_tensor(f"e_attq_{s}", [128, 1024], BF16))
                stsnap = sg.enter_context(nc.sbuf_tensor(f"e_stsn_{s}", [128, 520], BF16))
                stf32 = sg.enter_context(nc.sbuf_tensor(f"e_stf_{s}", [128, 260], F32))
                zsc3 = sg.enter_context(nc.sbuf_tensor(f"e_zsc3_{s}", [1, T], F32))

                for g in range(8):
                    with ExitStack() as pg:
                        ptu = pg.enter_context(nc.psum_tensor(f"e_ptu_{s}_{g}", [16, 2 * T], BF16))
                        with nc.Block() as blk:
                            b0 = c.base('lnu', 'lnd')

                            def s_vector(vector, g=g):
                                for tc in range(NTC):
                                    for r in range(2):
                                        h = 2 * g + r
                                        for (dst, off) in ((qfg, tc * 512 + h * 16), (kfg, tc * 512 + 256 + h * 16)):
                                            vector.tensor_tensor(
                                                dst[:, tc * 512 + r * 256:tc * 512 + (r + 1) * 256]
                                                .rearrange("p (a b) -> p a b", b=16),
                                                AP(uslab, off, [[NTC * 512, 128], [1, 16], [0, 16]]),
                                                AP(uslab, off, [[NTC * 512, 128], [0, 16], [1, 16]]),
                                                ALU.mult)
                                c.inc(vector.drain(), 'lnr', 1)
                                for n in range(4):
                                    r, tn = n % 2, n // 2
                                    dstT = uqTg if tn == 0 else ukTg
                                    vector.wait_ge(c.S('lnu'), b0['lnu'] + n + 1)
                                    vector.drain()
                                    c.inc(vector.tensor_copy(dstT[0:16, r * T:(r + 1) * T],
                                                             ptu[0:16, (n % 2) * T:(n % 2) * T + T]), 'lnd', 1)
                            blk.vector(s_vector)

                            def s_tensor(tensor, g=g):
                                c.wait(tensor, 'lnr')
                                for n in range(4):
                                    r, tn = n % 2, n // 2
                                    h = 2 * g + r
                                    if n >= 2:
                                        tensor.wait_ge(c.S('lnd'), b0['lnd'] + n - 1)
                                    for tc in range(NTC):
                                        ins = tensor.transpose(
                                            ptu[0:16, (n % 2) * T + tc * 128:(n % 2) * T + (tc + 1) * 128],
                                            uslab[:, tc * 512 + tn * 256 + h * 16:tc * 512 + tn * 256 + h * 16 + 16],
                                            ident[:])
                                    c.inc(ins, 'lnu', 1)
                            blk.tensor(s_tensor)

                    bstk2 = ExitStack()
                    blk2 = bstk2.enter_context(nc.Block())
                    with ExitStack() as ph:
                        ptq = ph.enter_context(nc.psum_tensor(f"e_ptq_{s}_{g}", [128, 2 * T], BF16))
                        if True:
                            def s_tensor(tensor):
                                b0d = c.base('dav')
                                for r in range(2):
                                    if r == 1:
                                        tensor.wait_ge(c.S('dav'), b0d['dav'] + 1)
                                    for pc in range(2):
                                        for tc in range(NTC):
                                            ins = tensor.transpose(
                                                ptq[:, pc * T + tc * 128:pc * T + (tc + 1) * 128],
                                                qfg[:, tc * 512 + r * 256 + pc * 128:tc * 512 + r * 256 + pc * 128 + 128],
                                                ident[:])
                                    c.inc(ins, 'lnsq', 1)
                            blk2.tensor(s_tensor)

                            def s_vector(vector):
                                b0q = c.base('lnsq')
                                for r in range(2):
                                    vector.wait_ge(c.S('lnsq'), b0q['lnsq'] + r - 1)
                                    vector.drain()
                                    c.inc(vector.tensor_copy(qfT[:, r * 2 * T:(r + 1) * 2 * T], ptq[:]), 'dav', 1)
                            blk2.vector(s_vector)

                    with ExitStack() as ph:
                        pA = ph.enter_context(nc.psum_tensor(f"e_pa_{s}_{g}", [128, 1024], F32))
                        pOL = [ph.enter_context(nc.psum_tensor(f"e_pol{j}_{s}_{g}", [65, 1024], F32)) for j in range(2)]
                        pstate = [ph.enter_context(nc.psum_tensor(f"e_pst{j}_{s}_{g}", [128, 130], F32)) for j in range(2)]
                        if True:
                            b0 = c.base('dft', 'dfs', 'dfv', 'dxs', 'dxv', 'dlsv', 'avv', 'dpk')

                            def s_tensor(tensor, g=g):
                                c.wait(tensor, 'dav')
                                for k in range(4):
                                    r, w = k % 2, k // 2
                                    if k >= 1:
                                        tensor.wait_ge(c.S('dfs'), b0['dfs'] + k)
                                    for j8 in range(8):
                                        p = 8 * w + j8
                                        ins = tensor.matmul(
                                            pA[:, j8 * 128:(j8 + 1) * 128],
                                            ukTg[0:16, r * T + p * 128:r * T + (p + 1) * 128],
                                            uqTg[0:16, r * T + p * 128:r * T + (p + 1) * 128],
                                            start=True, stop=True)
                                    c.inc(ins, 'dft', 1)
                                for p in range(NTC):
                                    if p == 8:
                                        tensor.wait_ge(c.S('avv'), b0['avv'] + 2)
                                    for j in range(2):
                                        h = 2 * g + j
                                        tensor.wait_ge(c.S('dfv'), b0['dfv'] + 2 * (p // 8) + j + 1)
                                        if p > 0:
                                            tensor.wait_ge(c.S('dxv'), b0['dxv'] + 2 * (p - 1) + j + 1)
                                        va = VaL1[:, h * 1040 + p * 65:h * 1040 + (p + 1) * 65]
                                        po = pOL[j]
                                        cc = (p % 8) * 128
                                        tensor.matmul(po[:, cc:cc + 128], va,
                                                      attm2[j][:, p * 128:(p + 1) * 128],
                                                      start=True, stop=(p == 0))
                                        if p > 0:
                                            tensor.matmul(po[:, cc:cc + 128], stsnap[:, 260 * j:260 * j + 65],
                                                          qfT[:, j * 2 * T + p * 128:j * 2 * T + (p + 1) * 128],
                                                          start=False, stop=False)
                                            tensor.matmul(po[:, cc:cc + 128], stsnap[:, 260 * j + 65:260 * j + 130],
                                                          qfT[:, j * 2 * T + T + p * 128:j * 2 * T + T + (p + 1) * 128],
                                                          start=False, stop=True)
                                        tensor.matmul(pstate[j][:, 0:65],
                                                      kfg[:, p * 512 + j * 256:p * 512 + j * 256 + 128],
                                                      va, start=True, stop=True)
                                        ins = tensor.matmul(pstate[j][:, 65:130],
                                                            kfg[:, p * 512 + j * 256 + 128:p * 512 + j * 256 + 256],
                                                            va, start=True, stop=True)
                                        c.inc(ins, 'dxs', 1)
                            blk2.tensor(s_tensor)

                            def s_scalar(scalar):
                                for k in range(4):
                                    scalar.wait_ge(c.S('dft'), b0['dft'] + k + 1)
                                    if k >= 1:
                                        scalar.wait_ge(c.S('dfv'), b0['dfv'] + k)
                                    c.inc(scalar.activation(attsq[:], pA[:], AT.Square), 'dfs', 1)
                                for p in range(NTC):
                                    for j in range(2):
                                        scalar.wait_ge(c.S('dlsv'), b0['dlsv'] + 2 * p + j + 1)
                                        c.inc(scalar.activation(stsnap[:, 260 * j:260 * j + 130],
                                                                stf32[:, 130 * j:130 * j + 130], AT.Identity),
                                              'dxv', 1)
                            blk2.scalar(s_scalar)

                            def s_vector(vector, g=g):
                                vector.memset(stf32[:], 0.0)
                                vector.drain()
                                for k in range(4):
                                    r, w = k % 2, k // 2
                                    vector.wait_ge(c.S('dfs'), b0['dfs'] + k + 1)
                                    vector.drain()
                                    c.inc(vector.tensor_tensor(
                                        attm2[r][:, w * 1024:(w + 1) * 1024].rearrange("p (a b) -> p a b", b=128),
                                        attsq[:].rearrange("p (a b) -> p a b", b=128),
                                        AP(triu128p, 0, [[128, 128], [0, 8], [1, 128]]),
                                        ALU.mult), 'dfv', 1)
                                for p in range(NTC):
                                    for j in range(2):
                                        h = 2 * g + j
                                        vector.wait_ge(c.S('dxs'), b0['dxs'] + 2 * p + j + 1)
                                        if p >= 1:
                                            vector.wait_ge(c.S('dxv'), b0['dxv'] + 2 * (p - 1) + j + 1)
                                        vector.drain()
                                        c.inc(vector.tensor_tensor(stf32[:, 130 * j:130 * j + 130],
                                                                   stf32[:, 130 * j:130 * j + 130],
                                                                   pstate[j][:], ALU.add), 'dlsv', 1)
                                    if p == 7 or p == 15:
                                        hf = p // 8
                                        c.wait(vector, 'dpk', c.cnt['dpk'])
                                        for j in range(2):
                                            h = 2 * g + j
                                            zdst = zsc2 if j == 0 else zsc3
                                            vector.tensor_copy(
                                                oslab1[(h % 2) * 64:(h % 2) * 64 + 64,
                                                       (h // 2) * T + hf * 1024:(h // 2) * T + (hf + 1) * 1024],
                                                pOL[j][0:64, :])
                                            vector.drain()
                                            c.inc(vector.tensor_copy(zdst[0:1, hf * 1024:(hf + 1) * 1024],
                                                                     pOL[j][64:65, :]), 'avv', 1)
                            blk2.vector(s_vector)

                            def s_gp(gp, g=g):
                                c.wait(gp, 'avv')
                                c.inc(gp.dma_start(out=Zs2[2 * g:2 * g + 1, :], in_=zsc2[0:1, :]), 'dpk', 16)
                                gp.wait_ge(c.S('dpk'), c.cnt['dpk'])
                                c.inc(gp.dma_start(out=Zs2[2 * g + 1:2 * g + 2, :], in_=zsc3[0:1, :]), 'dpk', 16)
                            blk2.gpsimd(s_gp)
                    bstk2.close()

            nc.leave_named_scope("p2scan", sg_scope[0], False)
            with nc.Block() as blk:
                def s_vector(vector):
                    c.wait(vector, 'dpk')
                    vector.tensor_scalar(Zs2[:], Zs2[:], 1e-10, None, ALU.add)
                    vector.drain()
                    c.inc(vector.reciprocal(Zs2[:], Zs2[:]), 'nzv', 1)
                blk.vector(s_vector)

            if DBG:
                with nc.Block() as blk:
                    def s_gp(gp):
                        c.wait(gp, 'nzv')
                        c.inc(gp.dma_start(out=dbg['zs1'].ap()[s], in_=Zs2[:]), 'dbg', 16)
                        c.wait(gp, 'dbg')
                    blk.gpsimd(s_gp)

            with ExitStack() as pnr:
                prz = [pnr.enter_context(nc.psum_tensor(f"e_prz{i}_{s}", [128, T], F32)) for i in range(2)]
                with nc.Block() as blk:
                    b0 = c.base('nzt', 'nzv')

                    def s_tensor(tensor):
                        c.wait(tensor, 'nzv')
                        for vch in range(NCH):
                            if vch >= 2:
                                tensor.wait_ge(c.S('nzv'), b0['nzv'] + vch - 1)
                            for qt in range(NQT):
                                ins = tensor.matmul(prz[vch % 2][:, qt * 512:(qt + 1) * 512],
                                                    ehd[0:16, vch * 128:(vch + 1) * 128],
                                                    Zs2[0:16, qt * 512:(qt + 1) * 512],
                                                    start=True, stop=True)
                            c.inc(ins, 'nzt', 1)
                    blk.tensor(s_tensor)

                    def s_vector(vector):
                        for vch in range(NCH):
                            vector.wait_ge(c.S('nzt'), b0['nzt'] + vch + 1)
                            vector.drain()
                            sl = oslab1[:, vch * T:(vch + 1) * T]
                            c.inc(vector.tensor_tensor(sl, sl, prz[vch % 2][:], ALU.mult), 'nzv', 1)
                    blk.vector(s_vector)

            wo2b = ps.enter_context(nc.sbuf_tensor(f"e_wo2_{s}", [128, NCH * D], BF16))
            s1slab = ps.enter_context(nc.sbuf_tensor(f"e_s1_{s}", [128, NCH * T], BF16))
            load_weights_bf16(ps, f"l1wo_{s}", [("wo2T", wo2b, NCH)])
            with ExitStack() as pr:
                pP2 = pr.enter_context(nc.psum_tensor(f"e_pp2_{s}", [128, T], F32))
                with nc.Block() as blk:
                    b0 = c.base('opt', 'ops', 'dx1')

                    def s_tensor(tensor):
                        c.wait(tensor, 'wcast')
                        c.wait(tensor, 'nzv')
                        for och in range(NCH):
                            if och >= 1:
                                tensor.wait_ge(c.S('ops'), b0['ops'] + och)
                            for vch in range(NCH):
                                wsl = wo2b[:, vch * D + och * 128:vch * D + (och + 1) * 128]
                                for qt in range(NQT):
                                    ins = tensor.matmul(pP2[:, qt * 512:(qt + 1) * 512], wsl,
                                                        oslab1[:, vch * T + qt * 512:vch * T + (qt + 1) * 512],
                                                        start=(vch == 0), stop=(vch == NCH - 1))
                            c.inc(ins, 'opt', 1)
                    blk.tensor(s_tensor)

                    def s_scalar(scalar):
                        for och in range(NCH):
                            scalar.wait_ge(c.S('opt'), b0['opt'] + och + 1)
                            if och >= 2:
                                scalar.wait_ge(c.S('dx1'), b0['dx1'] + 16 * (och - 1))
                            c.inc(scalar.activation(s1slab[:, och * T:(och + 1) * T], pP2[:], AT.Identity), 'ops', 1)
                    blk.scalar(s_scalar)

                    def s_gp(gp):
                        for och in range(NCH):
                            gp.wait_ge(c.S('ops'), b0['ops'] + och + 1)
                            if och >= 1:
                                gp.wait_ge(c.S('dx1'), b0['dx1'] + 16 * och)
                                if DBG:
                                    gp.wait_ge(c.S('dbg'), c.cnt['dbg'])
                            c.inc(gp.dma_start(out=h_s1.ap()[s, och], in_=s1slab[:, och * T:(och + 1) * T]), 'dx1', 16)
                            if DBG:
                                c.inc(gp.dma_start(out=dbg['s1'].ap()[s, och],
                                                   in_=s1slab[:, och * T:(och + 1) * T]), 'dbg', 16)
                    blk.gpsimd(s_gp)
            c.dma_fence('dx1', 'dbg', 'nzb', 'nzc')


    def p4_l1ffn(s):
        with ExitStack() as ps:
            rhob = ps.enter_context(nc.sbuf_tensor(f"f_rho_{s}", [128, T], F32))
            delb = ps.enter_context(nc.sbuf_tensor(f"f_del_{s}", [128, T], F32))

            with ExitStack() as pq:
                xq1 = pq.enter_context(nc.sbuf_tensor(f"f_xq_{s}", [128, NCH * T], BF16))
                gw = pq.enter_context(nc.sbuf_tensor(f"f_gw_{s}", [128, T], F32))
                gw2 = pq.enter_context(nc.sbuf_tensor(f"f_gw2_{s}", [128, T], F32))
                guw = [pq.enter_context(nc.sbuf_tensor(f"f_gu{i}_{s}", [128, T], F32)) for i in range(2)]
                amx4 = pq.enter_context(nc.sbuf_tensor(f"f_amx_{s}", [128, T], F32))
                ggb = pq.enter_context(nc.sbuf_tensor(f"f_ggb_{s}", [128, T], F32))
                bca = pq.enter_context(nc.sbuf_tensor(f"f_bca_{s}", [128, T], F32))
                wgb = [pq.enter_context(nc.sbuf_tensor(f"f_wg{i}_{s}", [128, NCH * 512], BF16)) for i in range(2)]
                wub = [pq.enter_context(nc.sbuf_tensor(f"f_wu{i}_{s}", [128, NCH * 512], BF16)) for i in range(2)]
                pa = ExitStack()
                xc = [pa.enter_context(nc.sbuf_tensor(f"f_xc{i}_{s}", [128, T], F32)) for i in range(2)]
                bcb = pa.enter_context(nc.sbuf_tensor(f"f_bcb_{s}", [128, T], F32))
                lsb = pa.enter_context(nc.sbuf_tensor(f"f_lsb_{s}", [128, T], F32))
                q_ti = pa.enter_context(nc.sbuf_tensor(f"f_ti_{s}", [128, T], I32))
                q_d = gw2
                q_f = guw[0]
                q_sg = amx4

                with nc.Block() as blk:
                    def s_gp(gp):
                        c.wait(gp, 'dpk')
                        c.inc(gp.dma_start(out=gw[0:1, 0:T], in_=rows[8 * s + R_A1:8 * s + R_A1 + 1, :]), 'dh', 16)
                        c.wait(gp, 'dh')
                        bcast_row(gp, c, bca, gw[0:1, 0:T], T, 'dbc')
                        c.wait(gp, 'dbc')
                        c.inc(gp.dma_start(out=gw[0:1, 0:T], in_=rows[8 * s + R_B1:8 * s + R_B1 + 1, :]), 'dh', 16)
                        c.wait(gp, 'dh')
                        bcast_row(gp, c, bcb, gw[0:1, 0:T], T, 'dbc')
                        c.wait(gp, 'dbc')
                        c.inc(gp.dma_start(out=gw[0:1, 0:T], in_=rows[8 * s + R_LS1:8 * s + R_LS1 + 1, :]), 'dh', 16)
                        c.wait(gp, 'dh')
                        bcast_row(gp, c, lsb, gw[0:1, 0:T], T, 'dbc')
                        c.wait(gp, 'dbc')
                    blk.gpsimd(s_gp)

                    def s_vector(vector):
                        c.wait(vector, 'dbc')
                        vector.reciprocal(gw2[0:1, 0:T], gw[0:1, 0:T])
                        vector.drain()
                        c.inc(vector.tensor_scalar(gw2[0:1, 0:T], gw2[0:1, 0:T], swcol[0:1, 4:5], None,
                                                   ALU.mult), 'dv', 1)
                    blk.vector(s_vector)

                with nc.Block() as blk:
                    def s_gp(gp):
                        c.wait(gp, 'dv')
                        bcast_row(gp, c, ggb, gw2[0:1, 0:T], T, 'dbc')
                    blk.gpsimd(s_gp)

                with nc.Block() as blk:
                    b0 = c.base('dxl0', 'dxl1', 'dxv', 'dxs')

                    def s_sync(sync):
                        c.wait(sync, 'dbc')
                        for ch in range(NCH):
                            if ch >= 2:
                                sync.wait_ge(c.S('dxv'), b0['dxv'] + 2 * ch - 3)
                            c.inc(sync.dma_start(out=xc[ch % 2][:], in_=h_x1.ap()[s, ch]),
                                  'dxl%d' % (ch % 2), 16)
                    blk.sync(s_sync)

                    def s_vector(vector):
                        c.wait(vector, 'dbc')
                        for ch in range(NCH):
                            vector.wait_ge(c.S('dxl%d' % (ch % 2)), b0['dxl%d' % (ch % 2)] + 16 * (ch // 2 + 1))
                            vector.drain()
                            vector.tensor_tensor(gw[:], xc[ch % 2][:], bca[:], ALU.mult)
                            vector.drain()
                            vector.tensor_tensor(gw[:], gw[:], bcb[:], ALU.add)
                            vector.drain()
                            vector.tensor_tensor(q_d[:], gw[:], lsb[:], ALU.mult)
                            vector.drain()
                            vector.tensor_copy(q_ti[:], q_d[:])
                            vector.drain()
                            vector.tensor_copy(q_f[:], q_ti[:])
                            vector.drain()
                            vector.tensor_tensor(q_d[:], q_d[:], q_f[:], ALU.subtract)
                            vector.drain()
                            vector.tensor_scalar(q_sg[:], q_d[:], 0.5, -0.5, ALU.min, ALU.max)
                            vector.drain()
                            c.inc(vector.tensor_tensor(q_d[:], q_d[:], q_sg[:], ALU.subtract), 'dxv', 1)
                            vector.wait_ge(c.S('dxs'), b0['dxs'] + ch + 1)
                            vector.drain()
                            c.inc(vector.tensor_tensor(xq1[:, ch * T:(ch + 1) * T], q_f[:], q_sg[:], ALU.add), 'dxv', 1)
                    blk.vector(s_vector)

                    def s_scalar(scalar):
                        for ch in range(NCH):
                            scalar.wait_ge(c.S('dxv'), b0['dxv'] + 2 * ch + 1)
                            c.inc(scalar.activation(q_sg[:], q_d[:], AT.Sign), 'dxs', 1)
                    blk.scalar(s_scalar)

                pa.close()
                with ExitStack() as pr:
                    pG = [pr.enter_context(nc.psum_tensor(f"f_pG{i}_{s}", [128, 1024], F32)) for i in range(2)]
                    pU = [pr.enter_context(nc.psum_tensor(f"f_pU{i}_{s}", [128, 1024], F32)) for i in range(2)]
                    with nc.Block() as blk:
                        b0 = c.base('lnl0', 'lnl1', 'dft', 'dfv', 'dfs', 'qst')

                        def s_sync(sync):
                            for sg4 in range(NIC // 4):
                                if sg4 >= 2:
                                    sync.wait_ge(c.S('dft'), b0['dft'] + 8 * (sg4 - 1))
                                for ich in range(NCH):
                                    c.inc(sync.dma_start(out=wgb[sg4 % 2][:, ich * 512:(ich + 1) * 512],
                                                         in_=h_wgq.ap()[ich][:, sg4 * 512:(sg4 + 1) * 512]),
                                          'lnl%d' % (sg4 % 2), 16)
                                    c.inc(sync.dma_start(out=wub[sg4 % 2][:, ich * 512:(ich + 1) * 512],
                                                         in_=h_wuq.ap()[ich][:, sg4 * 512:(sg4 + 1) * 512]),
                                          'lnl%d' % (sg4 % 2), 16)
                        blk.sync(s_sync)

                        def s_tensor(tensor):
                            c.wait(tensor, 'dxv')
                            for n in range(2 * NIC):
                                oc, h2 = n // 2, n % 2
                                sg4 = oc // 4
                                if n % 8 == 0:
                                    tensor.wait_ge(c.S('lnl%d' % (sg4 % 2)),
                                                   b0['lnl%d' % (sg4 % 2)] + 256 * (sg4 // 2 + 1))
                                if n >= 2:
                                    tensor.wait_ge(c.S('dfv'), b0['dfv'] + 3 * (n - 2) + 2)
                                for ich in range(NCH):
                                    wcol = ich * 512 + (oc % 4) * 128
                                    for q in range(2):
                                        xsl = xq1[:, ich * T + h2 * 1024 + q * 512:
                                                  ich * T + h2 * 1024 + (q + 1) * 512]
                                        tensor.matmul(pG[n % 2][:, q * 512:(q + 1) * 512],
                                                      wgb[sg4 % 2][:, wcol:wcol + 128], xsl,
                                                      start=(ich == 0), stop=(ich == NCH - 1))
                                        ins = tensor.matmul(pU[n % 2][:, q * 512:(q + 1) * 512],
                                                            wub[sg4 % 2][:, wcol:wcol + 128], xsl,
                                                            start=(ich == 0), stop=(ich == NCH - 1))
                                c.inc(ins, 'dft', 1)
                        blk.tensor(s_tensor)

                        def s_vector(vector):
                            c.wait(vector, 'dbc')
                            for n in range(2 * NIC):
                                oc, h2 = n // 2, n % 2
                                hs = slice(h2 * 1024, (h2 + 1) * 1024)
                                vector.wait_ge(c.S('dft'), b0['dft'] + n + 1)
                                vector.drain()
                                c.inc(vector.tensor_tensor(gw[:, hs], pG[n % 2][:], ggb[:, hs], ALU.mult),
                                      'dfv', 1)
                                vector.wait_ge(c.S('dfs'), b0['dfs'] + 2 * n + 1)
                                if oc >= 2 and h2 == 0:
                                    vector.wait_ge(c.S('qst'), b0['qst'] + 16 * (oc - 1))
                                vector.drain()
                                c.inc(vector.tensor_tensor(guw[oc % 2][:, hs], gw2[:, hs], pU[n % 2][:],
                                                           ALU.mult), 'dfv', 1)
                                vector.wait_ge(c.S('dfs'), b0['dfs'] + 2 * n + 2)
                                vector.drain()
                                if oc == 0:
                                    ins = vector.tensor_copy(amx4[:, hs], bca[:, hs])
                                else:
                                    ins = vector.tensor_tensor(amx4[:, hs], amx4[:, hs], bca[:, hs], ALU.max)
                                c.inc(ins, 'dfv', 1)
                        blk.vector(s_vector)

                        def s_scalar(scalar):
                            for n in range(2 * NIC):
                                oc, h2 = n // 2, n % 2
                                hs = slice(h2 * 1024, (h2 + 1) * 1024)
                                scalar.wait_ge(c.S('dfv'), b0['dfv'] + 3 * n + 1)
                                c.inc(scalar.activation(gw2[:, hs], gw[:, hs], AT.Silu), 'dfs', 1)
                                scalar.wait_ge(c.S('dfv'), b0['dfv'] + 3 * n + 2)
                                c.inc(scalar.activation(bca[:, hs], guw[oc % 2][:, hs], AT.Abs), 'dfs', 1)
                        blk.scalar(s_scalar)

                        def s_gp(gp):
                            for oc in range(NIC):
                                gp.wait_ge(c.S('dfv'), b0['dfv'] + 3 * (2 * oc + 1) + 2)
                                if oc >= 1:
                                    gp.wait_ge(c.S('qst'), b0['qst'] + 16 * oc)
                                c.inc(gp.dma_start(out=h_gu.ap()[oc], in_=guw[oc % 2][:]), 'qst', 16)
                        blk.gpsimd(s_gp)

                with nc.Block() as blk:
                    def s_vector(vector):
                        part_tree(c, vector, amx4, gw2, T)
                        c.inc(vector.drain(), 'dv', 1)
                    blk.vector(s_vector)

                    def s_gp(gp):
                        c.wait(gp, 'dv')
                        c.inc(gp.dma_start(out=gw[0:1, 0:T], in_=rows[8 * s + R_LS1:8 * s + R_LS1 + 1, :]), 'dh', 16)
                    blk.gpsimd(s_gp)

                with nc.Block() as blk:
                    def s_vector(vector):
                        c.wait(vector, 'dh')
                        vector.reciprocal(gw[0:1, 0:T], gw[0:1, 0:T])
                        vector.drain()
                        vector.tensor_scalar(gw[0:1, 0:T], gw[0:1, 0:T], swcol[0:1, 5:6], None, ALU.mult)
                        vector.drain()
                        vector.tensor_tensor(gw2[0:1, 0:T], amx4[0:1, 0:T], gw[0:1, 0:T], ALU.mult)
                        vector.drain()
                        vector.tensor_scalar(gw2[0:1, 0:T], gw2[0:1, 0:T], 1e-5, None, ALU.max)
                        vector.drain()
                        vector.reciprocal(gw2[0:1, 0:T], gw2[0:1, 0:T])
                        vector.drain()
                        vector.tensor_scalar_mul(gw2[0:1, 0:T], gw2[0:1, 0:T], 127.0)
                        vector.drain()
                        vector.tensor_tensor(amx4[0:1, 0:T], gw2[0:1, 0:T], gw[0:1, 0:T], ALU.mult)
                        vector.drain()
                        vector.reciprocal(gw[0:1, 0:T], gw2[0:1, 0:T])
                        vector.drain()
                        c.inc(vector.tensor_scalar(gw[0:1, 0:T], gw[0:1, 0:T], swcol[0:1, 6:7], None,
                                                   ALU.mult), 'dv', 1)
                    blk.vector(s_vector)

                    def s_gp(gp):
                        c.wait(gp, 'dv')
                        bcast_row(gp, c, rhob, amx4[0:1, 0:T], T, 'dbc2')
                        bcast_row(gp, c, delb, gw[0:1, 0:T], T, 'dbc2')
                        c.wait(gp, 'dbc2')
                    blk.gpsimd(s_gp)
            c.dma_fence('qst', 'dbc2', 'dbc', 'dh')

            with ExitStack() as pd:
                v2slab = pd.enter_context(nc.sbuf_tensor(f"f_v2_{s}", [128, NIC * T], BF16))
                gul = pd.enter_context(nc.sbuf_tensor(f"f_gul_{s}", [128, T], F32))
                q_ti = pd.enter_context(nc.sbuf_tensor(f"f_ti2_{s}", [128, T], I32))
                q_f = pd.enter_context(nc.sbuf_tensor(f"f_f2_{s}", [128, T], F32))
                q_d = pd.enter_context(nc.sbuf_tensor(f"f_d2_{s}", [128, T], F32))
                q_sg = pd.enter_context(nc.sbuf_tensor(f"f_sg2_{s}", [128, T], F32))
                s1st = pd.enter_context(nc.sbuf_tensor(f"f_s1_{s}", [128, T], BF16))
                wdb = [pd.enter_context(nc.sbuf_tensor(f"f_wd{i}_{s}", [128, 128], BF16)) for i in range(2)]

                with nc.Block() as blk:
                    b0 = c.base('dal0', 'dal1', 'dxv', 'dxs')

                    def s_sync(sync):
                        for oc in range(NIC):
                            if oc >= 1:
                                sync.wait_ge(c.S('dxv'), b0['dxv'] + 3 * (oc - 1) + 1)
                            c.inc(sync.dma_start(out=gul[:], in_=h_gu.ap()[oc]), 'dal%d' % (oc % 2), 16)
                    blk.sync(s_sync)

                    def s_vector(vector):
                        for oc in range(NIC):
                            vector.wait_ge(c.S('dal%d' % (oc % 2)), b0['dal%d' % (oc % 2)] + 16 * (oc // 2 + 1))
                            vector.drain()
                            c.inc(vector.tensor_tensor(q_d[:], gul[:], rhob[:], ALU.mult), 'dxv', 1)
                            vector.drain()
                            vector.tensor_copy(q_ti[:], q_d[:])
                            vector.drain()
                            vector.tensor_copy(q_f[:], q_ti[:])
                            vector.drain()
                            vector.tensor_tensor(q_d[:], q_d[:], q_f[:], ALU.subtract)
                            vector.drain()
                            vector.tensor_scalar(q_sg[:], q_d[:], 0.5, -0.5, ALU.min, ALU.max)
                            vector.drain()
                            vector.tensor_tensor(q_d[:], q_d[:], q_sg[:], ALU.subtract)
                            vector.drain()
                            c.inc(vector.drain(), 'dxv', 1)
                            vector.wait_ge(c.S('dxs'), b0['dxs'] + oc + 1)
                            vector.drain()
                            c.inc(vector.tensor_tensor(v2slab[:, oc * T:(oc + 1) * T], q_f[:], q_sg[:],
                                                       ALU.add), 'dxv', 1)
                    blk.vector(s_vector)

                    def s_scalar(scalar):
                        for oc in range(NIC):
                            scalar.wait_ge(c.S('dxv'), b0['dxv'] + 3 * oc + 2)
                            c.inc(scalar.activation(q_sg[:], q_d[:], AT.Sign), 'dxs', 1)
                    blk.scalar(s_scalar)

                with ExitStack() as pr:
                    pD = [pr.enter_context(nc.psum_tensor(f"f_pD{i}_{s}", [128, T], F32)) for i in range(2)]
                    with nc.Block() as blk:
                        b0 = c.base('lnm0', 'lnm1', 'opt', 'dfv', 'dfs', 'dal0', 'dal1', 'fin', 'dbgg')

                        def s_sync(sync):
                            for och in range(NCH):
                                if och >= 1:
                                    sync.wait_ge(c.S('dfv'), b0['dfv'] + 4 * och)
                                c.inc(sync.dma_start(out=s1st[:], in_=h_s1.ap()[s, och]), 'dal0', 16)
                                c.inc(sync.dma_start(out=gul[:], in_=h_x1.ap()[s, och]), 'dal1', 16)
                                for oc in range(NIC):
                                    n = och * NIC + oc
                                    if n >= 2:
                                        sync.wait_ge(c.S('opt'), b0['opt'] + n - 1)
                                    c.inc(sync.dma_start(out=wdb[n % 2][:],
                                                         in_=h_wdq.ap()[och][oc]),
                                          'lnm%d' % (n % 2), 16)
                        blk.sync(s_sync)

                        def s_tensor(tensor):
                            for och in range(NCH):
                                pD_ = pD[och % 2]
                                for oc in range(NIC):
                                    n = och * NIC + oc
                                    tensor.wait_ge(c.S('lnm%d' % (n % 2)), b0['lnm%d' % (n % 2)] + 16 * (n // 2 + 1))
                                    if oc == 0 and och >= 2:
                                        tensor.wait_ge(c.S('dfv'), b0['dfv'] + 4 * (och - 1) - 3)
                                    for qt in range(NQT):
                                        ins = tensor.matmul(pD_[:, qt * 512:(qt + 1) * 512], wdb[n % 2][:],
                                                            v2slab[:, oc * T + qt * 512:oc * T + (qt + 1) * 512],
                                                            start=(oc == 0), stop=(oc == NIC - 1))
                                    c.inc(ins, 'opt', 1)
                        blk.tensor(s_tensor)

                        def s_vector(vector):
                            for och in range(NCH):
                                vector.wait_ge(c.S('opt'), b0['opt'] + (och + 1) * NIC)
                                vector.drain()
                                c.inc(vector.tensor_tensor(q_d[:], pD[och % 2][:], delb[:], ALU.mult), 'dfv', 1)
                                vector.wait_ge(c.S('dfs'), b0['dfs'] + och + 1)
                                vector.wait_ge(c.S('dal0'), b0['dal0'] + 16 * (och + 1))
                                vector.wait_ge(c.S('dal1'), b0['dal1'] + 16 * (och + 1))
                                if och >= 1:
                                    vector.wait_ge(c.S('fin'), b0['fin'] + 16 * och)
                                    if DBG:
                                        vector.wait_ge(c.S('dbgg'), b0['dbgg'] + 16 * och)
                                vector.drain()
                                c.inc(vector.tensor_copy(q_d[:], s1st[:]), 'dfv', 1)
                                vector.drain()
                                vector.tensor_tensor(q_sg[:], q_d[:], q_f[:], ALU.mult)
                                vector.drain()
                                c.inc(vector.tensor_tensor(q_sg[:], gul[:], q_sg[:], ALU.add), 'dfv', 2)
                        blk.vector(s_vector)

                        def s_scalar(scalar):
                            for och in range(NCH):
                                scalar.wait_ge(c.S('dfv'), b0['dfv'] + 4 * och + 1)
                                if DBG and och >= 1:
                                    scalar.wait_ge(c.S('dbgg'), b0['dbgg'] + 16 * och)
                                c.inc(scalar.activation(q_f[:], q_d[:], AT.Silu), 'dfs', 1)
                        blk.scalar(s_scalar)

                        def s_gp(gp):
                            for och in range(NCH):
                                gp.wait_ge(c.S('dfv'), b0['dfv'] + 4 * (och + 1))
                                if och >= 1:
                                    gp.wait_ge(c.S('fin'), b0['fin'] + 16 * och)
                                c.inc(gp.dma_start(out=out_t.ap()[s, och * 128:(och + 1) * 128, :],
                                                   in_=q_sg[:]), 'fin', 16)
                                if DBG:
                                    if och >= 1:
                                        gp.wait_ge(c.S('dbgg'), b0['dbgg'] + 16 * och)
                                    c.inc(gp.dma_start(out=dbg['g1'].ap()[s, och], in_=q_f[:]), 'dbgg', 16)
                        blk.gpsimd(s_gp)
            c.dma_fence('fin', 'dbgg', 'dal0', 'dal1')

    for s in range(NSEQ):
        with ExitStack() as sps1:
            xn0 = sps1.enter_context(nc.sbuf_tensor(f"xn0_{s}", [128, NCH * T], BF16))
            w_qkv = [sps1.enter_context(nc.sbuf_tensor(f"wqkv{k}_{s}", [128, NCH * D], BF16)) for k in range(3)]
            with nc.named_scope("p1a"):
                load_weights_bf16(sps1, f"l0w_{s}", [("wqT", w_qkv[0], NCH), ("wkT", w_qkv[1], NCH), ("wvT", w_qkv[2], NCH)])
                p1a(s, xn0)
            with nc.named_scope("l0proj"):
                l0_proj(s, xn0, w_qkv)
        with ExitStack() as sps2:
            s0slab = sps2.enter_context(nc.sbuf_tensor(f"s0slab_{s}", [128, NCH * T], BF16))
            l0_attn(s, s0slab)
            with nc.named_scope("p1d"):
                p1d(s, s0slab)
        p2_l1attn(s)
        with nc.named_scope("p4"):
            p4_l1ffn(s)

    with nc.Block() as blk:
        def s_gp(g):
            c.wait(g, 'fin')
        blk.gpsimd(s_gp)

    return nc, st, c, dbg



def _host_consts():
    m = {}
    m["ident"] = np.eye(128).astype(ml_dtypes.bfloat16)
    m["triu64"] = np.triu(np.ones((64, 64), np.float32)).astype(ml_dtypes.bfloat16)
    t128 = np.zeros((128, 128), np.float32)
    t128[0:64, 0:64] = np.triu(np.ones((64, 64)))
    t128[0:64, 64:128] = 1.0
    t128[64:128, 64:128] = np.triu(np.ones((64, 64)))
    m["triu128p"] = t128.astype(ml_dtypes.bfloat16)
    e = np.zeros((16, 8 * 128), np.float32)
    for vch in range(8):
        for p in range(128):
            e[2 * vch + p // 64, vch * 128 + p] = 1.0
    m["ehd"] = e
    p = np.arange(128)[:, None]
    w = np.arange(276)[None, :]
    m["adm"] = (p + w <= 275).astype(ml_dtypes.bfloat16)
    return m


def _host_inputs(inputs, core):
    x = inputs["L0_attn_hidden_states"]
    xc = np.ascontiguousarray(np.transpose(np.asarray(x)[2 * core:2 * core + 2], (0, 2, 1))).astype(np.float32)
    m = {"x": xc}
    m["wqT"] = np.ascontiguousarray(np.asarray(inputs["L0_attn_Wq"]).T.astype(np.float32))
    m["wkT"] = np.ascontiguousarray(np.asarray(inputs["L0_attn_Wk"]).T.astype(np.float32))
    m["wvT"] = np.ascontiguousarray(np.asarray(inputs["L0_attn_Wv"]).T.astype(np.float32))
    m["woT"] = np.ascontiguousarray(np.asarray(inputs["L0_attn_Wo"]).T.astype(np.float32))
    m["rhsw"] = np.ascontiguousarray(np.asarray(inputs["L0_ffn_rhs"]).astype(np.float32))
    m["wq2T"] = np.ascontiguousarray(np.asarray(inputs["L1_attn_Wq"]).T.astype(np.float32))
    m["wk2T"] = np.ascontiguousarray(np.asarray(inputs["L1_attn_Wk"]).T.astype(np.float32))
    m["wv2T"] = np.ascontiguousarray(np.asarray(inputs["L1_attn_Wv"]).T.astype(np.float32))
    m["wo2T"] = np.ascontiguousarray(np.asarray(inputs["L1_attn_Wo"]).T.astype(np.float32))
    m["wgT"] = np.ascontiguousarray(np.asarray(inputs["L1_ffn_w_gate"]).T.astype(np.float32))
    m["wuT"] = np.ascontiguousarray(np.asarray(inputs["L1_ffn_w_up"]).T.astype(np.float32))
    m["wdT"] = np.ascontiguousarray(np.asarray(inputs["L1_ffn_w_down"]).T.astype(np.float32))
    bias = np.stack([np.asarray(inputs["L0_attn_bq"]), np.asarray(inputs["L0_attn_bk"]),
                     np.asarray(inputs["L0_attn_bv"]), np.asarray(inputs["L0_attn_bo"])], 0).astype(np.float32)
    bcols = np.zeros((128, 4 * NCH), np.float32)
    for k in range(4):
        for ch in range(NCH):
            bcols[:, k * NCH + ch] = bias[k, ch * 128:(ch + 1) * 128]
    m["bqkvo"] = bcols
    m["relk"] = np.ascontiguousarray(np.asarray(inputs["L0_attn_rel_k"]).astype(np.float32)[::-1])
    m["relv"] = np.ascontiguousarray(np.asarray(inputs["L0_attn_rel_v"]).astype(np.float32))
    m["relvf"] = np.ascontiguousarray(np.asarray(inputs["L0_attn_rel_v"]).astype(np.float32)[::-1])
    m["gb16"] = np.stack([np.asarray(inputs["L1_attn_gamma"]),
                          np.asarray(inputs["L1_attn_beta"])], 0).astype(np.float32)
    m.update(_host_consts())
    return m


_NC_CACHE = {}


def kernel(**inputs):
    if "nc" not in _NC_CACHE:
        _NC_CACHE["nc"] = build_nc()
    nc, st_, c, dbg = _NC_CACHE["nc"]
    in_maps = [_host_inputs(inputs, core) for core in range(NCORES)]
    res = run_bass_kernel_spmd(nc, in_maps, core_ids=list(range(NCORES)))
    outs = []
    for core in range(NCORES):
        o = np.asarray(res.results[core]["out"])
        outs.append(np.transpose(o, (0, 2, 1)))
    return np.concatenate(outs, 0).astype(np.float32)


# version 14
# speedup vs baseline: 1.0140x; 1.0140x over previous
import os
import numpy as np
import ml_dtypes
from contextlib import ExitStack

import concourse.bass as bass
import concourse.mybir as mybir
from concourse.bass_utils import run_bass_kernel_spmd

F32 = mybir.dt.float32
BF16 = mybir.dt.bfloat16
I32 = mybir.dt.int32
ALU = mybir.AluOpType
AT = mybir.ActivationFunctionType
AX = mybir.AxisListType
AP = bass.AP

B, T, D = 16, 2048, 1024
NCORES = 8
NSEQ = 2
NCH = D // 128
NTC = T // 128
NQT = T // 512
H0, HD0 = 16, 64
H1, FD1, HD1 = 16, 16, 64
INNER = 4096
NIC = INNER // 128

DBG = int(os.environ.get("KDBG", "0"))


class Ctx:
    def __init__(self, nc, stack):
        self.nc = nc
        self.stack = stack
        self.sem = {}
        self.cnt = {}

    def S(self, name):
        if name not in self.sem:
            self.sem[name] = self.stack.enter_context(self.nc.semaphore(name))
            self.cnt[name] = 0
        return self.sem[name]

    def inc(self, instr, name, k=1):
        instr.then_inc(self.S(name), k)
        self.cnt[name] += k
        return self.cnt[name]

    def wait(self, eng, name, at=None):
        eng.wait_ge(self.S(name), self.cnt[name] if at is None else at)

    def base(self, *names):
        for n in names:
            self.S(n)
        return {n: self.cnt[n] for n in names}

    def dma_fence(self, *names):
        with self.nc.Block() as blk:
            def s_sync(sync):
                for n in names:
                    if n in self.cnt:
                        sync.wait_ge(self.S(n), self.cnt[n])
            blk.sync(s_sync)



def part_max_32(c, vector, tile, n, tmp=None):
    buf = tile if tmp is None else tmp
    off = 32 if tmp is None else 0
    for hh in (16, 8, 4, 2, 1):
        mask = [(i + hh) % 32 for i in range(32)]
        vector.stream_shuffle(buf[off:off + 32, 0:n], tile[0:32, 0:n], mask)
        vector.drain()
        vector.tensor_tensor(tile[0:32, 0:n], tile[0:32, 0:n], buf[off:off + 32, 0:n], ALU.max)
        vector.drain()


def part_tree(c, vector, tile, tmp, n):
    for h in (64, 32):
        vector.drain()
        vector.tensor_copy(tmp[0:h, 0:n], tile[h:2 * h, 0:n])
        vector.drain()
        vector.tensor_tensor(tile[0:h, 0:n], tile[0:h, 0:n], tmp[0:h, 0:n], ALU.max)
    vector.drain()
    part_max_32(c, vector, tile, n, tmp=tmp)


def bcast_row(eng_dma, c, dst_tile, row_ap, n, sem, parts=128, dst_off=0):
    assert int(row_ap.offset) == 0, "bcast source must be offset-0"
    src = AP(row_ap.tensor, 0, [[1, 1], [0, parts], [1, n]])
    return c.inc(eng_dma.dma_start(out=dst_tile[0:parts, dst_off:dst_off + n], in_=src), sem, 16)


def build_nc():
    nc = bass.Bass("TRN2", target_bir_lowering=False)
    st = ExitStack()
    c = Ctx(nc, st)

    x_in = nc.dram_tensor("x", [NSEQ, D, T], F32, kind="ExternalInput")
    out_t = nc.dram_tensor("out", [NSEQ, D, T], F32, kind="ExternalOutput")
    wT = {}
    for nm, sh in [("wqT", [D, D]), ("wkT", [D, D]), ("wvT", [D, D]),
                   ("woT", [D, D]), ("rhsw", [D, D]),
                   ("wq2T", [D, 256]), ("wk2T", [D, 256]),
                   ("wv2T", [D, D]), ("wo2T", [D, D]),
                   ("wgT", [D, INNER]), ("wuT", [D, INNER]), ("wdT", [INNER, D])]:
        wT[nm] = nc.dram_tensor(nm, sh, F32, kind="ExternalInput")
    bias_in = nc.dram_tensor("bqkvo", [128, 4 * NCH], F32, kind="ExternalInput")
    relk_in = nc.dram_tensor("relk", [41, 64], F32, kind="ExternalInput")
    relv_in = nc.dram_tensor("relv", [41, 64], F32, kind="ExternalInput")
    relvf_in = nc.dram_tensor("relvf", [41, 64], F32, kind="ExternalInput")
    gb_in = nc.dram_tensor("gb16", [2, 16], F32, kind="ExternalInput")
    ident_in = nc.dram_tensor("ident", [128, 128], BF16, kind="ExternalInput")
    triu_in = nc.dram_tensor("triu64", [64, 64], BF16, kind="ExternalInput")
    adm_in = nc.dram_tensor("adm", [128, 276], BF16, kind="ExternalInput")
    triu128_in = nc.dram_tensor("triu128p", [128, 128], BF16, kind="ExternalInput")
    ehd_in = nc.dram_tensor("ehd", [16, 8 * 128], F32, kind="ExternalInput")

    h_x1 = nc.dram_tensor("h_x1", [NSEQ, NCH, 128, T], F32, kind="Internal")
    h_xn1 = nc.dram_tensor("h_xn1", [NSEQ, NCH, 128, T], BF16, kind="Internal")
    h_s1 = nc.dram_tensor("h_s1", [NSEQ, NCH, 128, T], BF16, kind="Internal")
    h_rhsq = nc.dram_tensor("h_rhsq", [NCH, 128, D], BF16, kind="Internal")
    h_qt = nc.dram_tensor("h_qt", [H0, 64, T], BF16, kind="Internal")
    h_kt = nc.dram_tensor("h_kt", [H0, 64, T], BF16, kind="Internal")
    h_va = nc.dram_tensor("h_va", [NTC, 128, H0 * 65], BF16, kind="Internal")
    h_ps = nc.dram_tensor("h_ps", [2, 41, 2130], BF16, kind="Internal")
    h_gp = nc.dram_tensor("h_gp", [2, 1, 2208], BF16, kind="Internal")
    h_zz = nc.dram_tensor("h_zz", [1, 4096], BF16, kind="Internal")
    h_rowf = nc.dram_tensor("h_rowf", [1, 4096], F32, kind="Internal")
    h_c = nc.dram_tensor("h_c", [2, NTC, 128, 464], BF16, kind="Internal")
    h_strip = nc.dram_tensor("h_strip", [NTC, 128, 404], BF16, kind="Internal")
    h_h6 = nc.dram_tensor("h_h6", [276, 2624], BF16, kind="Internal")
    h_gu = nc.dram_tensor("h_gu", [NIC, 128, T], F32, kind="Internal")
    h_wgq = nc.dram_tensor("h_wgq", [NCH, 128, INNER], BF16, kind="Internal")
    h_wuq = nc.dram_tensor("h_wuq", [NCH, 128, INNER], BF16, kind="Internal")
    h_wdq = nc.dram_tensor("h_wdq", [NCH, NIC, 128, 128], BF16, kind="Internal")

    dbg = {}
    if DBG:
        for nm, sh, dt in [("ln0", [NSEQ, NCH, 128, T], BF16),
                           ("rows", [NSEQ, 16, T], F32),
                           ("qrhs", [NCH, 128, D], BF16),
                           ("qt0", [64, 2 * T], BF16),
                           ("aslab", [128, NTC * T], BF16), ("oslab", [128, NCH * T], BF16), ("zs", [16, T], F32),
                           ("s0", [NSEQ, NCH, 128, T], BF16),
                           ("g0", [NSEQ, NCH, 128, T], F32),
                           ("s1", [NSEQ, NCH, 128, T], BF16),
                           ("ufm", [NSEQ, 128, 16 * 512], BF16),
                           ("val1", [NSEQ, 128, 16 * 16 * 65], BF16),
                           ("zs1", [NSEQ, 16, T], F32),
                           ("g1", [NSEQ, NCH, 128, T], F32)]:
            dbg[nm] = nc.dram_tensor("dbg_" + nm, sh, dt, kind="ExternalOutput")

    bias_sb = st.enter_context(nc.sbuf_tensor("bias_sb", [128, 4 * NCH], F32))
    ident = st.enter_context(nc.sbuf_tensor("sb_ident", [128, 128], BF16))
    triu64 = st.enter_context(nc.sbuf_tensor("sb_triu64", [64, 64], BF16))
    adm = st.enter_context(nc.sbuf_tensor("sb_adm", [128, 276], BF16))
    triu128p = st.enter_context(nc.sbuf_tensor("sb_triu128p", [128, 128], BF16))
    ehd = st.enter_context(nc.sbuf_tensor("sb_ehd", [16, 8 * 128], F32))
    relkT = st.enter_context(nc.sbuf_tensor("relkT", [64, 41], BF16))
    relk_m40T = st.enter_context(nc.sbuf_tensor("relk_m40T", [64, 41], BF16))
    relcols = st.enter_context(nc.sbuf_tensor("relcols", [64, 2], F32))
    relv40r = st.enter_context(nc.sbuf_tensor("relv40r", [1, 64], BF16))
    dvrow = st.enter_context(nc.sbuf_tensor("dvrow", [1, 64], BF16))
    relvp = st.enter_context(nc.sbuf_tensor("relvp", [128, 128], BF16))
    gbrow = st.enter_context(nc.sbuf_tensor("gbrow", [2, 16], F32))
    gbhop = st.enter_context(nc.sbuf_tensor("gbhop", [1, 32], F32))
    swrow = st.enter_context(nc.sbuf_tensor("swrow", [1, 8], F32))
    swcol = st.enter_context(nc.sbuf_tensor("swcol", [128, 8], F32))
    gbig = st.enter_context(nc.sbuf_tensor("gbig", [128, 32], F32))
    onesrow = st.enter_context(nc.sbuf_tensor("onesrow", [1, 128], BF16))
    onescol_f = st.enter_context(nc.sbuf_tensor("onescol_f", [128, 1], F32))
    onescol_b = st.enter_context(nc.sbuf_tensor("onescol_b", [128, 1], BF16))
    rows = st.enter_context(nc.sbuf_tensor("rows", [16, T], F32))
    grows = st.enter_context(nc.sbuf_tensor("grows", [4, 1024], F32))
    cvals = st.enter_context(nc.sbuf_tensor("cvals", [128, 4], F32))
    R_A0, R_B0, R_LS0, R_A1, R_B1, R_LS1, R_T1, R_T2 = range(8)

    def row(s, r):
        return rows[8 * s + r:8 * s + r + 1, :]

    def load_weights_bf16(ps, tag, specs):
        f32x = ps.enter_context(nc.sbuf_tensor(f"wl_a_{tag}", [128, 1024], F32))
        f32y = ps.enter_context(nc.sbuf_tensor(f"wl_b_{tag}", [128, 1024], F32))
        with nc.Block() as blk:
            b0 = c.base('wld0', 'wld1', 'wcast')
            items = []
            for spec in specs:
                nm, dst, nch = spec[0], spec[1], spec[2]
                w = spec[3] if len(spec) > 3 else 1024
                stride = spec[4] if len(spec) > 4 else w
                off = spec[5] if len(spec) > 5 else 0
                for ch in range(nch):
                    items.append((nm, dst, ch, w, stride, off))

            def s_sync(sync):
                for n, (nm, dst, ch, w, stride, off) in enumerate(items):
                    if n >= 2:
                        sync.wait_ge(c.S('wcast'), b0['wcast'] + n - 1)
                    buf = f32x if n % 2 == 0 else f32y
                    c.inc(sync.dma_start(out=buf[:, 0:w], in_=wT[nm].ap()[ch * 128:(ch + 1) * 128, :]),
                          'wld%d' % (n % 2), 16)
            blk.sync(s_sync)

            def s_vector(vector):
                for n, (nm, dst, ch, w, stride, off) in enumerate(items):
                    vector.wait_ge(c.S('wld%d' % (n % 2)), b0['wld%d' % (n % 2)] + 16 * (n // 2 + 1))
                    buf = f32x if n % 2 == 0 else f32y
                    c.inc(vector.tensor_copy(dst[:, off + ch * stride:off + ch * stride + w], buf[:, 0:w]), 'wcast', 1)
            blk.vector(s_vector)

    with ExitStack() as ps:
        f32a = ps.enter_context(nc.sbuf_tensor("p0_f32a", [128, 1024], F32))
        f32b = ps.enter_context(nc.sbuf_tensor("p0_f32b", [128, 1024], F32))
        relf = ps.enter_context(nc.sbuf_tensor("p0_relf", [41, 192], F32))
        relb = ps.enter_context(nc.sbuf_tensor("p0_relb", [41, 192], BF16))
        zz = ps.enter_context(nc.sbuf_tensor("p0_zz", [128, 2600], BF16))
        oneb = ps.enter_context(nc.sbuf_tensor("p0_oneb", [128, 16], BF16))
        pr = ps.enter_context(nc.psum_tensor("p0_ps", [128, 128], F32))
        prb = ps.enter_context(nc.psum_tensor("p0_psb", [128, 128], BF16))

        with nc.Block() as blk:
            def s_sync(sync):
                for t_, src in [(ident, ident_in), (triu64, triu_in), (adm, adm_in), (triu128p, triu128_in), (ehd, ehd_in),
                                (bias_sb, bias_in), (gbrow, gb_in)]:
                    c.inc(sync.dma_start(out=t_[:], in_=src.ap()), 'p0c', 16)
                c.inc(sync.dma_start(out=relf[0:41, 0:64], in_=relk_in.ap()), 'p0c', 16)
                c.inc(sync.dma_start(out=relf[0:41, 64:128], in_=relv_in.ap()), 'p0c', 16)
                c.inc(sync.dma_start(out=relf[0:41, 128:192], in_=relvf_in.ap()), 'p0c', 16)
            blk.sync(s_sync)

            def s_vector(vector):
                c.wait(vector, 'p0c')
                vector.memset(onesrow[:], 1.0)
                vector.memset(onescol_f[:], 1.0)
                vector.memset(onescol_b[:], 1.0)
                vector.memset(zz[:], 0.0)
                vector.memset(oneb[:], 1.0)
                vector.memset(cvals[:, 0:1], 1e-5)
                vector.memset(cvals[:, 1:2], 1e-10)
                vector.memset(cvals[:, 2:3], 0.0)
                vector.memset(cvals[:, 3:4], 1.0)
                vector.memset(rows[:], 0.0)
                vector.memset(grows[:], 0.0)
                vector.tensor_copy(relb[:], relf[:])
                vector.tensor_scalar_mul(gbrow[:], gbrow[:], 0.5)
                c.inc(vector.drain(), 'p0v', 1)
            blk.vector(s_vector)

        with nc.Block() as blk:
            def s_sync(sync):
                c.wait(sync, 'p0v', 1)
                for par in range(2):
                    for ch in range(NTC):
                        c.inc(sync.dma_start(out=h_c.ap()[par, ch, :, 0:128], in_=zz[:, 0:128]), 'p0z', 16)
                        c.inc(sync.dma_start(out=h_c.ap()[par, ch, :, 296:464], in_=zz[:, 0:168]), 'p0z', 16)
                for ch in range(NTC):
                    c.inc(sync.dma_start(out=h_strip.ap()[ch, :, :], in_=zz[:, 0:404]), 'p0z', 16)
                for r0, n in ((0, 128), (128, 128), (256, 20)):
                    c.inc(sync.dma_start(out=h_h6.ap()[r0:r0 + n, 0:2600], in_=zz[0:n, :]), 'p0z', 16)
                    c.inc(sync.dma_start(out=h_h6.ap()[r0:r0 + n, 2600:2624], in_=zz[0:n, 0:24]), 'p0z', 16)
                for par in range(2):
                    c.inc(sync.dma_start(out=h_ps.ap()[par, :, :], in_=zz[0:41, 0:2130]), 'p0z', 16)
                c.inc(sync.dma_start(out=h_zz.ap()[0:1, 0:2600], in_=zz[0:1, 0:2600]), 'p0z', 16)
                c.inc(sync.dma_start(out=h_zz.ap()[0:1, 2600:4096], in_=zz[0:1, 0:1496]), 'p0z', 16)
                for tc in range(NTC):
                    dst = AP(h_va.ap().tensor, tc * 128 * 1040 + 64, [[1040, 128], [65, 16], [1, 1]])
                    c.inc(sync.dma_start(out=dst, in_=AP(oneb, 0, [[16, 128], [1, 16], [1, 1]]),
                                         allow_slow_non_contiguous=True), 'p0z', 16)
            blk.sync(s_sync)

            def s_tensor(tensor):
                c.wait(tensor, 'p0v', 1)
                c.inc(tensor.transpose(prb[0:64, 0:41], relb[0:41, 0:64], ident[0:41, 0:41]), 'p0t', 1)
            blk.tensor(s_tensor)

            def s_gp(g):
                c.wait(g, 'p0v', 1)
                c.inc(g.dma_start(out=relv40r[:], in_=relb[40:41, 64:128]), 'p0g', 16)
                c.inc(g.dma_start(out=gbhop[0:1, 0:16], in_=gbrow[0:1, :]), 'p0gb', 16)
                c.inc(g.dma_start(out=gbhop[0:1, 16:32], in_=gbrow[1:2, :]), 'p0gb', 16)
                c.wait(g, 'p0gb')
                bcast_row(g, c, gbig, gbhop[0:1, 0:32], 32, 'p0gb')
            blk.gpsimd(s_gp)

            def s_vector(vector):
                c.wait(vector, 'p0t', 1)
                c.wait(vector, 'p0g', 16)
                vector.tensor_copy(relkT[:], prb[0:64, 0:41])
                vector.drain()
                vector.tensor_copy(relcols[:, 0:1], relkT[:, 0:1])
                vector.tensor_copy(relcols[:, 1:2], relkT[:, 40:41])
                vector.drain()
                vector.tensor_scalar(relk_m40T[:], relkT[:], relcols[:, 0:1], None, ALU.subtract)
                vector.tensor_tensor(dvrow[:], relb[0:1, 64:128], relv40r[:], ALU.subtract)
                c.inc(vector.drain(), 'p0v', 1)
            blk.vector(s_vector)

        with nc.Block() as blk:
            def s_tensor(tensor):
                c.wait(tensor, 'p0v', 2)
                tensor.matmul(pr[:, 0:64], onesrow[:], relv40r[:], start=True, stop=True)
                c.inc(tensor.matmul(pr[:, 64:128], onesrow[:], dvrow[:], start=True, stop=True), 'p0t', 1)
            blk.tensor(s_tensor)

            def s_vector(vector):
                c.wait(vector, 'p0t', 2)
                vector.tensor_copy(relvp[:, 64:128], pr[:, 64:128])
                vector.tensor_copy(relvp[:, 0:64], pr[:, 64:128])
                vector.drain()
                vector.tensor_tensor(relvp[0:41, 0:64], relb[0:41, 128:192], pr[0:41, 0:64], ALU.subtract)
                c.inc(vector.drain(), 'p0v', 1)
            blk.vector(s_vector)


    c.dma_fence('p0z', 'p0g', 'p0gb')

    with ExitStack() as ps:
        rhsf = ps.enter_context(nc.sbuf_tensor("p0_rhsf", [128, NCH * 1024], F32))
        amx = ps.enter_context(nc.sbuf_tensor("p0_amx", [128, 1024], F32))
        rrow = ps.enter_context(nc.sbuf_tensor("p0_rrow", [1, 1024], F32))
        bcr = ps.enter_context(nc.sbuf_tensor("p0_bcr", [128, 1024], F32))
        q_ti = ps.enter_context(nc.sbuf_tensor("p0_ti", [128, 1024], I32))
        q_f = ps.enter_context(nc.sbuf_tensor("p0_f", [128, 1024], F32))
        q_d = ps.enter_context(nc.sbuf_tensor("p0_d", [128, 1024], F32))
        q_sg = ps.enter_context(nc.sbuf_tensor("p0_sg", [128, 1024], F32))
        q_o = ps.enter_context(nc.sbuf_tensor("p0_qo", [NCH, 128, 0] if False else [128, 1024], BF16))

        with nc.Block() as blk:
            def s_sync(sync):
                for ch in range(NCH):
                    c.inc(sync.dma_start(out=rhsf[:, ch * 1024:(ch + 1) * 1024],
                                         in_=wT["rhsw"].ap()[ch * 128:(ch + 1) * 128, :]), 'rld', 16)
            blk.sync(s_sync)

            def s_scalar(scalar):
                c.wait(scalar, 'rld')
                b0s = c.base('qs', 'qv')
                for ch in range(NCH):
                    if ch >= 2:
                        scalar.wait_ge(c.S('qv'), b0s['qv'] + ch - 1)
                    ab = q_f if ch % 2 == 0 else q_d
                    c.inc(scalar.activation(ab[:], rhsf[:, ch * 1024:(ch + 1) * 1024], AT.Abs), 'qs', 1)
            blk.scalar(s_scalar)

            def s_vector(vector):
                b0v = {'qs': c.cnt['qs'] - NCH, 'qv': c.cnt['qv']}
                for ch in range(NCH):
                    vector.wait_ge(c.S('qs'), b0v['qs'] + ch + 1)
                    ab = q_f if ch % 2 == 0 else q_d
                    vector.drain()
                    if ch == 0:
                        ins = vector.tensor_copy(amx[:], ab[:])
                    else:
                        ins = vector.tensor_tensor(amx[:], amx[:], ab[:], ALU.max)
                    c.inc(ins, 'qv', 1)
                part_tree(c, vector, amx, q_d, 1024)
                vector.tensor_scalar(grows[0:1, :], amx[0:1, :], 1e-12, 1.0 / 127.0, ALU.max, ALU.mult)
                vector.drain()
                c.inc(vector.reciprocal(rrow[:], grows[0:1, :]), 'rv', 1)
            blk.vector(s_vector)

            def s_gp(g):
                c.wait(g, 'rv', 1)
                bcast_row(g, c, bcr, rrow[0:1, 0:1024], 1024, 'rbc')
            blk.gpsimd(s_gp)

        with nc.Block() as blk:
            b0 = c.base('qv', 'qs', 'qst', 'qst2')

            def s_vector(vector):
                for ch in range(NCH):
                    src = rhsf[:, ch * 1024:(ch + 1) * 1024]
                    if ch == 0:
                        c.wait(vector, 'rbc')
                    if ch >= 1:
                        vector.wait_ge(c.S('qst'), b0['qst'] + 16 * ch)
                        if DBG:
                            vector.wait_ge(c.S('qst2'), b0['qst2'] + 16 * ch)
                    vector.tensor_tensor(q_d[:], src, bcr[:], ALU.mult)
                    vector.drain()
                    vector.tensor_copy(q_ti[:], q_d[:])
                    vector.drain()
                    vector.tensor_copy(q_f[:], q_ti[:])
                    vector.drain()
                    vector.tensor_tensor(q_d[:], q_d[:], q_f[:], ALU.subtract)
                    vector.drain()
                    vector.tensor_scalar(q_sg[:], q_d[:], 0.5, -0.5, ALU.min, ALU.max)
                    vector.drain()
                    c.inc(vector.tensor_tensor(q_d[:], q_d[:], q_sg[:], ALU.subtract), 'qv', 1)
                    vector.wait_ge(c.S('qs'), b0['qs'] + ch + 1)
                    vector.drain()
                    c.inc(vector.tensor_tensor(q_o[:], q_f[:], q_sg[:], ALU.add), 'qv', 1)
            blk.vector(s_vector)

            def s_scalar(scalar):
                for ch in range(NCH):
                    scalar.wait_ge(c.S('qv'), b0['qv'] + 2 * ch + 1)
                    c.inc(scalar.activation(q_sg[:], q_d[:], AT.Sign), 'qs', 1)
            blk.scalar(s_scalar)

            def s_gp(g):
                for ch in range(NCH):
                    g.wait_ge(c.S('qv'), b0['qv'] + 2 * ch + 2)
                    c.inc(g.dma_start(out=h_rhsq.ap()[ch], in_=q_o[:]), 'qst', 16)
                    if DBG:
                        c.inc(g.dma_start(out=dbg['qrhs'].ap()[ch], in_=q_o[:]), 'qst2', 16)
            blk.gpsimd(s_gp)

    c.dma_fence('qst', 'qst2', 'rbc')



    with ExitStack() as ps:
        wcf = [ps.enter_context(nc.sbuf_tensor(f"p3_wc{i}", [128, 4096], F32)) for i in range(2)]
        acc3 = ps.enter_context(nc.sbuf_tensor("p3_acc", [128, 48], F32))
        q_ti = ps.enter_context(nc.sbuf_tensor("p3_ti", [128, 4096], I32))
        q_f = ps.enter_context(nc.sbuf_tensor("p3_f", [128, 4096], F32))
        q_d = ps.enter_context(nc.sbuf_tensor("p3_d", [128, 4096], F32))
        q_sg = ps.enter_context(nc.sbuf_tensor("p3_sg", [128, 4096], F32))
        q_o = ps.enter_context(nc.sbuf_tensor("p3_qo", [128, 4096], BF16))
        prow3 = ps.enter_context(nc.psum_tensor("p3_pr", [1, 48], F32))

        items3 = ([("wgT", ch, 4096, h_wgq, ch, 0) for ch in range(NCH)]
                  + [("wuT", ch, 4096, h_wuq, 8 + ch, 1) for ch in range(NCH)]
                  + [("wdT", ch, 1024, h_wdq, 16 + ch, 2) for ch in range(NIC)])

        with nc.Block() as blk:
            b0 = c.base('lnl0', 'lnl1', 'qv')

            def s_sync(sync):
                for n, (nm, ch, w, hdst, j, mcol) in enumerate(items3):
                    if n >= 2:
                        sync.wait_ge(c.S('qv'), b0['qv'] + n - 1)
                    c.inc(sync.dma_start(out=wcf[n % 2][:, 0:w], in_=wT[nm].ap()[ch * 128:(ch + 1) * 128, :]),
                          'lnl%d' % (n % 2), 16)
            blk.sync(s_sync)

            def s_vector(vector):
                for n, (nm, ch, w, hdst, j, mcol) in enumerate(items3):
                    vector.wait_ge(c.S('lnl%d' % (n % 2)), b0['lnl%d' % (n % 2)] + 16 * (n // 2 + 1))
                    vector.drain()
                    c.inc(vector.tensor_reduce(acc3[:, j:j + 1], wcf[n % 2][:, 0:w], AX.X, ALU.add,
                                               apply_absolute_value=True), 'qv', 1)
            blk.vector(s_vector)

        with nc.Block() as blk:
            def s_tensor(tensor):
                c.inc(tensor.matmul(prow3[0:1, :], onescol_f[:], acc3[:, :], start=True, stop=True), 'pjt', 1)
            blk.tensor(s_tensor)

            def s_vector(vector):
                c.wait(vector, 'pjt')
                vector.memset(swrow[:], 0.0)
                vector.drain()
                vector.tensor_reduce(swrow[0:1, 4:5], prow3[0:1, 0:8], AX.X, ALU.add)
                vector.tensor_reduce(swrow[0:1, 5:6], prow3[0:1, 8:16], AX.X, ALU.add)
                vector.tensor_reduce(swrow[0:1, 6:7], prow3[0:1, 16:48], AX.X, ALU.add)
                vector.drain()
                vector.tensor_scalar(swrow[0:1, 4:7], swrow[0:1, 4:7], 1.0 / (1024.0 * 4096.0), 1e-5,
                                     ALU.mult, ALU.max)
                vector.drain()
                c.inc(vector.reciprocal(swrow[0:1, 0:3], swrow[0:1, 4:7]), 'rv', 1)
            blk.vector(s_vector)

            def s_gp(g):
                c.wait(g, 'rv')
                bcast_row(g, c, swcol, swrow[0:1, 0:8], 8, 'rbc')
            blk.gpsimd(s_gp)

        with nc.Block() as blk:
            b0 = c.base('lnm0', 'lnm1', 'qv', 'qs', 'qst')

            def s_sync(sync):
                c.wait(sync, 'rbc')
                for n, (nm, ch, w, hdst, j, mcol) in enumerate(items3):
                    if n >= 2:
                        sync.wait_ge(c.S('qv'), b0['qv'] + 2 * (n - 1))
                    c.inc(sync.dma_start(out=wcf[n % 2][:, 0:w], in_=wT[nm].ap()[ch * 128:(ch + 1) * 128, :]),
                          'lnm%d' % (n % 2), 16)
            blk.sync(s_sync)

            def s_vector(vector):
                c.wait(vector, 'rbc')
                for n, (nm, ch, w, hdst, j, mcol) in enumerate(items3):
                    vector.wait_ge(c.S('lnm%d' % (n % 2)), b0['lnm%d' % (n % 2)] + 16 * (n // 2 + 1))
                    vector.drain()
                    vector.tensor_scalar(q_d[:, 0:w], wcf[n % 2][:, 0:w], swcol[:, mcol:mcol + 1], None, ALU.mult)
                    vector.drain()
                    vector.tensor_scalar(q_f[:, 0:w], q_d[:, 0:w], 0.5, None, ALU.is_ge)
                    c.inc(vector.tensor_scalar(q_sg[:, 0:w], q_d[:, 0:w], -0.5, None, ALU.is_le), 'qv', 1)
                    if n >= 1:
                        nprev = 16 + 8 * max(0, n - 16) + (8 if n > 16 else 0) if False else (
                            n if n <= 16 else 16 + 8 * (n - 16))
                        vector.wait_ge(c.S('qst'), b0['qst'] + 16 * nprev)
                    vector.drain()
                    c.inc(vector.tensor_tensor(q_o[:, 0:w], q_f[:, 0:w], q_sg[:, 0:w], ALU.subtract), 'qv', 1)
            blk.vector(s_vector)

            def s_gp(g):
                nst = 0
                for n, (nm, ch, w, hdst, j, mcol) in enumerate(items3):
                    g.wait_ge(c.S('qv'), b0['qv'] + 2 * n + 2)
                    if nst >= 1:
                        g.wait_ge(c.S('qst'), b0['qst'] + 16 * nst)
                    if nm == "wdT":
                        for och in range(NCH):
                            nst += 1
                            c.inc(g.dma_start(out=h_wdq.ap()[och][ch],
                                              in_=q_o[:, och * 128:(och + 1) * 128]), 'qst', 16)
                    else:
                        nst += 1
                        c.inc(g.dma_start(out=hdst.ap()[ch], in_=q_o[:, 0:w]), 'qst', 16)
            blk.gpsimd(s_gp)
    c.dma_fence('qst', 'rbc')

    def p1a(s, xn0):
        with ExitStack() as ps:
            xc = [ps.enter_context(nc.sbuf_tensor(f"a_xc{i}_{s}", [128, T], F32)) for i in range(2)]
            sq = ps.enter_context(nc.sbuf_tensor(f"a_sq_{s}", [128, T], F32))
            amx = ps.enter_context(nc.sbuf_tensor(f"a_amx_{s}", [128, T], F32))
            abt = ps.enter_context(nc.sbuf_tensor(f"a_abt_{s}", [128, T], F32))
            bca = ps.enter_context(nc.sbuf_tensor(f"a_bca_{s}", [128, T], F32))
            bcb = ps.enter_context(nc.sbuf_tensor(f"a_bcb_{s}", [128, T], F32))
            rw = [bca, bcb, xc[0], xc[1]]
            pstat = ps.enter_context(nc.psum_tensor(f"a_pstat_{s}", [1, T], F32))
            pstatq = ps.enter_context(nc.psum_tensor(f"a_pstatq_{s}", [1, T], F32))
            rws = lambda k: rw[k][0:1, 0:T]

            with nc.Block() as blk:
                b0 = c.base('lnl0', 'lnl1', 'lnsq', 'lnu')

                def s_sync(sync):
                    for ch in range(NCH):
                        if ch >= 2:
                            sync.wait_ge(c.S('lnu'), b0['lnu'] + ch - 1)
                        c.inc(sync.dma_start(out=xc[ch % 2][:], in_=x_in.ap()[s, ch * 128:(ch + 1) * 128, :]),
                              'lnl%d' % (ch % 2), 16)
                blk.sync(s_sync)

                def s_scalar(scalar):
                    for ch in range(NCH):
                        scalar.wait_ge(c.S('lnl%d' % (ch % 2)), b0['lnl%d' % (ch % 2)] + 16 * (ch // 2 + 1))
                        if ch >= 1:
                            scalar.wait_ge(c.S('lnu'), b0['lnu'] + ch)
                        c.inc(scalar.activation(sq[:], xc[ch % 2][:], AT.Square), 'lnsq', 1)
                blk.scalar(s_scalar)

                def s_tensor(tensor):
                    for ch in range(NCH):
                        tensor.wait_ge(c.S('lnsq'), b0['lnsq'] + ch + 1)
                        for qt in range(NQT):
                            tensor.matmul(pstat[0:1, qt * 512:(qt + 1) * 512], onescol_f[:],
                                          xc[ch % 2][:, qt * 512:(qt + 1) * 512],
                                          start=(ch == 0), stop=(ch == NCH - 1))
                            ins = tensor.matmul(pstatq[0:1, qt * 512:(qt + 1) * 512], onescol_f[:],
                                                sq[:, qt * 512:(qt + 1) * 512],
                                                start=(ch == 0), stop=(ch == NCH - 1))
                        c.inc(ins, 'lnu', 1)
                blk.tensor(s_tensor)

            with nc.Block() as blk:
                def s_vector(vector):
                    vector.tensor_scalar_mul(rws(0), pstat[0:1, :], 1.0 / D)
                    vector.tensor_scalar_mul(rws(1), pstatq[0:1, :], 1.0 / D)
                    vector.drain()
                    vector.tensor_tensor(rws(2), rws(0), rws(0), ALU.mult)
                    vector.drain()
                    c.inc(vector.tensor_tensor(rws(1), rws(1), rws(2), ALU.subtract), 'lnr', 1)
                blk.vector(s_vector)

                def s_scalar(scalar):
                    c.wait(scalar, 'lnr')
                    c.inc(scalar.activation(rws(1), rws(1), AT.Sqrt, bias=cvals[0:1, 0:1]), 'lns', 1)
                blk.scalar(s_scalar)

            with nc.Block() as blk:
                def s_vector(vector):
                    c.wait(vector, 'lns')
                    vector.reciprocal(rws(2), rws(1))
                    vector.drain()
                    vector.tensor_tensor(rws(3), rws(0), rws(2), ALU.mult)
                    vector.drain()
                    c.inc(vector.tensor_scalar_mul(rws(3), rws(3), -1.0), 'lnr', 1)
                blk.vector(s_vector)

                def s_gp(g):
                    c.wait(g, 'lnr')
                    c.inc(g.dma_start(out=rows[8 * s + R_A0:8 * s + R_A0 + 1, :], in_=rws(2)), 'lnpk', 16)
                    c.inc(g.dma_start(out=rows[8 * s + R_B0:8 * s + R_B0 + 1, :], in_=rws(3)), 'lnpk', 16)
                    c.wait(g, 'lnpk')
                    bcast_row(g, c, bca, rws(2), T, 'lnbc')
                    bcast_row(g, c, bcb, rws(3), T, 'lnbc')
                blk.gpsimd(s_gp)

            with nc.Block() as blk:
                b0 = c.base('lnm0', 'lnm1', 'lnx', 'lnq')

                def s_sync(sync):
                    c.wait(sync, 'lnbc')
                    c.wait(sync, 'lnpk')
                    for ch in range(NCH):
                        if ch >= 2:
                            sync.wait_ge(c.S('lnx'), b0['lnx'] + 2 * (ch - 1))
                        c.inc(sync.dma_start(out=xc[ch % 2][:], in_=x_in.ap()[s, ch * 128:(ch + 1) * 128, :]),
                              'lnm%d' % (ch % 2), 16)
                blk.sync(s_sync)

                def s_vector(vector):
                    c.wait(vector, 'lnbc')
                    for ch in range(NCH):
                        vector.wait_ge(c.S('lnm%d' % (ch % 2)), b0['lnm%d' % (ch % 2)] + 16 * (ch // 2 + 1))
                        vector.drain()
                        vector.tensor_tensor(sq[:], xc[ch % 2][:], bca[:], ALU.mult)
                        vector.drain()
                        vector.tensor_tensor(sq[:], sq[:], bcb[:], ALU.add)
                        vector.drain()
                        c.inc(vector.tensor_copy(xn0[:, ch * T:(ch + 1) * T], sq[:]), 'lnx', 1)
                        vector.wait_ge(c.S('lnq'), b0['lnq'] + ch + 1)
                        vector.drain()
                        if ch == 0:
                            ins = vector.tensor_copy(amx[:], abt[:])
                        else:
                            ins = vector.tensor_tensor(amx[:], amx[:], abt[:], ALU.max)
                        c.inc(ins, 'lnx', 1)
                blk.vector(s_vector)

                def s_scalar(scalar):
                    for ch in range(NCH):
                        scalar.wait_ge(c.S('lnx'), b0['lnx'] + 2 * ch + 1)
                        c.inc(scalar.activation(abt[:], sq[:], AT.Abs), 'lnq', 1)
                blk.scalar(s_scalar)

            with nc.Block() as blk:
                def s_vector(vector):
                    part_tree(c, vector, amx, sq, T)
                    c.inc(vector.tensor_scalar(amx[0:1, 0:T], amx[0:1, :], 1e-12, 1.0 / 127.0,
                                               ALU.max, ALU.mult), 'lnd', 1)
                blk.vector(s_vector)

                def s_gp(g):
                    c.wait(g, 'lnd')
                    c.inc(g.dma_start(out=rows[8 * s + R_LS0:8 * s + R_LS0 + 1, :], in_=amx[0:1, 0:T]), 'lnbc', 16)
                    if DBG:
                        for ch in range(NCH):
                            c.inc(g.dma_start(out=dbg['ln0'].ap()[s, ch], in_=xn0[:, ch * T:(ch + 1) * T]), 'dbg', 16)
                        c.wait(g, 'lnbc')
                        c.inc(g.dma_start(out=dbg['rows'].ap()[s], in_=rows[:, :]), 'dbg', 16)
                blk.gpsimd(s_gp)

            c.dma_fence('lnbc', 'dbg')


    def l0_proj(s, xn0, w_qkv):
        with ExitStack() as ps:
            qtev = ps.enter_context(nc.sbuf_tensor(f"b_qtev_{s}", [128, T], BF16))
            ktev = ps.enter_context(nc.sbuf_tensor(f"b_ktev_{s}", [128, T], BF16))
            vtmp = ps.enter_context(nc.sbuf_tensor(f"b_vtmp_{s}", [128, T], BF16))
            vatm = ps.enter_context(nc.sbuf_tensor(f"b_vatm_{s}", [128, T], BF16))
            bstk = ExitStack()
            blk_all = bstk.enter_context(nc.Block())
            for g in range(8):
                gps = ExitStack()
                pQ = gps.enter_context(nc.psum_tensor(f"b_pQ_{s}_{g}", [128, T], F32))
                pK = gps.enter_context(nc.psum_tensor(f"b_pK_{s}_{g}", [128, T], F32))
                if True:
                    blk = blk_all
                    def s_tensor(tensor, g=g):
                        c.wait(tensor, 'pjv')
                        for i in range(NCH):
                            wq = w_qkv[0][:, i * D + g * 128:i * D + (g + 1) * 128]
                            wk = w_qkv[1][:, i * D + g * 128:i * D + (g + 1) * 128]
                            for qt in range(NQT):
                                xsl = xn0[:, i * T + qt * 512:i * T + (qt + 1) * 512]
                                tensor.matmul(pQ[:, qt * 512:(qt + 1) * 512], wq, xsl,
                                              start=(i == 0), stop=(i == NCH - 1))
                                ins = tensor.matmul(pK[:, qt * 512:(qt + 1) * 512], wk, xsl,
                                                    start=(i == 0), stop=(i == NCH - 1))
                        c.inc(ins, 'pjt', 1)
                    blk.tensor(s_tensor)

                    def s_scalar(scalar, g=g):
                        c.wait(scalar, 'pjt')
                        c.wait(scalar, 'pjk')
                        scalar.activation(qtev[:], pQ[:], AT.Identity, bias=bias_sb[:, 0 * NCH + g:0 * NCH + g + 1])
                        c.inc(scalar.activation(ktev[:], pK[:], AT.Identity, bias=bias_sb[:, 1 * NCH + g:1 * NCH + g + 1]), 'pjs', 1)
                    blk.scalar(s_scalar)

                    def s_gp(gp, g=g):
                        c.wait(gp, 'pjs')
                        c.inc(gp.dma_start(out=h_qt.ap()[2 * g], in_=qtev[0:64, :]), 'pjk', 16)
                        c.inc(gp.dma_start(out=h_qt.ap()[2 * g + 1], in_=qtev[64:128, :]), 'pjk', 16)
                        c.inc(gp.dma_start(out=h_kt.ap()[2 * g], in_=ktev[0:64, :]), 'pjk', 16)
                        c.inc(gp.dma_start(out=h_kt.ap()[2 * g + 1], in_=ktev[64:128, :]), 'pjk', 16)
                    blk.gpsimd(s_gp)

                gps.close()
                gps = ExitStack()
                pQ = gps.enter_context(nc.psum_tensor(f"b_pV_{s}_{g}", [128, T], F32))
                pVt = gps.enter_context(nc.psum_tensor(f"b_pVt_{s}_{g}", [128, T], BF16))
                if True:
                    def s_tensor(tensor, g=g):
                        c.wait(tensor, 'pjs')
                        for i in range(NCH):
                            wv = w_qkv[2][:, i * D + g * 128:i * D + (g + 1) * 128]
                            for qt in range(NQT):
                                xsl = xn0[:, i * T + qt * 512:i * T + (qt + 1) * 512]
                                ins = tensor.matmul(pQ[:, qt * 512:(qt + 1) * 512], wv, xsl,
                                                    start=(i == 0), stop=(i == NCH - 1))
                        c.inc(ins, 'pjt', 1)
                    blk.tensor(s_tensor)

                    def s_scalar(scalar, g=g):
                        c.wait(scalar, 'pjt')
                        c.inc(scalar.activation(vtmp[:], pQ[:], AT.Identity, bias=bias_sb[:, 2 * NCH + g:2 * NCH + g + 1]), 'pjs', 1)
                    blk.scalar(s_scalar)

                if True:
                    def s_tensor(tensor, g=g):
                        c.wait(tensor, 'pjs')
                        for tc in range(NTC):
                            ins = tensor.transpose(pVt[:, tc * 128:(tc + 1) * 128],
                                                   vtmp[:, tc * 128:(tc + 1) * 128], ident[:])
                        c.inc(ins, 'pjt', 1)
                    blk.tensor(s_tensor)

                    def s_vector(vector, g=g):
                        c.wait(vector, 'pjt')
                        c.wait(vector, 'pjk')
                        c.inc(vector.tensor_copy(vatm[:], pVt[:]), 'pjv', 1)
                    blk.vector(s_vector)

                    def s_gp(gp, g=g):
                        c.wait(gp, 'pjv')
                        for k in range(2):
                            src = AP(vatm, 64 * k, [[T, 128], [128, NTC], [1, 64]])
                            dst = AP(h_va.ap().tensor, (2 * g + k) * 65,
                                     [[1040, 128], [128 * 1040, NTC], [1, 64]])
                            c.inc(gp.dma_start(out=dst, in_=src), 'pjk', 16)
                    blk.gpsimd(s_gp)
                gps.close()
            bstk.close()
            c.dma_fence('pjk')

    def l0_r1(s, h, bufs, pst, blk_all):
        QTh, KTh, Vah, pslab, cstr, ptil = bufs[h % 2]
        psPh, ptp = pst['psPh'], pst['ptp']

        if True:
            def s_sync(sync):
                c.wait(sync, 'sct')
                c.wait(sync, 'kbv')
                c.wait(sync, 'avt')
                c.inc(sync.dma_start(out=QTh[:], in_=h_qt.ap()[h]), 'hld', 16)
                c.inc(sync.dma_start(out=KTh[:], in_=h_kt.ap()[h]), 'hld', 16)
                src = AP(h_va.ap().tensor, h * 65, [[1040, 128], [128 * 1040, NTC], [1, 65]])
                c.inc(sync.dma_start(out=Vah[:].rearrange("p (c u) -> p c u", u=65), in_=src), 'hld', 16)
            blk_all.sync(s_sync)

            def s_tensor(tensor):
                b0v = c.base('rlv')
                c.wait(tensor, 'hld')
                c.wait(tensor, 'rlv')
                for q4 in range(4):
                    if q4 >= 1:
                        tensor.wait_ge(c.S('rlv'), b0v['rlv'] + q4)
                    ins = tensor.matmul(psPh[:, 0:512], relk_m40T[:],
                                        QTh[:, q4 * 512:(q4 + 1) * 512], start=True, stop=True)
                    c.inc(ins, 'rlt', 1)
            blk_all.tensor(s_tensor)

            def s_vector(vector):
                b0r = c.base('rlt')
                c.wait(vector, 'w2t')
                c.wait(vector, 'rcw')
                vector.memset(pslab[0:41, 2048:2208], 0.0)
                vector.drain()
                for q4 in range(4):
                    vector.wait_ge(c.S('rlt'), b0r['rlt'] + q4 - 3)
                    vector.drain()
                    c.inc(vector.tensor_copy(pslab[0:41, q4 * 512:(q4 + 1) * 512], psPh[0:41, :]), 'rlv', 1)
            blk_all.vector(s_vector)

            def s_sync(sync):
                c.wait(sync, 'rlv')
                c.inc(sync.dma_start(out=h_ps.ap()[h % 2][0:41, 21:2069], in_=pslab[0:41, 0:2048]), 'rps', 16)
                c.inc(sync.dma_start(out=h_gp.ap()[h % 2][0:1, 0:2208], in_=pslab[40:41, :]), 'rps', 16)
                c.wait(sync, 'rps')
                c.inc(sync.dma_start(out=pslab[0:41, 0:2089],
                                     in_=AP(h_ps.ap().tensor, (h % 2) * 41 * 2130 + 1, [[2131, 41], [1, 2089]])),
                      'rps', 16)
                c.inc(sync.dma_start(
                    out=AP(h_c.ap().tensor, (h % 2) * NTC * 128 * 464 + 169, [[59392, NTC], [464, 128], [1, 127]]),
                    in_=AP(h_gp.ap().tensor, (h % 2) * 2208 + 21, [[128, NTC], [1, 128], [1, 127]])), 'rcw', 16)
            blk_all.sync(s_sync)

    def l0_r1b(s, h, bufs, pst, blk_all):
        QTh, KTh, Vah, pslab, cstr, ptil = bufs[h % 2]
        psPh, ptp = pst['psPh'], pst['ptp']
        if True:
            def s_tensor(tensor):
                c.wait(tensor, 'rps')
                for ci in range(NTC):
                    ins = tensor.transpose(ptp[:, ci * 42:ci * 42 + 41],
                                           pslab[0:41, ci * 128:(ci + 1) * 128], ident[0:41, 0:41])
                c.inc(ins, 'rlt', 1)
            blk_all.tensor(s_tensor)

            def s_vector(vector):
                c.wait(vector, 'rlt')
                c.inc(vector.tensor_copy(
                    AP(ptil, 0, [[672, 128], [42, NTC], [1, 41]]),
                    AP(ptp, 0, [[672, 128], [42, NTC], [1, 41]])), 'rlv', 1)
            blk_all.vector(s_vector)

            def s_sync(sync):
                c.wait(sync, 'rlv')
                c.inc(sync.dma_start(
                    out=AP(h_c.ap().tensor, (h % 2) * NTC * 128 * 464 + 128, [[464, 128], [59392, NTC], [1, 41]]),
                    in_=AP(ptil, 0, [[672, 128], [42, NTC], [1, 41]])), 'rcw', 16)
                c.wait(sync, 'rcw')
                hb = (h % 2) * NTC * 128 * 464
                c.inc(sync.dma_start(out=cstr[:, 0:148],
                                     in_=AP(h_c.ap().tensor, hb + 148, [[463, 128], [1, 148]])), 'ccr0', 16)
                bulk_src = AP(h_c.ap().tensor, hb + 59392 + 128, [[463, 128], [59392, 14], [1, 168]])
                bulk_dst = AP(cstr, 168, [[16 * 168, 128], [168, 14], [1, 168]])
                c.inc(sync.dma_start(out=bulk_dst, in_=bulk_src), 'ccr0', 16)
                c.inc(sync.dma_start(out=cstr[:, 15 * 168:15 * 168 + 148],
                                     in_=AP(h_c.ap().tensor, hb + 15 * 59392 + 128, [[463, 128], [1, 148]])),
                      'ccr0', 16)
            blk_all.sync(s_sync)

    def l0_r234(s, h, bufs, pst, slabO, Zs, Aslab, kw, h4t, stg, zrow, zrowb, zsnb, blk_all, mid_hook=None):
        QTh, KTh, Vah, pslab, cstr, ptil = bufs[h % 2]
        pscA, pscB, psPh, ptp = pst['pscA'], pst['pscB'], pst['psPh'], pst['ptp']
        psc = [pscA, pscB]
        dgt2 = [stg[:, 3072:4608], stg[:, 4608:6144]]
        zsg = pslab[0:1, 0:1536]

        if True:
            b0 = c.base('sct', 'scs', 'kbv', 'ccr0')

            def s_vector(vector):
                for ci in range(NTC):
                    if ci >= 1:
                        vector.wait_ge(c.S('sct'), b0['sct'] + 2 * ci)
                    else:
                        vector.wait_ge(c.S('sct'), b0['sct'])
                    vector.tensor_scalar(kw[:, 0:128], KTh[:, ci * 128:(ci + 1) * 128],
                                         relcols[:, 0:1], None, ALU.add)
                    c.inc(vector.tensor_scalar(kw[:, 128:256], KTh[:, ci * 128:(ci + 1) * 128],
                                               relcols[:, 1:2], None, ALU.add), 'kbv', 1)
            blk_all.vector(s_vector)

            def s_tensor(tensor):
                tensor.wait_ge(c.S('ccr0'), b0['ccr0'])
                c.wait(tensor, 'avo')
                for ci in range(NTC):
                    k0 = ci * 128
                    tensor.wait_ge(c.S('kbv'), b0['kbv'] + ci + 1)
                    hi40 = min(k0 + 148, T)
                    lo = max(k0 - 20, 0)
                    for hf in range(2):
                        n = 2 * ci + hf
                        if n >= 2:
                            tensor.wait_ge(c.S('scs'), b0['scs'] + n - 1)
                        pc_ = psc[n % 2]
                        ins = None
                        for q in range(2):
                            w0, w1 = hf * 1024 + q * 512, hf * 1024 + (q + 1) * 512
                            pq0 = q * 512
                            a0, a1 = w0, max(min(lo, w1), w0)
                            if a1 > a0:
                                ins = tensor.matmul(pc_[:, pq0 + (a0 - w0):pq0 + (a1 - w0)], kw[:, 0:128],
                                                    QTh[:, a0:a1], start=True, stop=True)
                            c0_, c1_ = max(lo, w0), min(hi40, w1)
                            if c1_ > c0_:
                                tensor.matmul(pc_[:, pq0 + (c0_ - w0):pq0 + (c1_ - w0)], kw[:, 0:128],
                                              QTh[:, c0_:c1_], start=True, stop=False)
                                ins = tensor.matmul(pc_[:, pq0 + (c0_ - w0):pq0 + (c1_ - w0)], ident[:],
                                                    cstr[:, ci * 168 + (c0_ - lo):ci * 168 + (c1_ - lo)],
                                                    start=False, stop=True)
                            b0_, b1_ = max(hi40, w0), w1
                            if b1_ > b0_:
                                ins = tensor.matmul(pc_[:, pq0 + (b0_ - w0):pq0 + (b1_ - w0)], kw[:, 128:256],
                                                    QTh[:, b0_:b1_], start=True, stop=True)
                        c.inc(ins, 'sct', 1)
            blk_all.tensor(s_tensor)

            def s_scalar(scalar):
                for n in range(2 * NTC):
                    ci, hf = n // 2, n % 2
                    scalar.wait_ge(c.S('sct'), b0['sct'] + n + 1)
                    c.inc(scalar.activation(Aslab[:, ci * T + hf * 1024:ci * T + (hf + 1) * 1024],
                                            psc[n % 2][:], AT.Exp, scale=0.125), 'scs', 1)
            blk_all.scalar(s_scalar)

        if mid_hook is not None:
            mid_hook()
        if True:
            ptp2 = pst['ptp2']
            if True:
                b0 = c.base('dsw', 'h4r0', 'h4r1', 'h4v', 'h4t', 'h4e', 'h6w', 'h6g')

                def s_sync(sync):
                    c.wait(sync, 'dgr0')
                    c.wait(sync, 'dgr1')
                    c.inc(sync.dma_start(out=AP(h_h6.ap().tensor, 0, [[2625, 276], [1, 256]]),
                                         in_=AP(h_zz.ap().tensor, 0, [[1, 276], [1, 256]])), 'h6w', 16)
                    c.inc(sync.dma_start(out=AP(h_h6.ap().tensor, 2304, [[2625, 276], [1, 32]]),
                                         in_=AP(h_zz.ap().tensor, 256, [[1, 276], [1, 32]])), 'h6w', 16)
                    b0scs = c.cnt['scs'] - 2 * NTC
                    for ci in range(NTC):
                        k0 = ci * 128
                        lo = max(k0 - 20, 0)
                        hi = min(k0 + 383, T)
                        w0 = lo - (k0 - 20)
                        sync.wait_ge(c.S('scs'), b0scs + 2 * ci + 2)
                        c.inc(sync.dma_start(out=h_strip.ap()[ci, :, w0:w0 + hi - lo],
                                             in_=Aslab[:, ci * T + lo:ci * T + hi]), 'dsw', 16)
                    c.wait(sync, 'dsw')
                    for ci in range(NTC):
                        if ci >= 2:
                            sync.wait_ge(c.S('h4t'), b0['h4t'] + ci - 1)
                        src = AP(h_strip.ap().tensor, ci * 128 * 404, [[405, 128], [1, 276]])
                        c.inc(sync.dma_start(out=h4t[:, (ci % 2) * 276:(ci % 2 + 1) * 276], in_=src),
                              'h4r%d' % (ci % 2), 16)
                blk_all.sync(s_sync)

                def s_vector(vector):
                    c.wait(vector, 'h6g')
                    c.wait(vector, 'w2t')
                    for ci in range(NTC):
                        vector.wait_ge(c.S('h4r%d' % (ci % 2)), b0['h4r%d' % (ci % 2)] + 16 * (ci // 2 + 1))
                        if ci >= 2:
                            vector.wait_ge(c.S('h4t'), b0['h4t'] + ci - 1)
                        sl = h4t[:, (ci % 2) * 276:(ci % 2 + 1) * 276]
                        c.inc(vector.tensor_tensor(sl, sl, adm[:], ALU.mult), 'h4v', 1)
                        if ci >= 1:
                            vector.wait_ge(c.S('h4t'), b0['h4t'] + ci)
                            cj = (ci - 1) * 128
                            vector.tensor_copy(stg[:, cj:cj + 128], ptp2[:, 0:128])
                            vector.tensor_copy(stg[:, 2048 + cj:2048 + cj + 128], ptp2[:, 128:256])
                            c.inc(vector.tensor_copy(stg[0:20, 4096 + cj:4096 + cj + 128],
                                                     ptp2[0:20, 256:384]), 'h4e', 1)
                    vector.wait_ge(c.S('h4t'), b0['h4t'] + NTC)
                    cj = (NTC - 1) * 128
                    vector.tensor_copy(stg[:, cj:cj + 128], ptp2[:, 0:128])
                    vector.tensor_copy(stg[:, 2048 + cj:2048 + cj + 128], ptp2[:, 128:256])
                    c.inc(vector.tensor_copy(stg[0:20, 4096 + cj:4096 + cj + 128],
                                             ptp2[0:20, 256:384]), 'h4e', 1)
                blk_all.vector(s_vector)

                def s_tensor(tensor):
                    for ci in range(NTC):
                        tensor.wait_ge(c.S('h4v'), b0['h4v'] + ci + 1)
                        if ci >= 1:
                            tensor.wait_ge(c.S('h4e'), b0['h4e'] + ci)
                        sl = h4t[:, (ci % 2) * 276:(ci % 2 + 1) * 276]
                        tensor.transpose(ptp2[:, 0:128], sl[:, 0:128], ident[:])
                        tensor.transpose(ptp2[:, 128:256], sl[:, 128:256], ident[:])
                        ins = tensor.transpose(ptp2[0:20, 256:384], sl[:, 256:276], ident[:])
                        c.inc(ins, 'h4t', 1)
                blk_all.tensor(s_tensor)

                def s_gp(gp):
                    gp.wait_ge(c.S('h4e'), b0['h4e'] + NTC)
                    for pi, (u0, nu) in enumerate(((0, 128), (128, 128), (256, 20))):
                        if pi >= 1:
                            gp.wait_ge(c.S('h6g'), b0['h6g'] + 16 * pi)
                        dst = AP(h_h6.ap().tensor, u0 * 2625 + 256, [[2625, nu], [1, 2048]])
                        srcs = AP(stg, pi * 2048, [[3 * 2048, nu], [1, 2048]])
                        c.inc(gp.dma_start(out=dst, in_=srcs), 'h6g', 16)
                blk_all.gpsimd(s_gp)

        if True:
            pOa = pscA[0:65, 0:512]
            pOm = pscA[0:65, 512:1024]
            pOw = pscB[0:64, 0:512]
            sng = [pscB[0:1, 512:896], pst['pZ3'][0:1, 0:256], pscB[0:1, 896:1024]]
            b0 = c.base('avt', 'avv', 'avz', 'avo', 'snv', 'dgr0', 'dgr1', 'w2t', 'h6w', 'h6g', 'scs')

            def s_sync(sync):
                sync.wait_ge(c.S('h6w'), b0['h6w'])
                sync.wait_ge(c.S('h6g'), b0['h6g'])
                for qt in range(NQT):
                    if qt >= 2:
                        sync.wait_ge(c.S('w2t'), b0['w2t'] + qt - 1)
                    for pi, (u0, nu) in enumerate(((0, 128), (128, 128), (256, 20))):
                        src = AP(h_h6.ap().tensor, u0 * 2624 + 276 + qt * 512, [[2624, nu], [1, 512]])
                        c.inc(sync.dma_start(out=dgt2[qt % 2][0:nu, pi * 512:pi * 512 + 512], in_=src),
                              'dgr%d' % (qt % 2), 16)
            blk_all.sync(s_sync)

            def s_tensor(tensor):
                tensor.wait_ge(c.S('scs'), b0['scs'])
                for qt in range(NQT):
                    w0 = qt * 512
                    cb = 4 * qt - 2
                    if qt >= 1:
                        tensor.wait_ge(c.S('avo'), b0['avo'] + qt)
                    for ci in range(0, cb + 1):
                        tensor.matmul(pOa, Vah[:, ci * 65:(ci + 1) * 65],
                                      Aslab[:, ci * T + w0:ci * T + w0 + 512],
                                      start=(ci == 0), stop=(ci == cb))
                    for i in range(3):
                        cs = 4 * qt - 1 + i
                        if cs < 0:
                            continue
                        col0 = 128 * (1 + i)
                        tensor.matmul(sng[i][0:1, 0:512 - col0], onescol_b[:],
                                      Aslab[:, cs * T + w0 + col0:cs * T + w0 + 512],
                                      start=True, stop=True)
                    cm0 = max(4 * qt - 1, 0)
                    for ci in range(cm0, NTC):
                        ins = tensor.matmul(pOm, Vah[:, ci * 65:(ci + 1) * 65],
                                            Aslab[:, ci * T + w0:ci * T + w0 + 512],
                                            start=(ci == cm0), stop=(ci == NTC - 1))
                    c.inc(ins, 'avt', 1)
                    tensor.wait_ge(c.S('avv'), b0['avv'] + 2 * qt + 1)
                    tensor.wait_ge(c.S('snv'), b0['snv'] + qt + 1)
                    tensor.wait_ge(c.S('dgr%d' % (qt % 2)), b0['dgr%d' % (qt % 2)] + 48 * (qt // 2 + 1))
                    tensor.matmul(pOw, relv40r[:], zrowb[:], start=True, stop=False)
                    if cb >= 0:
                        tensor.matmul(pOw, dvrow[:], zsnb[0:1, 0:512], start=False, stop=False)
                    for i in range(3):
                        cs = 4 * qt - 1 + i
                        if cs < 0:
                            continue
                        col0 = 128 * (1 + i)
                        tensor.matmul(pOw[0:64, col0:512], dvrow[:],
                                      zsg[0:1, 512 * i:512 * i + (512 - col0)],
                                      start=False, stop=False)
                    dg_ = dgt2[qt % 2]
                    tensor.matmul(pOw, relvp[:, 0:64], dg_[0:128, 0:512], start=False, stop=False)
                    tensor.matmul(pOw, relvp[:, 64:128], dg_[0:128, 512:1024], start=False, stop=False)
                    ins = tensor.matmul(pOw[0:20, 0:512] if False else pOw, relvp[0:20, 64:128],
                                        dg_[0:20, 1024:1536], start=False, stop=True)
                    c.inc(ins, 'w2t', 1)
            blk_all.tensor(s_tensor)

            def s_vector(vector):
                for qt in range(NQT):
                    cb = 4 * qt - 2
                    vector.wait_ge(c.S('avt'), b0['avt'] + qt + 1)
                    if qt >= 1:
                        vector.wait_ge(c.S('avz'), b0['avz'] + 16 * qt)
                    else:
                        vector.wait_ge(c.S('avz'), b0['avz'])
                    vector.drain()
                    if cb >= 0:
                        vector.tensor_copy(zsnb[0:1, 0:512], pOa[64:65, :])
                    for i in range(3):
                        cs = 4 * qt - 1 + i
                        if cs < 0:
                            continue
                        col0 = 128 * (1 + i)
                        vector.tensor_copy(zsg[0:1, 512 * i:512 * i + (512 - col0)],
                                           sng[i][0:1, 0:512 - col0])
                    vector.drain()
                    if cb >= 0:
                        vector.tensor_copy(zrow[:], pOa[64:65, :])
                        vector.drain()
                        vector.tensor_tensor(zrow[:], zrow[:], pOm[64:65, :], ALU.add)
                    else:
                        vector.tensor_copy(zrow[:], pOm[64:65, :])
                    vector.drain()
                    c.inc(vector.tensor_copy(zrowb[:], zrow[:]), 'avv', 1)
                    c.inc(vector.drain(), 'snv', 1)
                    c.inc(vector.drain(), 'avv', 1)
                    vector.wait_ge(c.S('w2t'), b0['w2t'] + qt + 1)
                    vector.drain()
                    sl = slabO[(h % 2) * 64:(h % 2) * 64 + 64,
                               (h // 2) * T + qt * 512:(h // 2) * T + (qt + 1) * 512]
                    if cb >= 0:
                        vector.tensor_copy(sl, pOa[0:64, :])
                        vector.drain()
                        vector.tensor_tensor(sl, sl, pOm[0:64, :], ALU.add)
                    else:
                        vector.tensor_copy(sl, pOm[0:64, :])
                    vector.drain()
                    vector.tensor_tensor(sl, sl, pOw, ALU.add)
                    c.inc(vector.drain(), 'avo', 1)
            blk_all.vector(s_vector)

            def s_gp(gp):
                for qt in range(NQT):
                    gp.wait_ge(c.S('avv'), b0['avv'] + 2 * qt + 2)
                    c.inc(gp.dma_start(out=Zs[h:h + 1, qt * 512:(qt + 1) * 512], in_=zrow[:]), 'avz', 16)
            blk_all.gpsimd(s_gp)

    def l0_attn(s, s0slab):
        with ExitStack() as ps:
            slabO = ps.enter_context(nc.sbuf_tensor(f"slabO_{s}", [128, NCH * T], BF16))
            Zs = ps.enter_context(nc.sbuf_tensor(f"b_zs_{s}", [16, T], F32))
            htil = ExitStack()
            Aslab = htil.enter_context(nc.sbuf_tensor(f"b_aslab_{s}", [128, NTC * T], BF16))
            kw = htil.enter_context(nc.sbuf_tensor(f"b_kw_{s}", [64, 256], BF16))
            h4t = htil.enter_context(nc.sbuf_tensor(f"b_h4t_{s}", [128, 2 * 276], BF16))
            stg = htil.enter_context(nc.sbuf_tensor(f"b_stg_{s}", [128, 3 * 2048], BF16))
            zrow = htil.enter_context(nc.sbuf_tensor(f"b_zrow_{s}", [1, 512], F32))
            zrowb = htil.enter_context(nc.sbuf_tensor(f"b_zrowb_{s}", [1, 512], BF16))
            zsnb = htil.enter_context(nc.sbuf_tensor(f"b_zsnb_{s}", [1, 512], BF16))
            bufs = []
            for par in range(2):
                qkt = htil.enter_context(nc.sbuf_tensor(f"b_qkt{par}_{s}", [128, T], BF16))
                QTh, KTh = qkt[0:64, :], qkt[64:128, :]
                Vah = htil.enter_context(nc.sbuf_tensor(f"b_vah{par}_{s}", [128, NTC * 65], BF16))
                pslab = htil.enter_context(nc.sbuf_tensor(f"b_pslab{par}_{s}", [41, 2208], BF16))
                cstr = htil.enter_context(nc.sbuf_tensor(f"b_cstr{par}_{s}", [128, 16 * 168], BF16))
                ptil = htil.enter_context(nc.sbuf_tensor(f"b_ptil{par}_{s}", [128, 672], BF16))
                bufs.append((QTh, KTh, Vah, pslab, cstr, ptil))
            pstk = ExitStack()
            pst = {
                'pscA': pstk.enter_context(nc.psum_tensor(f"b_pscA_{s}", [128, 1024], F32)),
                'pscB': pstk.enter_context(nc.psum_tensor(f"b_pscB_{s}", [128, 1024], F32)),
                'psPh': pstk.enter_context(nc.psum_tensor(f"b_psPh_{s}", [41, 512], F32)),
                'ptp': pstk.enter_context(nc.psum_tensor(f"b_ptp_{s}", [128, 672], BF16)),
                'ptp2': pstk.enter_context(nc.psum_tensor(f"b_ptp2_{s}", [128, 384], BF16)),
                'pZ3': pstk.enter_context(nc.psum_tensor(f"b_pZ3_{s}", [1, 256], F32)),
            }
            hstk = ExitStack()
            blk_heads = hstk.enter_context(nc.Block())
            with nc.named_scope("l0head"):
                l0_r1(s, 0, bufs, pst, blk_heads)
                l0_r1b(s, 0, bufs, pst, blk_heads)
                for h in range(H0):
                    if h + 1 < H0:
                        l0_r1(s, h + 1, bufs, pst, blk_heads)
                        mid = (lambda hh: (lambda: l0_r1b(s, hh, bufs, pst, blk_heads)))(h + 1)
                    else:
                        mid = None
                    l0_r234(s, h, bufs, pst, slabO, Zs, Aslab, kw, h4t, stg,
                            zrow, zrowb, zsnb, blk_heads, mid_hook=mid)
            hstk.close()
            pstk.close()
            c.dma_fence('avz')
            htil.close()

            if DBG:
                with nc.Block() as blk:
                    def s_gp(gp):
                        for ch in range(NCH):
                            c.inc(gp.dma_start(out=dbg['oslab'].ap()[:, ch * T:(ch + 1) * T],
                                               in_=slabO[:, ch * T:(ch + 1) * T]), 'dbg', 16)
                        c.inc(gp.dma_start(out=dbg['zs'].ap()[:], in_=Zs[:]), 'dbg', 16)
                    blk.gpsimd(s_gp)

            with nc.Block() as blk:
                def s_vector(vector):
                    if DBG:
                        c.wait(vector, 'dbg')
                    c.inc(vector.reciprocal(Zs[:], Zs[:]), 'nzv', 1)
                blk.vector(s_vector)

            with ExitStack() as pnr:
                prz = [pnr.enter_context(nc.psum_tensor(f"b_prz{i}_{s}", [128, T], F32)) for i in range(2)]
                with nc.Block() as blk:
                    b0 = c.base('nzt', 'nzv')

                    def s_tensor(tensor):
                        c.wait(tensor, 'nzv')
                        for vch in range(NCH):
                            if vch >= 2:
                                tensor.wait_ge(c.S('nzv'), b0['nzv'] + vch - 1)
                            for qt in range(NQT):
                                ins = tensor.matmul(prz[vch % 2][:, qt * 512:(qt + 1) * 512],
                                                    ehd[0:16, vch * 128:(vch + 1) * 128],
                                                    Zs[0:16, qt * 512:(qt + 1) * 512],
                                                    start=True, stop=True)
                            c.inc(ins, 'nzt', 1)
                    blk.tensor(s_tensor)

                    def s_vector(vector):
                        for vch in range(NCH):
                            vector.wait_ge(c.S('nzt'), b0['nzt'] + vch + 1)
                            vector.drain()
                            sl = slabO[:, vch * T:(vch + 1) * T]
                            c.inc(vector.tensor_tensor(sl, sl, prz[vch % 2][:], ALU.mult), 'nzv', 1)
                    blk.vector(s_vector)

            wo_b = ps.enter_context(nc.sbuf_tensor(f"b_wo_{s}", [128, NCH * D], BF16))
            f32a = ps.enter_context(nc.sbuf_tensor(f"b_wof_{s}", [128, 1024], F32))
            f32b = ps.enter_context(nc.sbuf_tensor(f"b_wof2_{s}", [128, 1024], F32))
            pP = ps.enter_context(nc.psum_tensor(f"b_pP_{s}", [128, T], F32))
            with nc.Block() as blk:
                b0 = c.base('wol0', 'wol1', 'woc')

                def s_sync(sync):
                    for n in range(NCH):
                        if n >= 2:
                            sync.wait_ge(c.S('woc'), b0['woc'] + n - 1)
                        buf = f32a if n % 2 == 0 else f32b
                        c.inc(sync.dma_start(out=buf[:], in_=wT["woT"].ap()[n * 128:(n + 1) * 128, :]),
                              'wol%d' % (n % 2), 16)
                blk.sync(s_sync)

                def s_vector(vector):
                    for n in range(NCH):
                        vector.wait_ge(c.S('wol%d' % (n % 2)), b0['wol%d' % (n % 2)] + 16 * (n // 2 + 1))
                        buf = f32a if n % 2 == 0 else f32b
                        vector.drain()
                        c.inc(vector.tensor_copy(wo_b[:, n * D:(n + 1) * D], buf[:]), 'woc', 1)
                blk.vector(s_vector)

            with nc.Block() as blk:
                b0 = c.base('opt', 'ops')

                def s_tensor(tensor):
                    c.wait(tensor, 'woc')
                    c.wait(tensor, 'nzv')
                    for och in range(NCH):
                        if och >= 1:
                            tensor.wait_ge(c.S('ops'), b0['ops'] + och)
                        for vch in range(NCH):
                            wsl = wo_b[:, vch * D + och * 128:vch * D + (och + 1) * 128]
                            for qt in range(NQT):
                                ins = tensor.matmul(pP[:, qt * 512:(qt + 1) * 512], wsl,
                                                    slabO[:, vch * T + qt * 512:vch * T + (qt + 1) * 512],
                                                    start=(vch == 0), stop=(vch == NCH - 1))
                        c.inc(ins, 'opt', 1)
                blk.tensor(s_tensor)

                def s_scalar(scalar):
                    for och in range(NCH):
                        scalar.wait_ge(c.S('opt'), b0['opt'] + och + 1)
                        c.inc(scalar.activation(s0slab[:, och * T:(och + 1) * T], pP[:], AT.Identity,
                                                bias=bias_sb[:, 3 * NCH + och:3 * NCH + och + 1]), 'ops', 1)
                blk.scalar(s_scalar)

            if DBG:
                with nc.Block() as blk:
                    def s_gp(gp):
                        c.wait(gp, 'ops')
                        for ch in range(NCH):
                            c.inc(gp.dma_start(out=dbg['s0'].ap()[s, ch], in_=s0slab[:, ch * T:(ch + 1) * T]), 'dbg', 16)
                    blk.gpsimd(s_gp)
            c.dma_fence('dbg', 'nzb', 'nzc')


    def p1d(s, s0slab):
        with ExitStack() as ps:
            rhsq = ps.enter_context(nc.sbuf_tensor(f"d_rhsq_{s}", [128, NCH * 1024], BF16))
            xq0 = ps.enter_context(nc.sbuf_tensor(f"d_xq0_{s}", [128, NCH * T], BF16))
            xc = [ps.enter_context(nc.sbuf_tensor(f"d_xc{i}_{s}", [128, T], F32)) for i in range(2)]
            xnw = ps.enter_context(nc.sbuf_tensor(f"d_xnw_{s}", [128, T], F32))
            gw = ps.enter_context(nc.sbuf_tensor(f"d_gw_{s}", [128, T], F32))
            bca = ps.enter_context(nc.sbuf_tensor(f"d_bca_{s}", [128, T], F32))
            bcb = ps.enter_context(nc.sbuf_tensor(f"d_bcb_{s}", [128, T], F32))
            lsb = ps.enter_context(nc.sbuf_tensor(f"d_lsb_{s}", [128, T], F32))
            hop = ps.enter_context(nc.sbuf_tensor(f"d_hop_{s}", [1, T], F32))
            rscol = ps.enter_context(nc.sbuf_tensor(f"d_rscol_{s}", [128, NCH], F32))
            q_ti = ps.enter_context(nc.sbuf_tensor(f"d_ti_{s}", [128, T], I32))
            q_f = ps.enter_context(nc.sbuf_tensor(f"d_f_{s}", [128, T], F32))
            q_d = ps.enter_context(nc.sbuf_tensor(f"d_d_{s}", [128, T], F32))
            q_sg = ps.enter_context(nc.sbuf_tensor(f"d_sg_{s}", [128, T], F32))
            amx = ps.enter_context(nc.sbuf_tensor(f"d_amx_{s}", [128, T], F32))
            h_row = h_gp

            with nc.Block() as blk:
                def s_gp(gp):
                    c.wait(gp, 'qst')
                    for ch in range(NCH):
                        c.inc(gp.dma_start(out=rhsq[:, ch * 1024:(ch + 1) * 1024], in_=h_rhsq.ap()[ch]), 'dld', 16)
                    c.inc(gp.dma_start(out=h_rowf.ap()[0:1, 0:1024], in_=grows[0:1, :]), 'dld', 16)
                    c.wait(gp, 'dld')
                    c.inc(gp.dma_start(out=rscol[:, :],
                                       in_=AP(h_rowf.ap().tensor, 0, [[1, 128], [128, NCH], [1, 1]]),
                                       allow_slow_non_contiguous=True), 'dld', 16)
                    c.inc(gp.dma_start(out=hop[:], in_=rows[8 * s + R_A0:8 * s + R_A0 + 1, :]), 'dh', 16)
                    c.wait(gp, 'dh')
                    bcast_row(gp, c, bca, hop[0:1, :], T, 'dbc')
                    c.wait(gp, 'dbc')
                    c.inc(gp.dma_start(out=hop[:], in_=rows[8 * s + R_B0:8 * s + R_B0 + 1, :]), 'dh', 16)
                    c.wait(gp, 'dh')
                    bcast_row(gp, c, bcb, hop[0:1, :], T, 'dbc')
                    c.wait(gp, 'dbc')
                    c.inc(gp.dma_start(out=hop[:], in_=rows[8 * s + R_LS0:8 * s + R_LS0 + 1, :]), 'dh', 16)
                    c.wait(gp, 'dh')
                blk.gpsimd(s_gp)

                def s_vector(vector):
                    c.wait(vector, 'dh')
                    c.inc(vector.reciprocal(hop[:], hop[:]), 'dv', 1)
                blk.vector(s_vector)

            with nc.Block() as blk:
                def s_gp(gp):
                    c.wait(gp, 'dv')
                    bcast_row(gp, c, lsb, hop[0:1, :], T, 'dbc')
                blk.gpsimd(s_gp)

            with nc.Block() as blk:
                b0 = c.base('dxl0', 'dxl1', 'dxv', 'dxs')

                def s_sync(sync):
                    c.wait(sync, 'dbc')
                    for ch in range(NCH):
                        if ch >= 2:
                            sync.wait_ge(c.S('dxv'), b0['dxv'] + 2 * ch - 3)
                        c.inc(sync.dma_start(out=xc[ch % 2][:], in_=x_in.ap()[s, ch * 128:(ch + 1) * 128, :]),
                              'dxl%d' % (ch % 2), 16)
                blk.sync(s_sync)

                def s_vector(vector):
                    c.wait(vector, 'dbc')
                    for ch in range(NCH):
                        vector.wait_ge(c.S('dxl%d' % (ch % 2)), b0['dxl%d' % (ch % 2)] + 16 * (ch // 2 + 1))
                        vector.drain()
                        vector.tensor_tensor(xnw[:], xc[ch % 2][:], bca[:], ALU.mult)
                        vector.drain()
                        vector.tensor_tensor(xnw[:], xnw[:], bcb[:], ALU.add)
                        vector.drain()
                        vector.tensor_tensor(q_d[:], xnw[:], lsb[:], ALU.mult)
                        vector.drain()
                        vector.tensor_copy(q_ti[:], q_d[:])
                        vector.drain()
                        vector.tensor_copy(q_f[:], q_ti[:])
                        vector.drain()
                        vector.tensor_tensor(q_d[:], q_d[:], q_f[:], ALU.subtract)
                        vector.drain()
                        vector.tensor_scalar(q_sg[:], q_d[:], 0.5, -0.5, ALU.min, ALU.max)
                        vector.drain()
                        c.inc(vector.tensor_tensor(q_d[:], q_d[:], q_sg[:], ALU.subtract), 'dxv', 1)
                        vector.wait_ge(c.S('dxs'), b0['dxs'] + ch + 1)
                        vector.drain()
                        c.inc(vector.tensor_tensor(xq0[:, ch * T:(ch + 1) * T], q_f[:], q_sg[:], ALU.add), 'dxv', 1)
                blk.vector(s_vector)

                def s_scalar(scalar):
                    for ch in range(NCH):
                        scalar.wait_ge(c.S('dxv'), b0['dxv'] + 2 * ch + 1)
                        c.inc(scalar.activation(q_sg[:], q_d[:], AT.Sign), 'dxs', 1)
                blk.scalar(s_scalar)

            with nc.Block() as blk:
                def s_gp(gp):
                    c.wait(gp, 'dxv')
                    c.inc(gp.dma_start(out=hop[:], in_=rows[8 * s + R_LS0:8 * s + R_LS0 + 1, :]), 'dh', 16)
                    c.wait(gp, 'dh')
                    bcast_row(gp, c, bca, hop[0:1, :], T, 'dbc3')
                blk.gpsimd(s_gp)

            with ExitStack() as psr:
                pF = psr.enter_context(nc.psum_tensor(f"d_pF_{s}", [128, T], F32))
                with nc.Block() as blk:
                    b0 = c.base('dft', 'dfv', 'dfs', 'dfx', 'dx1', 'dxr0', 'dxr1', 'dbgg')

                    def s_tensor(tensor):
                        c.wait(tensor, 'dxv')
                        for och in range(NCH):
                            if och >= 1:
                                tensor.wait_ge(c.S('dfv'), b0['dfv'] + och)
                            for ich in range(NCH):
                                wsl = rhsq[:, ich * 1024 + och * 128:ich * 1024 + (och + 1) * 128]
                                for qt in range(NQT):
                                    ins = tensor.matmul(pF[:, qt * 512:(qt + 1) * 512], wsl,
                                                        xq0[:, ich * T + qt * 512:ich * T + (qt + 1) * 512],
                                                        start=(ich == 0), stop=(ich == NCH - 1))
                            c.inc(ins, 'dft', 1)
                    blk.tensor(s_tensor)

                    def s_sync(sync):
                        c.wait(sync, 'dxv')
                        for och in range(NCH):
                            if och >= 2:
                                sync.wait_ge(c.S('dfx'), b0['dfx'] + och - 1)
                                sync.wait_ge(c.S('dx1'), b0['dx1'] + 16 * (och - 1))
                            c.inc(sync.dma_start(out=xc[och % 2][:], in_=x_in.ap()[s, och * 128:(och + 1) * 128, :]),
                                  'dxr%d' % (och % 2), 16)
                    blk.sync(s_sync)

                    def s_vector(vector):
                        c.wait(vector, 'dbc3')
                        for och in range(NCH):
                            vector.wait_ge(c.S('dft'), b0['dft'] + och + 1)
                            if DBG and och >= 1:
                                vector.wait_ge(c.S('dbgg'), b0['dbgg'] + 16 * och)
                            vector.drain()
                            c.inc(vector.tensor_tensor(gw[:], pF[:], bca[:], ALU.mult), 'dfv', 1)
                            vector.wait_ge(c.S('dfs'), b0['dfs'] + och + 1)
                            vector.wait_ge(c.S('dxr%d' % (och % 2)), b0['dxr%d' % (och % 2)] + 16 * (och // 2 + 1))
                            vector.drain()
                            vector.tensor_tensor(xnw[:], s0slab[:, och * T:(och + 1) * T], gw[:], ALU.mult)
                            vector.drain()
                            c.inc(vector.tensor_tensor(xc[och % 2][:], xc[och % 2][:], xnw[:], ALU.add), 'dfx', 1)
                    blk.vector(s_vector)

                    def s_scalar(scalar):
                        c.wait(scalar, 'dld')
                        for och in range(NCH):
                            scalar.wait_ge(c.S('dfv'), b0['dfv'] + och + 1)
                            c.inc(scalar.activation(gw[:], gw[:], AT.Silu,
                                                    scale=rscol[:, och:och + 1]), 'dfs', 1)
                    blk.scalar(s_scalar)

                    def s_gp(gp):
                        for och in range(NCH):
                            gp.wait_ge(c.S('dfx'), b0['dfx'] + och + 1)
                            if och >= 1:
                                gp.wait_ge(c.S('dx1'), b0['dx1'] + 16 * och)
                                if DBG:
                                    gp.wait_ge(c.S('dbgg'), b0['dbgg'] + 16 * och)
                            c.inc(gp.dma_start(out=h_x1.ap()[s, och], in_=xc[och % 2][:]), 'dx1', 16)
                            if DBG:
                                c.inc(gp.dma_start(out=dbg['g0'].ap()[s, och], in_=gw[:]), 'dbgg', 16)
                    blk.gpsimd(s_gp)
            c.dma_fence('dx1', 'dbgg')

            with ExitStack() as psr:
                pstat = psr.enter_context(nc.psum_tensor(f"d_pstat_{s}", [1, T], F32))
                pstatq = psr.enter_context(nc.psum_tensor(f"d_pstatq_{s}", [1, T], F32))
                with nc.Block() as blk:
                    b0 = c.base('dll0', 'dll1', 'dlq', 'dlu')

                    def s_sync(sync):
                        for ch in range(NCH):
                            if ch >= 2:
                                sync.wait_ge(c.S('dlu'), b0['dlu'] + ch - 1)
                            c.inc(sync.dma_start(out=xc[ch % 2][:], in_=h_x1.ap()[s, ch]),
                                  'dll%d' % (ch % 2), 16)
                    blk.sync(s_sync)

                    def s_scalar(scalar):
                        for ch in range(NCH):
                            scalar.wait_ge(c.S('dll%d' % (ch % 2)), b0['dll%d' % (ch % 2)] + 16 * (ch // 2 + 1))
                            if ch >= 1:
                                scalar.wait_ge(c.S('dlu'), b0['dlu'] + ch)
                            c.inc(scalar.activation(q_d[:], xc[ch % 2][:], AT.Square), 'dlq', 1)
                    blk.scalar(s_scalar)

                    def s_tensor(tensor):
                        for ch in range(NCH):
                            tensor.wait_ge(c.S('dlq'), b0['dlq'] + ch + 1)
                            for qt in range(NQT):
                                tensor.matmul(pstat[0:1, qt * 512:(qt + 1) * 512], onescol_f[:],
                                              xc[ch % 2][:, qt * 512:(qt + 1) * 512],
                                              start=(ch == 0), stop=(ch == NCH - 1))
                                ins = tensor.matmul(pstatq[0:1, qt * 512:(qt + 1) * 512], onescol_f[:],
                                                    q_d[:, qt * 512:(qt + 1) * 512],
                                                    start=(ch == 0), stop=(ch == NCH - 1))
                            c.inc(ins, 'dlu', 1)
                    blk.tensor(s_tensor)

                rw0, rw1, rw2, rw3 = bca, bcb, gw, xnw
                with nc.Block() as blk:
                    def s_vector(vector):
                        vector.tensor_scalar_mul(rw0[0:1, 0:T], pstat[0:1, :], 1.0 / D)
                        vector.tensor_scalar_mul(rw1[0:1, 0:T], pstatq[0:1, :], 1.0 / D)
                        vector.drain()
                        vector.tensor_tensor(rw2[0:1, 0:T], rw0[0:1, 0:T], rw0[0:1, 0:T], ALU.mult)
                        vector.drain()
                        c.inc(vector.tensor_tensor(rw1[0:1, 0:T], rw1[0:1, 0:T], rw2[0:1, 0:T], ALU.subtract), 'dlr', 1)
                    blk.vector(s_vector)

                    def s_scalar(scalar):
                        c.wait(scalar, 'dlr')
                        c.inc(scalar.activation(rw1[0:1, 0:T], rw1[0:1, 0:T], AT.Sqrt, bias=cvals[0:1, 0:1]), 'dls', 1)
                    blk.scalar(s_scalar)

                with nc.Block() as blk:
                    def s_vector(vector):
                        c.wait(vector, 'dls')
                        vector.reciprocal(rw2[0:1, 0:T], rw1[0:1, 0:T])
                        vector.drain()
                        vector.tensor_tensor(rw3[0:1, 0:T], rw0[0:1, 0:T], rw2[0:1, 0:T], ALU.mult)
                        vector.drain()
                        c.inc(vector.tensor_scalar_mul(rw3[0:1, 0:T], rw3[0:1, 0:T], -1.0), 'dlr', 1)
                    blk.vector(s_vector)

                    def s_gp(gp):
                        c.wait(gp, 'dlr')
                        c.inc(gp.dma_start(out=rows[8 * s + R_A1:8 * s + R_A1 + 1, :], in_=rw2[0:1, 0:T]), 'dpk', 16)
                        c.inc(gp.dma_start(out=rows[8 * s + R_B1:8 * s + R_B1 + 1, :], in_=rw3[0:1, 0:T]), 'dpk', 16)
                        c.wait(gp, 'dpk')
                        bcast_row(gp, c, bca, rw2[0:1, 0:T], T, 'dbc2')
                        bcast_row(gp, c, bcb, rw3[0:1, 0:T], T, 'dbc2')
                    blk.gpsimd(s_gp)

            with nc.Block() as blk:
                b0 = c.base('dal0', 'dal1', 'dav', 'dast', 'dlq')

                def s_sync(sync):
                    c.wait(sync, 'dbc2')
                    for ch in range(NCH):
                        if ch >= 2:
                            sync.wait_ge(c.S('dav'), b0['dav'] + 2 * (ch - 1))
                        c.inc(sync.dma_start(out=xc[ch % 2][:], in_=h_x1.ap()[s, ch]),
                              'dal%d' % (ch % 2), 16)
                blk.sync(s_sync)

                def s_vector(vector):
                    c.wait(vector, 'dbc2')
                    for ch in range(NCH):
                        vector.wait_ge(c.S('dal%d' % (ch % 2)), b0['dal%d' % (ch % 2)] + 16 * (ch // 2 + 1))
                        if ch >= 2:
                            vector.wait_ge(c.S('dast'), b0['dast'] + 16 * (ch - 1))
                        vector.drain()
                        vector.tensor_tensor(q_d[:], xc[ch % 2][:], bca[:], ALU.mult)
                        vector.drain()
                        vector.tensor_tensor(q_d[:], q_d[:], bcb[:], ALU.add)
                        vector.drain()
                        c.inc(vector.tensor_copy(xq0[:, ch * T:(ch + 1) * T], q_d[:]), 'dav', 1)
                        vector.wait_ge(c.S('dlq'), b0['dlq'] + ch + 1)
                        vector.drain()
                        if ch == 0:
                            ins = vector.tensor_copy(amx[:], gw[:])
                        else:
                            ins = vector.tensor_tensor(amx[:], amx[:], gw[:], ALU.max)
                        c.inc(ins, 'dav', 1)
                blk.vector(s_vector)

                def s_scalar(scalar):
                    for ch in range(NCH):
                        scalar.wait_ge(c.S('dav'), b0['dav'] + 2 * ch + 1)
                        c.inc(scalar.activation(gw[:], q_d[:], AT.Abs), 'dlq', 1)
                blk.scalar(s_scalar)

                def s_gp(gp):
                    for ch in range(NCH):
                        gp.wait_ge(c.S('dav'), b0['dav'] + 2 * ch + 1)
                        if ch >= 1:
                            gp.wait_ge(c.S('dast'), b0['dast'] + 16 * ch)
                        c.inc(gp.dma_start(out=h_xn1.ap()[s, ch], in_=xq0[:, ch * T:(ch + 1) * T]), 'dast', 16)
                blk.gpsimd(s_gp)

            with nc.Block() as blk:
                def s_vector(vector):
                    c.wait(vector, 'dav')
                    part_tree(c, vector, amx, q_d, T)
                    vector.tensor_scalar(amx[0:1, 0:T], amx[0:1, :], 1e-5, None, ALU.max)
                    vector.drain()
                    vector.reciprocal(amx[0:1, 0:T], amx[0:1, 0:T])
                    vector.drain()
                    c.inc(vector.tensor_scalar_mul(amx[0:1, 0:T], amx[0:1, 0:T], 127.0), 'dlsv', 1)
                blk.vector(s_vector)

                def s_gp(gp):
                    c.wait(gp, 'dlsv')
                    c.inc(gp.dma_start(out=rows[8 * s + R_LS1:8 * s + R_LS1 + 1, :], in_=amx[0:1, 0:T]), 'dpk', 16)
                blk.gpsimd(s_gp)
            c.dma_fence('dast', 'dpk')


    def p2_l1attn(s):
        with ExitStack() as ps:
            uslab = ps.enter_context(nc.sbuf_tensor(f"e_u_{s}", [128, NTC * 512], BF16))
            VaL1 = ps.enter_context(nc.sbuf_tensor(f"e_va_{s}", [128, H1 * NTC * 65], BF16))
            oslab1 = ps.enter_context(nc.sbuf_tensor(f"e_os_{s}", [128, NCH * T], BF16))
            Zs2 = ps.enter_context(nc.sbuf_tensor(f"e_zs_{s}", [16, T], F32))
            zsc2 = ps.enter_context(nc.sbuf_tensor(f"e_zsc_{s}", [1, T], F32))

            pp_scope = nc.enter_named_scope("p2proj", False)
            with ExitStack() as pp:
                xn1b = pp.enter_context(nc.sbuf_tensor(f"e_xn_{s}", [128, NCH * T], BF16))
                w2qk = pp.enter_context(nc.sbuf_tensor(f"e_wqk_{s}", [128, NCH * 512], BF16))
                w2v = pp.enter_context(nc.sbuf_tensor(f"e_wv_{s}", [128, NCH * 1024], BF16))
                uw = pp.enter_context(nc.sbuf_tensor(f"e_uw_{s}", [128, 512], F32))
                sqw = pp.enter_context(nc.sbuf_tensor(f"e_sqw_{s}", [128, 512], F32))
                m32 = pp.enter_context(nc.sbuf_tensor(f"e_m32_{s}", [128, 32], F32))
                v32 = pp.enter_context(nc.sbuf_tensor(f"e_v32_{s}", [128, 32], F32))
                r32 = pp.enter_context(nc.sbuf_tensor(f"e_r32_{s}", [128, 32], F32))
                pQ = [pp.enter_context(nc.psum_tensor(f"e_pq{i}_{s}", [128, 1536], F32)) for i in range(2)]
                load_weights_bf16(pp, f"l1w_{s}", [("wq2T", w2qk, NCH, 256, 512, 0),
                                                   ("wk2T", w2qk, NCH, 256, 512, 256),
                                                   ("wv2T", w2v, NCH)])

                with nc.Block() as blk:
                    def s_sync(sync):
                        for ch in range(NCH):
                            c.inc(sync.dma_start(out=xn1b[:, ch * T:(ch + 1) * T], in_=h_xn1.ap()[s, ch]), 'rld', 16)
                    blk.sync(s_sync)

                    def s_vector(vector):
                        ones_ap = AP(VaL1, 64, [[H1 * NTC * 65, 128], [NTC * 65, H1], [65, NTC]])
                        c.inc(vector.memset(ones_ap, 1.0), 'qv', 1)
                    blk.vector(s_vector)

                with nc.Block() as blk:
                    b0 = c.base('pjt', 'pjv', 'lnx', 'lns')

                    def s_tensor(tensor):
                        c.wait(tensor, 'rld')
                        for tc in range(NTC):
                            p = pQ[tc % 2]
                            if tc >= 2:
                                tensor.wait_ge(c.S('pjv'), b0['pjv'] + tc - 1)
                            for i in range(NCH):
                                xsl = xn1b[:, i * T + tc * 128:i * T + (tc + 1) * 128]
                                tensor.matmul(p[:, 0:512], xsl, w2qk[:, i * 512:(i + 1) * 512],
                                              start=(i == 0), stop=(i == NCH - 1))
                                tensor.matmul(p[:, 512:1024], xsl, w2v[:, i * 1024:i * 1024 + 512],
                                              start=(i == 0), stop=(i == NCH - 1))
                                ins = tensor.matmul(p[:, 1024:1536], xsl, w2v[:, i * 1024 + 512:(i + 1) * 1024],
                                                    start=(i == 0), stop=(i == NCH - 1))
                            c.inc(ins, 'pjt', 1)
                    blk.tensor(s_tensor)

                    def s_vector(vector):
                        for tc in range(NTC):
                            p = pQ[tc % 2]
                            vector.wait_ge(c.S('pjt'), b0['pjt'] + tc + 1)
                            vector.drain()
                            vector.tensor_reduce(m32[:], p[:, 0:512].rearrange("p (a b) -> p a b", b=16),
                                                 AX.X, ALU.add)
                            vector.drain()
                            vector.tensor_scalar_mul(m32[:], m32[:], 1.0 / 16.0)
                            vector.drain()
                            c.inc(vector.tensor_tensor(uw[:].rearrange("p (a b) -> p a b", b=16),
                                                       p[:, 0:512].rearrange("p (a b) -> p a b", b=16),
                                                       AP(m32, 0, [[32, 128], [1, 32], [0, 16]]),
                                                       ALU.subtract), 'lnx', 1)
                            vector.wait_ge(c.S('lns'), b0['lns'] + 2 * tc + 1)
                            vector.drain()
                            vector.tensor_reduce(v32[:], sqw[:].rearrange("p (a b) -> p a b", b=16),
                                                 AX.X, ALU.add)
                            vector.drain()
                            c.inc(vector.tensor_scalar_mul(v32[:], v32[:], 1.0 / 16.0), 'lnx', 1)
                            vector.wait_ge(c.S('lns'), b0['lns'] + 2 * tc + 2)
                            vector.drain()
                            vector.reciprocal(r32[:], r32[:])
                            vector.drain()
                            vector.tensor_tensor(uw[:].rearrange("p (a b) -> p a b", b=16),
                                                 uw[:].rearrange("p (a b) -> p a b", b=16),
                                                 AP(r32, 0, [[32, 128], [1, 32], [0, 16]]), ALU.mult)
                            vector.drain()
                            vector.tensor_tensor(uw[:].rearrange("p (a b) -> p a b", b=16),
                                                 uw[:].rearrange("p (a b) -> p a b", b=16),
                                                 AP(gbig, 0, [[32, 128], [0, 32], [1, 16]]), ALU.mult)
                            vector.drain()
                            vector.tensor_tensor(uw[:].rearrange("p (a b) -> p a b", b=16),
                                                 uw[:].rearrange("p (a b) -> p a b", b=16),
                                                 AP(gbig, 16, [[32, 128], [0, 32], [1, 16]]), ALU.add)
                            vector.drain()
                            vector.tensor_copy(uslab[:, tc * 512:(tc + 1) * 512], uw[:])
                            vector.drain()
                            c.inc(vector.tensor_copy(
                                AP(VaL1, tc * 65, [[H1 * NTC * 65, 128], [NTC * 65, H1], [1, 64]]),
                                p[:, 512:1536].rearrange("p (a b) -> p a b", b=64)), 'pjv', 1)
                    blk.vector(s_vector)

                    def s_scalar(scalar):
                        for tc in range(NTC):
                            scalar.wait_ge(c.S('lnx'), b0['lnx'] + 2 * tc + 1)
                            c.inc(scalar.activation(sqw[:], uw[:], AT.Square), 'lns', 1)
                            scalar.wait_ge(c.S('lnx'), b0['lnx'] + 2 * tc + 2)
                            c.inc(scalar.activation(r32[:], v32[:], AT.Sqrt, bias=cvals[:, 0:1]), 'lns', 1)
                    blk.scalar(s_scalar)

            nc.leave_named_scope("p2proj", pp_scope[0], False)
            if DBG:
                with nc.Block() as blk:
                    def s_gp(gp):
                        c.inc(gp.dma_start(out=dbg['ufm'].ap()[s], in_=uslab[:]), 'dbg', 16)
                        for hh in range(H1):
                            c.inc(gp.dma_start(out=dbg['val1'].ap()[s][:, hh * 1040:(hh + 1) * 1040],
                                               in_=VaL1[:, hh * 1040:(hh + 1) * 1040]), 'dbg', 16)
                        c.wait(gp, 'dbg')
                    blk.gpsimd(s_gp)

            sg_scope = nc.enter_named_scope("p2scan", False)
            with ExitStack() as sg:
                qfg = sg.enter_context(nc.sbuf_tensor(f"e_qfg_{s}", [128, NTC * 512], BF16))
                kfg = sg.enter_context(nc.sbuf_tensor(f"e_kfg_{s}", [128, NTC * 512], BF16))
                uqTg = sg.enter_context(nc.sbuf_tensor(f"e_uqt_{s}", [16, 2 * T], BF16))
                ukTg = sg.enter_context(nc.sbuf_tensor(f"e_ukt_{s}", [16, 2 * T], BF16))
                qfT = sg.enter_context(nc.sbuf_tensor(f"e_qft_{s}", [128, 4 * T], BF16))
                attm2 = [sg.enter_context(nc.sbuf_tensor(f"e_attm{j}_{s}", [128, NTC * 128], BF16)) for j in range(2)]
                attsq = sg.enter_context(nc.sbuf_tensor(f"e_attq_{s}", [128, 1024], BF16))
                stsnap = sg.enter_context(nc.sbuf_tensor(f"e_stsn_{s}", [128, 520], BF16))
                stf32 = sg.enter_context(nc.sbuf_tensor(f"e_stf_{s}", [128, 260], F32))
                zsc3 = sg.enter_context(nc.sbuf_tensor(f"e_zsc3_{s}", [1, T], F32))

                for g in range(8):
                    with ExitStack() as pg:
                        ptu = pg.enter_context(nc.psum_tensor(f"e_ptu_{s}_{g}", [16, 2 * T], BF16))
                        with nc.Block() as blk:
                            b0 = c.base('lnu', 'lnd')

                            def s_vector(vector, g=g):
                                for tc in range(NTC):
                                    for r in range(2):
                                        h = 2 * g + r
                                        for (dst, off) in ((qfg, tc * 512 + h * 16), (kfg, tc * 512 + 256 + h * 16)):
                                            vector.tensor_tensor(
                                                dst[:, tc * 512 + r * 256:tc * 512 + (r + 1) * 256]
                                                .rearrange("p (a b) -> p a b", b=16),
                                                AP(uslab, off, [[NTC * 512, 128], [1, 16], [0, 16]]),
                                                AP(uslab, off, [[NTC * 512, 128], [0, 16], [1, 16]]),
                                                ALU.mult)
                                c.inc(vector.drain(), 'lnr', 1)
                                for n in range(4):
                                    r, tn = n % 2, n // 2
                                    dstT = uqTg if tn == 0 else ukTg
                                    vector.wait_ge(c.S('lnu'), b0['lnu'] + n + 1)
                                    vector.drain()
                                    c.inc(vector.tensor_copy(dstT[0:16, r * T:(r + 1) * T],
                                                             ptu[0:16, (n % 2) * T:(n % 2) * T + T]), 'lnd', 1)
                            blk.vector(s_vector)

                            def s_tensor(tensor, g=g):
                                c.wait(tensor, 'lnr')
                                for n in range(4):
                                    r, tn = n % 2, n // 2
                                    h = 2 * g + r
                                    if n >= 2:
                                        tensor.wait_ge(c.S('lnd'), b0['lnd'] + n - 1)
                                    for tc in range(NTC):
                                        ins = tensor.transpose(
                                            ptu[0:16, (n % 2) * T + tc * 128:(n % 2) * T + (tc + 1) * 128],
                                            uslab[:, tc * 512 + tn * 256 + h * 16:tc * 512 + tn * 256 + h * 16 + 16],
                                            ident[:])
                                    c.inc(ins, 'lnu', 1)
                            blk.tensor(s_tensor)

                    bstk2 = ExitStack()
                    blk2 = bstk2.enter_context(nc.Block())
                    with ExitStack() as ph:
                        ptq = ph.enter_context(nc.psum_tensor(f"e_ptq_{s}_{g}", [128, 2 * T], BF16))
                        if True:
                            def s_tensor(tensor):
                                b0d = c.base('dav')
                                for r in range(2):
                                    if r == 1:
                                        tensor.wait_ge(c.S('dav'), b0d['dav'] + 1)
                                    for pc in range(2):
                                        for tc in range(NTC):
                                            ins = tensor.transpose(
                                                ptq[:, pc * T + tc * 128:pc * T + (tc + 1) * 128],
                                                qfg[:, tc * 512 + r * 256 + pc * 128:tc * 512 + r * 256 + pc * 128 + 128],
                                                ident[:])
                                    c.inc(ins, 'lnsq', 1)
                            blk2.tensor(s_tensor)

                            def s_vector(vector):
                                b0q = c.base('lnsq')
                                for r in range(2):
                                    vector.wait_ge(c.S('lnsq'), b0q['lnsq'] + r - 1)
                                    vector.drain()
                                    c.inc(vector.tensor_copy(qfT[:, r * 2 * T:(r + 1) * 2 * T], ptq[:]), 'dav', 1)
                            blk2.vector(s_vector)

                    with ExitStack() as ph:
                        pA = ph.enter_context(nc.psum_tensor(f"e_pa_{s}_{g}", [128, 1024], F32))
                        pOL = [ph.enter_context(nc.psum_tensor(f"e_pol{j}_{s}_{g}", [65, 1024], F32)) for j in range(2)]
                        pstate = [ph.enter_context(nc.psum_tensor(f"e_pst{j}_{s}_{g}", [128, 130], F32)) for j in range(2)]
                        if True:
                            b0 = c.base('dft', 'dfs', 'dfv', 'dxs', 'dxv', 'dlsv', 'avv', 'dpk')

                            def s_tensor(tensor, g=g):
                                c.wait(tensor, 'dav')
                                for k in range(4):
                                    r, w = k % 2, k // 2
                                    if k >= 1:
                                        tensor.wait_ge(c.S('dfs'), b0['dfs'] + k)
                                    for j8 in range(8):
                                        p = 8 * w + j8
                                        ins = tensor.matmul(
                                            pA[:, j8 * 128:(j8 + 1) * 128],
                                            ukTg[0:16, r * T + p * 128:r * T + (p + 1) * 128],
                                            uqTg[0:16, r * T + p * 128:r * T + (p + 1) * 128],
                                            start=True, stop=True)
                                    c.inc(ins, 'dft', 1)
                                for p in range(NTC):
                                    if p == 8:
                                        tensor.wait_ge(c.S('avv'), b0['avv'] + 2)
                                    for j in range(2):
                                        h = 2 * g + j
                                        tensor.wait_ge(c.S('dfv'), b0['dfv'] + 2 * (p // 8) + j + 1)
                                        if p > 0:
                                            tensor.wait_ge(c.S('dxv'), b0['dxv'] + 2 * (p - 1) + j + 1)
                                        va = VaL1[:, h * 1040 + p * 65:h * 1040 + (p + 1) * 65]
                                        po = pOL[j]
                                        cc = (p % 8) * 128
                                        tensor.matmul(po[:, cc:cc + 128], va,
                                                      attm2[j][:, p * 128:(p + 1) * 128],
                                                      start=True, stop=(p == 0))
                                        if p > 0:
                                            tensor.matmul(po[:, cc:cc + 128], stsnap[:, 260 * j:260 * j + 65],
                                                          qfT[:, j * 2 * T + p * 128:j * 2 * T + (p + 1) * 128],
                                                          start=False, stop=False)
                                            tensor.matmul(po[:, cc:cc + 128], stsnap[:, 260 * j + 65:260 * j + 130],
                                                          qfT[:, j * 2 * T + T + p * 128:j * 2 * T + T + (p + 1) * 128],
                                                          start=False, stop=True)
                                        tensor.matmul(pstate[j][:, 0:65],
                                                      kfg[:, p * 512 + j * 256:p * 512 + j * 256 + 128],
                                                      va, start=True, stop=True)
                                        ins = tensor.matmul(pstate[j][:, 65:130],
                                                            kfg[:, p * 512 + j * 256 + 128:p * 512 + j * 256 + 256],
                                                            va, start=True, stop=True)
                                        c.inc(ins, 'dxs', 1)
                            blk2.tensor(s_tensor)

                            def s_scalar(scalar):
                                for k in range(4):
                                    scalar.wait_ge(c.S('dft'), b0['dft'] + k + 1)
                                    if k >= 1:
                                        scalar.wait_ge(c.S('dfv'), b0['dfv'] + k)
                                    c.inc(scalar.activation(attsq[:], pA[:], AT.Square), 'dfs', 1)
                                for p in range(NTC):
                                    for j in range(2):
                                        scalar.wait_ge(c.S('dlsv'), b0['dlsv'] + 2 * p + j + 1)
                                        c.inc(scalar.activation(stsnap[:, 260 * j:260 * j + 130],
                                                                stf32[:, 130 * j:130 * j + 130], AT.Identity),
                                              'dxv', 1)
                            blk2.scalar(s_scalar)

                            def s_vector(vector, g=g):
                                vector.memset(stf32[:], 0.0)
                                vector.drain()
                                for k in range(4):
                                    r, w = k % 2, k // 2
                                    vector.wait_ge(c.S('dfs'), b0['dfs'] + k + 1)
                                    vector.drain()
                                    c.inc(vector.tensor_tensor(
                                        attm2[r][:, w * 1024:(w + 1) * 1024].rearrange("p (a b) -> p a b", b=128),
                                        attsq[:].rearrange("p (a b) -> p a b", b=128),
                                        AP(triu128p, 0, [[128, 128], [0, 8], [1, 128]]),
                                        ALU.mult), 'dfv', 1)
                                for p in range(NTC):
                                    for j in range(2):
                                        h = 2 * g + j
                                        vector.wait_ge(c.S('dxs'), b0['dxs'] + 2 * p + j + 1)
                                        if p >= 1:
                                            vector.wait_ge(c.S('dxv'), b0['dxv'] + 2 * (p - 1) + j + 1)
                                        vector.drain()
                                        c.inc(vector.tensor_tensor(stf32[:, 130 * j:130 * j + 130],
                                                                   stf32[:, 130 * j:130 * j + 130],
                                                                   pstate[j][:], ALU.add), 'dlsv', 1)
                                    if p == 7 or p == 15:
                                        hf = p // 8
                                        c.wait(vector, 'dpk', c.cnt['dpk'])
                                        for j in range(2):
                                            h = 2 * g + j
                                            zdst = zsc2 if j == 0 else zsc3
                                            vector.tensor_copy(
                                                oslab1[(h % 2) * 64:(h % 2) * 64 + 64,
                                                       (h // 2) * T + hf * 1024:(h // 2) * T + (hf + 1) * 1024],
                                                pOL[j][0:64, :])
                                            vector.drain()
                                            c.inc(vector.tensor_copy(zdst[0:1, hf * 1024:(hf + 1) * 1024],
                                                                     pOL[j][64:65, :]), 'avv', 1)
                            blk2.vector(s_vector)

                            def s_gp(gp, g=g):
                                c.wait(gp, 'avv')
                                c.inc(gp.dma_start(out=Zs2[2 * g:2 * g + 1, :], in_=zsc2[0:1, :]), 'dpk', 16)
                                gp.wait_ge(c.S('dpk'), c.cnt['dpk'])
                                c.inc(gp.dma_start(out=Zs2[2 * g + 1:2 * g + 2, :], in_=zsc3[0:1, :]), 'dpk', 16)
                            blk2.gpsimd(s_gp)
                    bstk2.close()

            nc.leave_named_scope("p2scan", sg_scope[0], False)
            with nc.Block() as blk:
                def s_vector(vector):
                    c.wait(vector, 'dpk')
                    vector.tensor_scalar(Zs2[:], Zs2[:], 1e-10, None, ALU.add)
                    vector.drain()
                    c.inc(vector.reciprocal(Zs2[:], Zs2[:]), 'nzv', 1)
                blk.vector(s_vector)

            if DBG:
                with nc.Block() as blk:
                    def s_gp(gp):
                        c.wait(gp, 'nzv')
                        c.inc(gp.dma_start(out=dbg['zs1'].ap()[s], in_=Zs2[:]), 'dbg', 16)
                        c.wait(gp, 'dbg')
                    blk.gpsimd(s_gp)

            with ExitStack() as pnr:
                prz = [pnr.enter_context(nc.psum_tensor(f"e_prz{i}_{s}", [128, T], F32)) for i in range(2)]
                with nc.Block() as blk:
                    b0 = c.base('nzt', 'nzv')

                    def s_tensor(tensor):
                        c.wait(tensor, 'nzv')
                        for vch in range(NCH):
                            if vch >= 2:
                                tensor.wait_ge(c.S('nzv'), b0['nzv'] + vch - 1)
                            for qt in range(NQT):
                                ins = tensor.matmul(prz[vch % 2][:, qt * 512:(qt + 1) * 512],
                                                    ehd[0:16, vch * 128:(vch + 1) * 128],
                                                    Zs2[0:16, qt * 512:(qt + 1) * 512],
                                                    start=True, stop=True)
                            c.inc(ins, 'nzt', 1)
                    blk.tensor(s_tensor)

                    def s_vector(vector):
                        for vch in range(NCH):
                            vector.wait_ge(c.S('nzt'), b0['nzt'] + vch + 1)
                            vector.drain()
                            sl = oslab1[:, vch * T:(vch + 1) * T]
                            c.inc(vector.tensor_tensor(sl, sl, prz[vch % 2][:], ALU.mult), 'nzv', 1)
                    blk.vector(s_vector)

            wo2b = ps.enter_context(nc.sbuf_tensor(f"e_wo2_{s}", [128, NCH * D], BF16))
            s1slab = ps.enter_context(nc.sbuf_tensor(f"e_s1_{s}", [128, NCH * T], BF16))
            load_weights_bf16(ps, f"l1wo_{s}", [("wo2T", wo2b, NCH)])
            with ExitStack() as pr:
                pP2 = pr.enter_context(nc.psum_tensor(f"e_pp2_{s}", [128, T], F32))
                with nc.Block() as blk:
                    b0 = c.base('opt', 'ops', 'dx1')

                    def s_tensor(tensor):
                        c.wait(tensor, 'wcast')
                        c.wait(tensor, 'nzv')
                        for och in range(NCH):
                            if och >= 1:
                                tensor.wait_ge(c.S('ops'), b0['ops'] + och)
                            for vch in range(NCH):
                                wsl = wo2b[:, vch * D + och * 128:vch * D + (och + 1) * 128]
                                for qt in range(NQT):
                                    ins = tensor.matmul(pP2[:, qt * 512:(qt + 1) * 512], wsl,
                                                        oslab1[:, vch * T + qt * 512:vch * T + (qt + 1) * 512],
                                                        start=(vch == 0), stop=(vch == NCH - 1))
                            c.inc(ins, 'opt', 1)
                    blk.tensor(s_tensor)

                    def s_scalar(scalar):
                        for och in range(NCH):
                            scalar.wait_ge(c.S('opt'), b0['opt'] + och + 1)
                            if och >= 2:
                                scalar.wait_ge(c.S('dx1'), b0['dx1'] + 16 * (och - 1))
                            c.inc(scalar.activation(s1slab[:, och * T:(och + 1) * T], pP2[:], AT.Identity), 'ops', 1)
                    blk.scalar(s_scalar)

                    def s_gp(gp):
                        for och in range(NCH):
                            gp.wait_ge(c.S('ops'), b0['ops'] + och + 1)
                            if och >= 1:
                                gp.wait_ge(c.S('dx1'), b0['dx1'] + 16 * och)
                                if DBG:
                                    gp.wait_ge(c.S('dbg'), c.cnt['dbg'])
                            c.inc(gp.dma_start(out=h_s1.ap()[s, och], in_=s1slab[:, och * T:(och + 1) * T]), 'dx1', 16)
                            if DBG:
                                c.inc(gp.dma_start(out=dbg['s1'].ap()[s, och],
                                                   in_=s1slab[:, och * T:(och + 1) * T]), 'dbg', 16)
                    blk.gpsimd(s_gp)
            c.dma_fence('dx1', 'dbg', 'nzb', 'nzc')


    def p4_l1ffn(s):
        with ExitStack() as ps:
            rhob = ps.enter_context(nc.sbuf_tensor(f"f_rho_{s}", [128, T], F32))
            delb = ps.enter_context(nc.sbuf_tensor(f"f_del_{s}", [128, T], F32))

            with ExitStack() as pq:
                xq1 = pq.enter_context(nc.sbuf_tensor(f"f_xq_{s}", [128, NCH * T], BF16))
                gw = pq.enter_context(nc.sbuf_tensor(f"f_gw_{s}", [128, T], F32))
                gw2 = pq.enter_context(nc.sbuf_tensor(f"f_gw2_{s}", [128, T], F32))
                guw = [pq.enter_context(nc.sbuf_tensor(f"f_gu{i}_{s}", [128, T], F32)) for i in range(2)]
                amx4 = pq.enter_context(nc.sbuf_tensor(f"f_amx_{s}", [128, T], F32))
                ggb = pq.enter_context(nc.sbuf_tensor(f"f_ggb_{s}", [128, T], F32))
                bca = pq.enter_context(nc.sbuf_tensor(f"f_bca_{s}", [128, T], F32))
                wgb = [pq.enter_context(nc.sbuf_tensor(f"f_wg{i}_{s}", [128, NCH * 512], BF16)) for i in range(2)]
                wub = [pq.enter_context(nc.sbuf_tensor(f"f_wu{i}_{s}", [128, NCH * 512], BF16)) for i in range(2)]
                pa = ExitStack()
                xc = [pa.enter_context(nc.sbuf_tensor(f"f_xc{i}_{s}", [128, T], F32)) for i in range(2)]
                bcb = pa.enter_context(nc.sbuf_tensor(f"f_bcb_{s}", [128, T], F32))
                lsb = pa.enter_context(nc.sbuf_tensor(f"f_lsb_{s}", [128, T], F32))
                q_ti = pa.enter_context(nc.sbuf_tensor(f"f_ti_{s}", [128, T], I32))
                q_d = gw2
                q_f = guw[0]
                q_sg = amx4

                with nc.Block() as blk:
                    def s_gp(gp):
                        c.wait(gp, 'dpk')
                        c.inc(gp.dma_start(out=gw[0:1, 0:T], in_=rows[8 * s + R_A1:8 * s + R_A1 + 1, :]), 'dh', 16)
                        c.wait(gp, 'dh')
                        bcast_row(gp, c, bca, gw[0:1, 0:T], T, 'dbc')
                        c.wait(gp, 'dbc')
                        c.inc(gp.dma_start(out=gw[0:1, 0:T], in_=rows[8 * s + R_B1:8 * s + R_B1 + 1, :]), 'dh', 16)
                        c.wait(gp, 'dh')
                        bcast_row(gp, c, bcb, gw[0:1, 0:T], T, 'dbc')
                        c.wait(gp, 'dbc')
                        c.inc(gp.dma_start(out=gw[0:1, 0:T], in_=rows[8 * s + R_LS1:8 * s + R_LS1 + 1, :]), 'dh', 16)
                        c.wait(gp, 'dh')
                        bcast_row(gp, c, lsb, gw[0:1, 0:T], T, 'dbc')
                        c.wait(gp, 'dbc')
                    blk.gpsimd(s_gp)

                    def s_vector(vector):
                        c.wait(vector, 'dbc')
                        vector.reciprocal(gw2[0:1, 0:T], gw[0:1, 0:T])
                        vector.drain()
                        c.inc(vector.tensor_scalar(gw2[0:1, 0:T], gw2[0:1, 0:T], swcol[0:1, 4:5], None,
                                                   ALU.mult), 'dv', 1)
                    blk.vector(s_vector)

                with nc.Block() as blk:
                    def s_gp(gp):
                        c.wait(gp, 'dv')
                        bcast_row(gp, c, ggb, gw2[0:1, 0:T], T, 'dbc')
                    blk.gpsimd(s_gp)

                with nc.Block() as blk:
                    b0 = c.base('dxl0', 'dxl1', 'dxv', 'dxs')

                    def s_sync(sync):
                        c.wait(sync, 'dbc')
                        for ch in range(NCH):
                            if ch >= 2:
                                sync.wait_ge(c.S('dxv'), b0['dxv'] + 2 * ch - 3)
                            c.inc(sync.dma_start(out=xc[ch % 2][:], in_=h_x1.ap()[s, ch]),
                                  'dxl%d' % (ch % 2), 16)
                    blk.sync(s_sync)

                    def s_vector(vector):
                        c.wait(vector, 'dbc')
                        for ch in range(NCH):
                            vector.wait_ge(c.S('dxl%d' % (ch % 2)), b0['dxl%d' % (ch % 2)] + 16 * (ch // 2 + 1))
                            vector.drain()
                            vector.tensor_tensor(gw[:], xc[ch % 2][:], bca[:], ALU.mult)
                            vector.drain()
                            vector.tensor_tensor(gw[:], gw[:], bcb[:], ALU.add)
                            vector.drain()
                            vector.tensor_tensor(q_d[:], gw[:], lsb[:], ALU.mult)
                            vector.drain()
                            vector.tensor_copy(q_ti[:], q_d[:])
                            vector.drain()
                            vector.tensor_copy(q_f[:], q_ti[:])
                            vector.drain()
                            vector.tensor_tensor(q_d[:], q_d[:], q_f[:], ALU.subtract)
                            vector.drain()
                            vector.tensor_scalar(q_sg[:], q_d[:], 0.5, -0.5, ALU.min, ALU.max)
                            vector.drain()
                            c.inc(vector.tensor_tensor(q_d[:], q_d[:], q_sg[:], ALU.subtract), 'dxv', 1)
                            vector.wait_ge(c.S('dxs'), b0['dxs'] + ch + 1)
                            vector.drain()
                            c.inc(vector.tensor_tensor(xq1[:, ch * T:(ch + 1) * T], q_f[:], q_sg[:], ALU.add), 'dxv', 1)
                    blk.vector(s_vector)

                    def s_scalar(scalar):
                        for ch in range(NCH):
                            scalar.wait_ge(c.S('dxv'), b0['dxv'] + 2 * ch + 1)
                            c.inc(scalar.activation(q_sg[:], q_d[:], AT.Sign), 'dxs', 1)
                    blk.scalar(s_scalar)

                pa.close()
                with ExitStack() as pr:
                    pG = [pr.enter_context(nc.psum_tensor(f"f_pG{i}_{s}", [128, 1024], F32)) for i in range(2)]
                    pU = [pr.enter_context(nc.psum_tensor(f"f_pU{i}_{s}", [128, 1024], F32)) for i in range(2)]
                    with nc.Block() as blk:
                        b0 = c.base('lnl0', 'lnl1', 'dft', 'dfv', 'dfs', 'qst')

                        def s_sync(sync):
                            for sg4 in range(NIC // 4):
                                if sg4 >= 2:
                                    sync.wait_ge(c.S('dft'), b0['dft'] + 8 * (sg4 - 1))
                                for ich in range(NCH):
                                    c.inc(sync.dma_start(out=wgb[sg4 % 2][:, ich * 512:(ich + 1) * 512],
                                                         in_=h_wgq.ap()[ich][:, sg4 * 512:(sg4 + 1) * 512]),
                                          'lnl%d' % (sg4 % 2), 16)
                                    c.inc(sync.dma_start(out=wub[sg4 % 2][:, ich * 512:(ich + 1) * 512],
                                                         in_=h_wuq.ap()[ich][:, sg4 * 512:(sg4 + 1) * 512]),
                                          'lnl%d' % (sg4 % 2), 16)
                        blk.sync(s_sync)

                        def s_tensor(tensor):
                            c.wait(tensor, 'dxv')
                            for n in range(2 * NIC):
                                oc, h2 = n // 2, n % 2
                                sg4 = oc // 4
                                if n % 8 == 0:
                                    tensor.wait_ge(c.S('lnl%d' % (sg4 % 2)),
                                                   b0['lnl%d' % (sg4 % 2)] + 256 * (sg4 // 2 + 1))
                                if n >= 2:
                                    tensor.wait_ge(c.S('dfv'), b0['dfv'] + 3 * (n - 2) + 2)
                                for ich in range(NCH):
                                    wcol = ich * 512 + (oc % 4) * 128
                                    for q in range(2):
                                        xsl = xq1[:, ich * T + h2 * 1024 + q * 512:
                                                  ich * T + h2 * 1024 + (q + 1) * 512]
                                        tensor.matmul(pG[n % 2][:, q * 512:(q + 1) * 512],
                                                      wgb[sg4 % 2][:, wcol:wcol + 128], xsl,
                                                      start=(ich == 0), stop=(ich == NCH - 1))
                                        ins = tensor.matmul(pU[n % 2][:, q * 512:(q + 1) * 512],
                                                            wub[sg4 % 2][:, wcol:wcol + 128], xsl,
                                                            start=(ich == 0), stop=(ich == NCH - 1))
                                c.inc(ins, 'dft', 1)
                        blk.tensor(s_tensor)

                        def s_vector(vector):
                            c.wait(vector, 'dbc')
                            for n in range(2 * NIC):
                                oc, h2 = n // 2, n % 2
                                hs = slice(h2 * 1024, (h2 + 1) * 1024)
                                vector.wait_ge(c.S('dft'), b0['dft'] + n + 1)
                                vector.drain()
                                c.inc(vector.tensor_tensor(gw[:, hs], pG[n % 2][:], ggb[:, hs], ALU.mult),
                                      'dfv', 1)
                                vector.wait_ge(c.S('dfs'), b0['dfs'] + 2 * n + 1)
                                if oc >= 2 and h2 == 0:
                                    vector.wait_ge(c.S('qst'), b0['qst'] + 16 * (oc - 1))
                                vector.drain()
                                c.inc(vector.tensor_tensor(guw[oc % 2][:, hs], gw2[:, hs], pU[n % 2][:],
                                                           ALU.mult), 'dfv', 1)
                                vector.wait_ge(c.S('dfs'), b0['dfs'] + 2 * n + 2)
                                vector.drain()
                                if oc == 0:
                                    ins = vector.tensor_copy(amx4[:, hs], bca[:, hs])
                                else:
                                    ins = vector.tensor_tensor(amx4[:, hs], amx4[:, hs], bca[:, hs], ALU.max)
                                c.inc(ins, 'dfv', 1)
                        blk.vector(s_vector)

                        def s_scalar(scalar):
                            for n in range(2 * NIC):
                                oc, h2 = n // 2, n % 2
                                hs = slice(h2 * 1024, (h2 + 1) * 1024)
                                scalar.wait_ge(c.S('dfv'), b0['dfv'] + 3 * n + 1)
                                c.inc(scalar.activation(gw2[:, hs], gw[:, hs], AT.Silu), 'dfs', 1)
                                scalar.wait_ge(c.S('dfv'), b0['dfv'] + 3 * n + 2)
                                c.inc(scalar.activation(bca[:, hs], guw[oc % 2][:, hs], AT.Abs), 'dfs', 1)
                        blk.scalar(s_scalar)

                        def s_gp(gp):
                            for oc in range(NIC):
                                gp.wait_ge(c.S('dfv'), b0['dfv'] + 3 * (2 * oc + 1) + 2)
                                if oc >= 1:
                                    gp.wait_ge(c.S('qst'), b0['qst'] + 16 * oc)
                                c.inc(gp.dma_start(out=h_gu.ap()[oc], in_=guw[oc % 2][:]), 'qst', 16)
                        blk.gpsimd(s_gp)

                with nc.Block() as blk:
                    def s_vector(vector):
                        part_tree(c, vector, amx4, gw2, T)
                        c.inc(vector.drain(), 'dv', 1)
                    blk.vector(s_vector)

                    def s_gp(gp):
                        c.wait(gp, 'dv')
                        c.inc(gp.dma_start(out=gw[0:1, 0:T], in_=rows[8 * s + R_LS1:8 * s + R_LS1 + 1, :]), 'dh', 16)
                    blk.gpsimd(s_gp)

                with nc.Block() as blk:
                    def s_vector(vector):
                        c.wait(vector, 'dh')
                        vector.reciprocal(gw[0:1, 0:T], gw[0:1, 0:T])
                        vector.drain()
                        vector.tensor_scalar(gw[0:1, 0:T], gw[0:1, 0:T], swcol[0:1, 5:6], None, ALU.mult)
                        vector.drain()
                        vector.tensor_tensor(gw2[0:1, 0:T], amx4[0:1, 0:T], gw[0:1, 0:T], ALU.mult)
                        vector.drain()
                        vector.tensor_scalar(gw2[0:1, 0:T], gw2[0:1, 0:T], 1e-5, None, ALU.max)
                        vector.drain()
                        vector.reciprocal(gw2[0:1, 0:T], gw2[0:1, 0:T])
                        vector.drain()
                        vector.tensor_scalar_mul(gw2[0:1, 0:T], gw2[0:1, 0:T], 127.0)
                        vector.drain()
                        vector.tensor_tensor(amx4[0:1, 0:T], gw2[0:1, 0:T], gw[0:1, 0:T], ALU.mult)
                        vector.drain()
                        vector.reciprocal(gw[0:1, 0:T], gw2[0:1, 0:T])
                        vector.drain()
                        c.inc(vector.tensor_scalar(gw[0:1, 0:T], gw[0:1, 0:T], swcol[0:1, 6:7], None,
                                                   ALU.mult), 'dv', 1)
                    blk.vector(s_vector)

                    def s_gp(gp):
                        c.wait(gp, 'dv')
                        bcast_row(gp, c, rhob, amx4[0:1, 0:T], T, 'dbc2')
                        bcast_row(gp, c, delb, gw[0:1, 0:T], T, 'dbc2')
                        c.wait(gp, 'dbc2')
                    blk.gpsimd(s_gp)
            c.dma_fence('qst', 'dbc2', 'dbc', 'dh')

            with ExitStack() as pd:
                v2slab = pd.enter_context(nc.sbuf_tensor(f"f_v2_{s}", [128, NIC * T], BF16))
                gul = pd.enter_context(nc.sbuf_tensor(f"f_gul_{s}", [128, T], F32))
                q_ti = pd.enter_context(nc.sbuf_tensor(f"f_ti2_{s}", [128, T], I32))
                q_f = pd.enter_context(nc.sbuf_tensor(f"f_f2_{s}", [128, T], F32))
                q_d = pd.enter_context(nc.sbuf_tensor(f"f_d2_{s}", [128, T], F32))
                q_sg = pd.enter_context(nc.sbuf_tensor(f"f_sg2_{s}", [128, T], F32))
                s1st = pd.enter_context(nc.sbuf_tensor(f"f_s1_{s}", [128, T], BF16))
                wdb = [pd.enter_context(nc.sbuf_tensor(f"f_wd{i}_{s}", [128, 128], BF16)) for i in range(2)]

                with nc.Block() as blk:
                    b0 = c.base('dal0', 'dal1', 'dxv', 'dxs')

                    def s_sync(sync):
                        for oc in range(NIC):
                            if oc >= 1:
                                sync.wait_ge(c.S('dxv'), b0['dxv'] + 3 * (oc - 1) + 1)
                            c.inc(sync.dma_start(out=gul[:], in_=h_gu.ap()[oc]), 'dal%d' % (oc % 2), 16)
                    blk.sync(s_sync)

                    def s_vector(vector):
                        for oc in range(NIC):
                            vector.wait_ge(c.S('dal%d' % (oc % 2)), b0['dal%d' % (oc % 2)] + 16 * (oc // 2 + 1))
                            vector.drain()
                            c.inc(vector.tensor_tensor(q_d[:], gul[:], rhob[:], ALU.mult), 'dxv', 1)
                            vector.drain()
                            vector.tensor_copy(q_ti[:], q_d[:])
                            vector.drain()
                            vector.tensor_copy(q_f[:], q_ti[:])
                            vector.drain()
                            vector.tensor_tensor(q_d[:], q_d[:], q_f[:], ALU.subtract)
                            vector.drain()
                            vector.tensor_scalar(q_sg[:], q_d[:], 0.5, -0.5, ALU.min, ALU.max)
                            vector.drain()
                            vector.tensor_tensor(q_d[:], q_d[:], q_sg[:], ALU.subtract)
                            vector.drain()
                            c.inc(vector.drain(), 'dxv', 1)
                            vector.wait_ge(c.S('dxs'), b0['dxs'] + oc + 1)
                            vector.drain()
                            c.inc(vector.tensor_tensor(v2slab[:, oc * T:(oc + 1) * T], q_f[:], q_sg[:],
                                                       ALU.add), 'dxv', 1)
                    blk.vector(s_vector)

                    def s_scalar(scalar):
                        for oc in range(NIC):
                            scalar.wait_ge(c.S('dxv'), b0['dxv'] + 3 * oc + 2)
                            c.inc(scalar.activation(q_sg[:], q_d[:], AT.Sign), 'dxs', 1)
                    blk.scalar(s_scalar)

                with ExitStack() as pr:
                    pD = [pr.enter_context(nc.psum_tensor(f"f_pD{i}_{s}", [128, T], F32)) for i in range(2)]
                    with nc.Block() as blk:
                        b0 = c.base('lnm0', 'lnm1', 'opt', 'dfv', 'dfs', 'dal0', 'dal1', 'fin', 'dbgg')

                        def s_sync(sync):
                            for och in range(NCH):
                                if och >= 1:
                                    sync.wait_ge(c.S('dfv'), b0['dfv'] + 4 * och)
                                c.inc(sync.dma_start(out=s1st[:], in_=h_s1.ap()[s, och]), 'dal0', 16)
                                c.inc(sync.dma_start(out=gul[:], in_=h_x1.ap()[s, och]), 'dal1', 16)
                                for oc in range(NIC):
                                    n = och * NIC + oc
                                    if n >= 2:
                                        sync.wait_ge(c.S('opt'), b0['opt'] + n - 1)
                                    c.inc(sync.dma_start(out=wdb[n % 2][:],
                                                         in_=h_wdq.ap()[och][oc]),
                                          'lnm%d' % (n % 2), 16)
                        blk.sync(s_sync)

                        def s_tensor(tensor):
                            for och in range(NCH):
                                pD_ = pD[och % 2]
                                for oc in range(NIC):
                                    n = och * NIC + oc
                                    tensor.wait_ge(c.S('lnm%d' % (n % 2)), b0['lnm%d' % (n % 2)] + 16 * (n // 2 + 1))
                                    if oc == 0 and och >= 2:
                                        tensor.wait_ge(c.S('dfv'), b0['dfv'] + 4 * (och - 1) - 3)
                                    for qt in range(NQT):
                                        ins = tensor.matmul(pD_[:, qt * 512:(qt + 1) * 512], wdb[n % 2][:],
                                                            v2slab[:, oc * T + qt * 512:oc * T + (qt + 1) * 512],
                                                            start=(oc == 0), stop=(oc == NIC - 1))
                                    c.inc(ins, 'opt', 1)
                        blk.tensor(s_tensor)

                        def s_vector(vector):
                            for och in range(NCH):
                                vector.wait_ge(c.S('opt'), b0['opt'] + (och + 1) * NIC)
                                vector.drain()
                                c.inc(vector.tensor_tensor(q_d[:], pD[och % 2][:], delb[:], ALU.mult), 'dfv', 1)
                                vector.wait_ge(c.S('dfs'), b0['dfs'] + och + 1)
                                vector.wait_ge(c.S('dal0'), b0['dal0'] + 16 * (och + 1))
                                vector.wait_ge(c.S('dal1'), b0['dal1'] + 16 * (och + 1))
                                if och >= 1:
                                    vector.wait_ge(c.S('fin'), b0['fin'] + 16 * och)
                                    if DBG:
                                        vector.wait_ge(c.S('dbgg'), b0['dbgg'] + 16 * och)
                                vector.drain()
                                c.inc(vector.tensor_copy(q_d[:], s1st[:]), 'dfv', 1)
                                vector.drain()
                                vector.tensor_tensor(q_sg[:], q_d[:], q_f[:], ALU.mult)
                                vector.drain()
                                c.inc(vector.tensor_tensor(q_sg[:], gul[:], q_sg[:], ALU.add), 'dfv', 2)
                        blk.vector(s_vector)

                        def s_scalar(scalar):
                            for och in range(NCH):
                                scalar.wait_ge(c.S('dfv'), b0['dfv'] + 4 * och + 1)
                                if DBG and och >= 1:
                                    scalar.wait_ge(c.S('dbgg'), b0['dbgg'] + 16 * och)
                                c.inc(scalar.activation(q_f[:], q_d[:], AT.Silu), 'dfs', 1)
                        blk.scalar(s_scalar)

                        def s_gp(gp):
                            for och in range(NCH):
                                gp.wait_ge(c.S('dfv'), b0['dfv'] + 4 * (och + 1))
                                if och >= 1:
                                    gp.wait_ge(c.S('fin'), b0['fin'] + 16 * och)
                                c.inc(gp.dma_start(out=out_t.ap()[s, och * 128:(och + 1) * 128, :],
                                                   in_=q_sg[:]), 'fin', 16)
                                if DBG:
                                    if och >= 1:
                                        gp.wait_ge(c.S('dbgg'), b0['dbgg'] + 16 * och)
                                    c.inc(gp.dma_start(out=dbg['g1'].ap()[s, och], in_=q_f[:]), 'dbgg', 16)
                        blk.gpsimd(s_gp)
            c.dma_fence('fin', 'dbgg', 'dal0', 'dal1')

    for s in range(NSEQ):
        with ExitStack() as sps1:
            xn0 = sps1.enter_context(nc.sbuf_tensor(f"xn0_{s}", [128, NCH * T], BF16))
            w_qkv = [sps1.enter_context(nc.sbuf_tensor(f"wqkv{k}_{s}", [128, NCH * D], BF16)) for k in range(3)]
            with nc.named_scope("p1a"):
                load_weights_bf16(sps1, f"l0w_{s}", [("wqT", w_qkv[0], NCH), ("wkT", w_qkv[1], NCH), ("wvT", w_qkv[2], NCH)])
                p1a(s, xn0)
            with nc.named_scope("l0proj"):
                l0_proj(s, xn0, w_qkv)
        with ExitStack() as sps2:
            s0slab = sps2.enter_context(nc.sbuf_tensor(f"s0slab_{s}", [128, NCH * T], BF16))
            l0_attn(s, s0slab)
            with nc.named_scope("p1d"):
                p1d(s, s0slab)
        p2_l1attn(s)
        with nc.named_scope("p4"):
            p4_l1ffn(s)

    with nc.Block() as blk:
        def s_gp(g):
            c.wait(g, 'fin')
        blk.gpsimd(s_gp)

    return nc, st, c, dbg



def _host_consts():
    m = {}
    m["ident"] = np.eye(128).astype(ml_dtypes.bfloat16)
    m["triu64"] = np.triu(np.ones((64, 64), np.float32)).astype(ml_dtypes.bfloat16)
    t128 = np.zeros((128, 128), np.float32)
    t128[0:64, 0:64] = np.triu(np.ones((64, 64)))
    t128[0:64, 64:128] = 1.0
    t128[64:128, 64:128] = np.triu(np.ones((64, 64)))
    m["triu128p"] = t128.astype(ml_dtypes.bfloat16)
    e = np.zeros((16, 8 * 128), np.float32)
    for vch in range(8):
        for p in range(128):
            e[2 * vch + p // 64, vch * 128 + p] = 1.0
    m["ehd"] = e
    p = np.arange(128)[:, None]
    w = np.arange(276)[None, :]
    m["adm"] = (p + w <= 275).astype(ml_dtypes.bfloat16)
    return m


def _host_inputs(inputs, core):
    x = inputs["L0_attn_hidden_states"]
    xc = np.ascontiguousarray(np.transpose(np.asarray(x)[2 * core:2 * core + 2], (0, 2, 1))).astype(np.float32)
    m = {"x": xc}
    m["wqT"] = np.ascontiguousarray(np.asarray(inputs["L0_attn_Wq"]).T.astype(np.float32))
    m["wkT"] = np.ascontiguousarray(np.asarray(inputs["L0_attn_Wk"]).T.astype(np.float32))
    m["wvT"] = np.ascontiguousarray(np.asarray(inputs["L0_attn_Wv"]).T.astype(np.float32))
    m["woT"] = np.ascontiguousarray(np.asarray(inputs["L0_attn_Wo"]).T.astype(np.float32))
    m["rhsw"] = np.ascontiguousarray(np.asarray(inputs["L0_ffn_rhs"]).astype(np.float32))
    m["wq2T"] = np.ascontiguousarray(np.asarray(inputs["L1_attn_Wq"]).T.astype(np.float32))
    m["wk2T"] = np.ascontiguousarray(np.asarray(inputs["L1_attn_Wk"]).T.astype(np.float32))
    m["wv2T"] = np.ascontiguousarray(np.asarray(inputs["L1_attn_Wv"]).T.astype(np.float32))
    m["wo2T"] = np.ascontiguousarray(np.asarray(inputs["L1_attn_Wo"]).T.astype(np.float32))
    m["wgT"] = np.ascontiguousarray(np.asarray(inputs["L1_ffn_w_gate"]).T.astype(np.float32))
    m["wuT"] = np.ascontiguousarray(np.asarray(inputs["L1_ffn_w_up"]).T.astype(np.float32))
    m["wdT"] = np.ascontiguousarray(np.asarray(inputs["L1_ffn_w_down"]).T.astype(np.float32))
    bias = np.stack([np.asarray(inputs["L0_attn_bq"]), np.asarray(inputs["L0_attn_bk"]),
                     np.asarray(inputs["L0_attn_bv"]), np.asarray(inputs["L0_attn_bo"])], 0).astype(np.float32)
    bcols = np.zeros((128, 4 * NCH), np.float32)
    for k in range(4):
        for ch in range(NCH):
            bcols[:, k * NCH + ch] = bias[k, ch * 128:(ch + 1) * 128]
    m["bqkvo"] = bcols
    m["relk"] = np.ascontiguousarray(np.asarray(inputs["L0_attn_rel_k"]).astype(np.float32)[::-1])
    m["relv"] = np.ascontiguousarray(np.asarray(inputs["L0_attn_rel_v"]).astype(np.float32))
    m["relvf"] = np.ascontiguousarray(np.asarray(inputs["L0_attn_rel_v"]).astype(np.float32)[::-1])
    m["gb16"] = np.stack([np.asarray(inputs["L1_attn_gamma"]),
                          np.asarray(inputs["L1_attn_beta"])], 0).astype(np.float32)
    m.update(_host_consts())
    return m


_NC_CACHE = {}


def kernel(**inputs):
    if "nc" not in _NC_CACHE:
        _NC_CACHE["nc"] = build_nc()
    nc, st_, c, dbg = _NC_CACHE["nc"]
    in_maps = [_host_inputs(inputs, core) for core in range(NCORES)]
    res = run_bass_kernel_spmd(nc, in_maps, core_ids=list(range(NCORES)))
    outs = []
    for core in range(NCORES):
        o = np.asarray(res.results[core]["out"])
        outs.append(np.transpose(o, (0, 2, 1)))
    return np.concatenate(outs, 0).astype(np.float32)


# version 15
# speedup vs baseline: 1.0486x; 1.0341x over previous
import os
import numpy as np
import ml_dtypes
from contextlib import ExitStack

import concourse.bass as bass
import concourse.mybir as mybir
from concourse.bass_utils import run_bass_kernel_spmd

F32 = mybir.dt.float32
BF16 = mybir.dt.bfloat16
I32 = mybir.dt.int32
ALU = mybir.AluOpType
AT = mybir.ActivationFunctionType
AX = mybir.AxisListType
AP = bass.AP

B, T, D = 16, 2048, 1024
NCORES = 8
NSEQ = 2
NCH = D // 128
NTC = T // 128
NQT = T // 512
H0, HD0 = 16, 64
H1, FD1, HD1 = 16, 16, 64
INNER = 4096
NIC = INNER // 128

DBG = int(os.environ.get("KDBG", "0"))


class Ctx:
    def __init__(self, nc, stack):
        self.nc = nc
        self.stack = stack
        self.sem = {}
        self.cnt = {}

    def S(self, name):
        if name not in self.sem:
            self.sem[name] = self.stack.enter_context(self.nc.semaphore(name))
            self.cnt[name] = 0
        return self.sem[name]

    def inc(self, instr, name, k=1):
        instr.then_inc(self.S(name), k)
        self.cnt[name] += k
        return self.cnt[name]

    def wait(self, eng, name, at=None):
        eng.wait_ge(self.S(name), self.cnt[name] if at is None else at)

    def base(self, *names):
        for n in names:
            self.S(n)
        return {n: self.cnt[n] for n in names}

    def dma_fence(self, *names):
        with self.nc.Block() as blk:
            def s_sync(sync):
                for n in names:
                    if n in self.cnt:
                        sync.wait_ge(self.S(n), self.cnt[n])
            blk.sync(s_sync)



def part_max_32(c, vector, tile, n, tmp=None):
    buf = tile if tmp is None else tmp
    off = 32 if tmp is None else 0
    for hh in (16, 8, 4, 2, 1):
        mask = [(i + hh) % 32 for i in range(32)]
        vector.stream_shuffle(buf[off:off + 32, 0:n], tile[0:32, 0:n], mask)
        vector.drain()
        vector.tensor_tensor(tile[0:32, 0:n], tile[0:32, 0:n], buf[off:off + 32, 0:n], ALU.max)
        vector.drain()


def part_tree(c, vector, tile, tmp, n):
    for h in (64, 32):
        vector.drain()
        vector.tensor_copy(tmp[0:h, 0:n], tile[h:2 * h, 0:n])
        vector.drain()
        vector.tensor_tensor(tile[0:h, 0:n], tile[0:h, 0:n], tmp[0:h, 0:n], ALU.max)
    vector.drain()
    part_max_32(c, vector, tile, n, tmp=tmp)


def bcast_row(eng_dma, c, dst_tile, row_ap, n, sem, parts=128, dst_off=0):
    assert int(row_ap.offset) == 0, "bcast source must be offset-0"
    src = AP(row_ap.tensor, 0, [[1, 1], [0, parts], [1, n]])
    return c.inc(eng_dma.dma_start(out=dst_tile[0:parts, dst_off:dst_off + n], in_=src), sem, 16)


def build_nc():
    nc = bass.Bass("TRN2", target_bir_lowering=False)
    st = ExitStack()
    c = Ctx(nc, st)

    x_in = nc.dram_tensor("x", [NSEQ, D, T], F32, kind="ExternalInput")
    out_t = nc.dram_tensor("out", [NSEQ, D, T], F32, kind="ExternalOutput")
    wT = {}
    for nm, sh in [("wqT", [D, D]), ("wkT", [D, D]), ("wvT", [D, D]),
                   ("woT", [D, D]), ("rhsw", [D, D]),
                   ("wq2T", [D, 256]), ("wk2T", [D, 256]),
                   ("wv2T", [D, D]), ("wo2T", [D, D]),
                   ("wgT", [D, INNER]), ("wuT", [D, INNER]), ("wdT", [INNER, D])]:
        wT[nm] = nc.dram_tensor(nm, sh, F32, kind="ExternalInput")
    bias_in = nc.dram_tensor("bqkvo", [128, 4 * NCH], F32, kind="ExternalInput")
    relk_in = nc.dram_tensor("relk", [41, 64], F32, kind="ExternalInput")
    relv_in = nc.dram_tensor("relv", [41, 64], F32, kind="ExternalInput")
    relvf_in = nc.dram_tensor("relvf", [41, 64], F32, kind="ExternalInput")
    gb_in = nc.dram_tensor("gb16", [2, 16], F32, kind="ExternalInput")
    ident_in = nc.dram_tensor("ident", [128, 128], BF16, kind="ExternalInput")
    triu_in = nc.dram_tensor("triu64", [64, 64], BF16, kind="ExternalInput")
    adm_in = nc.dram_tensor("adm", [128, 276], BF16, kind="ExternalInput")
    triu128_in = nc.dram_tensor("triu128p", [128, 128], BF16, kind="ExternalInput")
    ehd_in = nc.dram_tensor("ehd", [16, 8 * 128], F32, kind="ExternalInput")

    h_x1 = nc.dram_tensor("h_x1", [NSEQ, NCH, 128, T], F32, kind="Internal")
    h_xn1 = nc.dram_tensor("h_xn1", [NSEQ, NCH, 128, T], BF16, kind="Internal")
    h_s1 = nc.dram_tensor("h_s1", [NSEQ, NCH, 128, T], BF16, kind="Internal")
    h_rhsq = nc.dram_tensor("h_rhsq", [NCH, 128, D], BF16, kind="Internal")
    h_qt = nc.dram_tensor("h_qt", [H0, 64, T], BF16, kind="Internal")
    h_kt = nc.dram_tensor("h_kt", [H0, 64, T], BF16, kind="Internal")
    h_va = nc.dram_tensor("h_va", [NTC, 128, H0 * 65], BF16, kind="Internal")
    h_ps = nc.dram_tensor("h_ps", [2, 41, 2130], BF16, kind="Internal")
    h_gp = nc.dram_tensor("h_gp", [2, 1, 2208], BF16, kind="Internal")
    h_zz = nc.dram_tensor("h_zz", [1, 4096], BF16, kind="Internal")
    h_rowf = nc.dram_tensor("h_rowf", [1, 4096], F32, kind="Internal")
    h_c = nc.dram_tensor("h_c", [2, NTC, 128, 464], BF16, kind="Internal")
    h_strip = nc.dram_tensor("h_strip", [NTC, 128, 404], BF16, kind="Internal")
    h_h6 = nc.dram_tensor("h_h6", [276, 2624], BF16, kind="Internal")
    h_gu = nc.dram_tensor("h_gu", [NIC, 128, T], F32, kind="Internal")
    h_wgq = nc.dram_tensor("h_wgq", [NCH, 128, INNER], BF16, kind="Internal")
    h_wuq = nc.dram_tensor("h_wuq", [NCH, 128, INNER], BF16, kind="Internal")
    h_wdq = nc.dram_tensor("h_wdq", [NCH, NIC, 128, 128], BF16, kind="Internal")

    dbg = {}
    if DBG:
        for nm, sh, dt in [("ln0", [NSEQ, NCH, 128, T], BF16),
                           ("rows", [NSEQ, 16, T], F32),
                           ("qrhs", [NCH, 128, D], BF16),
                           ("qt0", [64, 2 * T], BF16),
                           ("aslab", [128, NTC * T], BF16), ("oslab", [128, NCH * T], BF16), ("zs", [16, T], F32),
                           ("s0", [NSEQ, NCH, 128, T], BF16),
                           ("g0", [NSEQ, NCH, 128, T], F32),
                           ("s1", [NSEQ, NCH, 128, T], BF16),
                           ("ufm", [NSEQ, 128, 16 * 512], BF16),
                           ("val1", [NSEQ, 128, 16 * 16 * 65], BF16),
                           ("zs1", [NSEQ, 16, T], F32),
                           ("g1", [NSEQ, NCH, 128, T], F32)]:
            dbg[nm] = nc.dram_tensor("dbg_" + nm, sh, dt, kind="ExternalOutput")

    bias_sb = st.enter_context(nc.sbuf_tensor("bias_sb", [128, 4 * NCH], F32))
    ident = st.enter_context(nc.sbuf_tensor("sb_ident", [128, 128], BF16))
    triu64 = st.enter_context(nc.sbuf_tensor("sb_triu64", [64, 64], BF16))
    adm = st.enter_context(nc.sbuf_tensor("sb_adm", [128, 276], BF16))
    triu128p = st.enter_context(nc.sbuf_tensor("sb_triu128p", [128, 128], BF16))
    ehd = st.enter_context(nc.sbuf_tensor("sb_ehd", [16, 8 * 128], F32))
    relkT = st.enter_context(nc.sbuf_tensor("relkT", [64, 41], BF16))
    relk_m40T = st.enter_context(nc.sbuf_tensor("relk_m40T", [64, 41], BF16))
    relcols = st.enter_context(nc.sbuf_tensor("relcols", [64, 2], F32))
    relv40r = st.enter_context(nc.sbuf_tensor("relv40r", [1, 64], BF16))
    dvrow = st.enter_context(nc.sbuf_tensor("dvrow", [1, 64], BF16))
    relvp = st.enter_context(nc.sbuf_tensor("relvp", [128, 128], BF16))
    gbrow = st.enter_context(nc.sbuf_tensor("gbrow", [2, 16], F32))
    gbhop = st.enter_context(nc.sbuf_tensor("gbhop", [1, 32], F32))
    swrow = st.enter_context(nc.sbuf_tensor("swrow", [1, 8], F32))
    swcol = st.enter_context(nc.sbuf_tensor("swcol", [128, 8], F32))
    gbig = st.enter_context(nc.sbuf_tensor("gbig", [128, 32], F32))
    onesrow = st.enter_context(nc.sbuf_tensor("onesrow", [1, 128], BF16))
    onescol_f = st.enter_context(nc.sbuf_tensor("onescol_f", [128, 1], F32))
    onescol_b = st.enter_context(nc.sbuf_tensor("onescol_b", [128, 1], BF16))
    rows = st.enter_context(nc.sbuf_tensor("rows", [16, T], F32))
    grows = st.enter_context(nc.sbuf_tensor("grows", [4, 1024], F32))
    cvals = st.enter_context(nc.sbuf_tensor("cvals", [128, 4], F32))
    R_A0, R_B0, R_LS0, R_A1, R_B1, R_LS1, R_T1, R_T2 = range(8)

    def row(s, r):
        return rows[8 * s + r:8 * s + r + 1, :]

    def load_weights_bf16(ps, tag, specs):
        f32x = ps.enter_context(nc.sbuf_tensor(f"wl_a_{tag}", [128, 1024], F32))
        f32y = ps.enter_context(nc.sbuf_tensor(f"wl_b_{tag}", [128, 1024], F32))
        with nc.Block() as blk:
            b0 = c.base('wld0', 'wld1', 'wcast')
            items = []
            for spec in specs:
                nm, dst, nch = spec[0], spec[1], spec[2]
                w = spec[3] if len(spec) > 3 else 1024
                stride = spec[4] if len(spec) > 4 else w
                off = spec[5] if len(spec) > 5 else 0
                for ch in range(nch):
                    items.append((nm, dst, ch, w, stride, off))

            def s_sync(sync):
                for n, (nm, dst, ch, w, stride, off) in enumerate(items):
                    if n >= 2:
                        sync.wait_ge(c.S('wcast'), b0['wcast'] + n - 1)
                    buf = f32x if n % 2 == 0 else f32y
                    c.inc(sync.dma_start(out=buf[:, 0:w], in_=wT[nm].ap()[ch * 128:(ch + 1) * 128, :]),
                          'wld%d' % (n % 2), 16)
            blk.sync(s_sync)

            def s_vector(vector):
                for n, (nm, dst, ch, w, stride, off) in enumerate(items):
                    vector.wait_ge(c.S('wld%d' % (n % 2)), b0['wld%d' % (n % 2)] + 16 * (n // 2 + 1))
                    buf = f32x if n % 2 == 0 else f32y
                    c.inc(vector.tensor_copy(dst[:, off + ch * stride:off + ch * stride + w], buf[:, 0:w]), 'wcast', 1)
            blk.vector(s_vector)

    with ExitStack() as ps:
        f32a = ps.enter_context(nc.sbuf_tensor("p0_f32a", [128, 1024], F32))
        f32b = ps.enter_context(nc.sbuf_tensor("p0_f32b", [128, 1024], F32))
        relf = ps.enter_context(nc.sbuf_tensor("p0_relf", [41, 192], F32))
        relb = ps.enter_context(nc.sbuf_tensor("p0_relb", [41, 192], BF16))
        zz = ps.enter_context(nc.sbuf_tensor("p0_zz", [128, 2600], BF16))
        oneb = ps.enter_context(nc.sbuf_tensor("p0_oneb", [128, 16], BF16))
        pr = ps.enter_context(nc.psum_tensor("p0_ps", [128, 128], F32))
        prb = ps.enter_context(nc.psum_tensor("p0_psb", [128, 128], BF16))

        with nc.Block() as blk:
            def s_sync(sync):
                for t_, src in [(ident, ident_in), (triu64, triu_in), (adm, adm_in), (triu128p, triu128_in), (ehd, ehd_in),
                                (bias_sb, bias_in), (gbrow, gb_in)]:
                    c.inc(sync.dma_start(out=t_[:], in_=src.ap()), 'p0c', 16)
                c.inc(sync.dma_start(out=relf[0:41, 0:64], in_=relk_in.ap()), 'p0c', 16)
                c.inc(sync.dma_start(out=relf[0:41, 64:128], in_=relv_in.ap()), 'p0c', 16)
                c.inc(sync.dma_start(out=relf[0:41, 128:192], in_=relvf_in.ap()), 'p0c', 16)
            blk.sync(s_sync)

            def s_vector(vector):
                c.wait(vector, 'p0c')
                vector.memset(onesrow[:], 1.0)
                vector.memset(onescol_f[:], 1.0)
                vector.memset(onescol_b[:], 1.0)
                vector.memset(zz[:], 0.0)
                vector.memset(oneb[:], 1.0)
                vector.memset(cvals[:, 0:1], 1e-5)
                vector.memset(cvals[:, 1:2], 1e-10)
                vector.memset(cvals[:, 2:3], 0.0)
                vector.memset(cvals[:, 3:4], 1.0)
                vector.memset(rows[:], 0.0)
                vector.memset(grows[:], 0.0)
                vector.tensor_copy(relb[:], relf[:])
                vector.tensor_scalar_mul(gbrow[:], gbrow[:], 0.5)
                c.inc(vector.drain(), 'p0v', 1)
            blk.vector(s_vector)

        with nc.Block() as blk:
            def s_sync(sync):
                c.wait(sync, 'p0v', 1)
                for par in range(2):
                    for ch in range(NTC):
                        c.inc(sync.dma_start(out=h_c.ap()[par, ch, :, 0:128], in_=zz[:, 0:128]), 'p0z', 16)
                        c.inc(sync.dma_start(out=h_c.ap()[par, ch, :, 296:464], in_=zz[:, 0:168]), 'p0z', 16)
                for ch in range(NTC):
                    c.inc(sync.dma_start(out=h_strip.ap()[ch, :, :], in_=zz[:, 0:404]), 'p0z', 16)
                for r0, n in ((0, 128), (128, 128), (256, 20)):
                    c.inc(sync.dma_start(out=h_h6.ap()[r0:r0 + n, 0:2600], in_=zz[0:n, :]), 'p0z', 16)
                    c.inc(sync.dma_start(out=h_h6.ap()[r0:r0 + n, 2600:2624], in_=zz[0:n, 0:24]), 'p0z', 16)
                for par in range(2):
                    c.inc(sync.dma_start(out=h_ps.ap()[par, :, :], in_=zz[0:41, 0:2130]), 'p0z', 16)
                c.inc(sync.dma_start(out=h_zz.ap()[0:1, 0:2600], in_=zz[0:1, 0:2600]), 'p0z', 16)
                c.inc(sync.dma_start(out=h_zz.ap()[0:1, 2600:4096], in_=zz[0:1, 0:1496]), 'p0z', 16)
                for tc in range(NTC):
                    dst = AP(h_va.ap().tensor, tc * 128 * 1040 + 64, [[1040, 128], [65, 16], [1, 1]])
                    c.inc(sync.dma_start(out=dst, in_=AP(oneb, 0, [[16, 128], [1, 16], [1, 1]]),
                                         allow_slow_non_contiguous=True), 'p0z', 16)
            blk.sync(s_sync)

            def s_tensor(tensor):
                c.wait(tensor, 'p0v', 1)
                c.inc(tensor.transpose(prb[0:64, 0:41], relb[0:41, 0:64], ident[0:41, 0:41]), 'p0t', 1)
            blk.tensor(s_tensor)

            def s_gp(g):
                c.wait(g, 'p0v', 1)
                c.inc(g.dma_start(out=relv40r[:], in_=relb[40:41, 64:128]), 'p0g', 16)
                c.inc(g.dma_start(out=gbhop[0:1, 0:16], in_=gbrow[0:1, :]), 'p0gb', 16)
                c.inc(g.dma_start(out=gbhop[0:1, 16:32], in_=gbrow[1:2, :]), 'p0gb', 16)
                c.wait(g, 'p0gb')
                bcast_row(g, c, gbig, gbhop[0:1, 0:32], 32, 'p0gb')
            blk.gpsimd(s_gp)

            def s_vector(vector):
                c.wait(vector, 'p0t', 1)
                c.wait(vector, 'p0g', 16)
                vector.tensor_copy(relkT[:], prb[0:64, 0:41])
                vector.drain()
                vector.tensor_copy(relcols[:, 0:1], relkT[:, 0:1])
                vector.tensor_copy(relcols[:, 1:2], relkT[:, 40:41])
                vector.drain()
                vector.tensor_scalar(relk_m40T[:], relkT[:], relcols[:, 0:1], None, ALU.subtract)
                vector.tensor_tensor(dvrow[:], relb[0:1, 64:128], relv40r[:], ALU.subtract)
                c.inc(vector.drain(), 'p0v', 1)
            blk.vector(s_vector)

        with nc.Block() as blk:
            def s_tensor(tensor):
                c.wait(tensor, 'p0v', 2)
                tensor.matmul(pr[:, 0:64], onesrow[:], relv40r[:], start=True, stop=True)
                c.inc(tensor.matmul(pr[:, 64:128], onesrow[:], dvrow[:], start=True, stop=True), 'p0t', 1)
            blk.tensor(s_tensor)

            def s_vector(vector):
                c.wait(vector, 'p0t', 2)
                vector.tensor_copy(relvp[:, 64:128], pr[:, 64:128])
                vector.tensor_copy(relvp[:, 0:64], pr[:, 64:128])
                vector.drain()
                vector.tensor_tensor(relvp[0:41, 0:64], relb[0:41, 128:192], pr[0:41, 0:64], ALU.subtract)
                c.inc(vector.drain(), 'p0v', 1)
            blk.vector(s_vector)


    c.dma_fence('p0z', 'p0g', 'p0gb')

    with ExitStack() as ps:
        rhsf = ps.enter_context(nc.sbuf_tensor("p0_rhsf", [128, NCH * 1024], F32))
        amx = ps.enter_context(nc.sbuf_tensor("p0_amx", [128, 1024], F32))
        rrow = ps.enter_context(nc.sbuf_tensor("p0_rrow", [1, 1024], F32))
        bcr = ps.enter_context(nc.sbuf_tensor("p0_bcr", [128, 1024], F32))
        q_ti = ps.enter_context(nc.sbuf_tensor("p0_ti", [128, 1024], I32))
        q_f = ps.enter_context(nc.sbuf_tensor("p0_f", [128, 1024], F32))
        q_d = ps.enter_context(nc.sbuf_tensor("p0_d", [128, 1024], F32))
        q_sg = ps.enter_context(nc.sbuf_tensor("p0_sg", [128, 1024], F32))
        q_o = ps.enter_context(nc.sbuf_tensor("p0_qo", [NCH, 128, 0] if False else [128, 1024], BF16))

        with nc.Block() as blk:
            def s_sync(sync):
                for ch in range(NCH):
                    c.inc(sync.dma_start(out=rhsf[:, ch * 1024:(ch + 1) * 1024],
                                         in_=wT["rhsw"].ap()[ch * 128:(ch + 1) * 128, :]), 'rld', 16)
            blk.sync(s_sync)

            def s_scalar(scalar):
                c.wait(scalar, 'rld')
                b0s = c.base('qs', 'qv')
                for ch in range(NCH):
                    if ch >= 2:
                        scalar.wait_ge(c.S('qv'), b0s['qv'] + ch - 1)
                    ab = q_f if ch % 2 == 0 else q_d
                    c.inc(scalar.activation(ab[:], rhsf[:, ch * 1024:(ch + 1) * 1024], AT.Abs), 'qs', 1)
            blk.scalar(s_scalar)

            def s_vector(vector):
                b0v = {'qs': c.cnt['qs'] - NCH, 'qv': c.cnt['qv']}
                for ch in range(NCH):
                    vector.wait_ge(c.S('qs'), b0v['qs'] + ch + 1)
                    ab = q_f if ch % 2 == 0 else q_d
                    vector.drain()
                    if ch == 0:
                        ins = vector.tensor_copy(amx[:], ab[:])
                    else:
                        ins = vector.tensor_tensor(amx[:], amx[:], ab[:], ALU.max)
                    c.inc(ins, 'qv', 1)
                part_tree(c, vector, amx, q_d, 1024)
                vector.tensor_scalar(grows[0:1, :], amx[0:1, :], 1e-12, 1.0 / 127.0, ALU.max, ALU.mult)
                vector.drain()
                c.inc(vector.reciprocal(rrow[:], grows[0:1, :]), 'rv', 1)
            blk.vector(s_vector)

            def s_gp(g):
                c.wait(g, 'rv', 1)
                bcast_row(g, c, bcr, rrow[0:1, 0:1024], 1024, 'rbc')
            blk.gpsimd(s_gp)

        with nc.Block() as blk:
            b0 = c.base('qv', 'qs', 'qst', 'qst2')

            def s_vector(vector):
                for ch in range(NCH):
                    src = rhsf[:, ch * 1024:(ch + 1) * 1024]
                    if ch == 0:
                        c.wait(vector, 'rbc')
                    if ch >= 1:
                        vector.wait_ge(c.S('qst'), b0['qst'] + 16 * ch)
                        if DBG:
                            vector.wait_ge(c.S('qst2'), b0['qst2'] + 16 * ch)
                    vector.tensor_tensor(q_d[:], src, bcr[:], ALU.mult)
                    vector.drain()
                    vector.tensor_copy(q_ti[:], q_d[:])
                    vector.drain()
                    vector.tensor_copy(q_f[:], q_ti[:])
                    vector.drain()
                    vector.tensor_tensor(q_d[:], q_d[:], q_f[:], ALU.subtract)
                    vector.drain()
                    vector.tensor_scalar(q_sg[:], q_d[:], 0.5, -0.5, ALU.min, ALU.max)
                    vector.drain()
                    c.inc(vector.tensor_tensor(q_d[:], q_d[:], q_sg[:], ALU.subtract), 'qv', 1)
                    vector.wait_ge(c.S('qs'), b0['qs'] + ch + 1)
                    vector.drain()
                    c.inc(vector.tensor_tensor(q_o[:], q_f[:], q_sg[:], ALU.add), 'qv', 1)
            blk.vector(s_vector)

            def s_scalar(scalar):
                for ch in range(NCH):
                    scalar.wait_ge(c.S('qv'), b0['qv'] + 2 * ch + 1)
                    c.inc(scalar.activation(q_sg[:], q_d[:], AT.Sign), 'qs', 1)
            blk.scalar(s_scalar)

            def s_gp(g):
                for ch in range(NCH):
                    g.wait_ge(c.S('qv'), b0['qv'] + 2 * ch + 2)
                    c.inc(g.dma_start(out=h_rhsq.ap()[ch], in_=q_o[:]), 'qst', 16)
                    if DBG:
                        c.inc(g.dma_start(out=dbg['qrhs'].ap()[ch], in_=q_o[:]), 'qst2', 16)
            blk.gpsimd(s_gp)

    c.dma_fence('qst', 'qst2', 'rbc')



    with ExitStack() as ps:
        wcf = [ps.enter_context(nc.sbuf_tensor(f"p3_wc{i}", [128, 4096], F32)) for i in range(2)]
        acc3 = ps.enter_context(nc.sbuf_tensor("p3_acc", [128, 48], F32))
        q_ti = ps.enter_context(nc.sbuf_tensor("p3_ti", [128, 4096], I32))
        q_f = ps.enter_context(nc.sbuf_tensor("p3_f", [128, 4096], F32))
        q_d = ps.enter_context(nc.sbuf_tensor("p3_d", [128, 4096], F32))
        q_sg = ps.enter_context(nc.sbuf_tensor("p3_sg", [128, 4096], F32))
        q_o = ps.enter_context(nc.sbuf_tensor("p3_qo", [128, 4096], BF16))
        prow3 = ps.enter_context(nc.psum_tensor("p3_pr", [1, 48], F32))

        items3 = ([("wgT", ch, 4096, h_wgq, ch, 0) for ch in range(NCH)]
                  + [("wuT", ch, 4096, h_wuq, 8 + ch, 1) for ch in range(NCH)]
                  + [("wdT", ch, 1024, h_wdq, 16 + ch, 2) for ch in range(NIC)])

        with nc.Block() as blk:
            b0 = c.base('lnl0', 'lnl1', 'qv')

            def s_sync(sync):
                for n, (nm, ch, w, hdst, j, mcol) in enumerate(items3):
                    if n >= 2:
                        sync.wait_ge(c.S('qv'), b0['qv'] + n - 1)
                    c.inc(sync.dma_start(out=wcf[n % 2][:, 0:w], in_=wT[nm].ap()[ch * 128:(ch + 1) * 128, :]),
                          'lnl%d' % (n % 2), 16)
            blk.sync(s_sync)

            def s_vector(vector):
                for n, (nm, ch, w, hdst, j, mcol) in enumerate(items3):
                    vector.wait_ge(c.S('lnl%d' % (n % 2)), b0['lnl%d' % (n % 2)] + 16 * (n // 2 + 1))
                    vector.drain()
                    c.inc(vector.tensor_reduce(acc3[:, j:j + 1], wcf[n % 2][:, 0:w], AX.X, ALU.add,
                                               apply_absolute_value=True), 'qv', 1)
            blk.vector(s_vector)

        with nc.Block() as blk:
            def s_tensor(tensor):
                c.inc(tensor.matmul(prow3[0:1, :], onescol_f[:], acc3[:, :], start=True, stop=True), 'pjt', 1)
            blk.tensor(s_tensor)

            def s_vector(vector):
                c.wait(vector, 'pjt')
                vector.memset(swrow[:], 0.0)
                vector.drain()
                vector.tensor_reduce(swrow[0:1, 4:5], prow3[0:1, 0:8], AX.X, ALU.add)
                vector.tensor_reduce(swrow[0:1, 5:6], prow3[0:1, 8:16], AX.X, ALU.add)
                vector.tensor_reduce(swrow[0:1, 6:7], prow3[0:1, 16:48], AX.X, ALU.add)
                vector.drain()
                vector.tensor_scalar(swrow[0:1, 4:7], swrow[0:1, 4:7], 1.0 / (1024.0 * 4096.0), 1e-5,
                                     ALU.mult, ALU.max)
                vector.drain()
                c.inc(vector.reciprocal(swrow[0:1, 0:3], swrow[0:1, 4:7]), 'rv', 1)
            blk.vector(s_vector)

            def s_gp(g):
                c.wait(g, 'rv')
                bcast_row(g, c, swcol, swrow[0:1, 0:8], 8, 'rbc')
            blk.gpsimd(s_gp)

        with nc.Block() as blk:
            b0 = c.base('lnm0', 'lnm1', 'qv', 'qs', 'qst')

            def s_sync(sync):
                c.wait(sync, 'rbc')
                for n, (nm, ch, w, hdst, j, mcol) in enumerate(items3):
                    if n >= 2:
                        sync.wait_ge(c.S('qv'), b0['qv'] + 2 * (n - 1))
                    c.inc(sync.dma_start(out=wcf[n % 2][:, 0:w], in_=wT[nm].ap()[ch * 128:(ch + 1) * 128, :]),
                          'lnm%d' % (n % 2), 16)
            blk.sync(s_sync)

            def s_vector(vector):
                c.wait(vector, 'rbc')
                for n, (nm, ch, w, hdst, j, mcol) in enumerate(items3):
                    vector.wait_ge(c.S('lnm%d' % (n % 2)), b0['lnm%d' % (n % 2)] + 16 * (n // 2 + 1))
                    vector.drain()
                    vector.tensor_scalar(q_d[:, 0:w], wcf[n % 2][:, 0:w], swcol[:, mcol:mcol + 1], None, ALU.mult)
                    vector.drain()
                    vector.tensor_scalar(q_f[:, 0:w], q_d[:, 0:w], 0.5, None, ALU.is_ge)
                    c.inc(vector.tensor_scalar(q_sg[:, 0:w], q_d[:, 0:w], -0.5, None, ALU.is_le), 'qv', 1)
                    if n >= 1:
                        nprev = 16 + 8 * max(0, n - 16) + (8 if n > 16 else 0) if False else (
                            n if n <= 16 else 16 + 8 * (n - 16))
                        vector.wait_ge(c.S('qst'), b0['qst'] + 16 * nprev)
                    vector.drain()
                    c.inc(vector.tensor_tensor(q_o[:, 0:w], q_f[:, 0:w], q_sg[:, 0:w], ALU.subtract), 'qv', 1)
            blk.vector(s_vector)

            def s_gp(g):
                nst = 0
                for n, (nm, ch, w, hdst, j, mcol) in enumerate(items3):
                    g.wait_ge(c.S('qv'), b0['qv'] + 2 * n + 2)
                    if nst >= 1:
                        g.wait_ge(c.S('qst'), b0['qst'] + 16 * nst)
                    if nm == "wdT":
                        for och in range(NCH):
                            nst += 1
                            c.inc(g.dma_start(out=h_wdq.ap()[och][ch],
                                              in_=q_o[:, och * 128:(och + 1) * 128]), 'qst', 16)
                    else:
                        nst += 1
                        c.inc(g.dma_start(out=hdst.ap()[ch], in_=q_o[:, 0:w]), 'qst', 16)
            blk.gpsimd(s_gp)
    c.dma_fence('qst', 'rbc')

    def p1a(s, xn0):
        with ExitStack() as ps:
            xc = [ps.enter_context(nc.sbuf_tensor(f"a_xc{i}_{s}", [128, T], F32)) for i in range(2)]
            sq = ps.enter_context(nc.sbuf_tensor(f"a_sq_{s}", [128, T], F32))
            amx = ps.enter_context(nc.sbuf_tensor(f"a_amx_{s}", [128, T], F32))
            abt = ps.enter_context(nc.sbuf_tensor(f"a_abt_{s}", [128, T], F32))
            bca = ps.enter_context(nc.sbuf_tensor(f"a_bca_{s}", [128, T], F32))
            bcb = ps.enter_context(nc.sbuf_tensor(f"a_bcb_{s}", [128, T], F32))
            rw = [bca, bcb, xc[0], xc[1]]
            pstat = ps.enter_context(nc.psum_tensor(f"a_pstat_{s}", [1, T], F32))
            pstatq = ps.enter_context(nc.psum_tensor(f"a_pstatq_{s}", [1, T], F32))
            rws = lambda k: rw[k][0:1, 0:T]

            with nc.Block() as blk:
                b0 = c.base('lnl0', 'lnl1', 'lnsq', 'lnu')

                def s_sync(sync):
                    for ch in range(NCH):
                        if ch >= 2:
                            sync.wait_ge(c.S('lnu'), b0['lnu'] + ch - 1)
                        c.inc(sync.dma_start(out=xc[ch % 2][:], in_=x_in.ap()[s, ch * 128:(ch + 1) * 128, :]),
                              'lnl%d' % (ch % 2), 16)
                blk.sync(s_sync)

                def s_scalar(scalar):
                    for ch in range(NCH):
                        scalar.wait_ge(c.S('lnl%d' % (ch % 2)), b0['lnl%d' % (ch % 2)] + 16 * (ch // 2 + 1))
                        if ch >= 1:
                            scalar.wait_ge(c.S('lnu'), b0['lnu'] + ch)
                        c.inc(scalar.activation(sq[:], xc[ch % 2][:], AT.Square), 'lnsq', 1)
                blk.scalar(s_scalar)

                def s_tensor(tensor):
                    for ch in range(NCH):
                        tensor.wait_ge(c.S('lnsq'), b0['lnsq'] + ch + 1)
                        for qt in range(NQT):
                            tensor.matmul(pstat[0:1, qt * 512:(qt + 1) * 512], onescol_f[:],
                                          xc[ch % 2][:, qt * 512:(qt + 1) * 512],
                                          start=(ch == 0), stop=(ch == NCH - 1))
                            ins = tensor.matmul(pstatq[0:1, qt * 512:(qt + 1) * 512], onescol_f[:],
                                                sq[:, qt * 512:(qt + 1) * 512],
                                                start=(ch == 0), stop=(ch == NCH - 1))
                        c.inc(ins, 'lnu', 1)
                blk.tensor(s_tensor)

            with nc.Block() as blk:
                def s_vector(vector):
                    vector.tensor_scalar_mul(rws(0), pstat[0:1, :], 1.0 / D)
                    vector.tensor_scalar_mul(rws(1), pstatq[0:1, :], 1.0 / D)
                    vector.drain()
                    vector.tensor_tensor(rws(2), rws(0), rws(0), ALU.mult)
                    vector.drain()
                    c.inc(vector.tensor_tensor(rws(1), rws(1), rws(2), ALU.subtract), 'lnr', 1)
                blk.vector(s_vector)

                def s_scalar(scalar):
                    c.wait(scalar, 'lnr')
                    c.inc(scalar.activation(rws(1), rws(1), AT.Sqrt, bias=cvals[0:1, 0:1]), 'lns', 1)
                blk.scalar(s_scalar)

            with nc.Block() as blk:
                def s_vector(vector):
                    c.wait(vector, 'lns')
                    vector.reciprocal(rws(2), rws(1))
                    vector.drain()
                    vector.tensor_tensor(rws(3), rws(0), rws(2), ALU.mult)
                    vector.drain()
                    c.inc(vector.tensor_scalar_mul(rws(3), rws(3), -1.0), 'lnr', 1)
                blk.vector(s_vector)

                def s_gp(g):
                    c.wait(g, 'lnr')
                    c.inc(g.dma_start(out=rows[8 * s + R_A0:8 * s + R_A0 + 1, :], in_=rws(2)), 'lnpk', 16)
                    c.inc(g.dma_start(out=rows[8 * s + R_B0:8 * s + R_B0 + 1, :], in_=rws(3)), 'lnpk', 16)
                    c.wait(g, 'lnpk')
                    bcast_row(g, c, bca, rws(2), T, 'lnbc')
                    bcast_row(g, c, bcb, rws(3), T, 'lnbc')
                blk.gpsimd(s_gp)

            with nc.Block() as blk:
                b0 = c.base('lnm0', 'lnm1', 'lnx', 'lnq')

                def s_sync(sync):
                    c.wait(sync, 'lnbc')
                    c.wait(sync, 'lnpk')
                    for ch in range(NCH):
                        if ch >= 2:
                            sync.wait_ge(c.S('lnx'), b0['lnx'] + 2 * (ch - 1))
                        c.inc(sync.dma_start(out=xc[ch % 2][:], in_=x_in.ap()[s, ch * 128:(ch + 1) * 128, :]),
                              'lnm%d' % (ch % 2), 16)
                blk.sync(s_sync)

                def s_vector(vector):
                    c.wait(vector, 'lnbc')
                    for ch in range(NCH):
                        vector.wait_ge(c.S('lnm%d' % (ch % 2)), b0['lnm%d' % (ch % 2)] + 16 * (ch // 2 + 1))
                        vector.drain()
                        vector.tensor_tensor(sq[:], xc[ch % 2][:], bca[:], ALU.mult)
                        vector.drain()
                        vector.tensor_tensor(sq[:], sq[:], bcb[:], ALU.add)
                        vector.drain()
                        c.inc(vector.tensor_copy(xn0[:, ch * T:(ch + 1) * T], sq[:]), 'lnx', 1)
                        vector.wait_ge(c.S('lnq'), b0['lnq'] + ch + 1)
                        vector.drain()
                        if ch == 0:
                            ins = vector.tensor_copy(amx[:], abt[:])
                        else:
                            ins = vector.tensor_tensor(amx[:], amx[:], abt[:], ALU.max)
                        c.inc(ins, 'lnx', 1)
                blk.vector(s_vector)

                def s_scalar(scalar):
                    for ch in range(NCH):
                        scalar.wait_ge(c.S('lnx'), b0['lnx'] + 2 * ch + 1)
                        c.inc(scalar.activation(abt[:], sq[:], AT.Abs), 'lnq', 1)
                blk.scalar(s_scalar)

            with nc.Block() as blk:
                def s_vector(vector):
                    part_tree(c, vector, amx, sq, T)
                    c.inc(vector.tensor_scalar(amx[0:1, 0:T], amx[0:1, :], 1e-12, 1.0 / 127.0,
                                               ALU.max, ALU.mult), 'lnd', 1)
                blk.vector(s_vector)

                def s_gp(g):
                    c.wait(g, 'lnd')
                    c.inc(g.dma_start(out=rows[8 * s + R_LS0:8 * s + R_LS0 + 1, :], in_=amx[0:1, 0:T]), 'lnbc', 16)
                    if DBG:
                        for ch in range(NCH):
                            c.inc(g.dma_start(out=dbg['ln0'].ap()[s, ch], in_=xn0[:, ch * T:(ch + 1) * T]), 'dbg', 16)
                        c.wait(g, 'lnbc')
                        c.inc(g.dma_start(out=dbg['rows'].ap()[s], in_=rows[:, :]), 'dbg', 16)
                blk.gpsimd(s_gp)

            c.dma_fence('lnbc', 'dbg')


    def l0_proj(s, xn0, w_qkv):
        with ExitStack() as ps:
            qtev = ps.enter_context(nc.sbuf_tensor(f"b_qtev_{s}", [128, T], BF16))
            ktev = ps.enter_context(nc.sbuf_tensor(f"b_ktev_{s}", [128, T], BF16))
            vtmp = ps.enter_context(nc.sbuf_tensor(f"b_vtmp_{s}", [128, T], BF16))
            vatm = ps.enter_context(nc.sbuf_tensor(f"b_vatm_{s}", [128, T], BF16))
            bstk = ExitStack()
            blk_all = bstk.enter_context(nc.Block())
            for g in range(8):
                gps = ExitStack()
                pQ = gps.enter_context(nc.psum_tensor(f"b_pQ_{s}_{g}", [128, T], F32))
                pK = gps.enter_context(nc.psum_tensor(f"b_pK_{s}_{g}", [128, T], F32))
                if True:
                    blk = blk_all
                    def s_tensor(tensor, g=g):
                        c.wait(tensor, 'pjv')
                        for i in range(NCH):
                            wq = w_qkv[0][:, i * D + g * 128:i * D + (g + 1) * 128]
                            wk = w_qkv[1][:, i * D + g * 128:i * D + (g + 1) * 128]
                            for qt in range(NQT):
                                xsl = xn0[:, i * T + qt * 512:i * T + (qt + 1) * 512]
                                tensor.matmul(pQ[:, qt * 512:(qt + 1) * 512], wq, xsl,
                                              start=(i == 0), stop=(i == NCH - 1))
                                ins = tensor.matmul(pK[:, qt * 512:(qt + 1) * 512], wk, xsl,
                                                    start=(i == 0), stop=(i == NCH - 1))
                        c.inc(ins, 'pjt', 1)
                    blk.tensor(s_tensor)

                    def s_scalar(scalar, g=g):
                        c.wait(scalar, 'pjt')
                        c.wait(scalar, 'pjk')
                        scalar.activation(qtev[:], pQ[:], AT.Identity, bias=bias_sb[:, 0 * NCH + g:0 * NCH + g + 1])
                        c.inc(scalar.activation(ktev[:], pK[:], AT.Identity, bias=bias_sb[:, 1 * NCH + g:1 * NCH + g + 1]), 'pjs', 1)
                    blk.scalar(s_scalar)

                    def s_gp(gp, g=g):
                        c.wait(gp, 'pjs')
                        c.inc(gp.dma_start(out=h_qt.ap()[2 * g], in_=qtev[0:64, :]), 'pjk', 16)
                        c.inc(gp.dma_start(out=h_qt.ap()[2 * g + 1], in_=qtev[64:128, :]), 'pjk', 16)
                        c.inc(gp.dma_start(out=h_kt.ap()[2 * g], in_=ktev[0:64, :]), 'pjk', 16)
                        c.inc(gp.dma_start(out=h_kt.ap()[2 * g + 1], in_=ktev[64:128, :]), 'pjk', 16)
                    blk.gpsimd(s_gp)

                gps.close()
                gps = ExitStack()
                pQ = gps.enter_context(nc.psum_tensor(f"b_pV_{s}_{g}", [128, T], F32))
                pVt = gps.enter_context(nc.psum_tensor(f"b_pVt_{s}_{g}", [128, T], BF16))
                if True:
                    def s_tensor(tensor, g=g):
                        c.wait(tensor, 'pjs')
                        for i in range(NCH):
                            wv = w_qkv[2][:, i * D + g * 128:i * D + (g + 1) * 128]
                            for qt in range(NQT):
                                xsl = xn0[:, i * T + qt * 512:i * T + (qt + 1) * 512]
                                ins = tensor.matmul(pQ[:, qt * 512:(qt + 1) * 512], wv, xsl,
                                                    start=(i == 0), stop=(i == NCH - 1))
                        c.inc(ins, 'pjt', 1)
                    blk.tensor(s_tensor)

                    def s_scalar(scalar, g=g):
                        c.wait(scalar, 'pjt')
                        c.inc(scalar.activation(vtmp[:], pQ[:], AT.Identity, bias=bias_sb[:, 2 * NCH + g:2 * NCH + g + 1]), 'pjs', 1)
                    blk.scalar(s_scalar)

                if True:
                    def s_tensor(tensor, g=g):
                        c.wait(tensor, 'pjs')
                        for tc in range(NTC):
                            ins = tensor.transpose(pVt[:, tc * 128:(tc + 1) * 128],
                                                   vtmp[:, tc * 128:(tc + 1) * 128], ident[:])
                        c.inc(ins, 'pjt', 1)
                    blk.tensor(s_tensor)

                    def s_vector(vector, g=g):
                        c.wait(vector, 'pjt')
                        c.wait(vector, 'pjk')
                        c.inc(vector.tensor_copy(vatm[:], pVt[:]), 'pjv', 1)
                    blk.vector(s_vector)

                    def s_gp(gp, g=g):
                        c.wait(gp, 'pjv')
                        for k in range(2):
                            src = AP(vatm, 64 * k, [[T, 128], [128, NTC], [1, 64]])
                            dst = AP(h_va.ap().tensor, (2 * g + k) * 65,
                                     [[1040, 128], [128 * 1040, NTC], [1, 64]])
                            c.inc(gp.dma_start(out=dst, in_=src), 'pjk', 16)
                    blk.gpsimd(s_gp)
                gps.close()
            bstk.close()
            c.dma_fence('pjk')

    def l0_r1(s, h, bufs, pst, blk_all):
        QTh, KTh, Vah, pslab, cstr, ptil = bufs[h % 2]
        psPh, ptp = pst['psPh'], pst['ptp']

        if True:
            def s_sync(sync):
                c.wait(sync, 'sct')
                c.wait(sync, 'kbv')
                c.wait(sync, 'avt')
                c.inc(sync.dma_start(out=QTh[:], in_=h_qt.ap()[h]), 'hld', 16)
                c.inc(sync.dma_start(out=KTh[:], in_=h_kt.ap()[h]), 'hld', 16)
                src = AP(h_va.ap().tensor, h * 65, [[1040, 128], [128 * 1040, NTC], [1, 65]])
                c.inc(sync.dma_start(out=Vah[:].rearrange("p (c u) -> p c u", u=65), in_=src), 'hld', 16)
            blk_all.sync(s_sync)

            def s_tensor(tensor):
                b0v = c.base('rlv')
                c.wait(tensor, 'hld')
                c.wait(tensor, 'rlv')
                for q4 in range(4):
                    if q4 >= 1:
                        tensor.wait_ge(c.S('rlv'), b0v['rlv'] + q4)
                    ins = tensor.matmul(psPh[:, 0:512], relk_m40T[:],
                                        QTh[:, q4 * 512:(q4 + 1) * 512], start=True, stop=True)
                    c.inc(ins, 'rlt', 1)
            blk_all.tensor(s_tensor)

            def s_vector(vector):
                b0r = c.base('rlt')
                c.wait(vector, 'w2t')
                c.wait(vector, 'rcw')
                vector.memset(pslab[0:41, 2048:2208], 0.0)
                vector.drain()
                for q4 in range(4):
                    vector.wait_ge(c.S('rlt'), b0r['rlt'] + q4 - 3)
                    vector.drain()
                    c.inc(vector.tensor_copy(pslab[0:41, q4 * 512:(q4 + 1) * 512], psPh[0:41, :]), 'rlv', 1)
            blk_all.vector(s_vector)

            def s_sync(sync):
                c.wait(sync, 'rlv')
                c.inc(sync.dma_start(out=h_ps.ap()[h % 2][0:41, 21:2069], in_=pslab[0:41, 0:2048]), 'rps', 16)
                c.inc(sync.dma_start(out=h_gp.ap()[h % 2][0:1, 0:2208], in_=pslab[40:41, :]), 'rps', 16)
                c.wait(sync, 'rps')
                c.inc(sync.dma_start(out=pslab[0:41, 0:2089],
                                     in_=AP(h_ps.ap().tensor, (h % 2) * 41 * 2130 + 1, [[2131, 41], [1, 2089]])),
                      'rps', 16)
                c.inc(sync.dma_start(
                    out=AP(h_c.ap().tensor, (h % 2) * NTC * 128 * 464 + 169, [[59392, NTC], [464, 128], [1, 127]]),
                    in_=AP(h_gp.ap().tensor, (h % 2) * 2208 + 21, [[128, NTC], [1, 128], [1, 127]])), 'rcw', 16)
            blk_all.sync(s_sync)

    def l0_r1b(s, h, bufs, pst, blk_all):
        QTh, KTh, Vah, pslab, cstr, ptil = bufs[h % 2]
        psPh, ptp = pst['psPh'], pst['ptp']
        if True:
            def s_tensor(tensor):
                c.wait(tensor, 'rps')
                for ci in range(NTC):
                    ins = tensor.transpose(ptp[:, ci * 42:ci * 42 + 41],
                                           pslab[0:41, ci * 128:(ci + 1) * 128], ident[0:41, 0:41])
                c.inc(ins, 'rlt', 1)
            blk_all.tensor(s_tensor)

            def s_vector(vector):
                c.wait(vector, 'rlt')
                c.inc(vector.tensor_copy(
                    AP(ptil, 0, [[672, 128], [42, NTC], [1, 41]]),
                    AP(ptp, 0, [[672, 128], [42, NTC], [1, 41]])), 'rlv', 1)
            blk_all.vector(s_vector)

            def s_sync(sync):
                c.wait(sync, 'rlv')
                c.inc(sync.dma_start(
                    out=AP(h_c.ap().tensor, (h % 2) * NTC * 128 * 464 + 128, [[464, 128], [59392, NTC], [1, 41]]),
                    in_=AP(ptil, 0, [[672, 128], [42, NTC], [1, 41]])), 'rcw', 16)
                c.wait(sync, 'rcw')
                hb = (h % 2) * NTC * 128 * 464
                c.inc(sync.dma_start(out=cstr[:, 0:148],
                                     in_=AP(h_c.ap().tensor, hb + 148, [[463, 128], [1, 148]])), 'ccr0', 16)
                bulk_src = AP(h_c.ap().tensor, hb + 59392 + 128, [[463, 128], [59392, 14], [1, 168]])
                bulk_dst = AP(cstr, 168, [[16 * 168, 128], [168, 14], [1, 168]])
                c.inc(sync.dma_start(out=bulk_dst, in_=bulk_src), 'ccr0', 16)
                c.inc(sync.dma_start(out=cstr[:, 15 * 168:15 * 168 + 148],
                                     in_=AP(h_c.ap().tensor, hb + 15 * 59392 + 128, [[463, 128], [1, 148]])),
                      'ccr0', 16)
            blk_all.sync(s_sync)

    def l0_r234(s, h, bufs, pst, slabO, Zs, Aslab, kw, h4t, stg, zrow, zrowb, zsnb, blk_all, mid_hook=None):
        QTh, KTh, Vah, pslab, cstr, ptil = bufs[h % 2]
        pscA, pscB, psPh, ptp = pst['pscA'], pst['pscB'], pst['psPh'], pst['ptp']
        psc = [pscA, pscB]
        dgt2 = [stg[:, 3072:4608], stg[:, 4608:6144]]
        zsg = pslab[0:1, 0:1536]

        if True:
            b0 = c.base('sct', 'scs', 'kbv', 'ccr0')

            def s_vector(vector):
                for ci in range(NTC):
                    if ci >= 1:
                        vector.wait_ge(c.S('sct'), b0['sct'] + 2 * ci)
                    else:
                        vector.wait_ge(c.S('sct'), b0['sct'])
                    vector.tensor_scalar(kw[:, 0:128], KTh[:, ci * 128:(ci + 1) * 128],
                                         relcols[:, 0:1], None, ALU.add)
                    c.inc(vector.tensor_scalar(kw[:, 128:256], KTh[:, ci * 128:(ci + 1) * 128],
                                               relcols[:, 1:2], None, ALU.add), 'kbv', 1)
            blk_all.vector(s_vector)

            def s_tensor(tensor):
                tensor.wait_ge(c.S('ccr0'), b0['ccr0'])
                c.wait(tensor, 'avo')
                for ci in range(NTC):
                    k0 = ci * 128
                    tensor.wait_ge(c.S('kbv'), b0['kbv'] + ci + 1)
                    hi40 = min(k0 + 148, T)
                    lo = max(k0 - 20, 0)
                    for hf in range(2):
                        n = 2 * ci + hf
                        if n >= 2:
                            tensor.wait_ge(c.S('scs'), b0['scs'] + n - 1)
                        pc_ = psc[n % 2]
                        ins = None
                        for q in range(2):
                            w0, w1 = hf * 1024 + q * 512, hf * 1024 + (q + 1) * 512
                            pq0 = q * 512
                            a0, a1 = w0, max(min(lo, w1), w0)
                            if a1 > a0:
                                ins = tensor.matmul(pc_[:, pq0 + (a0 - w0):pq0 + (a1 - w0)], kw[:, 0:128],
                                                    QTh[:, a0:a1], start=True, stop=True)
                            c0_, c1_ = max(lo, w0), min(hi40, w1)
                            if c1_ > c0_:
                                tensor.matmul(pc_[:, pq0 + (c0_ - w0):pq0 + (c1_ - w0)], kw[:, 0:128],
                                              QTh[:, c0_:c1_], start=True, stop=False)
                                ins = tensor.matmul(pc_[:, pq0 + (c0_ - w0):pq0 + (c1_ - w0)], ident[:],
                                                    cstr[:, ci * 168 + (c0_ - lo):ci * 168 + (c1_ - lo)],
                                                    start=False, stop=True)
                            b0_, b1_ = max(hi40, w0), w1
                            if b1_ > b0_:
                                ins = tensor.matmul(pc_[:, pq0 + (b0_ - w0):pq0 + (b1_ - w0)], kw[:, 128:256],
                                                    QTh[:, b0_:b1_], start=True, stop=True)
                        c.inc(ins, 'sct', 1)
            blk_all.tensor(s_tensor)

            def s_scalar(scalar):
                for n in range(2 * NTC):
                    ci, hf = n // 2, n % 2
                    scalar.wait_ge(c.S('sct'), b0['sct'] + n + 1)
                    c.inc(scalar.activation(Aslab[:, ci * T + hf * 1024:ci * T + (hf + 1) * 1024],
                                            psc[n % 2][:], AT.Exp, scale=0.125), 'scs', 1)
            blk_all.scalar(s_scalar)

        if True:
            ptp2 = pst['ptp2']
            if True:
                b0 = c.base('dsw', 'h4r0', 'h4r1', 'h4v', 'h4t', 'h4e', 'h6w', 'h6g')

                def s_sync(sync):
                    c.wait(sync, 'dgr0')
                    c.wait(sync, 'dgr1')
                    c.inc(sync.dma_start(out=AP(h_h6.ap().tensor, 0, [[2625, 276], [1, 256]]),
                                         in_=AP(h_zz.ap().tensor, 0, [[1, 276], [1, 256]])), 'h6w', 16)
                    c.inc(sync.dma_start(out=AP(h_h6.ap().tensor, 2304, [[2625, 276], [1, 32]]),
                                         in_=AP(h_zz.ap().tensor, 256, [[1, 276], [1, 32]])), 'h6w', 16)
                    b0scs = c.cnt['scs'] - 2 * NTC
                    for ci in range(NTC):
                        k0 = ci * 128
                        lo = max(k0 - 20, 0)
                        hi = min(k0 + 383, T)
                        w0 = lo - (k0 - 20)
                        sync.wait_ge(c.S('scs'), b0scs + 2 * ci + 2)
                        c.inc(sync.dma_start(out=h_strip.ap()[ci, :, w0:w0 + hi - lo],
                                             in_=Aslab[:, ci * T + lo:ci * T + hi]), 'dsw', 16)
                    c.wait(sync, 'dsw')
                    for ci in range(NTC):
                        if ci >= 2:
                            sync.wait_ge(c.S('h4t'), b0['h4t'] + ci - 1)
                        src = AP(h_strip.ap().tensor, ci * 128 * 404, [[405, 128], [1, 276]])
                        c.inc(sync.dma_start(out=h4t[:, (ci % 2) * 276:(ci % 2 + 1) * 276], in_=src),
                              'h4r%d' % (ci % 2), 16)
                blk_all.sync(s_sync)

                def s_vector(vector):
                    c.wait(vector, 'h6g')
                    c.wait(vector, 'w2t')
                    for ci in range(NTC):
                        vector.wait_ge(c.S('h4r%d' % (ci % 2)), b0['h4r%d' % (ci % 2)] + 16 * (ci // 2 + 1))
                        if ci >= 2:
                            vector.wait_ge(c.S('h4t'), b0['h4t'] + ci - 1)
                        sl = h4t[:, (ci % 2) * 276:(ci % 2 + 1) * 276]
                        c.inc(vector.tensor_tensor(sl, sl, adm[:], ALU.mult), 'h4v', 1)
                        if ci >= 1:
                            vector.wait_ge(c.S('h4t'), b0['h4t'] + ci)
                            cj = (ci - 1) * 128
                            vector.tensor_copy(stg[:, cj:cj + 128], ptp2[:, 0:128])
                            vector.tensor_copy(stg[:, 2048 + cj:2048 + cj + 128], ptp2[:, 128:256])
                            c.inc(vector.tensor_copy(stg[0:20, 4096 + cj:4096 + cj + 128],
                                                     ptp2[0:20, 256:384]), 'h4e', 1)
                    vector.wait_ge(c.S('h4t'), b0['h4t'] + NTC)
                    cj = (NTC - 1) * 128
                    vector.tensor_copy(stg[:, cj:cj + 128], ptp2[:, 0:128])
                    vector.tensor_copy(stg[:, 2048 + cj:2048 + cj + 128], ptp2[:, 128:256])
                    c.inc(vector.tensor_copy(stg[0:20, 4096 + cj:4096 + cj + 128],
                                             ptp2[0:20, 256:384]), 'h4e', 1)
                blk_all.vector(s_vector)

                def s_tensor(tensor):
                    for ci in range(NTC):
                        tensor.wait_ge(c.S('h4v'), b0['h4v'] + ci + 1)
                        if ci >= 1:
                            tensor.wait_ge(c.S('h4e'), b0['h4e'] + ci)
                        sl = h4t[:, (ci % 2) * 276:(ci % 2 + 1) * 276]
                        tensor.transpose(ptp2[:, 0:128], sl[:, 0:128], ident[:])
                        tensor.transpose(ptp2[:, 128:256], sl[:, 128:256], ident[:])
                        ins = tensor.transpose(ptp2[0:20, 256:384], sl[:, 256:276], ident[:])
                        c.inc(ins, 'h4t', 1)
                blk_all.tensor(s_tensor)

                def s_gp(gp):
                    gp.wait_ge(c.S('h4e'), b0['h4e'] + NTC)
                    for pi, (u0, nu) in enumerate(((0, 128), (128, 128), (256, 20))):
                        if pi >= 1:
                            gp.wait_ge(c.S('h6g'), b0['h6g'] + 16 * pi)
                        dst = AP(h_h6.ap().tensor, u0 * 2625 + 256, [[2625, nu], [1, 2048]])
                        srcs = AP(stg, pi * 2048, [[3 * 2048, nu], [1, 2048]])
                        c.inc(gp.dma_start(out=dst, in_=srcs), 'h6g', 16)
                blk_all.gpsimd(s_gp)

        if mid_hook is not None:
            mid_hook()
        if True:
            pOa = pscA[0:65, 0:512]
            pOm = pscA[0:65, 512:1024]
            pOw = pscB[0:64, 0:512]
            sng = [pscB[0:1, 512:896], pst['pZ3'][0:1, 0:256], pscB[0:1, 896:1024]]
            b0 = c.base('avt', 'avv', 'avz', 'avo', 'snv', 'dgr0', 'dgr1', 'w2t', 'h6w', 'h6g', 'scs')

            def s_sync(sync):
                sync.wait_ge(c.S('h6w'), b0['h6w'])
                sync.wait_ge(c.S('h6g'), b0['h6g'])
                for qt in range(NQT):
                    if qt >= 2:
                        sync.wait_ge(c.S('w2t'), b0['w2t'] + qt - 1)
                    for pi, (u0, nu) in enumerate(((0, 128), (128, 128), (256, 20))):
                        src = AP(h_h6.ap().tensor, u0 * 2624 + 276 + qt * 512, [[2624, nu], [1, 512]])
                        c.inc(sync.dma_start(out=dgt2[qt % 2][0:nu, pi * 512:pi * 512 + 512], in_=src),
                              'dgr%d' % (qt % 2), 16)
            blk_all.sync(s_sync)

            def s_tensor(tensor):
                tensor.wait_ge(c.S('scs'), b0['scs'])
                for qt in range(NQT):
                    w0 = qt * 512
                    cb = 4 * qt - 2
                    if qt >= 1:
                        tensor.wait_ge(c.S('avo'), b0['avo'] + qt)
                    for ci in range(0, cb + 1):
                        tensor.matmul(pOa, Vah[:, ci * 65:(ci + 1) * 65],
                                      Aslab[:, ci * T + w0:ci * T + w0 + 512],
                                      start=(ci == 0), stop=(ci == cb))
                    for i in range(3):
                        cs = 4 * qt - 1 + i
                        if cs < 0:
                            continue
                        col0 = 128 * (1 + i)
                        tensor.matmul(sng[i][0:1, 0:512 - col0], onescol_b[:],
                                      Aslab[:, cs * T + w0 + col0:cs * T + w0 + 512],
                                      start=True, stop=True)
                    cm0 = max(4 * qt - 1, 0)
                    for ci in range(cm0, NTC):
                        ins = tensor.matmul(pOm, Vah[:, ci * 65:(ci + 1) * 65],
                                            Aslab[:, ci * T + w0:ci * T + w0 + 512],
                                            start=(ci == cm0), stop=(ci == NTC - 1))
                    c.inc(ins, 'avt', 1)
                    tensor.wait_ge(c.S('avv'), b0['avv'] + 2 * qt + 1)
                    tensor.wait_ge(c.S('snv'), b0['snv'] + qt + 1)
                    tensor.wait_ge(c.S('dgr%d' % (qt % 2)), b0['dgr%d' % (qt % 2)] + 48 * (qt // 2 + 1))
                    tensor.matmul(pOw, relv40r[:], zrowb[:], start=True, stop=False)
                    if cb >= 0:
                        tensor.matmul(pOw, dvrow[:], zsnb[0:1, 0:512], start=False, stop=False)
                    for i in range(3):
                        cs = 4 * qt - 1 + i
                        if cs < 0:
                            continue
                        col0 = 128 * (1 + i)
                        tensor.matmul(pOw[0:64, col0:512], dvrow[:],
                                      zsg[0:1, 512 * i:512 * i + (512 - col0)],
                                      start=False, stop=False)
                    dg_ = dgt2[qt % 2]
                    tensor.matmul(pOw, relvp[:, 0:64], dg_[0:128, 0:512], start=False, stop=False)
                    tensor.matmul(pOw, relvp[:, 64:128], dg_[0:128, 512:1024], start=False, stop=False)
                    ins = tensor.matmul(pOw[0:20, 0:512] if False else pOw, relvp[0:20, 64:128],
                                        dg_[0:20, 1024:1536], start=False, stop=True)
                    c.inc(ins, 'w2t', 1)
            blk_all.tensor(s_tensor)

            def s_vector(vector):
                for qt in range(NQT):
                    cb = 4 * qt - 2
                    vector.wait_ge(c.S('avt'), b0['avt'] + qt + 1)
                    if qt >= 1:
                        vector.wait_ge(c.S('avz'), b0['avz'] + 16 * qt)
                    else:
                        vector.wait_ge(c.S('avz'), b0['avz'])
                    vector.drain()
                    if cb >= 0:
                        vector.tensor_copy(zsnb[0:1, 0:512], pOa[64:65, :])
                    for i in range(3):
                        cs = 4 * qt - 1 + i
                        if cs < 0:
                            continue
                        col0 = 128 * (1 + i)
                        vector.tensor_copy(zsg[0:1, 512 * i:512 * i + (512 - col0)],
                                           sng[i][0:1, 0:512 - col0])
                    vector.drain()
                    if cb >= 0:
                        vector.tensor_copy(zrow[:], pOa[64:65, :])
                        vector.drain()
                        vector.tensor_tensor(zrow[:], zrow[:], pOm[64:65, :], ALU.add)
                    else:
                        vector.tensor_copy(zrow[:], pOm[64:65, :])
                    vector.drain()
                    c.inc(vector.tensor_copy(zrowb[:], zrow[:]), 'avv', 1)
                    c.inc(vector.drain(), 'snv', 1)
                    c.inc(vector.drain(), 'avv', 1)
                    vector.wait_ge(c.S('w2t'), b0['w2t'] + qt + 1)
                    vector.drain()
                    sl = slabO[(h % 2) * 64:(h % 2) * 64 + 64,
                               (h // 2) * T + qt * 512:(h // 2) * T + (qt + 1) * 512]
                    if cb >= 0:
                        vector.tensor_copy(sl, pOa[0:64, :])
                        vector.drain()
                        vector.tensor_tensor(sl, sl, pOm[0:64, :], ALU.add)
                    else:
                        vector.tensor_copy(sl, pOm[0:64, :])
                    vector.drain()
                    vector.tensor_tensor(sl, sl, pOw, ALU.add)
                    c.inc(vector.drain(), 'avo', 1)
            blk_all.vector(s_vector)

            def s_gp(gp):
                for qt in range(NQT):
                    gp.wait_ge(c.S('avv'), b0['avv'] + 2 * qt + 2)
                    c.inc(gp.dma_start(out=Zs[h:h + 1, qt * 512:(qt + 1) * 512], in_=zrow[:]), 'avz', 16)
            blk_all.gpsimd(s_gp)

    def l0_attn(s, s0slab):
        with ExitStack() as ps:
            slabO = ps.enter_context(nc.sbuf_tensor(f"slabO_{s}", [128, NCH * T], BF16))
            Zs = ps.enter_context(nc.sbuf_tensor(f"b_zs_{s}", [16, T], F32))
            htil = ExitStack()
            Aslab = htil.enter_context(nc.sbuf_tensor(f"b_aslab_{s}", [128, NTC * T], BF16))
            kw = htil.enter_context(nc.sbuf_tensor(f"b_kw_{s}", [64, 256], BF16))
            h4t = htil.enter_context(nc.sbuf_tensor(f"b_h4t_{s}", [128, 2 * 276], BF16))
            stg = htil.enter_context(nc.sbuf_tensor(f"b_stg_{s}", [128, 3 * 2048], BF16))
            zrow = htil.enter_context(nc.sbuf_tensor(f"b_zrow_{s}", [1, 512], F32))
            zrowb = htil.enter_context(nc.sbuf_tensor(f"b_zrowb_{s}", [1, 512], BF16))
            zsnb = htil.enter_context(nc.sbuf_tensor(f"b_zsnb_{s}", [1, 512], BF16))
            bufs = []
            for par in range(2):
                qkt = htil.enter_context(nc.sbuf_tensor(f"b_qkt{par}_{s}", [128, T], BF16))
                QTh, KTh = qkt[0:64, :], qkt[64:128, :]
                Vah = htil.enter_context(nc.sbuf_tensor(f"b_vah{par}_{s}", [128, NTC * 65], BF16))
                pslab = htil.enter_context(nc.sbuf_tensor(f"b_pslab{par}_{s}", [41, 2208], BF16))
                cstr = htil.enter_context(nc.sbuf_tensor(f"b_cstr{par}_{s}", [128, 16 * 168], BF16))
                ptil = htil.enter_context(nc.sbuf_tensor(f"b_ptil{par}_{s}", [128, 672], BF16))
                bufs.append((QTh, KTh, Vah, pslab, cstr, ptil))
            pstk = ExitStack()
            pst = {
                'pscA': pstk.enter_context(nc.psum_tensor(f"b_pscA_{s}", [128, 1024], F32)),
                'pscB': pstk.enter_context(nc.psum_tensor(f"b_pscB_{s}", [128, 1024], F32)),
                'psPh': pstk.enter_context(nc.psum_tensor(f"b_psPh_{s}", [41, 512], F32)),
                'ptp': pstk.enter_context(nc.psum_tensor(f"b_ptp_{s}", [128, 672], BF16)),
                'ptp2': pstk.enter_context(nc.psum_tensor(f"b_ptp2_{s}", [128, 384], BF16)),
                'pZ3': pstk.enter_context(nc.psum_tensor(f"b_pZ3_{s}", [1, 256], F32)),
            }
            hstk = ExitStack()
            blk_heads = hstk.enter_context(nc.Block())
            with nc.named_scope("l0head"):
                l0_r1(s, 0, bufs, pst, blk_heads)
                l0_r1b(s, 0, bufs, pst, blk_heads)
                for h in range(H0):
                    if h + 1 < H0:
                        l0_r1(s, h + 1, bufs, pst, blk_heads)
                        mid = (lambda hh: (lambda: l0_r1b(s, hh, bufs, pst, blk_heads)))(h + 1)
                    else:
                        mid = None
                    l0_r234(s, h, bufs, pst, slabO, Zs, Aslab, kw, h4t, stg,
                            zrow, zrowb, zsnb, blk_heads, mid_hook=mid)
            hstk.close()
            pstk.close()
            c.dma_fence('avz')
            htil.close()

            if DBG:
                with nc.Block() as blk:
                    def s_gp(gp):
                        for ch in range(NCH):
                            c.inc(gp.dma_start(out=dbg['oslab'].ap()[:, ch * T:(ch + 1) * T],
                                               in_=slabO[:, ch * T:(ch + 1) * T]), 'dbg', 16)
                        c.inc(gp.dma_start(out=dbg['zs'].ap()[:], in_=Zs[:]), 'dbg', 16)
                    blk.gpsimd(s_gp)

            with nc.Block() as blk:
                def s_vector(vector):
                    if DBG:
                        c.wait(vector, 'dbg')
                    c.inc(vector.reciprocal(Zs[:], Zs[:]), 'nzv', 1)
                blk.vector(s_vector)

            with ExitStack() as pnr:
                prz = [pnr.enter_context(nc.psum_tensor(f"b_prz{i}_{s}", [128, T], F32)) for i in range(2)]
                with nc.Block() as blk:
                    b0 = c.base('nzt', 'nzv')

                    def s_tensor(tensor):
                        c.wait(tensor, 'nzv')
                        for vch in range(NCH):
                            if vch >= 2:
                                tensor.wait_ge(c.S('nzv'), b0['nzv'] + vch - 1)
                            for qt in range(NQT):
                                ins = tensor.matmul(prz[vch % 2][:, qt * 512:(qt + 1) * 512],
                                                    ehd[0:16, vch * 128:(vch + 1) * 128],
                                                    Zs[0:16, qt * 512:(qt + 1) * 512],
                                                    start=True, stop=True)
                            c.inc(ins, 'nzt', 1)
                    blk.tensor(s_tensor)

                    def s_vector(vector):
                        for vch in range(NCH):
                            vector.wait_ge(c.S('nzt'), b0['nzt'] + vch + 1)
                            vector.drain()
                            sl = slabO[:, vch * T:(vch + 1) * T]
                            c.inc(vector.tensor_tensor(sl, sl, prz[vch % 2][:], ALU.mult), 'nzv', 1)
                    blk.vector(s_vector)

            wo_b = ps.enter_context(nc.sbuf_tensor(f"b_wo_{s}", [128, NCH * D], BF16))
            f32a = ps.enter_context(nc.sbuf_tensor(f"b_wof_{s}", [128, 1024], F32))
            f32b = ps.enter_context(nc.sbuf_tensor(f"b_wof2_{s}", [128, 1024], F32))
            pP = ps.enter_context(nc.psum_tensor(f"b_pP_{s}", [128, T], F32))
            with nc.Block() as blk:
                b0 = c.base('wol0', 'wol1', 'woc')

                def s_sync(sync):
                    for n in range(NCH):
                        if n >= 2:
                            sync.wait_ge(c.S('woc'), b0['woc'] + n - 1)
                        buf = f32a if n % 2 == 0 else f32b
                        c.inc(sync.dma_start(out=buf[:], in_=wT["woT"].ap()[n * 128:(n + 1) * 128, :]),
                              'wol%d' % (n % 2), 16)
                blk.sync(s_sync)

                def s_vector(vector):
                    for n in range(NCH):
                        vector.wait_ge(c.S('wol%d' % (n % 2)), b0['wol%d' % (n % 2)] + 16 * (n // 2 + 1))
                        buf = f32a if n % 2 == 0 else f32b
                        vector.drain()
                        c.inc(vector.tensor_copy(wo_b[:, n * D:(n + 1) * D], buf[:]), 'woc', 1)
                blk.vector(s_vector)

            with nc.Block() as blk:
                b0 = c.base('opt', 'ops')

                def s_tensor(tensor):
                    c.wait(tensor, 'woc')
                    c.wait(tensor, 'nzv')
                    for och in range(NCH):
                        if och >= 1:
                            tensor.wait_ge(c.S('ops'), b0['ops'] + och)
                        for vch in range(NCH):
                            wsl = wo_b[:, vch * D + och * 128:vch * D + (och + 1) * 128]
                            for qt in range(NQT):
                                ins = tensor.matmul(pP[:, qt * 512:(qt + 1) * 512], wsl,
                                                    slabO[:, vch * T + qt * 512:vch * T + (qt + 1) * 512],
                                                    start=(vch == 0), stop=(vch == NCH - 1))
                        c.inc(ins, 'opt', 1)
                blk.tensor(s_tensor)

                def s_scalar(scalar):
                    for och in range(NCH):
                        scalar.wait_ge(c.S('opt'), b0['opt'] + och + 1)
                        c.inc(scalar.activation(s0slab[:, och * T:(och + 1) * T], pP[:], AT.Identity,
                                                bias=bias_sb[:, 3 * NCH + och:3 * NCH + och + 1]), 'ops', 1)
                blk.scalar(s_scalar)

            if DBG:
                with nc.Block() as blk:
                    def s_gp(gp):
                        c.wait(gp, 'ops')
                        for ch in range(NCH):
                            c.inc(gp.dma_start(out=dbg['s0'].ap()[s, ch], in_=s0slab[:, ch * T:(ch + 1) * T]), 'dbg', 16)
                    blk.gpsimd(s_gp)
            c.dma_fence('dbg', 'nzb', 'nzc')


    def p1d(s, s0slab):
        with ExitStack() as ps:
            rhsq = ps.enter_context(nc.sbuf_tensor(f"d_rhsq_{s}", [128, NCH * 1024], BF16))
            xq0 = ps.enter_context(nc.sbuf_tensor(f"d_xq0_{s}", [128, NCH * T], BF16))
            xc = [ps.enter_context(nc.sbuf_tensor(f"d_xc{i}_{s}", [128, T], F32)) for i in range(2)]
            xnw = ps.enter_context(nc.sbuf_tensor(f"d_xnw_{s}", [128, T], F32))
            gw = ps.enter_context(nc.sbuf_tensor(f"d_gw_{s}", [128, T], F32))
            bca = ps.enter_context(nc.sbuf_tensor(f"d_bca_{s}", [128, T], F32))
            bcb = ps.enter_context(nc.sbuf_tensor(f"d_bcb_{s}", [128, T], F32))
            lsb = ps.enter_context(nc.sbuf_tensor(f"d_lsb_{s}", [128, T], F32))
            hop = ps.enter_context(nc.sbuf_tensor(f"d_hop_{s}", [1, T], F32))
            rscol = ps.enter_context(nc.sbuf_tensor(f"d_rscol_{s}", [128, NCH], F32))
            q_ti = ps.enter_context(nc.sbuf_tensor(f"d_ti_{s}", [128, T], I32))
            q_f = ps.enter_context(nc.sbuf_tensor(f"d_f_{s}", [128, T], F32))
            q_d = ps.enter_context(nc.sbuf_tensor(f"d_d_{s}", [128, T], F32))
            q_sg = ps.enter_context(nc.sbuf_tensor(f"d_sg_{s}", [128, T], F32))
            amx = ps.enter_context(nc.sbuf_tensor(f"d_amx_{s}", [128, T], F32))
            h_row = h_gp

            with nc.Block() as blk:
                def s_gp(gp):
                    c.wait(gp, 'qst')
                    for ch in range(NCH):
                        c.inc(gp.dma_start(out=rhsq[:, ch * 1024:(ch + 1) * 1024], in_=h_rhsq.ap()[ch]), 'dld', 16)
                    c.inc(gp.dma_start(out=h_rowf.ap()[0:1, 0:1024], in_=grows[0:1, :]), 'dld', 16)
                    c.wait(gp, 'dld')
                    c.inc(gp.dma_start(out=rscol[:, :],
                                       in_=AP(h_rowf.ap().tensor, 0, [[1, 128], [128, NCH], [1, 1]]),
                                       allow_slow_non_contiguous=True), 'dld', 16)
                    c.inc(gp.dma_start(out=hop[:], in_=rows[8 * s + R_A0:8 * s + R_A0 + 1, :]), 'dh', 16)
                    c.wait(gp, 'dh')
                    bcast_row(gp, c, bca, hop[0:1, :], T, 'dbc')
                    c.wait(gp, 'dbc')
                    c.inc(gp.dma_start(out=hop[:], in_=rows[8 * s + R_B0:8 * s + R_B0 + 1, :]), 'dh', 16)
                    c.wait(gp, 'dh')
                    bcast_row(gp, c, bcb, hop[0:1, :], T, 'dbc')
                    c.wait(gp, 'dbc')
                    c.inc(gp.dma_start(out=hop[:], in_=rows[8 * s + R_LS0:8 * s + R_LS0 + 1, :]), 'dh', 16)
                    c.wait(gp, 'dh')
                blk.gpsimd(s_gp)

                def s_vector(vector):
                    c.wait(vector, 'dh')
                    c.inc(vector.reciprocal(hop[:], hop[:]), 'dv', 1)
                blk.vector(s_vector)

            with nc.Block() as blk:
                def s_gp(gp):
                    c.wait(gp, 'dv')
                    bcast_row(gp, c, lsb, hop[0:1, :], T, 'dbc')
                blk.gpsimd(s_gp)

            with nc.Block() as blk:
                b0 = c.base('dxl0', 'dxl1', 'dxv', 'dxs')

                def s_sync(sync):
                    c.wait(sync, 'dbc')
                    for ch in range(NCH):
                        if ch >= 2:
                            sync.wait_ge(c.S('dxv'), b0['dxv'] + 2 * ch - 3)
                        c.inc(sync.dma_start(out=xc[ch % 2][:], in_=x_in.ap()[s, ch * 128:(ch + 1) * 128, :]),
                              'dxl%d' % (ch % 2), 16)
                blk.sync(s_sync)

                def s_vector(vector):
                    c.wait(vector, 'dbc')
                    for ch in range(NCH):
                        vector.wait_ge(c.S('dxl%d' % (ch % 2)), b0['dxl%d' % (ch % 2)] + 16 * (ch // 2 + 1))
                        vector.drain()
                        vector.tensor_tensor(xnw[:], xc[ch % 2][:], bca[:], ALU.mult)
                        vector.drain()
                        vector.tensor_tensor(xnw[:], xnw[:], bcb[:], ALU.add)
                        vector.drain()
                        vector.tensor_tensor(q_d[:], xnw[:], lsb[:], ALU.mult)
                        vector.drain()
                        vector.tensor_copy(q_ti[:], q_d[:])
                        vector.drain()
                        vector.tensor_copy(q_f[:], q_ti[:])
                        vector.drain()
                        vector.tensor_tensor(q_d[:], q_d[:], q_f[:], ALU.subtract)
                        vector.drain()
                        vector.tensor_scalar(q_sg[:], q_d[:], 0.5, -0.5, ALU.min, ALU.max)
                        vector.drain()
                        c.inc(vector.tensor_tensor(q_d[:], q_d[:], q_sg[:], ALU.subtract), 'dxv', 1)
                        vector.wait_ge(c.S('dxs'), b0['dxs'] + ch + 1)
                        vector.drain()
                        c.inc(vector.tensor_tensor(xq0[:, ch * T:(ch + 1) * T], q_f[:], q_sg[:], ALU.add), 'dxv', 1)
                blk.vector(s_vector)

                def s_scalar(scalar):
                    for ch in range(NCH):
                        scalar.wait_ge(c.S('dxv'), b0['dxv'] + 2 * ch + 1)
                        c.inc(scalar.activation(q_sg[:], q_d[:], AT.Sign), 'dxs', 1)
                blk.scalar(s_scalar)

            with nc.Block() as blk:
                def s_gp(gp):
                    c.wait(gp, 'dxv')
                    c.inc(gp.dma_start(out=hop[:], in_=rows[8 * s + R_LS0:8 * s + R_LS0 + 1, :]), 'dh', 16)
                    c.wait(gp, 'dh')
                    bcast_row(gp, c, bca, hop[0:1, :], T, 'dbc3')
                blk.gpsimd(s_gp)

            with ExitStack() as psr:
                pF = psr.enter_context(nc.psum_tensor(f"d_pF_{s}", [128, T], F32))
                with nc.Block() as blk:
                    b0 = c.base('dft', 'dfv', 'dfs', 'dfx', 'dx1', 'dxr0', 'dxr1', 'dbgg')

                    def s_tensor(tensor):
                        c.wait(tensor, 'dxv')
                        for och in range(NCH):
                            if och >= 1:
                                tensor.wait_ge(c.S('dfv'), b0['dfv'] + och)
                            for ich in range(NCH):
                                wsl = rhsq[:, ich * 1024 + och * 128:ich * 1024 + (och + 1) * 128]
                                for qt in range(NQT):
                                    ins = tensor.matmul(pF[:, qt * 512:(qt + 1) * 512], wsl,
                                                        xq0[:, ich * T + qt * 512:ich * T + (qt + 1) * 512],
                                                        start=(ich == 0), stop=(ich == NCH - 1))
                            c.inc(ins, 'dft', 1)
                    blk.tensor(s_tensor)

                    def s_sync(sync):
                        c.wait(sync, 'dxv')
                        for och in range(NCH):
                            if och >= 2:
                                sync.wait_ge(c.S('dfx'), b0['dfx'] + och - 1)
                                sync.wait_ge(c.S('dx1'), b0['dx1'] + 16 * (och - 1))
                            c.inc(sync.dma_start(out=xc[och % 2][:], in_=x_in.ap()[s, och * 128:(och + 1) * 128, :]),
                                  'dxr%d' % (och % 2), 16)
                    blk.sync(s_sync)

                    def s_vector(vector):
                        c.wait(vector, 'dbc3')
                        for och in range(NCH):
                            vector.wait_ge(c.S('dft'), b0['dft'] + och + 1)
                            if DBG and och >= 1:
                                vector.wait_ge(c.S('dbgg'), b0['dbgg'] + 16 * och)
                            vector.drain()
                            c.inc(vector.tensor_tensor(gw[:], pF[:], bca[:], ALU.mult), 'dfv', 1)
                            vector.wait_ge(c.S('dfs'), b0['dfs'] + och + 1)
                            vector.wait_ge(c.S('dxr%d' % (och % 2)), b0['dxr%d' % (och % 2)] + 16 * (och // 2 + 1))
                            vector.drain()
                            vector.tensor_tensor(xnw[:], s0slab[:, och * T:(och + 1) * T], gw[:], ALU.mult)
                            vector.drain()
                            c.inc(vector.tensor_tensor(xc[och % 2][:], xc[och % 2][:], xnw[:], ALU.add), 'dfx', 1)
                    blk.vector(s_vector)

                    def s_scalar(scalar):
                        c.wait(scalar, 'dld')
                        for och in range(NCH):
                            scalar.wait_ge(c.S('dfv'), b0['dfv'] + och + 1)
                            c.inc(scalar.activation(gw[:], gw[:], AT.Silu,
                                                    scale=rscol[:, och:och + 1]), 'dfs', 1)
                    blk.scalar(s_scalar)

                    def s_gp(gp):
                        for och in range(NCH):
                            gp.wait_ge(c.S('dfx'), b0['dfx'] + och + 1)
                            if och >= 1:
                                gp.wait_ge(c.S('dx1'), b0['dx1'] + 16 * och)
                                if DBG:
                                    gp.wait_ge(c.S('dbgg'), b0['dbgg'] + 16 * och)
                            c.inc(gp.dma_start(out=h_x1.ap()[s, och], in_=xc[och % 2][:]), 'dx1', 16)
                            if DBG:
                                c.inc(gp.dma_start(out=dbg['g0'].ap()[s, och], in_=gw[:]), 'dbgg', 16)
                    blk.gpsimd(s_gp)
            c.dma_fence('dx1', 'dbgg')

            with ExitStack() as psr:
                pstat = psr.enter_context(nc.psum_tensor(f"d_pstat_{s}", [1, T], F32))
                pstatq = psr.enter_context(nc.psum_tensor(f"d_pstatq_{s}", [1, T], F32))
                with nc.Block() as blk:
                    b0 = c.base('dll0', 'dll1', 'dlq', 'dlu')

                    def s_sync(sync):
                        for ch in range(NCH):
                            if ch >= 2:
                                sync.wait_ge(c.S('dlu'), b0['dlu'] + ch - 1)
                            c.inc(sync.dma_start(out=xc[ch % 2][:], in_=h_x1.ap()[s, ch]),
                                  'dll%d' % (ch % 2), 16)
                    blk.sync(s_sync)

                    def s_scalar(scalar):
                        for ch in range(NCH):
                            scalar.wait_ge(c.S('dll%d' % (ch % 2)), b0['dll%d' % (ch % 2)] + 16 * (ch // 2 + 1))
                            if ch >= 1:
                                scalar.wait_ge(c.S('dlu'), b0['dlu'] + ch)
                            c.inc(scalar.activation(q_d[:], xc[ch % 2][:], AT.Square), 'dlq', 1)
                    blk.scalar(s_scalar)

                    def s_tensor(tensor):
                        for ch in range(NCH):
                            tensor.wait_ge(c.S('dlq'), b0['dlq'] + ch + 1)
                            for qt in range(NQT):
                                tensor.matmul(pstat[0:1, qt * 512:(qt + 1) * 512], onescol_f[:],
                                              xc[ch % 2][:, qt * 512:(qt + 1) * 512],
                                              start=(ch == 0), stop=(ch == NCH - 1))
                                ins = tensor.matmul(pstatq[0:1, qt * 512:(qt + 1) * 512], onescol_f[:],
                                                    q_d[:, qt * 512:(qt + 1) * 512],
                                                    start=(ch == 0), stop=(ch == NCH - 1))
                            c.inc(ins, 'dlu', 1)
                    blk.tensor(s_tensor)

                rw0, rw1, rw2, rw3 = bca, bcb, gw, xnw
                with nc.Block() as blk:
                    def s_vector(vector):
                        vector.tensor_scalar_mul(rw0[0:1, 0:T], pstat[0:1, :], 1.0 / D)
                        vector.tensor_scalar_mul(rw1[0:1, 0:T], pstatq[0:1, :], 1.0 / D)
                        vector.drain()
                        vector.tensor_tensor(rw2[0:1, 0:T], rw0[0:1, 0:T], rw0[0:1, 0:T], ALU.mult)
                        vector.drain()
                        c.inc(vector.tensor_tensor(rw1[0:1, 0:T], rw1[0:1, 0:T], rw2[0:1, 0:T], ALU.subtract), 'dlr', 1)
                    blk.vector(s_vector)

                    def s_scalar(scalar):
                        c.wait(scalar, 'dlr')
                        c.inc(scalar.activation(rw1[0:1, 0:T], rw1[0:1, 0:T], AT.Sqrt, bias=cvals[0:1, 0:1]), 'dls', 1)
                    blk.scalar(s_scalar)

                with nc.Block() as blk:
                    def s_vector(vector):
                        c.wait(vector, 'dls')
                        vector.reciprocal(rw2[0:1, 0:T], rw1[0:1, 0:T])
                        vector.drain()
                        vector.tensor_tensor(rw3[0:1, 0:T], rw0[0:1, 0:T], rw2[0:1, 0:T], ALU.mult)
                        vector.drain()
                        c.inc(vector.tensor_scalar_mul(rw3[0:1, 0:T], rw3[0:1, 0:T], -1.0), 'dlr', 1)
                    blk.vector(s_vector)

                    def s_gp(gp):
                        c.wait(gp, 'dlr')
                        c.inc(gp.dma_start(out=rows[8 * s + R_A1:8 * s + R_A1 + 1, :], in_=rw2[0:1, 0:T]), 'dpk', 16)
                        c.inc(gp.dma_start(out=rows[8 * s + R_B1:8 * s + R_B1 + 1, :], in_=rw3[0:1, 0:T]), 'dpk', 16)
                        c.wait(gp, 'dpk')
                        bcast_row(gp, c, bca, rw2[0:1, 0:T], T, 'dbc2')
                        bcast_row(gp, c, bcb, rw3[0:1, 0:T], T, 'dbc2')
                    blk.gpsimd(s_gp)

            with nc.Block() as blk:
                b0 = c.base('dal0', 'dal1', 'dav', 'dast', 'dlq')

                def s_sync(sync):
                    c.wait(sync, 'dbc2')
                    for ch in range(NCH):
                        if ch >= 2:
                            sync.wait_ge(c.S('dav'), b0['dav'] + 2 * (ch - 1))
                        c.inc(sync.dma_start(out=xc[ch % 2][:], in_=h_x1.ap()[s, ch]),
                              'dal%d' % (ch % 2), 16)
                blk.sync(s_sync)

                def s_vector(vector):
                    c.wait(vector, 'dbc2')
                    for ch in range(NCH):
                        vector.wait_ge(c.S('dal%d' % (ch % 2)), b0['dal%d' % (ch % 2)] + 16 * (ch // 2 + 1))
                        if ch >= 2:
                            vector.wait_ge(c.S('dast'), b0['dast'] + 16 * (ch - 1))
                        vector.drain()
                        vector.tensor_tensor(q_d[:], xc[ch % 2][:], bca[:], ALU.mult)
                        vector.drain()
                        vector.tensor_tensor(q_d[:], q_d[:], bcb[:], ALU.add)
                        vector.drain()
                        c.inc(vector.tensor_copy(xq0[:, ch * T:(ch + 1) * T], q_d[:]), 'dav', 1)
                        vector.wait_ge(c.S('dlq'), b0['dlq'] + ch + 1)
                        vector.drain()
                        if ch == 0:
                            ins = vector.tensor_copy(amx[:], gw[:])
                        else:
                            ins = vector.tensor_tensor(amx[:], amx[:], gw[:], ALU.max)
                        c.inc(ins, 'dav', 1)
                blk.vector(s_vector)

                def s_scalar(scalar):
                    for ch in range(NCH):
                        scalar.wait_ge(c.S('dav'), b0['dav'] + 2 * ch + 1)
                        c.inc(scalar.activation(gw[:], q_d[:], AT.Abs), 'dlq', 1)
                blk.scalar(s_scalar)

                def s_gp(gp):
                    for ch in range(NCH):
                        gp.wait_ge(c.S('dav'), b0['dav'] + 2 * ch + 1)
                        if ch >= 1:
                            gp.wait_ge(c.S('dast'), b0['dast'] + 16 * ch)
                        c.inc(gp.dma_start(out=h_xn1.ap()[s, ch], in_=xq0[:, ch * T:(ch + 1) * T]), 'dast', 16)
                blk.gpsimd(s_gp)

            with nc.Block() as blk:
                def s_vector(vector):
                    c.wait(vector, 'dav')
                    part_tree(c, vector, amx, q_d, T)
                    vector.tensor_scalar(amx[0:1, 0:T], amx[0:1, :], 1e-5, None, ALU.max)
                    vector.drain()
                    vector.reciprocal(amx[0:1, 0:T], amx[0:1, 0:T])
                    vector.drain()
                    c.inc(vector.tensor_scalar_mul(amx[0:1, 0:T], amx[0:1, 0:T], 127.0), 'dlsv', 1)
                blk.vector(s_vector)

                def s_gp(gp):
                    c.wait(gp, 'dlsv')
                    c.inc(gp.dma_start(out=rows[8 * s + R_LS1:8 * s + R_LS1 + 1, :], in_=amx[0:1, 0:T]), 'dpk', 16)
                blk.gpsimd(s_gp)
            c.dma_fence('dast', 'dpk')


    def p2_l1attn(s):
        with ExitStack() as ps:
            uslab = ps.enter_context(nc.sbuf_tensor(f"e_u_{s}", [128, NTC * 512], BF16))
            VaL1 = ps.enter_context(nc.sbuf_tensor(f"e_va_{s}", [128, H1 * NTC * 65], BF16))
            oslab1 = ps.enter_context(nc.sbuf_tensor(f"e_os_{s}", [128, NCH * T], BF16))
            Zs2 = ps.enter_context(nc.sbuf_tensor(f"e_zs_{s}", [16, T], F32))
            zsc2 = ps.enter_context(nc.sbuf_tensor(f"e_zsc_{s}", [1, T], F32))

            pp_scope = nc.enter_named_scope("p2proj", False)
            with ExitStack() as pp:
                xn1b = pp.enter_context(nc.sbuf_tensor(f"e_xn_{s}", [128, NCH * T], BF16))
                w2qk = pp.enter_context(nc.sbuf_tensor(f"e_wqk_{s}", [128, NCH * 512], BF16))
                w2v = pp.enter_context(nc.sbuf_tensor(f"e_wv_{s}", [128, NCH * 1024], BF16))
                uw = pp.enter_context(nc.sbuf_tensor(f"e_uw_{s}", [128, 512], F32))
                sqw = pp.enter_context(nc.sbuf_tensor(f"e_sqw_{s}", [128, 512], F32))
                m32 = pp.enter_context(nc.sbuf_tensor(f"e_m32_{s}", [128, 32], F32))
                v32 = pp.enter_context(nc.sbuf_tensor(f"e_v32_{s}", [128, 32], F32))
                r32 = pp.enter_context(nc.sbuf_tensor(f"e_r32_{s}", [128, 32], F32))
                pQ = [pp.enter_context(nc.psum_tensor(f"e_pq{i}_{s}", [128, 1536], F32)) for i in range(2)]
                load_weights_bf16(pp, f"l1w_{s}", [("wq2T", w2qk, NCH, 256, 512, 0),
                                                   ("wk2T", w2qk, NCH, 256, 512, 256),
                                                   ("wv2T", w2v, NCH)])

                with nc.Block() as blk:
                    def s_sync(sync):
                        for ch in range(NCH):
                            c.inc(sync.dma_start(out=xn1b[:, ch * T:(ch + 1) * T], in_=h_xn1.ap()[s, ch]), 'rld', 16)
                    blk.sync(s_sync)

                    def s_vector(vector):
                        ones_ap = AP(VaL1, 64, [[H1 * NTC * 65, 128], [NTC * 65, H1], [65, NTC]])
                        c.inc(vector.memset(ones_ap, 1.0), 'qv', 1)
                    blk.vector(s_vector)

                with nc.Block() as blk:
                    b0 = c.base('pjt', 'pjv', 'lnx', 'lns')

                    def s_tensor(tensor):
                        c.wait(tensor, 'rld')
                        for tc in range(NTC):
                            p = pQ[tc % 2]
                            if tc >= 2:
                                tensor.wait_ge(c.S('pjv'), b0['pjv'] + tc - 1)
                            for i in range(NCH):
                                xsl = xn1b[:, i * T + tc * 128:i * T + (tc + 1) * 128]
                                tensor.matmul(p[:, 0:512], xsl, w2qk[:, i * 512:(i + 1) * 512],
                                              start=(i == 0), stop=(i == NCH - 1))
                                tensor.matmul(p[:, 512:1024], xsl, w2v[:, i * 1024:i * 1024 + 512],
                                              start=(i == 0), stop=(i == NCH - 1))
                                ins = tensor.matmul(p[:, 1024:1536], xsl, w2v[:, i * 1024 + 512:(i + 1) * 1024],
                                                    start=(i == 0), stop=(i == NCH - 1))
                            c.inc(ins, 'pjt', 1)
                    blk.tensor(s_tensor)

                    def s_vector(vector):
                        for tc in range(NTC):
                            p = pQ[tc % 2]
                            vector.wait_ge(c.S('pjt'), b0['pjt'] + tc + 1)
                            vector.drain()
                            vector.tensor_reduce(m32[:], p[:, 0:512].rearrange("p (a b) -> p a b", b=16),
                                                 AX.X, ALU.add)
                            vector.drain()
                            vector.tensor_scalar_mul(m32[:], m32[:], 1.0 / 16.0)
                            vector.drain()
                            c.inc(vector.tensor_tensor(uw[:].rearrange("p (a b) -> p a b", b=16),
                                                       p[:, 0:512].rearrange("p (a b) -> p a b", b=16),
                                                       AP(m32, 0, [[32, 128], [1, 32], [0, 16]]),
                                                       ALU.subtract), 'lnx', 1)
                            vector.wait_ge(c.S('lns'), b0['lns'] + 2 * tc + 1)
                            vector.drain()
                            vector.tensor_reduce(v32[:], sqw[:].rearrange("p (a b) -> p a b", b=16),
                                                 AX.X, ALU.add)
                            vector.drain()
                            c.inc(vector.tensor_scalar_mul(v32[:], v32[:], 1.0 / 16.0), 'lnx', 1)
                            vector.wait_ge(c.S('lns'), b0['lns'] + 2 * tc + 2)
                            vector.drain()
                            vector.reciprocal(r32[:], r32[:])
                            vector.drain()
                            vector.tensor_tensor(uw[:].rearrange("p (a b) -> p a b", b=16),
                                                 uw[:].rearrange("p (a b) -> p a b", b=16),
                                                 AP(r32, 0, [[32, 128], [1, 32], [0, 16]]), ALU.mult)
                            vector.drain()
                            vector.tensor_tensor(uw[:].rearrange("p (a b) -> p a b", b=16),
                                                 uw[:].rearrange("p (a b) -> p a b", b=16),
                                                 AP(gbig, 0, [[32, 128], [0, 32], [1, 16]]), ALU.mult)
                            vector.drain()
                            vector.tensor_tensor(uw[:].rearrange("p (a b) -> p a b", b=16),
                                                 uw[:].rearrange("p (a b) -> p a b", b=16),
                                                 AP(gbig, 16, [[32, 128], [0, 32], [1, 16]]), ALU.add)
                            vector.drain()
                            vector.tensor_copy(uslab[:, tc * 512:(tc + 1) * 512], uw[:])
                            vector.drain()
                            c.inc(vector.tensor_copy(
                                AP(VaL1, tc * 65, [[H1 * NTC * 65, 128], [NTC * 65, H1], [1, 64]]),
                                p[:, 512:1536].rearrange("p (a b) -> p a b", b=64)), 'pjv', 1)
                    blk.vector(s_vector)

                    def s_scalar(scalar):
                        for tc in range(NTC):
                            scalar.wait_ge(c.S('lnx'), b0['lnx'] + 2 * tc + 1)
                            c.inc(scalar.activation(sqw[:], uw[:], AT.Square), 'lns', 1)
                            scalar.wait_ge(c.S('lnx'), b0['lnx'] + 2 * tc + 2)
                            c.inc(scalar.activation(r32[:], v32[:], AT.Sqrt, bias=cvals[:, 0:1]), 'lns', 1)
                    blk.scalar(s_scalar)

            nc.leave_named_scope("p2proj", pp_scope[0], False)
            if DBG:
                with nc.Block() as blk:
                    def s_gp(gp):
                        c.inc(gp.dma_start(out=dbg['ufm'].ap()[s], in_=uslab[:]), 'dbg', 16)
                        for hh in range(H1):
                            c.inc(gp.dma_start(out=dbg['val1'].ap()[s][:, hh * 1040:(hh + 1) * 1040],
                                               in_=VaL1[:, hh * 1040:(hh + 1) * 1040]), 'dbg', 16)
                        c.wait(gp, 'dbg')
                    blk.gpsimd(s_gp)

            sg_scope = nc.enter_named_scope("p2scan", False)
            with ExitStack() as sg:
                qfg = sg.enter_context(nc.sbuf_tensor(f"e_qfg_{s}", [128, NTC * 512], BF16))
                kfg = sg.enter_context(nc.sbuf_tensor(f"e_kfg_{s}", [128, NTC * 512], BF16))
                uqTg = sg.enter_context(nc.sbuf_tensor(f"e_uqt_{s}", [16, 2 * T], BF16))
                ukTg = sg.enter_context(nc.sbuf_tensor(f"e_ukt_{s}", [16, 2 * T], BF16))
                qfT = sg.enter_context(nc.sbuf_tensor(f"e_qft_{s}", [128, 4 * T], BF16))
                attm2 = [sg.enter_context(nc.sbuf_tensor(f"e_attm{j}_{s}", [128, NTC * 128], BF16)) for j in range(2)]
                attsq = sg.enter_context(nc.sbuf_tensor(f"e_attq_{s}", [128, 1024], BF16))
                stsnap = sg.enter_context(nc.sbuf_tensor(f"e_stsn_{s}", [128, 520], BF16))
                stf32 = sg.enter_context(nc.sbuf_tensor(f"e_stf_{s}", [128, 260], F32))
                zsc3 = sg.enter_context(nc.sbuf_tensor(f"e_zsc3_{s}", [1, T], F32))

                for g in range(8):
                    with ExitStack() as pg:
                        ptu = pg.enter_context(nc.psum_tensor(f"e_ptu_{s}_{g}", [16, 2 * T], BF16))
                        with nc.Block() as blk:
                            b0 = c.base('lnu', 'lnd')

                            def s_vector(vector, g=g):
                                for tc in range(NTC):
                                    for r in range(2):
                                        h = 2 * g + r
                                        for (dst, off) in ((qfg, tc * 512 + h * 16), (kfg, tc * 512 + 256 + h * 16)):
                                            vector.tensor_tensor(
                                                dst[:, tc * 512 + r * 256:tc * 512 + (r + 1) * 256]
                                                .rearrange("p (a b) -> p a b", b=16),
                                                AP(uslab, off, [[NTC * 512, 128], [1, 16], [0, 16]]),
                                                AP(uslab, off, [[NTC * 512, 128], [0, 16], [1, 16]]),
                                                ALU.mult)
                                c.inc(vector.drain(), 'lnr', 1)
                                for n in range(4):
                                    r, tn = n % 2, n // 2
                                    dstT = uqTg if tn == 0 else ukTg
                                    vector.wait_ge(c.S('lnu'), b0['lnu'] + n + 1)
                                    vector.drain()
                                    c.inc(vector.tensor_copy(dstT[0:16, r * T:(r + 1) * T],
                                                             ptu[0:16, (n % 2) * T:(n % 2) * T + T]), 'lnd', 1)
                            blk.vector(s_vector)

                            def s_tensor(tensor, g=g):
                                c.wait(tensor, 'lnr')
                                for n in range(4):
                                    r, tn = n % 2, n // 2
                                    h = 2 * g + r
                                    if n >= 2:
                                        tensor.wait_ge(c.S('lnd'), b0['lnd'] + n - 1)
                                    for tc in range(NTC):
                                        ins = tensor.transpose(
                                            ptu[0:16, (n % 2) * T + tc * 128:(n % 2) * T + (tc + 1) * 128],
                                            uslab[:, tc * 512 + tn * 256 + h * 16:tc * 512 + tn * 256 + h * 16 + 16],
                                            ident[:])
                                    c.inc(ins, 'lnu', 1)
                            blk.tensor(s_tensor)

                    bstk2 = ExitStack()
                    blk2 = bstk2.enter_context(nc.Block())
                    with ExitStack() as ph:
                        ptq = ph.enter_context(nc.psum_tensor(f"e_ptq_{s}_{g}", [128, 2 * T], BF16))
                        if True:
                            def s_tensor(tensor):
                                b0d = c.base('dav')
                                for r in range(2):
                                    if r == 1:
                                        tensor.wait_ge(c.S('dav'), b0d['dav'] + 1)
                                    for pc in range(2):
                                        for tc in range(NTC):
                                            ins = tensor.transpose(
                                                ptq[:, pc * T + tc * 128:pc * T + (tc + 1) * 128],
                                                qfg[:, tc * 512 + r * 256 + pc * 128:tc * 512 + r * 256 + pc * 128 + 128],
                                                ident[:])
                                    c.inc(ins, 'lnsq', 1)
                            blk2.tensor(s_tensor)

                            def s_vector(vector):
                                b0q = c.base('lnsq')
                                for r in range(2):
                                    vector.wait_ge(c.S('lnsq'), b0q['lnsq'] + r - 1)
                                    vector.drain()
                                    c.inc(vector.tensor_copy(qfT[:, r * 2 * T:(r + 1) * 2 * T], ptq[:]), 'dav', 1)
                            blk2.vector(s_vector)

                    with ExitStack() as ph:
                        pA = ph.enter_context(nc.psum_tensor(f"e_pa_{s}_{g}", [128, 1024], F32))
                        pOL = [ph.enter_context(nc.psum_tensor(f"e_pol{j}_{s}_{g}", [65, 1024], F32)) for j in range(2)]
                        pstate = [ph.enter_context(nc.psum_tensor(f"e_pst{j}_{s}_{g}", [128, 130], F32)) for j in range(2)]
                        if True:
                            b0 = c.base('dft', 'dfs', 'dfv', 'dxs', 'dxv', 'dlsv', 'avv', 'dpk')

                            def s_tensor(tensor, g=g):
                                c.wait(tensor, 'dav')
                                for k in range(4):
                                    r, w = k % 2, k // 2
                                    if k >= 1:
                                        tensor.wait_ge(c.S('dfs'), b0['dfs'] + k)
                                    for j8 in range(8):
                                        p = 8 * w + j8
                                        ins = tensor.matmul(
                                            pA[:, j8 * 128:(j8 + 1) * 128],
                                            ukTg[0:16, r * T + p * 128:r * T + (p + 1) * 128],
                                            uqTg[0:16, r * T + p * 128:r * T + (p + 1) * 128],
                                            start=True, stop=True)
                                    c.inc(ins, 'dft', 1)
                                for p in range(NTC):
                                    if p == 8:
                                        tensor.wait_ge(c.S('avv'), b0['avv'] + 2)
                                    for j in range(2):
                                        h = 2 * g + j
                                        tensor.wait_ge(c.S('dfv'), b0['dfv'] + 2 * (p // 8) + j + 1)
                                        if p > 0:
                                            tensor.wait_ge(c.S('dxv'), b0['dxv'] + 2 * (p - 1) + j + 1)
                                        va = VaL1[:, h * 1040 + p * 65:h * 1040 + (p + 1) * 65]
                                        po = pOL[j]
                                        cc = (p % 8) * 128
                                        tensor.matmul(po[:, cc:cc + 128], va,
                                                      attm2[j][:, p * 128:(p + 1) * 128],
                                                      start=True, stop=(p == 0))
                                        if p > 0:
                                            tensor.matmul(po[:, cc:cc + 128], stsnap[:, 260 * j:260 * j + 65],
                                                          qfT[:, j * 2 * T + p * 128:j * 2 * T + (p + 1) * 128],
                                                          start=False, stop=False)
                                            tensor.matmul(po[:, cc:cc + 128], stsnap[:, 260 * j + 65:260 * j + 130],
                                                          qfT[:, j * 2 * T + T + p * 128:j * 2 * T + T + (p + 1) * 128],
                                                          start=False, stop=True)
                                        tensor.matmul(pstate[j][:, 0:65],
                                                      kfg[:, p * 512 + j * 256:p * 512 + j * 256 + 128],
                                                      va, start=True, stop=True)
                                        ins = tensor.matmul(pstate[j][:, 65:130],
                                                            kfg[:, p * 512 + j * 256 + 128:p * 512 + j * 256 + 256],
                                                            va, start=True, stop=True)
                                        c.inc(ins, 'dxs', 1)
                            blk2.tensor(s_tensor)

                            def s_scalar(scalar):
                                for k in range(4):
                                    scalar.wait_ge(c.S('dft'), b0['dft'] + k + 1)
                                    if k >= 1:
                                        scalar.wait_ge(c.S('dfv'), b0['dfv'] + k)
                                    c.inc(scalar.activation(attsq[:], pA[:], AT.Square), 'dfs', 1)
                                for p in range(NTC):
                                    for j in range(2):
                                        scalar.wait_ge(c.S('dlsv'), b0['dlsv'] + 2 * p + j + 1)
                                        c.inc(scalar.activation(stsnap[:, 260 * j:260 * j + 130],
                                                                stf32[:, 130 * j:130 * j + 130], AT.Identity),
                                              'dxv', 1)
                            blk2.scalar(s_scalar)

                            def s_vector(vector, g=g):
                                vector.memset(stf32[:], 0.0)
                                vector.drain()
                                for k in range(4):
                                    r, w = k % 2, k // 2
                                    vector.wait_ge(c.S('dfs'), b0['dfs'] + k + 1)
                                    vector.drain()
                                    c.inc(vector.tensor_tensor(
                                        attm2[r][:, w * 1024:(w + 1) * 1024].rearrange("p (a b) -> p a b", b=128),
                                        attsq[:].rearrange("p (a b) -> p a b", b=128),
                                        AP(triu128p, 0, [[128, 128], [0, 8], [1, 128]]),
                                        ALU.mult), 'dfv', 1)
                                for p in range(NTC):
                                    for j in range(2):
                                        h = 2 * g + j
                                        vector.wait_ge(c.S('dxs'), b0['dxs'] + 2 * p + j + 1)
                                        if p >= 1:
                                            vector.wait_ge(c.S('dxv'), b0['dxv'] + 2 * (p - 1) + j + 1)
                                        vector.drain()
                                        c.inc(vector.tensor_tensor(stf32[:, 130 * j:130 * j + 130],
                                                                   stf32[:, 130 * j:130 * j + 130],
                                                                   pstate[j][:], ALU.add), 'dlsv', 1)
                                    if p == 7 or p == 15:
                                        hf = p // 8
                                        c.wait(vector, 'dpk', c.cnt['dpk'])
                                        for j in range(2):
                                            h = 2 * g + j
                                            zdst = zsc2 if j == 0 else zsc3
                                            vector.tensor_copy(
                                                oslab1[(h % 2) * 64:(h % 2) * 64 + 64,
                                                       (h // 2) * T + hf * 1024:(h // 2) * T + (hf + 1) * 1024],
                                                pOL[j][0:64, :])
                                            vector.drain()
                                            c.inc(vector.tensor_copy(zdst[0:1, hf * 1024:(hf + 1) * 1024],
                                                                     pOL[j][64:65, :]), 'avv', 1)
                            blk2.vector(s_vector)

                            def s_gp(gp, g=g):
                                c.wait(gp, 'avv')
                                c.inc(gp.dma_start(out=Zs2[2 * g:2 * g + 1, :], in_=zsc2[0:1, :]), 'dpk', 16)
                                gp.wait_ge(c.S('dpk'), c.cnt['dpk'])
                                c.inc(gp.dma_start(out=Zs2[2 * g + 1:2 * g + 2, :], in_=zsc3[0:1, :]), 'dpk', 16)
                            blk2.gpsimd(s_gp)
                    bstk2.close()

            nc.leave_named_scope("p2scan", sg_scope[0], False)
            with nc.Block() as blk:
                def s_vector(vector):
                    c.wait(vector, 'dpk')
                    vector.tensor_scalar(Zs2[:], Zs2[:], 1e-10, None, ALU.add)
                    vector.drain()
                    c.inc(vector.reciprocal(Zs2[:], Zs2[:]), 'nzv', 1)
                blk.vector(s_vector)

            if DBG:
                with nc.Block() as blk:
                    def s_gp(gp):
                        c.wait(gp, 'nzv')
                        c.inc(gp.dma_start(out=dbg['zs1'].ap()[s], in_=Zs2[:]), 'dbg', 16)
                        c.wait(gp, 'dbg')
                    blk.gpsimd(s_gp)

            with ExitStack() as pnr:
                prz = [pnr.enter_context(nc.psum_tensor(f"e_prz{i}_{s}", [128, T], F32)) for i in range(2)]
                with nc.Block() as blk:
                    b0 = c.base('nzt', 'nzv')

                    def s_tensor(tensor):
                        c.wait(tensor, 'nzv')
                        for vch in range(NCH):
                            if vch >= 2:
                                tensor.wait_ge(c.S('nzv'), b0['nzv'] + vch - 1)
                            for qt in range(NQT):
                                ins = tensor.matmul(prz[vch % 2][:, qt * 512:(qt + 1) * 512],
                                                    ehd[0:16, vch * 128:(vch + 1) * 128],
                                                    Zs2[0:16, qt * 512:(qt + 1) * 512],
                                                    start=True, stop=True)
                            c.inc(ins, 'nzt', 1)
                    blk.tensor(s_tensor)

                    def s_vector(vector):
                        for vch in range(NCH):
                            vector.wait_ge(c.S('nzt'), b0['nzt'] + vch + 1)
                            vector.drain()
                            sl = oslab1[:, vch * T:(vch + 1) * T]
                            c.inc(vector.tensor_tensor(sl, sl, prz[vch % 2][:], ALU.mult), 'nzv', 1)
                    blk.vector(s_vector)

            wo2b = ps.enter_context(nc.sbuf_tensor(f"e_wo2_{s}", [128, NCH * D], BF16))
            s1slab = ps.enter_context(nc.sbuf_tensor(f"e_s1_{s}", [128, NCH * T], BF16))
            load_weights_bf16(ps, f"l1wo_{s}", [("wo2T", wo2b, NCH)])
            with ExitStack() as pr:
                pP2 = pr.enter_context(nc.psum_tensor(f"e_pp2_{s}", [128, T], F32))
                with nc.Block() as blk:
                    b0 = c.base('opt', 'ops', 'dx1')

                    def s_tensor(tensor):
                        c.wait(tensor, 'wcast')
                        c.wait(tensor, 'nzv')
                        for och in range(NCH):
                            if och >= 1:
                                tensor.wait_ge(c.S('ops'), b0['ops'] + och)
                            for vch in range(NCH):
                                wsl = wo2b[:, vch * D + och * 128:vch * D + (och + 1) * 128]
                                for qt in range(NQT):
                                    ins = tensor.matmul(pP2[:, qt * 512:(qt + 1) * 512], wsl,
                                                        oslab1[:, vch * T + qt * 512:vch * T + (qt + 1) * 512],
                                                        start=(vch == 0), stop=(vch == NCH - 1))
                            c.inc(ins, 'opt', 1)
                    blk.tensor(s_tensor)

                    def s_scalar(scalar):
                        for och in range(NCH):
                            scalar.wait_ge(c.S('opt'), b0['opt'] + och + 1)
                            if och >= 2:
                                scalar.wait_ge(c.S('dx1'), b0['dx1'] + 16 * (och - 1))
                            c.inc(scalar.activation(s1slab[:, och * T:(och + 1) * T], pP2[:], AT.Identity), 'ops', 1)
                    blk.scalar(s_scalar)

                    def s_gp(gp):
                        for och in range(NCH):
                            gp.wait_ge(c.S('ops'), b0['ops'] + och + 1)
                            if och >= 1:
                                gp.wait_ge(c.S('dx1'), b0['dx1'] + 16 * och)
                                if DBG:
                                    gp.wait_ge(c.S('dbg'), c.cnt['dbg'])
                            c.inc(gp.dma_start(out=h_s1.ap()[s, och], in_=s1slab[:, och * T:(och + 1) * T]), 'dx1', 16)
                            if DBG:
                                c.inc(gp.dma_start(out=dbg['s1'].ap()[s, och],
                                                   in_=s1slab[:, och * T:(och + 1) * T]), 'dbg', 16)
                    blk.gpsimd(s_gp)
            c.dma_fence('dx1', 'dbg', 'nzb', 'nzc')


    def p4_l1ffn(s):
        with ExitStack() as ps:
            rhob = ps.enter_context(nc.sbuf_tensor(f"f_rho_{s}", [128, T], F32))
            delb = ps.enter_context(nc.sbuf_tensor(f"f_del_{s}", [128, T], F32))

            with ExitStack() as pq:
                xq1 = pq.enter_context(nc.sbuf_tensor(f"f_xq_{s}", [128, NCH * T], BF16))
                gw = pq.enter_context(nc.sbuf_tensor(f"f_gw_{s}", [128, T], F32))
                gw2 = pq.enter_context(nc.sbuf_tensor(f"f_gw2_{s}", [128, T], F32))
                guw = [pq.enter_context(nc.sbuf_tensor(f"f_gu{i}_{s}", [128, T], F32)) for i in range(2)]
                amx4 = pq.enter_context(nc.sbuf_tensor(f"f_amx_{s}", [128, T], F32))
                ggb = pq.enter_context(nc.sbuf_tensor(f"f_ggb_{s}", [128, T], F32))
                bca = pq.enter_context(nc.sbuf_tensor(f"f_bca_{s}", [128, T], F32))
                wgb = [pq.enter_context(nc.sbuf_tensor(f"f_wg{i}_{s}", [128, NCH * 512], BF16)) for i in range(2)]
                wub = [pq.enter_context(nc.sbuf_tensor(f"f_wu{i}_{s}", [128, NCH * 512], BF16)) for i in range(2)]
                pa = ExitStack()
                xc = [pa.enter_context(nc.sbuf_tensor(f"f_xc{i}_{s}", [128, T], F32)) for i in range(2)]
                bcb = pa.enter_context(nc.sbuf_tensor(f"f_bcb_{s}", [128, T], F32))
                lsb = pa.enter_context(nc.sbuf_tensor(f"f_lsb_{s}", [128, T], F32))
                q_ti = pa.enter_context(nc.sbuf_tensor(f"f_ti_{s}", [128, T], I32))
                q_d = gw2
                q_f = guw[0]
                q_sg = amx4

                with nc.Block() as blk:
                    def s_gp(gp):
                        c.wait(gp, 'dpk')
                        c.inc(gp.dma_start(out=gw[0:1, 0:T], in_=rows[8 * s + R_A1:8 * s + R_A1 + 1, :]), 'dh', 16)
                        c.wait(gp, 'dh')
                        bcast_row(gp, c, bca, gw[0:1, 0:T], T, 'dbc')
                        c.wait(gp, 'dbc')
                        c.inc(gp.dma_start(out=gw[0:1, 0:T], in_=rows[8 * s + R_B1:8 * s + R_B1 + 1, :]), 'dh', 16)
                        c.wait(gp, 'dh')
                        bcast_row(gp, c, bcb, gw[0:1, 0:T], T, 'dbc')
                        c.wait(gp, 'dbc')
                        c.inc(gp.dma_start(out=gw[0:1, 0:T], in_=rows[8 * s + R_LS1:8 * s + R_LS1 + 1, :]), 'dh', 16)
                        c.wait(gp, 'dh')
                        bcast_row(gp, c, lsb, gw[0:1, 0:T], T, 'dbc')
                        c.wait(gp, 'dbc')
                    blk.gpsimd(s_gp)

                    def s_vector(vector):
                        c.wait(vector, 'dbc')
                        vector.reciprocal(gw2[0:1, 0:T], gw[0:1, 0:T])
                        vector.drain()
                        c.inc(vector.tensor_scalar(gw2[0:1, 0:T], gw2[0:1, 0:T], swcol[0:1, 4:5], None,
                                                   ALU.mult), 'dv', 1)
                    blk.vector(s_vector)

                with nc.Block() as blk:
                    def s_gp(gp):
                        c.wait(gp, 'dv')
                        bcast_row(gp, c, ggb, gw2[0:1, 0:T], T, 'dbc')
                    blk.gpsimd(s_gp)

                with nc.Block() as blk:
                    b0 = c.base('dxl0', 'dxl1', 'dxv', 'dxs')

                    def s_sync(sync):
                        c.wait(sync, 'dbc')
                        for ch in range(NCH):
                            if ch >= 2:
                                sync.wait_ge(c.S('dxv'), b0['dxv'] + 2 * ch - 3)
                            c.inc(sync.dma_start(out=xc[ch % 2][:], in_=h_x1.ap()[s, ch]),
                                  'dxl%d' % (ch % 2), 16)
                    blk.sync(s_sync)

                    def s_vector(vector):
                        c.wait(vector, 'dbc')
                        for ch in range(NCH):
                            vector.wait_ge(c.S('dxl%d' % (ch % 2)), b0['dxl%d' % (ch % 2)] + 16 * (ch // 2 + 1))
                            vector.drain()
                            vector.tensor_tensor(gw[:], xc[ch % 2][:], bca[:], ALU.mult)
                            vector.drain()
                            vector.tensor_tensor(gw[:], gw[:], bcb[:], ALU.add)
                            vector.drain()
                            vector.tensor_tensor(q_d[:], gw[:], lsb[:], ALU.mult)
                            vector.drain()
                            vector.tensor_copy(q_ti[:], q_d[:])
                            vector.drain()
                            vector.tensor_copy(q_f[:], q_ti[:])
                            vector.drain()
                            vector.tensor_tensor(q_d[:], q_d[:], q_f[:], ALU.subtract)
                            vector.drain()
                            vector.tensor_scalar(q_sg[:], q_d[:], 0.5, -0.5, ALU.min, ALU.max)
                            vector.drain()
                            c.inc(vector.tensor_tensor(q_d[:], q_d[:], q_sg[:], ALU.subtract), 'dxv', 1)
                            vector.wait_ge(c.S('dxs'), b0['dxs'] + ch + 1)
                            vector.drain()
                            c.inc(vector.tensor_tensor(xq1[:, ch * T:(ch + 1) * T], q_f[:], q_sg[:], ALU.add), 'dxv', 1)
                    blk.vector(s_vector)

                    def s_scalar(scalar):
                        for ch in range(NCH):
                            scalar.wait_ge(c.S('dxv'), b0['dxv'] + 2 * ch + 1)
                            c.inc(scalar.activation(q_sg[:], q_d[:], AT.Sign), 'dxs', 1)
                    blk.scalar(s_scalar)

                pa.close()
                with ExitStack() as pr:
                    pG = [pr.enter_context(nc.psum_tensor(f"f_pG{i}_{s}", [128, 1024], F32)) for i in range(2)]
                    pU = [pr.enter_context(nc.psum_tensor(f"f_pU{i}_{s}", [128, 1024], F32)) for i in range(2)]
                    with nc.Block() as blk:
                        b0 = c.base('lnl0', 'lnl1', 'dft', 'dfv', 'dfs', 'qst')

                        def s_sync(sync):
                            for sg4 in range(NIC // 4):
                                if sg4 >= 2:
                                    sync.wait_ge(c.S('dft'), b0['dft'] + 8 * (sg4 - 1))
                                for ich in range(NCH):
                                    c.inc(sync.dma_start(out=wgb[sg4 % 2][:, ich * 512:(ich + 1) * 512],
                                                         in_=h_wgq.ap()[ich][:, sg4 * 512:(sg4 + 1) * 512]),
                                          'lnl%d' % (sg4 % 2), 16)
                                    c.inc(sync.dma_start(out=wub[sg4 % 2][:, ich * 512:(ich + 1) * 512],
                                                         in_=h_wuq.ap()[ich][:, sg4 * 512:(sg4 + 1) * 512]),
                                          'lnl%d' % (sg4 % 2), 16)
                        blk.sync(s_sync)

                        def s_tensor(tensor):
                            c.wait(tensor, 'dxv')
                            for n in range(2 * NIC):
                                oc, h2 = n // 2, n % 2
                                sg4 = oc // 4
                                if n % 8 == 0:
                                    tensor.wait_ge(c.S('lnl%d' % (sg4 % 2)),
                                                   b0['lnl%d' % (sg4 % 2)] + 256 * (sg4 // 2 + 1))
                                if n >= 2:
                                    tensor.wait_ge(c.S('dfv'), b0['dfv'] + 3 * (n - 2) + 2)
                                for ich in range(NCH):
                                    wcol = ich * 512 + (oc % 4) * 128
                                    for q in range(2):
                                        xsl = xq1[:, ich * T + h2 * 1024 + q * 512:
                                                  ich * T + h2 * 1024 + (q + 1) * 512]
                                        tensor.matmul(pG[n % 2][:, q * 512:(q + 1) * 512],
                                                      wgb[sg4 % 2][:, wcol:wcol + 128], xsl,
                                                      start=(ich == 0), stop=(ich == NCH - 1))
                                        ins = tensor.matmul(pU[n % 2][:, q * 512:(q + 1) * 512],
                                                            wub[sg4 % 2][:, wcol:wcol + 128], xsl,
                                                            start=(ich == 0), stop=(ich == NCH - 1))
                                c.inc(ins, 'dft', 1)
                        blk.tensor(s_tensor)

                        def s_vector(vector):
                            c.wait(vector, 'dbc')
                            for n in range(2 * NIC):
                                oc, h2 = n // 2, n % 2
                                hs = slice(h2 * 1024, (h2 + 1) * 1024)
                                vector.wait_ge(c.S('dft'), b0['dft'] + n + 1)
                                vector.drain()
                                c.inc(vector.tensor_tensor(gw[:, hs], pG[n % 2][:], ggb[:, hs], ALU.mult),
                                      'dfv', 1)
                                vector.wait_ge(c.S('dfs'), b0['dfs'] + 2 * n + 1)
                                if oc >= 2 and h2 == 0:
                                    vector.wait_ge(c.S('qst'), b0['qst'] + 16 * (oc - 1))
                                vector.drain()
                                c.inc(vector.tensor_tensor(guw[oc % 2][:, hs], gw2[:, hs], pU[n % 2][:],
                                                           ALU.mult), 'dfv', 1)
                                vector.wait_ge(c.S('dfs'), b0['dfs'] + 2 * n + 2)
                                vector.drain()
                                if oc == 0:
                                    ins = vector.tensor_copy(amx4[:, hs], bca[:, hs])
                                else:
                                    ins = vector.tensor_tensor(amx4[:, hs], amx4[:, hs], bca[:, hs], ALU.max)
                                c.inc(ins, 'dfv', 1)
                        blk.vector(s_vector)

                        def s_scalar(scalar):
                            for n in range(2 * NIC):
                                oc, h2 = n // 2, n % 2
                                hs = slice(h2 * 1024, (h2 + 1) * 1024)
                                scalar.wait_ge(c.S('dfv'), b0['dfv'] + 3 * n + 1)
                                c.inc(scalar.activation(gw2[:, hs], gw[:, hs], AT.Silu), 'dfs', 1)
                                scalar.wait_ge(c.S('dfv'), b0['dfv'] + 3 * n + 2)
                                c.inc(scalar.activation(bca[:, hs], guw[oc % 2][:, hs], AT.Abs), 'dfs', 1)
                        blk.scalar(s_scalar)

                        def s_gp(gp):
                            for oc in range(NIC):
                                gp.wait_ge(c.S('dfv'), b0['dfv'] + 3 * (2 * oc + 1) + 2)
                                if oc >= 1:
                                    gp.wait_ge(c.S('qst'), b0['qst'] + 16 * oc)
                                c.inc(gp.dma_start(out=h_gu.ap()[oc], in_=guw[oc % 2][:]), 'qst', 16)
                        blk.gpsimd(s_gp)

                with nc.Block() as blk:
                    def s_vector(vector):
                        part_tree(c, vector, amx4, gw2, T)
                        c.inc(vector.drain(), 'dv', 1)
                    blk.vector(s_vector)

                    def s_gp(gp):
                        c.wait(gp, 'dv')
                        c.inc(gp.dma_start(out=gw[0:1, 0:T], in_=rows[8 * s + R_LS1:8 * s + R_LS1 + 1, :]), 'dh', 16)
                    blk.gpsimd(s_gp)

                with nc.Block() as blk:
                    def s_vector(vector):
                        c.wait(vector, 'dh')
                        vector.reciprocal(gw[0:1, 0:T], gw[0:1, 0:T])
                        vector.drain()
                        vector.tensor_scalar(gw[0:1, 0:T], gw[0:1, 0:T], swcol[0:1, 5:6], None, ALU.mult)
                        vector.drain()
                        vector.tensor_tensor(gw2[0:1, 0:T], amx4[0:1, 0:T], gw[0:1, 0:T], ALU.mult)
                        vector.drain()
                        vector.tensor_scalar(gw2[0:1, 0:T], gw2[0:1, 0:T], 1e-5, None, ALU.max)
                        vector.drain()
                        vector.reciprocal(gw2[0:1, 0:T], gw2[0:1, 0:T])
                        vector.drain()
                        vector.tensor_scalar_mul(gw2[0:1, 0:T], gw2[0:1, 0:T], 127.0)
                        vector.drain()
                        vector.tensor_tensor(amx4[0:1, 0:T], gw2[0:1, 0:T], gw[0:1, 0:T], ALU.mult)
                        vector.drain()
                        vector.reciprocal(gw[0:1, 0:T], gw2[0:1, 0:T])
                        vector.drain()
                        c.inc(vector.tensor_scalar(gw[0:1, 0:T], gw[0:1, 0:T], swcol[0:1, 6:7], None,
                                                   ALU.mult), 'dv', 1)
                    blk.vector(s_vector)

                    def s_gp(gp):
                        c.wait(gp, 'dv')
                        bcast_row(gp, c, rhob, amx4[0:1, 0:T], T, 'dbc2')
                        bcast_row(gp, c, delb, gw[0:1, 0:T], T, 'dbc2')
                        c.wait(gp, 'dbc2')
                    blk.gpsimd(s_gp)
            c.dma_fence('qst', 'dbc2', 'dbc', 'dh')

            with ExitStack() as pd:
                v2slab = pd.enter_context(nc.sbuf_tensor(f"f_v2_{s}", [128, NIC * T], BF16))
                gul = pd.enter_context(nc.sbuf_tensor(f"f_gul_{s}", [128, T], F32))
                q_ti = pd.enter_context(nc.sbuf_tensor(f"f_ti2_{s}", [128, T], I32))
                q_f = pd.enter_context(nc.sbuf_tensor(f"f_f2_{s}", [128, T], F32))
                q_d = pd.enter_context(nc.sbuf_tensor(f"f_d2_{s}", [128, T], F32))
                q_sg = pd.enter_context(nc.sbuf_tensor(f"f_sg2_{s}", [128, T], F32))
                s1st = pd.enter_context(nc.sbuf_tensor(f"f_s1_{s}", [128, T], BF16))
                wdb = [pd.enter_context(nc.sbuf_tensor(f"f_wd{i}_{s}", [128, 128], BF16)) for i in range(2)]

                with nc.Block() as blk:
                    b0 = c.base('dal0', 'dal1', 'dxv', 'dxs')

                    def s_sync(sync):
                        for oc in range(NIC):
                            if oc >= 1:
                                sync.wait_ge(c.S('dxv'), b0['dxv'] + 3 * (oc - 1) + 1)
                            c.inc(sync.dma_start(out=gul[:], in_=h_gu.ap()[oc]), 'dal%d' % (oc % 2), 16)
                    blk.sync(s_sync)

                    def s_vector(vector):
                        for oc in range(NIC):
                            vector.wait_ge(c.S('dal%d' % (oc % 2)), b0['dal%d' % (oc % 2)] + 16 * (oc // 2 + 1))
                            vector.drain()
                            c.inc(vector.tensor_tensor(q_d[:], gul[:], rhob[:], ALU.mult), 'dxv', 1)
                            vector.drain()
                            vector.tensor_copy(q_ti[:], q_d[:])
                            vector.drain()
                            vector.tensor_copy(q_f[:], q_ti[:])
                            vector.drain()
                            vector.tensor_tensor(q_d[:], q_d[:], q_f[:], ALU.subtract)
                            vector.drain()
                            vector.tensor_scalar(q_sg[:], q_d[:], 0.5, -0.5, ALU.min, ALU.max)
                            vector.drain()
                            vector.tensor_tensor(q_d[:], q_d[:], q_sg[:], ALU.subtract)
                            vector.drain()
                            c.inc(vector.drain(), 'dxv', 1)
                            vector.wait_ge(c.S('dxs'), b0['dxs'] + oc + 1)
                            vector.drain()
                            c.inc(vector.tensor_tensor(v2slab[:, oc * T:(oc + 1) * T], q_f[:], q_sg[:],
                                                       ALU.add), 'dxv', 1)
                    blk.vector(s_vector)

                    def s_scalar(scalar):
                        for oc in range(NIC):
                            scalar.wait_ge(c.S('dxv'), b0['dxv'] + 3 * oc + 2)
                            c.inc(scalar.activation(q_sg[:], q_d[:], AT.Sign), 'dxs', 1)
                    blk.scalar(s_scalar)

                with ExitStack() as pr:
                    pD = [pr.enter_context(nc.psum_tensor(f"f_pD{i}_{s}", [128, T], F32)) for i in range(2)]
                    with nc.Block() as blk:
                        b0 = c.base('lnm0', 'lnm1', 'opt', 'dfv', 'dfs', 'dal0', 'dal1', 'fin', 'dbgg')

                        def s_sync(sync):
                            for och in range(NCH):
                                if och >= 1:
                                    sync.wait_ge(c.S('dfv'), b0['dfv'] + 4 * och)
                                c.inc(sync.dma_start(out=s1st[:], in_=h_s1.ap()[s, och]), 'dal0', 16)
                                c.inc(sync.dma_start(out=gul[:], in_=h_x1.ap()[s, och]), 'dal1', 16)
                                for oc in range(NIC):
                                    n = och * NIC + oc
                                    if n >= 2:
                                        sync.wait_ge(c.S('opt'), b0['opt'] + n - 1)
                                    c.inc(sync.dma_start(out=wdb[n % 2][:],
                                                         in_=h_wdq.ap()[och][oc]),
                                          'lnm%d' % (n % 2), 16)
                        blk.sync(s_sync)

                        def s_tensor(tensor):
                            for och in range(NCH):
                                pD_ = pD[och % 2]
                                for oc in range(NIC):
                                    n = och * NIC + oc
                                    tensor.wait_ge(c.S('lnm%d' % (n % 2)), b0['lnm%d' % (n % 2)] + 16 * (n // 2 + 1))
                                    if oc == 0 and och >= 2:
                                        tensor.wait_ge(c.S('dfv'), b0['dfv'] + 4 * (och - 1) - 3)
                                    for qt in range(NQT):
                                        ins = tensor.matmul(pD_[:, qt * 512:(qt + 1) * 512], wdb[n % 2][:],
                                                            v2slab[:, oc * T + qt * 512:oc * T + (qt + 1) * 512],
                                                            start=(oc == 0), stop=(oc == NIC - 1))
                                    c.inc(ins, 'opt', 1)
                        blk.tensor(s_tensor)

                        def s_vector(vector):
                            for och in range(NCH):
                                vector.wait_ge(c.S('opt'), b0['opt'] + (och + 1) * NIC)
                                vector.drain()
                                c.inc(vector.tensor_tensor(q_d[:], pD[och % 2][:], delb[:], ALU.mult), 'dfv', 1)
                                vector.wait_ge(c.S('dfs'), b0['dfs'] + och + 1)
                                vector.wait_ge(c.S('dal0'), b0['dal0'] + 16 * (och + 1))
                                vector.wait_ge(c.S('dal1'), b0['dal1'] + 16 * (och + 1))
                                if och >= 1:
                                    vector.wait_ge(c.S('fin'), b0['fin'] + 16 * och)
                                    if DBG:
                                        vector.wait_ge(c.S('dbgg'), b0['dbgg'] + 16 * och)
                                vector.drain()
                                c.inc(vector.tensor_copy(q_d[:], s1st[:]), 'dfv', 1)
                                vector.drain()
                                vector.tensor_tensor(q_sg[:], q_d[:], q_f[:], ALU.mult)
                                vector.drain()
                                c.inc(vector.tensor_tensor(q_sg[:], gul[:], q_sg[:], ALU.add), 'dfv', 2)
                        blk.vector(s_vector)

                        def s_scalar(scalar):
                            for och in range(NCH):
                                scalar.wait_ge(c.S('dfv'), b0['dfv'] + 4 * och + 1)
                                if DBG and och >= 1:
                                    scalar.wait_ge(c.S('dbgg'), b0['dbgg'] + 16 * och)
                                c.inc(scalar.activation(q_f[:], q_d[:], AT.Silu), 'dfs', 1)
                        blk.scalar(s_scalar)

                        def s_gp(gp):
                            for och in range(NCH):
                                gp.wait_ge(c.S('dfv'), b0['dfv'] + 4 * (och + 1))
                                if och >= 1:
                                    gp.wait_ge(c.S('fin'), b0['fin'] + 16 * och)
                                c.inc(gp.dma_start(out=out_t.ap()[s, och * 128:(och + 1) * 128, :],
                                                   in_=q_sg[:]), 'fin', 16)
                                if DBG:
                                    if och >= 1:
                                        gp.wait_ge(c.S('dbgg'), b0['dbgg'] + 16 * och)
                                    c.inc(gp.dma_start(out=dbg['g1'].ap()[s, och], in_=q_f[:]), 'dbgg', 16)
                        blk.gpsimd(s_gp)
            c.dma_fence('fin', 'dbgg', 'dal0', 'dal1')

    for s in range(NSEQ):
        with ExitStack() as sps1:
            xn0 = sps1.enter_context(nc.sbuf_tensor(f"xn0_{s}", [128, NCH * T], BF16))
            w_qkv = [sps1.enter_context(nc.sbuf_tensor(f"wqkv{k}_{s}", [128, NCH * D], BF16)) for k in range(3)]
            with nc.named_scope("p1a"):
                load_weights_bf16(sps1, f"l0w_{s}", [("wqT", w_qkv[0], NCH), ("wkT", w_qkv[1], NCH), ("wvT", w_qkv[2], NCH)])
                p1a(s, xn0)
            with nc.named_scope("l0proj"):
                l0_proj(s, xn0, w_qkv)
        with ExitStack() as sps2:
            s0slab = sps2.enter_context(nc.sbuf_tensor(f"s0slab_{s}", [128, NCH * T], BF16))
            l0_attn(s, s0slab)
            with nc.named_scope("p1d"):
                p1d(s, s0slab)
        p2_l1attn(s)
        with nc.named_scope("p4"):
            p4_l1ffn(s)

    with nc.Block() as blk:
        def s_gp(g):
            c.wait(g, 'fin')
        blk.gpsimd(s_gp)

    return nc, st, c, dbg



def _host_consts():
    m = {}
    m["ident"] = np.eye(128).astype(ml_dtypes.bfloat16)
    m["triu64"] = np.triu(np.ones((64, 64), np.float32)).astype(ml_dtypes.bfloat16)
    t128 = np.zeros((128, 128), np.float32)
    t128[0:64, 0:64] = np.triu(np.ones((64, 64)))
    t128[0:64, 64:128] = 1.0
    t128[64:128, 64:128] = np.triu(np.ones((64, 64)))
    m["triu128p"] = t128.astype(ml_dtypes.bfloat16)
    e = np.zeros((16, 8 * 128), np.float32)
    for vch in range(8):
        for p in range(128):
            e[2 * vch + p // 64, vch * 128 + p] = 1.0
    m["ehd"] = e
    p = np.arange(128)[:, None]
    w = np.arange(276)[None, :]
    m["adm"] = (p + w <= 275).astype(ml_dtypes.bfloat16)
    return m


def _host_inputs(inputs, core):
    x = inputs["L0_attn_hidden_states"]
    xc = np.ascontiguousarray(np.transpose(np.asarray(x)[2 * core:2 * core + 2], (0, 2, 1))).astype(np.float32)
    m = {"x": xc}
    m["wqT"] = np.ascontiguousarray(np.asarray(inputs["L0_attn_Wq"]).T.astype(np.float32))
    m["wkT"] = np.ascontiguousarray(np.asarray(inputs["L0_attn_Wk"]).T.astype(np.float32))
    m["wvT"] = np.ascontiguousarray(np.asarray(inputs["L0_attn_Wv"]).T.astype(np.float32))
    m["woT"] = np.ascontiguousarray(np.asarray(inputs["L0_attn_Wo"]).T.astype(np.float32))
    m["rhsw"] = np.ascontiguousarray(np.asarray(inputs["L0_ffn_rhs"]).astype(np.float32))
    m["wq2T"] = np.ascontiguousarray(np.asarray(inputs["L1_attn_Wq"]).T.astype(np.float32))
    m["wk2T"] = np.ascontiguousarray(np.asarray(inputs["L1_attn_Wk"]).T.astype(np.float32))
    m["wv2T"] = np.ascontiguousarray(np.asarray(inputs["L1_attn_Wv"]).T.astype(np.float32))
    m["wo2T"] = np.ascontiguousarray(np.asarray(inputs["L1_attn_Wo"]).T.astype(np.float32))
    m["wgT"] = np.ascontiguousarray(np.asarray(inputs["L1_ffn_w_gate"]).T.astype(np.float32))
    m["wuT"] = np.ascontiguousarray(np.asarray(inputs["L1_ffn_w_up"]).T.astype(np.float32))
    m["wdT"] = np.ascontiguousarray(np.asarray(inputs["L1_ffn_w_down"]).T.astype(np.float32))
    bias = np.stack([np.asarray(inputs["L0_attn_bq"]), np.asarray(inputs["L0_attn_bk"]),
                     np.asarray(inputs["L0_attn_bv"]), np.asarray(inputs["L0_attn_bo"])], 0).astype(np.float32)
    bcols = np.zeros((128, 4 * NCH), np.float32)
    for k in range(4):
        for ch in range(NCH):
            bcols[:, k * NCH + ch] = bias[k, ch * 128:(ch + 1) * 128]
    m["bqkvo"] = bcols
    m["relk"] = np.ascontiguousarray(np.asarray(inputs["L0_attn_rel_k"]).astype(np.float32)[::-1])
    m["relv"] = np.ascontiguousarray(np.asarray(inputs["L0_attn_rel_v"]).astype(np.float32))
    m["relvf"] = np.ascontiguousarray(np.asarray(inputs["L0_attn_rel_v"]).astype(np.float32)[::-1])
    m["gb16"] = np.stack([np.asarray(inputs["L1_attn_gamma"]),
                          np.asarray(inputs["L1_attn_beta"])], 0).astype(np.float32)
    m.update(_host_consts())
    return m


_NC_CACHE = {}


def kernel(**inputs):
    if "nc" not in _NC_CACHE:
        _NC_CACHE["nc"] = build_nc()
    nc, st_, c, dbg = _NC_CACHE["nc"]
    in_maps = [_host_inputs(inputs, core) for core in range(NCORES)]
    res = run_bass_kernel_spmd(nc, in_maps, core_ids=list(range(NCORES)))
    outs = []
    for core in range(NCORES):
        o = np.asarray(res.results[core]["out"])
        outs.append(np.transpose(o, (0, 2, 1)))
    return np.concatenate(outs, 0).astype(np.float32)
